# version 02
# baseline (speedup 1.0000x reference)
import math
import numpy as np
from contextlib import ExitStack

import ml_dtypes
import concourse.bass as bass
import concourse.mybir as mybir
from concourse.bass import AP
from concourse.bass_utils import run_bass_kernel_spmd



ENGS = ("tensor", "vector", "scalar", "gpsimd", "sync")
COMPUTE_ENGS = ("tensor", "vector", "scalar", "gpsimd")
OPS = {
    "tensor_tensor", "tensor_scalar", "tensor_copy", "tensor_reduce", "reduce_sum",
    "reduce_max", "scalar_tensor_tensor", "tensor_tensor_reduce", "bn_stats",
    "bn_aggr", "reciprocal", "memset", "select", "copy_predicated", "iota",
    "activation", "copy", "add", "mul", "sign", "sqrt", "square",
    "matmul", "transpose", "ldweights", "tensor_tensor_scan", "stream_shuffle",
    "tensor_add", "tensor_mul", "tensor_sub", "tensor_max", "tensor_relu",
    "tensor_scalar_mul", "tensor_scalar_add", "partition_broadcast",
    "partition_all_reduce",
}


class Eng:

    def __init__(self, fw, name, handle):
        self._fw = fw
        self._name = name
        self._h = handle
        self.last = None

    def __getattr__(self, attr):
        real = getattr(self._h, attr)
        if attr not in OPS:
            return real
        fw, name = self._fw, self._name

        def call(*a, **k):
            inst = real(*a, **k)
            self.last = inst
            if name in ("vector", "scalar", "gpsimd"):
                nb = fw.n[name]
                if nb > 0:
                    inst._wait_ge(fw.F[name], nb)
                inst.then_inc(fw.F[name], 1)
                fw.n[name] += 1
            return inst
        return call

    def mark(self):
        assert self._name == "tensor"
        assert self.last is not None
        self.last.then_inc(self._fw.F["tensor"], 1)
        self._fw.n["tensor"] += 1
        self.last = None
        return self._fw.n["tensor"]


class FW:
    def __init__(self, nc, st, blk):
        self.nc = nc
        self.blk = blk
        self.F = {e: st.enter_context(nc.semaphore(f"F_{e}")) for e in COMPUTE_ENGS}
        self.n = {e: 0 for e in COMPUTE_ENGS}
        self.Dq = {q: st.enter_context(nc.semaphore(f"D_{q}")) for q in ("sync", "gpsimd")}
        self.dn = {q: 0 for q in ("sync", "gpsimd")}
        self._eng_cache = {}

    def _sect(self, name, body):
        getattr(self.blk, name)(body)

    def sec(self, name):
        raise NotImplementedError

    def run(self, name, fn):
        out = {}

        def body(h):
            out["p"] = Eng(self, name, h)
            fn(out["p"])
        self._sect(name, body)
        return out.get("p")

    def dma(self, qname, out_ap, in_ap, wait=()):
        def body(h):
            for (e, m) in wait:
                if m > 0:
                    if e.startswith("q:"):
                        h.wait_ge(self.Dq[e[2:]], m)
                    else:
                        h.wait_ge(self.F[e], m)
            h.dma_start(out=out_ap, in_=in_ap).then_inc(self.Dq[qname], 16)
        self._sect(qname, body)
        self.dn[qname] += 16
        return self.dn[qname]

    def wait(self, eng, deps=(), dmas=()):
        def body(h):
            for (e, m) in deps:
                if m > 0:
                    h.wait_ge(self.F[e], m)
            for (q, c) in dmas:
                if c > 0:
                    h.wait_ge(self.Dq[q], c)
        self._sect(eng, body)

    def marks(self):
        return dict(self.n)

    def barrier(self):
        snap_n = dict(self.n)
        snap_d = dict(self.dn)
        for e in COMPUTE_ENGS:
            def body(h, e=e):
                for o in COMPUTE_ENGS:
                    if o != e and snap_n[o] > 0:
                        h.wait_ge(self.F[o], snap_n[o])
                for q in ("sync", "gpsimd"):
                    if snap_d[q] > 0:
                        h.wait_ge(self.Dq[q], snap_d[q])
            self._sect(e, body)
        for q in ("sync", "gpsimd"):
            def body(h, q=q):
                for o in COMPUTE_ENGS:
                    if snap_n[o] > 0:
                        h.wait_ge(self.F[o], snap_n[o])
                oq = "gpsimd" if q == "sync" else "sync"
                if snap_d[oq] > 0:
                    h.wait_ge(self.Dq[oq], snap_d[oq])
            self._sect(q, body)


def _part2_emit(ctx):
    fw = ctx["fw"]; nc = ctx["nc"]
    VE, SC, TE, GP = ctx["VE"], ctx["SC"], ctx["TE"], ctx["GP"]
    R1, R2, R3, AUX = ctx["R1"], ctx["R2"], ctx["R3"], ctx["AUX"]
    IDENT, IDENTF, IOTAS, ONEC = ctx["IDENT"], ctx["IDENTF"], ctx["IOTAS"], ctx["ONEC"]
    BOB, SML, STT = ctx["BOB"], ctx["SML"], ctx["STT"]
    ASC, BSC, CBC, HST, ZST, GST, T1S, HCS = (ctx["ASC"], ctx["BSC"], ctx["CBC"],
                                              ctx["HST"], ctx["ZST"], ctx["GST"],
                                              ctx["T1S"], ctx["HCS"])
    P0, P1, P2, P3, P4, P5 = ctx["P0"], ctx["P1"], ctx["P2"], ctx["P3"], ctx["P4"], ctx["P5"]
    PB0, PB1, PW = ctx["PB0"], ctx["PB1"], ctx["PW"]
    OTS = None
    W_out_a, tern, w_in_b, w_out_b = ctx["W_out_a"], ctx["tern"], ctx["w_in_b"], ctx["w_out_b"]
    A_b, Bm_b, Cm_b = ctx["A_b"], ctx["Bm_b"], ctx["Cm_b"]
    x_in, out = ctx["x_in"], ctx["out"]
    hb_winT = ctx["hb_winT"]
    dbgf = ctx["dbg"]
    PHASES = ctx["PHASES"] if "PHASES" in ctx else 99

    def sbslice(col0, col1):
        return AUX[:, col0:col1]

    def wprep(Wsrc, dst, is_tern=False):
        WSTG = AUX[:, 8192:10240]
        for oc in range(8):
            slot = oc % 2
            stg = WSTG[:, slot * 1024:(slot + 1) * 1024]
            mprev = fw.marks()
            nld = fw.dma("gpsimd", stg, Wsrc[oc * 128:(oc + 1) * 128, :],
                         wait=[("vector", mprev["vector"]), ("tensor", mprev["tensor"])])
            if is_tern:
                def cvt(v, stg=stg, nld=nld):
                    v.wait_ge(fw.Dq["gpsimd"], nld)
                    v.tensor_scalar(out=stg, in0=stg, scalar1=-1.0, scalar2=None, op0=ALU.add)
                VE(cvt)
            mk = fw.marks()

            def trw(t, stg=stg, nld=nld, mk=mk, is_tern=is_tern):
                if not is_tern:
                    t.wait_ge(fw.Dq["gpsimd"], nld)
                else:
                    t.wait_ge(fw.F["vector"], mk["vector"])
                for k in range(8):
                    t.transpose(PB0[:, k * 128:(k + 1) * 128], stg[:, k * 128:(k + 1) * 128], IDENT[:])
                t.mark()
            TE(trw)
            mt = fw.marks()["tensor"]

            def evw(v, oc=oc, mt=mt, dst=dst):
                v.wait_ge(fw.F["tensor"], mt)
                dap = AP(tensor=dst.tensor, offset=dst.offset + oc * 128,
                         ap=[[dst.ap[0][0], 128], [1024, 8], [1, 128]])
                v.tensor_copy(out=dap, in_=PB0[:].rearrange("p (a b) -> p a b", a=8))
            VE(evw)

    WoT = AUX[:, 0:8192]
    XS = AUX[:, 10240:14336]
    X1S = AUX[:, 14336:18432]
    XTT = AUX[:, 18432:20480]
    wprep(W_out_a, WoT)
    for tau in range(8):
        b, q = divmod(tau, 4)
        for j in range(4):
            c = q * 4 + j
            slot = j % 2
            xstg = XTT[:, slot * 1024:(slot + 1) * 1024]
            mprev = fw.marks()
            nld = fw.dma("gpsimd", xstg, x_in[b, c * 128:(c + 1) * 128, :],
                         wait=[("tensor", mprev["tensor"]), ("vector", mprev["vector"])])

            def trx(t, xstg=xstg, nld=nld):
                t.wait_ge(fw.Dq["gpsimd"], nld)
                for k in range(8):
                    t.transpose(PB1[:, k * 128:(k + 1) * 128], xstg[:, k * 128:(k + 1) * 128], IDENT[:])
                t.mark()
            TE(trx)
            mt = fw.marks()["tensor"]

            def evx(v, j=j, mt=mt):
                v.wait_ge(fw.F["tensor"], mt)
                dap = AP(tensor=XS.tensor, offset=XS.offset + j * 128,
                         ap=[[XS.ap[0][0], 128], [512, 8], [1, 128]])
                v.tensor_copy(out=dap, in_=PB1[:].rearrange("p (a b) -> p a b", a=8))
            VE(evx)
        mxs = fw.marks()
        for w in range(2):
            mev = fw.marks()

            def wmm(t, b=b, q=q, w=w, mev=mev):
                t.wait_ge(fw.F["vector"], mev["vector"])
                for oci in range(4):
                    oc = w * 4 + oci
                    for k in range(8):
                        t.matmul(PW[oci][:], WoT[:, (k * 8 + oc) * 128:(k * 8 + oc + 1) * 128],
                                 R2[:, (b * 8 + k) * 2048 + q * 512:(b * 8 + k) * 2048 + (q + 1) * 512],
                                 start=(k == 0), stop=(k == 7))
                t.mark()
            TE(wmm)
            mw = fw.marks()["tensor"]

            def wev(v, w=w, mw=mw):
                v.wait_ge(fw.F["tensor"], mw)
                for oci in range(4):
                    oc = w * 4 + oci
                    v.scalar_tensor_tensor(out=X1S[:, oc * 512:(oc + 1) * 512], in0=PW[oci][:],
                                           scalar=BOB[:, oc:oc + 1], in1=XS[:, oc * 512:(oc + 1) * 512],
                                           op0=ALU.add, op1=ALU.add)
            VE(wev)

        def x1wr(v, b=b, q=q):
            dap = AP(tensor=R2[:].tensor, offset=R2[:].offset + b * 16384 + q * 512,
                     ap=[[32768, 128], [2048, 8], [1, 512]])
            v.tensor_copy(out=dap, in_=X1S[:].rearrange("p (a b) -> p a b", a=8))
        VE(x1wr)
    fw.barrier()
    if PHASES < 5:
        return

    def sbar_prep():
        def v(v):
            v.tensor_scalar(out=SML[0:1, 49:50], in0=SML[0:1, 48:49], scalar1=1e-5,
                            scalar2=None, op0=ALU.max)
            v.reciprocal(out=SML[0:1, 50:51], in_=SML[0:1, 49:50])
        VE(v)
        mk = fw.marks()

        def t(t):
            t.wait_ge(fw.F["vector"], mk["vector"])
            t.matmul(P4[:, 0:1], IOTAS[0:1, 384:512], SML[0:1, 49:50], start=True, stop=True)
            t.mark()
        TE(t)
        mt = fw.marks()["tensor"]

        def v2(v):
            v.wait_ge(fw.F["tensor"], mt)
            v.tensor_copy(out=SML[:, 51:52], in_=P4[:, 0:1])
        VE(v2)
    ffn_state = {"sbar_done": False}

    def emit_ffn():
        TTW = AUX[:, 0:8192]
        XQ = AUX[:, 10240:14336]
        QF1 = AUX[:, 14336:15360].bitcast(F32)
        QF2 = AUX[:, 15360:16384].bitcast(F32)
        EPS_ = AUX[:, 16384:20480]
        SQS = EPS_[:, 0:1024]
        BCSM = AUX[:, 8192:9216].bitcast(F32)
        BCSR = AUX[:, 9216:10240].bitcast(F32)
        RWB = AUX[0:1, 16384:17408].bitcast(F32)
        wprep(tern, TTW, is_tern=True)
        if not ffn_state["sbar_done"]:
            sbar_prep()
            ffn_state["sbar_done"] = True
        for tau in range(8):
            b, q = divmod(tau, 4)

            def xdo(k):
                return R2[:, (b * 8 + k) * 2048 + q * 512:(b * 8 + k) * 2048 + (q + 1) * 512]
            smm_marks = {}
            for do in range(8):
                slot = SQS[:, (do % 2) * 512:(do % 2) * 512 + 512]

                def sq(v, do=do, slot=slot):
                    if do >= 2:
                        v.wait_ge(fw.F["tensor"], smm_marks[do - 2])
                    v.tensor_tensor(out=slot, in0=xdo(do), in1=xdo(do), op=ALU.mult)
                VE(sq)
                mq = fw.marks()

                def smm(t, do=do, mq=mq, slot=slot):
                    t.wait_ge(fw.F["vector"], mq["vector"])
                    t.matmul(P4[0:1, 0:512], ONEC[:, 0:1], xdo(do), start=(do == 0), stop=(do == 7))
                    t.matmul(P5[0:1, 0:512], ONEC[:, 0:1], slot, start=(do == 0), stop=(do == 7))
                    t.mark()
                TE(smm)
                smm_marks[do] = fw.marks()["tensor"]
            ms = fw.marks()["tensor"]

            def spost(v, ms=ms):
                v.wait_ge(fw.F["tensor"], ms)
                v.tensor_scalar(out=QF1[0:1, 0:512], in0=P4[0:1, 0:512], scalar1=1.0 / D,
                                scalar2=None, op0=ALU.mult)
                v.tensor_tensor(out=QF2[0:1, 0:512], in0=QF1[0:1, 0:512], in1=QF1[0:1, 0:512], op=ALU.mult)
                v.scalar_tensor_tensor(out=QF2[0:1, 0:512], in0=P5[0:1, 0:512], scalar=1.0 / D,
                                       in1=QF2[0:1, 0:512], op0=ALU.mult, op1=ALU.subtract)
                v.tensor_scalar(out=QF2[0:1, 0:512], in0=QF2[0:1, 0:512], scalar1=EPS_LN,
                                scalar2=None, op0=ALU.add)
                v.reciprocal(out=RWB[:], in_=QF2[0:1, 0:512])
            VE(spost)
            mv = fw.marks()

            def ssq(s, mv=mv):
                s.wait_ge(fw.F["vector"], mv["vector"])
                s.activation(out=QF2[0:1, 0:512], in_=RWB[:], func=AF.Sqrt)
            SC(ssq)
            mss = fw.marks()

            def rfold(v, mss=mss):
                v.wait_ge(fw.F["scalar"], mss["scalar"])
                v.tensor_scalar(out=QF2[0:1, 0:512], in0=QF2[0:1, 0:512],
                                scalar1=SML[0:1, 50:51], scalar2=None, op0=ALU.mult)
            VE(rfold)
            mr = fw.marks()

            def bcmm(t, mr=mr):
                t.wait_ge(fw.F["vector"], mr["vector"])
                t.matmul(P4[:], IOTAS[0:1, 384:512], QF1[0:1, 0:512], start=True, stop=True)
                t.matmul(P5[:], IOTAS[0:1, 384:512], QF2[0:1, 0:512], start=True, stop=True)
                t.mark()
            TE(bcmm)
            mbc = fw.marks()["tensor"]

            def bcev(v, mbc=mbc):
                v.wait_ge(fw.F["tensor"], mbc)
                v.tensor_copy(out=BCSM[:], in_=P4[:])
                v.tensor_copy(out=BCSR[:], in_=P5[:])
            VE(bcev)
            for do in range(8):
                def qn(v, do=do):
                    v.tensor_tensor(out=QF1[:, 0:512], in0=xdo(do), in1=BCSM[:], op=ALU.subtract)
                    v.tensor_tensor(out=QF1[:, 0:512], in0=QF1[:, 0:512], in1=BCSR[:], op=ALU.mult)
                    v.tensor_scalar(out=QF1[:, 0:512], in0=QF1[:, 0:512], scalar1=127.0,
                                    scalar2=-127.0, op0=ALU.min, op1=ALU.max)
                    v.tensor_scalar(out=QF2[:, 0:512], in0=QF1[:, 0:512], scalar1=MAGIC,
                                    scalar2=-MAGIC, op0=ALU.add, op1=ALU.add)
                    v.tensor_scalar(out=XQ[:, do * 512:(do + 1) * 512], in0=QF1[:, 0:512],
                                    scalar1=MAGIC64, scalar2=-MAGIC64, op0=ALU.add, op1=ALU.add)
                    v.tensor_tensor(out=QF1[:, 0:512], in0=XQ[:, do * 512:(do + 1) * 512],
                                    in1=QF2[:, 0:512], op=ALU.subtract)
                    v.scalar_tensor_tensor(out=XQ[:, do * 512:(do + 1) * 512], in0=QF1[:, 0:512],
                                           scalar=SML[:, 53:54], in1=QF2[:, 0:512],
                                           op0=ALU.mult, op1=ALU.add)
                VE(qn)
            for w in range(2):
                mev = fw.marks()

                def fmm(t, w=w, mev=mev):
                    t.wait_ge(fw.F["vector"], mev["vector"])
                    for oci in range(4):
                        oc = w * 4 + oci
                        for k in range(8):
                            t.matmul(PW[oci][:], TTW[:, (k * 8 + oc) * 128:(k * 8 + oc + 1) * 128],
                                     XQ[:, k * 512:(k + 1) * 512], start=(k == 0), stop=(k == 7))
                    t.mark()
                TE(fmm)
                mw = fw.marks()["tensor"]

                def fev(v, w=w, mw=mw, b=b, q=q):
                    v.wait_ge(fw.F["tensor"], mw)
                    for oci in range(4):
                        oc = w * 4 + oci
                        v.tensor_scalar(out=QF1[:, 0:512], in0=PW[oci][:], scalar1=SML[:, 51:52],
                                        scalar2=BOB[:, 8 + oc:8 + oc + 1], op0=ALU.mult, op1=ALU.add)
                        v.tensor_tensor(out=EPS_[:, oc * 512:(oc + 1) * 512], in0=QF1[:, 0:512],
                                        in1=xdo(oc), op=ALU.add)
                VE(fev)

            def x2wr(v, b=b, q=q):
                dap = AP(tensor=R2[:].tensor, offset=R2[:].offset + b * 16384 + q * 512,
                         ap=[[32768, 128], [2048, 8], [1, 512]])
                v.tensor_copy(out=dap, in_=EPS_[:].rearrange("p (a b) -> p a b", a=8))
            VE(x2wr)
        fw.barrier()

    emit_ffn()
    if PHASES < 6:
        return

    WSTG = AUX[:, 8192:10240]
    for et in range(16):
        slot = et % 2
        stg = WSTG[:, slot * 1024:(slot + 1) * 1024]
        mprev = fw.marks()
        nld = fw.dma("gpsimd", stg, w_in_b[et * 128:(et + 1) * 128, :],
                     wait=[("vector", mprev["vector"]), ("tensor", mprev["tensor"])])

        def trwi(t, stg=stg, nld=nld):
            t.wait_ge(fw.Dq["gpsimd"], nld)
            for k in range(8):
                t.transpose(PB0[:, k * 128:(k + 1) * 128], stg[:, k * 128:(k + 1) * 128], IDENT[:])
            t.mark()
        TE(trwi)
        mt = fw.marks()["tensor"]

        def evwi(v, et=et, mt=mt):
            v.wait_ge(fw.F["tensor"], mt)
            v.tensor_copy(out=AUX[:, 10240:11264], in_=PB0[:])
        VE(evwi)
        mv2 = fw.marks()
        fw.dma("gpsimd", AP(tensor=hb_winT.tensor, offset=et * 128,
                            ap=[[2048, 128], [128 * 2048, 8], [1, 128]]),
               AUX[:, 10240:11264].rearrange("p (a b) -> p a b", a=8),
               wait=[("vector", mv2["vector"])])
    nwint = fw.dn["gpsimd"]
    fw.wait("vector", dmas=[("gpsimd", nwint)])
    WIN2 = AUX[:, 0:4096]
    XSUB = AUX[:, 4096:8192]
    BCS4M = AUX[:, 8192:9216].bitcast(F32)
    BCS4R = AUX[:, 9216:10240].bitcast(F32)
    XN = AUX[:, 10240:14336]
    QF4 = AUX[:, 14336:15360].bitcast(F32)
    QF5 = AUX[:, 15360:16384].bitcast(F32)
    SQ4 = AUX[:, 16384:17408]
    RW3 = AUX[0:1, 17408:18432].bitcast(F32)

    for tau in range(4):
        b, qs = divmod(tau, 2)

        def xe(do):
            base = (b * 8 + do) * 2048 + qs * 1024
            return AP(tensor=R2[:].tensor, offset=R2[:].offset + base,
                      ap=[[32768, 128], [2, 512]])
        smm3_marks = {}
        for do in range(8):
            slot = SQ4[:, (do % 2) * 512:(do % 2) * 512 + 512]

            def cpx(v, do=do, slot=slot):
                if do >= 2:
                    v.wait_ge(fw.F["tensor"], smm3_marks[do - 2])
                v.tensor_copy(out=XSUB[:, do * 512:(do + 1) * 512], in_=xe(do))
                v.tensor_tensor(out=slot, in0=XSUB[:, do * 512:(do + 1) * 512],
                                in1=XSUB[:, do * 512:(do + 1) * 512], op=ALU.mult)
            VE(cpx)
            mq = fw.marks()

            def smm3(t, do=do, mq=mq, slot=slot):
                t.wait_ge(fw.F["vector"], mq["vector"])
                t.matmul(P4[0:1, 0:512], ONEC[:, 0:1], XSUB[:, do * 512:(do + 1) * 512],
                         start=(do == 0), stop=(do == 7))
                t.matmul(P5[0:1, 0:512], ONEC[:, 0:1], slot, start=(do == 0), stop=(do == 7))
                t.mark()
            TE(smm3)
            smm3_marks[do] = fw.marks()["tensor"]
        ms3 = fw.marks()["tensor"]

        def spost3(v, ms3=ms3):
            v.wait_ge(fw.F["tensor"], ms3)
            v.tensor_scalar(out=QF4[0:1, 0:512], in0=P4[0:1, 0:512], scalar1=1.0 / D,
                            scalar2=None, op0=ALU.mult)
            v.tensor_tensor(out=QF5[0:1, 0:512], in0=QF4[0:1, 0:512], in1=QF4[0:1, 0:512], op=ALU.mult)
            v.scalar_tensor_tensor(out=QF5[0:1, 0:512], in0=P5[0:1, 0:512], scalar=1.0 / D,
                                   in1=QF5[0:1, 0:512], op0=ALU.mult, op1=ALU.subtract)
            v.tensor_scalar(out=QF5[0:1, 0:512], in0=QF5[0:1, 0:512], scalar1=1.0 + EPS_RMS,
                            scalar2=EPS_RMS * EPS_LN, op0=ALU.mult, op1=ALU.add)
            v.reciprocal(out=RW3[:], in_=QF5[0:1, 0:512])
        VE(spost3)
        mv3 = fw.marks()

        def ssq3(s, mv3=mv3):
            s.wait_ge(fw.F["vector"], mv3["vector"])
            s.activation(out=QF5[0:1, 0:512], in_=RW3[:], func=AF.Sqrt)
        SC(ssq3)
        ms4 = fw.marks()

        def bc3(t, ms4=ms4):
            t.wait_ge(fw.F["scalar"], ms4["scalar"])
            t.matmul(P4[:], IOTAS[0:1, 384:512], QF4[0:1, 0:512], start=True, stop=True)
            t.matmul(P5[:], IOTAS[0:1, 384:512], QF5[0:1, 0:512], start=True, stop=True)
            t.mark()
        TE(bc3)
        mb3 = fw.marks()["tensor"]

        def bce3(v, mb3=mb3):
            v.wait_ge(fw.F["tensor"], mb3)
            v.tensor_copy(out=BCS4M[:], in_=P4[:])
            v.tensor_copy(out=BCS4R[:], in_=P5[:])
        VE(bce3)
        for do in range(8):
            def nrm(v, do=do):
                v.tensor_tensor(out=QF4[:], in0=XSUB[:, do * 512:(do + 1) * 512],
                                in1=BCS4M[:], op=ALU.subtract)
                v.scalar_tensor_tensor(out=XN[:, do * 512:(do + 1) * 512], in0=QF4[:],
                                       scalar=BOB[:, 24 + do:24 + do + 1], in1=BCS4R[:],
                                       op0=ALU.mult, op1=ALU.mult)
            VE(nrm)
        for w in range(4):
            mev = fw.marks()
            for k in range(8):
                slot = k % 2
                wk = WIN2[:, slot * 2048:(slot + 1) * 2048]
                mprev2 = fw.marks()
                nldk = fw.dma("gpsimd", wk, hb_winT[k],
                              wait=[("tensor", mprev2["tensor"]), ("q:gpsimd", nwint)])

                def wmm4(t, w=w, k=k, wk=wk, nldk=nldk, mev=mev):
                    if k == 0:
                        t.wait_ge(fw.F["vector"], mev["vector"])
                    t.wait_ge(fw.Dq["gpsimd"], nldk)
                    for eci in range(4):
                        ec = w * 4 + eci
                        t.matmul(PW[eci][:], wk[:, ec * 128:(ec + 1) * 128],
                                 XN[:, k * 512:(k + 1) * 512], start=(k == 0), stop=(k == 7))
                    t.mark()
                TE(wmm4)
            mw4 = fw.marks()["tensor"]

            def wev4(v, w=w, mw4=mw4, b=b, qs=qs):
                v.wait_ge(fw.F["tensor"], mw4)
                for eci in range(4):
                    ec = w * 4 + eci
                    v.tensor_scalar(out=R1[:, (b * 16 + ec) * 1024 + qs * 512:(b * 16 + ec) * 1024 + (qs + 1) * 512],
                                    in0=PW[eci][:], scalar1=SML[:, 16 + ec:16 + ec + 1],
                                    scalar2=None, op0=ALU.add)
            VE(wev4)
    fw.barrier()

    for do in range(8):
        for b in range(B2):
            fw.dma("sync", ASC[:, (b * 8 + do) * 64:(b * 8 + do + 1) * 64],
                   A_b[do * 128:(do + 1) * 128, :])
            fw.dma("sync", BSC[:, (b * 8 + do) * 64:(b * 8 + do + 1) * 64],
                   Bm_b[do * 128:(do + 1) * 128, :])
    for do in range(8):
        fw.dma("sync", GST[:, do * 64:(do + 1) * 64], Cm_b[do * 128:(do + 1) * 128, :])
    ncm = fw.dn["sync"]

    def onef(v):
        v.memset(STT[:, 0:1], 1.0)
        v.memset(HST[:], 0.0)
    VE(onef)
    monef = fw.marks()

    def csum_mm(t, monef=monef):
        t.wait_ge(fw.Dq["sync"], ncm)
        t.wait_ge(fw.F["vector"], monef["vector"])
        for do in range(8):
            t.matmul(P4[0:1, 0:64], STT[:, 0:1], GST[:, do * 64:(do + 1) * 64],
                     start=(do == 0), stop=(do == 7))
        t.mark()
    TE(csum_mm)
    mcs = fw.marks()["tensor"]

    def csum_ev(v, mcs=mcs):
        v.wait_ge(fw.F["tensor"], mcs)
        v.tensor_copy(out=QF4[0:1, 0:64], in_=P4[0:1, 0:64])
    VE(csum_ev)
    mce = fw.marks()

    def cbc_mm(t, mce=mce):
        t.wait_ge(fw.F["vector"], mce["vector"])
        t.matmul(P5[:, 0:64], IOTAS[0:1, 384:512], QF4[0:1, 0:64], start=True, stop=True)
        t.mark()
    TE(cbc_mm)
    mcb = fw.marks()["tensor"]

    def cbc_ev(v, mcb=mcb):
        v.wait_ge(fw.F["tensor"], mcb)
        v.tensor_copy(out=CBC[:], in_=P5[:, 0:64])
    VE(cbc_ev)
    fw.barrier()
    if PHASES < 7:
        return

    def xt_bc(t):
        return AP(tensor=R1[:].tensor, offset=R1[:].offset + t,
                  ap=[[32768, 128], [16384, 2], [1024, 8], [0, 64]])

    cbc_ap = AP(tensor=CBC[:].tensor, offset=CBC[:].offset,
                ap=[[CBC[:].ap[0][0], 128], [0, 2], [0, 8], [1, 64]])
    hprev = {"m": fw.marks()["scalar"]}
    for t_ in range(1024):
        def step_v(v, t_=t_):
            v.wait_ge(fw.F["scalar"], hprev["m"])
            v.tensor_tensor(out=GST[:], in0=BSC[:], in1=xt_bc(t_), op=ALU.mult)
            v.tensor_tensor(out=T1S[:], in0=HST[:], in1=ASC[:], op=ALU.mult)
            v.tensor_tensor(out=ZST[:], in0=T1S[:], in1=GST[:], op=ALU.add)
        VE(step_v)
        mz = fw.marks()["vector"]

        def step_s(s, mz=mz):
            s.wait_ge(fw.F["vector"], mz)
            s.activation(out=HST[:], in_=ZST[:], func=AF.Tanh)
        SC(step_s)
        hprev["m"] = fw.marks()["scalar"]

        def step_y(v, t_=t_, mh=hprev["m"]):
            v.wait_ge(fw.F["scalar"], mh)
            v.tensor_tensor(out=HCS[:], in0=HST[:], in1=cbc_ap, op=ALU.mult)
            ysap = AP(tensor=R3[:].tensor, offset=R3[:].offset + t_,
                      ap=[[16384, 128], [8192, 2], [1024, 8]])
            with nc.allow_low_precision(reason="ys store bf16; DVE reduce accumulates fp32"):
                v.tensor_reduce(out=ysap, in_=HCS[:].rearrange("p (a b c) -> p a b c", a=2, b=8),
                                axis=AX.X, op=ALU.add)
        VE(step_y)
    fw.barrier()
    if PHASES < 8:
        return

    for b in range(B2):
        for do in range(8):
            xp = R1[:, (b * 16 + do) * 1024:(b * 16 + do + 1) * 1024]
            gcol = (b * 16 + 8 + do)
            gate = R1[:, gcol * 1024:(gcol + 1) * 1024]
            ys = R3[:, b * 8192 + do * 1024:b * 8192 + (do + 1) * 1024]

            mpv = fw.marks()["vector"]

            def silu(s, gate=gate, mpv=mpv):
                s.wait_ge(fw.F["vector"], mpv)
                s.activation(out=GST[:], in_=gate, func=AF.Silu)
            SC(silu)
            msl = fw.marks()["scalar"]

            def opre(v, xp=xp, gate=gate, ys=ys, do=do, msl=msl):
                v.wait_ge(fw.F["scalar"], msl)
                v.tensor_scalar(out=T1S[:], in0=xp, scalar1=BOB[:, 16 + do:16 + do + 1],
                                scalar2=None, op0=ALU.mult)
                v.tensor_tensor(out=HCS[:], in0=ys, in1=T1S[:], op=ALU.add)
                v.tensor_tensor(out=gate, in0=HCS[:], in1=GST[:], op=ALU.mult)
            VE(opre)
    WOB = AUX[:, 0:8192]
    wprep(w_out_b, WOB)
    OBS = AUX[:, 10240:10752]
    for tau in range(4):
        b, qs = divmod(tau, 2)
        for w in range(2):
            mev = fw.marks()

            def womm(t, b=b, qs=qs, w=w, mev=mev):
                t.wait_ge(fw.F["vector"], mev["vector"])
                for oci in range(4):
                    oc = w * 4 + oci
                    for k in range(8):
                        t.matmul(PW[oci][:], WOB[:, (k * 8 + oc) * 128:(k * 8 + oc + 1) * 128],
                                 R1[:, (b * 16 + 8 + k) * 1024 + qs * 512:(b * 16 + 8 + k) * 1024 + (qs + 1) * 512],
                                 start=(k == 0), stop=(k == 7))
                t.mark()
            TE(womm)
            mwo = fw.marks()["tensor"]

            def woev(v, b=b, qs=qs, w=w, mwo=mwo):
                v.wait_ge(fw.F["tensor"], mwo)
                for oci in range(4):
                    oc = w * 4 + oci
                    v.tensor_scalar(out=OBS[:], in0=PW[oci][:], scalar1=BOB[:, 32 + oc:32 + oc + 1],
                                    scalar2=None, op0=ALU.add)
                    for par in range(2):
                        base = (b * 8 + oc) * 2048 + qs * 1024 + par
                        rap = AP(tensor=R2[:].tensor, offset=R2[:].offset + base,
                                 ap=[[32768, 128], [2, 512]])
                        v.tensor_tensor(out=rap, in0=rap, in1=OBS[:], op=ALU.add)
            VE(woev)
    fw.barrier()
    if PHASES < 9:
        return

    emit_ffn()

    OTS_ = [AUX[:, 10240:12288].bitcast(F32), AUX[:, 12288:14336].bitcast(F32)]
    osem = [ctx["st"].enter_context(nc.semaphore("OSEM0")),
            ctx["st"].enter_context(nc.semaphore("OSEM1"))]
    ocnt = [0, 0]
    for i in range(B2 * NCH):
        b, c = divmod(i, NCH)
        sl = i % 2
        mprev = fw.marks()

        def otr(t, b=b, c=c, mprev=mprev):
            t.wait_ge(fw.F["vector"], mprev["vector"])
            for do in range(8):
                t.transpose(PB0[:, do * 128:(do + 1) * 128],
                            R2[:, (b * 8 + do) * 2048 + c * 128:(b * 8 + do) * 2048 + (c + 1) * 128],
                            IDENT[:])
            t.mark()
        TE(otr)
        mo = fw.marks()["tensor"]

        def oev(v, mo=mo, sl=sl, prevd=ocnt[sl]):
            v.wait_ge(fw.F["tensor"], mo)
            if prevd > 0:
                v.wait_ge(osem[sl], prevd)
            v.tensor_copy(out=OTS_[sl][:], in_=PB0[:])
        VE(oev)
        mvv = fw.marks()

        def ostore(h, b=b, c=c, sl=sl, mvv=mvv):
            h.wait_ge(fw.F["vector"], mvv["vector"])
            h.dma_start(out=out[b, c * 128:(c + 1) * 128, :],
                        in_=OTS_[sl][:]).then_inc(osem[sl], 16)
        fw._sect("sync", ostore)
        ocnt[sl] += 16
    def ofin(v, c0=ocnt[0], c1=ocnt[1]):
        v.wait_ge(osem[0], c0)
        v.wait_ge(osem[1], c1)
    VE(ofin)
    fw.barrier()




F32 = mybir.dt.float32
BF16 = mybir.dt.bfloat16
I32 = mybir.dt.int32
AF = mybir.ActivationFunctionType
ALU = mybir.AluOpType
AX = mybir.AxisListType

B2, L, D, NB = 2, 2048, 1024, 8
NCH = 16
N2 = 32
S64 = 64
EPS_LN, EPS_RMS = 1e-5, 1e-6
TWO_PI = 2.0 * math.pi
INV_2PI = 1.0 / TWO_PI
MAGIC = 1.5 * 2.0 ** 23
MAGIC64 = 1.5 * 2.0 ** 52
SLABS = NB
PHASES = 99


def build_program(debug=()):
    nc = bass.Bass("TRN2", target_bir_lowering=False)
    dbg_outs = {}

    def dram_in(name, shape, dt=F32):
        return nc.dram_tensor(name, shape, dt, kind="ExternalInput").ap()

    x_in = dram_in("x_in", [B2, L, D])
    out = nc.dram_tensor("out", [B2, L, D], F32, kind="ExternalOutput").ap()
    log_dt = dram_in("log_dt", [1, D])
    log_w_real = dram_in("log_w_real", [1, N2])
    w_imag = dram_in("w_imag", [1, N2])
    C_real = dram_in("C_real", [D, N2])
    C_imag = dram_in("C_imag", [D, N2])
    D_a = dram_in("D_a", [1, D])
    W_out_a = dram_in("W_out_a", [D, D])
    b_out_a = dram_in("b_out_a", [1, D])
    tern = dram_in("tern", [D, D], I32)
    bias_f = dram_in("bias_f", [1, D])
    act_scale = dram_in("act_scale", [1, 1])
    w_in_b = dram_in("w_in_b", [2 * D, D])
    b_in_b = dram_in("b_in_b", [1, 2 * D])
    A_b = dram_in("A_b", [D, S64])
    Bm_b = dram_in("Bm_b", [D, S64])
    Cm_b = dram_in("Cm_b", [D, S64])
    Dv_b = dram_in("Dv_b", [1, D])
    w_out_b = dram_in("w_out_b", [D, D])
    b_out_b = dram_in("b_out_b", [1, D])
    g_norm = dram_in("g_norm", [1, D])
    identb = dram_in("identb", [128, 128], BF16)
    iota_negj = dram_in("iota_negj", [1, 128])
    iota_t = dram_in("iota_t", [1, 128])
    iota_t128 = dram_in("iota_t128", [1, 128])
    mask_triu = dram_in("mask_triu", [128, 128], BF16)
    ones_row = dram_in("ones_row", [1, 128])

    hb_aim = nc.dram_tensor("hb_aim", [1, D * N2], F32, kind="Internal").ap()
    hb_are = nc.dram_tensor("hb_are", [1, D * N2], F32, kind="Internal").ap()
    hb_cwr = nc.dram_tensor("hb_cwr", [1, D * N2], F32, kind="Internal").ap()
    hb_cwi = nc.dram_tensor("hb_cwi", [1, D * N2], F32, kind="Internal").ap()
    hb_slab = nc.dram_tensor("hb_slab", [NB, 128, B2 * NCH * 128], BF16, kind="Internal").ap()
    hb_winT = nc.dram_tensor("hb_winT", [NB, 128, 2 * D], BF16, kind="Internal").ap()

    def dbg(name, shape, dt):
        if name in debug:
            dbg_outs[name] = nc.dram_tensor(name, shape, dt, kind="ExternalOutput").ap()
            return dbg_outs[name]
        return None

    st = ExitStack()
    sbuf = lambda name, shape, dt=BF16: st.enter_context(nc.sbuf_tensor(name, shape, dt))
    psum = lambda name, shape, dt=F32: st.enter_context(nc.psum_tensor(name, shape, dt))

    with st:
        R1 = sbuf("R1", [128, 32768])
        R2 = sbuf("R2", [128, 32768])
        R3 = sbuf("R3", [128, 16384])
        AUX = sbuf("AUX", [128, 20480])
        IDENT = sbuf("IDENTS", [128, 128])
        IDENTF = sbuf("IDENTF", [128, 128], F32)
        MASKT = sbuf("MASKT", [128, 128])
        IOTAS = sbuf("IOTAS", [1, 512], F32)
        ONEC = sbuf("ONEC", [128, 2])
        STM = sbuf("STM", [128, 96], F32)
        STT = sbuf("STT", [128, 16], F32)
        SML = sbuf("SML", [128, 64], F32)
        BOB = sbuf("BOB", [128, 64], F32)
        DBC = sbuf("DBC", [128, 1024])
        ASC = AUX[:, 0:2048].bitcast(F32)
        BSC = AUX[:, 2048:4096].bitcast(F32)
        HST = AUX[:, 4096:6144].bitcast(F32)
        ZST = AUX[:, 6144:8192].bitcast(F32)
        GST = AUX[:, 8192:10240].bitcast(F32)
        T1S = AUX[:, 10240:12288].bitcast(F32)
        HCS = AUX[:, 12288:14336].bitcast(F32)
        CBC = AUX[:, 14336:14464].bitcast(F32)
        PRM = AUX[:, 8192:12288].bitcast(F32)
        BCB = AUX[:, 12288:14336].bitcast(F32)
        DROW = AUX[0:1, 14336:16384].bitcast(F32)
        XT = AUX[:, 0:4096]

        P0 = psum("P0", [128, 512]); P1 = psum("P1", [128, 512]); P2 = psum("P2", [128, 512])
        P3 = psum("P3", [128, 512]); P4 = psum("P4", [128, 512]); P5 = psum("P5", [128, 512])
        PB0 = psum("PB0", [128, 1024], BF16)
        PB1 = psum("PB1", [128, 1024], BF16)
        PW = [P0, P1, P2, P3]

        st.enter_context(nc.allow_non_contiguous_dma(reason="strided staging loads/stores"))
        blk = st.enter_context(nc.Block())
        fw = FW(nc, st, blk)

        def VE(fn):
            fw.run("vector", fn)
        def SC(fn):
            fw.run("scalar", fn)
        def TE(fn):
            fw.run("tensor", fn)
        def GP(fn):
            fw.run("gpsimd", fn)

        fw.dma("sync", IDENT[:], identb[:])
        fw.dma("sync", MASKT[:], mask_triu[:])
        fw.dma("sync", IOTAS[0:1, 0:128], iota_negj[:])
        fw.dma("sync", IOTAS[0:1, 128:256], iota_t[:])
        fw.dma("sync", IOTAS[0:1, 256:384], iota_t128[:])
        fw.dma("sync", IOTAS[0:1, 384:512], ones_row[:])
        for (dst0, src) in ((0, b_out_a), (8, bias_f), (16, Dv_b), (24, g_norm), (32, b_out_b)):
            fw.dma("sync", BOB[:, dst0:dst0 + 8],
                   AP(tensor=src.tensor, offset=0, ap=[[1, 128], [128, 8]]))
        fw.dma("sync", SML[:, 0:8], AP(tensor=log_dt.tensor, offset=0, ap=[[1, 128], [128, 8]]))
        fw.dma("sync", SML[:, 16:32], AP(tensor=b_in_b.tensor, offset=0, ap=[[1, 128], [128, 16]]))
        fw.dma("sync", SML[0:1, 48:49], act_scale[:])
        fw.dma("sync", PRM[:, 1536:1792].rearrange("p (a b) -> p a b", a=8),
               AP(tensor=C_real.tensor, offset=0, ap=[[N2, 128], [128 * N2, 8], [1, N2]]))
        fw.dma("sync", PRM[:, 1792:2048].rearrange("p (a b) -> p a b", a=8),
               AP(tensor=C_imag.tensor, offset=0, ap=[[N2, 128], [128 * N2, 8], [1, N2]]))
        fw.dma("sync", BCB[0:1, 384:416], log_w_real[:])
        fw.dma("sync", BCB[0:1, 416:448], w_imag[:])
        fw.dma("sync", DROW[0:1, 0:1024], D_a[:])
        nconst = fw.dn["sync"]
        nconst2 = nconst

        def c_init(v):
            v.wait_ge(fw.Dq["sync"], nconst)
            v.tensor_copy(out=IDENTF[:], in_=IDENT[:])
            v.memset(ONEC[:], 1.0)
            v.memset(SML[:, 52:53], 1.0)
            v.tensor_scalar(out=SML[:, 53:54], in0=SML[:, 52:53], scalar1=MAGIC64,
                            scalar2=-MAGIC64, op0=ALU.add, op1=ALU.add)
        VE(c_init)
        m_init = fw.marks()

        NTILE = B2 * NCH
        sx = [st.enter_context(nc.semaphore(f"sx{i}")) for i in range(4)]

        def ph1(i):
            b, c = divmod(i, NCH)
            slot = i % 4
            xt = XT[:, slot * 1024:(slot + 1) * 1024]
            mk0 = fw.marks()

            def gp_ld(h):
                if i >= 4:
                    h.wait_ge(fw.F["vector"], ph1.done[i - 4])
                h.dma_start(out=xt, in_=x_in[b, c * 128:(c + 1) * 128, :]).then_inc(sx[slot], 16)
            fw._sect("gpsimd", gp_ld)

            def ve1(v):
                v.wait_ge(sx[slot], 16 * (i // 4 + 1))
                v.bn_stats(out=STT[:, 0:6], in_=xt[:, 0:512])
                v.bn_stats(out=STT[:, 6:12], in_=xt[:, 512:1024])
                v.bn_aggr(out=STT[:, 12:14], in_=STT[:, 0:12])
                v.tensor_copy(out=STM[:, i:i + 1], in_=STT[:, 12:13])
                v.tensor_scalar(out=STT[:, 14:15], in0=STT[:, 13:14], scalar1=EPS_LN,
                                scalar2=None, op0=ALU.add)
                v.reciprocal(out=STT[:, 15:16], in_=STT[:, 14:15])
            VE(ve1)
            mkv = fw.marks()

            def sc1(s):
                s.wait_ge(fw.F["vector"], mkv["vector"])
                s.activation(out=STM[:, 64 + i:64 + i + 1], in_=STT[:, 15:16], func=AF.Sqrt)
            SC(sc1)
            mks = fw.marks()

            def ve2(v):
                v.wait_ge(fw.F["scalar"], mks["scalar"])
                v.tensor_scalar(out=R1[:, i * 1024:(i + 1) * 1024], in0=xt,
                                scalar1=STM[:, i:i + 1], scalar2=STM[:, 64 + i:64 + i + 1],
                                op0=ALU.subtract, op1=ALU.mult)
            VE(ve2)
            ph1.done[i] = fw.marks()["vector"]
        ph1.done = {}
        for i in range(NTILE):
            ph1(i)
        fw.barrier()
        if PHASES < 2:
            return _finish(nc, fw, dbg_outs, dbg, R1, R2, R3, STM, PRM, BCB, AUX, hb_aim, hb_are, hb_cwr, hb_cwi, DBC)

        SCF = AUX[:, 4096:8192].bitcast(F32)

        def dt3(base, a, bsz, astr, bstr):
            return AP(tensor=base.tensor, offset=base.offset,
                      ap=[list(base.ap[0]), [astr, a], [bstr, bsz]])

        def mm1(t):
            t.wait_ge(fw.Dq["sync"], nconst2)
            t.matmul(P0[:, 0:64], IOTAS[0:1, 384:512], BCB[0:1, 384:448], start=True, stop=True)
            t.matmul(P1[:], IOTAS[0:1, 384:512], DROW[0:1, 0:512], start=True, stop=True)
            t.matmul(P2[:], IOTAS[0:1, 384:512], DROW[0:1, 512:1024], start=True, stop=True)
            t.mark()
        TE(mm1)
        mk = fw.marks()

        def sc_dt(s):
            s.activation(out=SML[:, 8:16], in_=SML[:, 0:8], func=AF.Exp)
        SC(sc_dt)
        mk_dt = fw.marks()

        def ve_bc(v):
            v.wait_ge(fw.F["tensor"], mk["tensor"])
            v.tensor_copy(out=BCB[:, 0:32], in_=P0[:, 0:32])
            v.tensor_copy(out=BCB[:, 32:64], in_=P0[:, 32:64])
        VE(ve_bc)
        mk_bc = fw.marks()

        def sc_wre(s):
            s.wait_ge(fw.F["vector"], mk_bc["vector"])
            s.activation(out=BCB[:, 0:32], in_=BCB[:, 0:32], func=AF.Exp)
        SC(sc_wre)
        mk_wre = fw.marks()

        def ve_args(v):
            v.wait_ge(fw.F["scalar"], mk_wre["scalar"])
            dt_ap = dt3(SML[:, 8:16], 8, 32, 1, 0)
            wim_ap = dt3(BCB[:, 32:64], 8, 32, 0, 1)
            wre_ap = dt3(BCB[:, 0:32], 8, 32, 0, 1)
            v.tensor_tensor(out=PRM[:, 0:256].rearrange("p (a b) -> p a b", a=8),
                            in0=dt_ap, in1=wim_ap, op=ALU.mult)
            v.tensor_tensor(out=PRM[:, 256:512].rearrange("p (a b) -> p a b", a=8),
                            in0=dt_ap, in1=wre_ap, op=ALU.mult)
            v.tensor_scalar(out=PRM[:, 256:512], in0=PRM[:, 256:512], scalar1=-1.0,
                            scalar2=None, op0=ALU.mult)
        VE(ve_args)
        mk_args = fw.marks()

        def sc_rr1(s):
            s.wait_ge(fw.F["vector"], mk_args["vector"])
            s.activation(out=SCF[:, 0:256], in_=PRM[:, 0:256], func=AF.Copy,
                         scale=INV_2PI, bias=MAGIC)
            s.activation(out=SCF[:, 0:256], in_=SCF[:, 0:256], func=AF.Copy, bias=-MAGIC)
        SC(sc_rr1)
        mk_rr1 = fw.marks()

        def ve_rr1(v):
            v.wait_ge(fw.F["scalar"], mk_rr1["scalar"])
            v.tensor_scalar(out=SCF[:, 1024:1280], in0=PRM[:, 0:256], scalar1=INV_2PI,
                            scalar2=None, op0=ALU.mult)
            v.tensor_scalar(out=SCF[:, 1280:1536], in0=SCF[:, 1024:1280], scalar1=MAGIC64,
                            scalar2=-MAGIC64, op0=ALU.add, op1=ALU.add)
            v.tensor_tensor(out=SCF[:, 1280:1536], in0=SCF[:, 1280:1536], in1=SCF[:, 0:256],
                            op=ALU.subtract)
            v.scalar_tensor_tensor(out=SCF[:, 0:256], in0=SCF[:, 1280:1536], scalar=SML[:, 53:54],
                                   in1=SCF[:, 0:256], op0=ALU.mult, op1=ALU.add)
            v.tensor_tensor(out=SCF[:, 256:512], in0=SCF[:, 1024:1280], in1=SCF[:, 0:256],
                            op=ALU.subtract)
            v.tensor_scalar(out=SCF[:, 512:768], in0=SCF[:, 256:512], scalar1=0.25,
                            scalar2=None, op0=ALU.add)
            v.tensor_scalar(out=SCF[:, 768:1024], in0=SCF[:, 512:768], scalar1=0.5,
                            scalar2=None, op0=ALU.is_gt)
            v.tensor_tensor(out=SCF[:, 512:768], in0=SCF[:, 512:768], in1=SCF[:, 768:1024],
                            op=ALU.subtract)
        VE(ve_rr1)
        mk_ve1 = fw.marks()
        rr1_dbg = dbg("rr1_dbg", [128, 1536], F32)
        if rr1_dbg is not None:
            fw.dma("gpsimd", rr1_dbg[:], SCF[:, 0:1536], wait=[("vector", mk_ve1["vector"])])
        stt_dbg = dbg("stt_dbg", [128, 16], F32)
        if stt_dbg is not None:
            fw.dma("gpsimd", stt_dbg[:], STT[:], wait=[("vector", mk_ve1["vector"])])


        def sc_rr2(s):
            s.wait_ge(fw.F["vector"], mk_ve1["vector"])
            s.activation(out=SCF[:, 768:1024], in_=SCF[:, 256:512], func=AF.Sin, scale=TWO_PI)
            s.activation(out=SCF[:, 1024:1280], in_=SCF[:, 512:768], func=AF.Sin, scale=TWO_PI)
            s.activation(out=SCF[:, 1280:1536], in_=PRM[:, 256:512], func=AF.Exp)
        SC(sc_rr2)
        mk_rr2 = fw.marks()

        def ve_cw(v):
            v.wait_ge(fw.F["scalar"], mk_rr2["scalar"])
            v.tensor_tensor(out=PRM[:, 512:768], in0=SCF[:, 1280:1536], in1=SCF[:, 1024:1280], op=ALU.mult)
            v.tensor_tensor(out=PRM[:, 768:1024], in0=SCF[:, 1280:1536], in1=SCF[:, 768:1024], op=ALU.mult)
            v.tensor_tensor(out=BCB[:, 64:96], in0=BCB[:, 0:32], in1=BCB[:, 0:32], op=ALU.mult)
            v.tensor_tensor(out=BCB[:, 96:128], in0=BCB[:, 32:64], in1=BCB[:, 32:64], op=ALU.mult)
            v.tensor_tensor(out=BCB[:, 64:96], in0=BCB[:, 64:96], in1=BCB[:, 96:128], op=ALU.add)
            v.reciprocal(out=BCB[:, 192:224], in_=BCB[:, 64:96])
            v.tensor_scalar(out=SCF[:, 1536:1792], in0=PRM[:, 512:768], scalar1=-1.0, scalar2=None, op0=ALU.add)
            v.tensor_scalar(out=BCB[:, 224:256], in0=BCB[:, 0:32], scalar1=-1.0, scalar2=None, op0=ALU.mult)
            v.tensor_scalar(out=BCB[:, 256:288], in0=BCB[:, 32:64], scalar1=-1.0, scalar2=None, op0=ALU.mult)
            v.tensor_tensor(out=BCB[:, 128:160], in0=BCB[:, 224:256], in1=BCB[:, 192:224], op=ALU.mult)
            v.tensor_tensor(out=BCB[:, 160:192], in0=BCB[:, 256:288], in1=BCB[:, 192:224], op=ALU.mult)
            ir = dt3(BCB[:, 128:160], 8, 32, 0, 1)
            ii = dt3(BCB[:, 160:192], 8, 32, 0, 1)
            nre3 = SCF[:, 1536:1792].rearrange("p (a b) -> p a b", a=8)
            nim3 = PRM[:, 768:1024].rearrange("p (a b) -> p a b", a=8)
            v.tensor_tensor(out=SCF[:, 0:256].rearrange("p (a b) -> p a b", a=8), in0=nre3, in1=ir, op=ALU.mult)
            v.tensor_tensor(out=SCF[:, 256:512].rearrange("p (a b) -> p a b", a=8), in0=nim3, in1=ii, op=ALU.mult)
            v.tensor_tensor(out=SCF[:, 0:256], in0=SCF[:, 0:256], in1=SCF[:, 256:512], op=ALU.subtract)
            v.tensor_tensor(out=SCF[:, 256:512].rearrange("p (a b) -> p a b", a=8), in0=nre3, in1=ii, op=ALU.mult)
            v.tensor_tensor(out=SCF[:, 512:768].rearrange("p (a b) -> p a b", a=8), in0=nim3, in1=ir, op=ALU.mult)
            v.tensor_tensor(out=SCF[:, 256:512], in0=SCF[:, 256:512], in1=SCF[:, 512:768], op=ALU.add)
            v.tensor_tensor(out=SCF[:, 512:768], in0=PRM[:, 1536:1792], in1=SCF[:, 0:256], op=ALU.mult)
            v.tensor_tensor(out=SCF[:, 768:1024], in0=PRM[:, 1792:2048], in1=SCF[:, 256:512], op=ALU.mult)
            v.tensor_tensor(out=SCF[:, 512:768], in0=SCF[:, 512:768], in1=SCF[:, 768:1024], op=ALU.subtract)
            v.tensor_scalar(out=SCF[:, 1536:1792], in0=SCF[:, 512:768], scalar1=2.0, scalar2=None, op0=ALU.mult)
            v.tensor_tensor(out=SCF[:, 512:768], in0=PRM[:, 1536:1792], in1=SCF[:, 256:512], op=ALU.mult)
            v.tensor_tensor(out=SCF[:, 768:1024], in0=PRM[:, 1792:2048], in1=SCF[:, 0:256], op=ALU.mult)
            v.tensor_tensor(out=SCF[:, 512:768], in0=SCF[:, 512:768], in1=SCF[:, 768:1024], op=ALU.add)
            v.tensor_scalar(out=SCF[:, 1792:2048], in0=SCF[:, 512:768], scalar1=2.0, scalar2=None, op0=ALU.mult)
            v.tensor_copy(out=PRM[:, 1024:1280], in_=PRM[:, 512:768])
            v.tensor_copy(out=PRM[:, 1280:1536], in_=PRM[:, 768:1024])
            for _ in range(7):
                v.tensor_tensor(out=SCF[:, 0:256], in0=PRM[:, 1024:1280], in1=PRM[:, 1024:1280], op=ALU.mult)
                v.tensor_tensor(out=SCF[:, 256:512], in0=PRM[:, 1280:1536], in1=PRM[:, 1280:1536], op=ALU.mult)
                v.tensor_tensor(out=SCF[:, 512:768], in0=PRM[:, 1024:1280], in1=PRM[:, 1280:1536], op=ALU.mult)
                v.tensor_tensor(out=PRM[:, 1024:1280], in0=SCF[:, 0:256], in1=SCF[:, 256:512], op=ALU.subtract)
                v.tensor_scalar(out=PRM[:, 1280:1536], in0=SCF[:, 512:768], scalar1=2.0, scalar2=None, op0=ALU.mult)
        VE(ve_cw)
        mk_cw = fw.marks()

        def ve_dbc(v):
            v.tensor_copy(out=DBC[:, 0:512], in_=P1[:])
            v.tensor_copy(out=DBC[:, 512:1024], in_=P2[:])
        VE(ve_dbc)

        def rows_dst(t):
            return AP(tensor=t.tensor, offset=0, ap=[[32, 128], [4096, 8], [1, 32]])
        mnow = fw.marks()
        fw.dma("gpsimd", rows_dst(hb_aim), PRM[:, 0:256].rearrange("p (a b) -> p a b", a=8),
               wait=[("vector", mnow["vector"])])
        fw.dma("gpsimd", rows_dst(hb_are), PRM[:, 256:512].rearrange("p (a b) -> p a b", a=8))
        fw.dma("gpsimd", rows_dst(hb_cwr), SCF[:, 1536:1792].rearrange("p (a b) -> p a b", a=8))
        fw.dma("gpsimd", rows_dst(hb_cwi), SCF[:, 1792:2048].rearrange("p (a b) -> p a b", a=8))
        fw.barrier()
        if PHASES < 3:
            return _finish(nc, fw, dbg_outs, dbg, R1, R2, R3, STM, PRM, BCB, AUX, hb_aim, hb_are, hb_cwr, hb_cwi, DBC)

        TBL = AUX[:, 0:1024]
        TBB = AUX[:, 1024:2048]
        SINT = AUX[:, 2048:2560]
        COST = AUX[:, 2560:3072]
        EXPT = AUX[:, 3072:3584]
        RAWS = AUX[:, 3584:4096]
        TMP2B = AUX[:, 15360:16384]
        TGA = AUX[0:64, 16384:18432]
        TGB = AUX[0:64, 18432:20480]
        BSLAB = R3[0:64, 0:4096]
        SSLAB = R3[0:64, 4096:8192]
        K8A = R3[:, 8192:9216]
        K8B = R3[:, 9216:10240]
        GOSL = R3[:, 10240:14336]
        SSTa = R3[:, 14336:14592].bitcast(F32)
        SSTb = R3[:, 14592:14848].bitcast(F32)
        TMPQ = R3[:, 14848:15360].bitcast(F32)
        AIMSEG = DROW[0:1, 0:512]
        ARESEG = SCF[0:1, 1536:2048]
        CW2R = BCB[:, 0:512]
        CW2I = BCB[:, 512:1024]

        def u_cols(h):
            return AP(tensor=R1[:].tensor, offset=h, ap=[[32768, 128], [NCH * 1024, B2], [1024, NCH]])

        def u_grp(h0):
            return AP(tensor=R1[:].tensor, offset=h0, ap=[[32768, 128], [1, 16], [NCH * 1024, B2], [1024, NCH]])

        def dbc_grp(h0):
            return AP(tensor=DBC[:].tensor, offset=DBC[:].offset + h0,
                      ap=[[1024, 128], [1, 16], [0, B2], [0, NCH]])

        def chain(pim, pexp, mk_args_t):
            def s1(s):
                s.wait_ge(fw.F["tensor"], mk_args_t)
                s.activation(out=SCF[:, 0:512], in_=pim, func=AF.Copy, scale=INV_2PI, bias=MAGIC)
                s.activation(out=SCF[:, 0:512], in_=SCF[:, 0:512], func=AF.Copy, bias=-MAGIC)
            SC(s1)
            m1 = fw.marks()

            def v1(v):
                v.wait_ge(fw.F["scalar"], m1["scalar"])
                v.tensor_scalar(out=SCF[:, 512:1024], in0=pim, scalar1=INV_2PI,
                                scalar2=None, op0=ALU.mult)
                v.tensor_scalar(out=SCF[:, 1024:1536], in0=SCF[:, 512:1024], scalar1=MAGIC64,
                                scalar2=-MAGIC64, op0=ALU.add, op1=ALU.add)
                v.tensor_tensor(out=SCF[:, 1024:1536], in0=SCF[:, 1024:1536], in1=SCF[:, 0:512],
                                op=ALU.subtract)
                v.scalar_tensor_tensor(out=SCF[:, 0:512], in0=SCF[:, 1024:1536], scalar=SML[:, 53:54],
                                       in1=SCF[:, 0:512], op0=ALU.mult, op1=ALU.add)
                v.tensor_tensor(out=SCF[:, 512:1024], in0=SCF[:, 512:1024], in1=SCF[:, 0:512],
                                op=ALU.subtract)
                v.tensor_scalar(out=SCF[:, 1024:1536], in0=SCF[:, 512:1024], scalar1=0.25,
                                scalar2=None, op0=ALU.add)
                v.tensor_scalar(out=SCF[:, 0:512], in0=SCF[:, 1024:1536], scalar1=0.5,
                                scalar2=None, op0=ALU.is_gt)
                v.tensor_tensor(out=SCF[:, 1024:1536], in0=SCF[:, 1024:1536], in1=SCF[:, 0:512],
                                op=ALU.subtract)
            VE(v1)
            m2 = fw.marks()

            def s2(s):
                s.wait_ge(fw.F["vector"], m2["vector"])
                s.activation(out=SINT[:], in_=SCF[:, 512:1024], func=AF.Sin, scale=TWO_PI)
                s.activation(out=COST[:], in_=SCF[:, 1024:1536], func=AF.Sin, scale=TWO_PI)
                s.activation(out=EXPT[:], in_=pexp, func=AF.Exp)
            SC(s2)
            return fw.marks()

        def bp_family(mk_sc):
            def v(v):
                v.wait_ge(fw.F["scalar"], mk_sc["scalar"])
                v.tensor_tensor(out=SCF[:, 0:512], in0=EXPT[:], in1=COST[:], op=ALU.mult)
                v.tensor_tensor(out=SCF[:, 512:1024], in0=EXPT[:], in1=SINT[:], op=ALU.mult)
                v.tensor_tensor(out=SCF[:, 1024:1536], in0=CW2R, in1=SCF[:, 0:512], op=ALU.mult)
                v.tensor_tensor(out=TMP2B[:].bitcast(F32), in0=CW2I, in1=SCF[:, 512:1024], op=ALU.mult)
                v.tensor_tensor(out=TBB[:, 0:1024:2], in0=SCF[:, 1024:1536], in1=TMP2B[:].bitcast(F32), op=ALU.subtract)
                v.tensor_tensor(out=SCF[:, 1024:1536], in0=CW2R, in1=SCF[:, 512:1024], op=ALU.mult)
                v.tensor_tensor(out=TMP2B[:].bitcast(F32), in0=CW2I, in1=SCF[:, 0:512], op=ALU.mult)
                v.tensor_tensor(out=SCF[:, 1024:1536], in0=SCF[:, 1024:1536], in1=TMP2B[:].bitcast(F32), op=ALU.add)
                v.tensor_scalar(out=TBB[:, 1:1024:2], in0=SCF[:, 1024:1536], scalar1=-1.0,
                                scalar2=None, op0=ALU.mult)
            VE(v)
            return fw.marks()

        raws_st = {"n": 0}
        for ho in range(SLABS):
            for g in range(8):
                h0 = ho * 128 + g * 16
                seg = h0 * N2
                mv = fw.marks()
                fw.dma("sync", AIMSEG, hb_aim[0:1, seg:seg + 512], wait=[("vector", mv["vector"]), ("tensor", mv["tensor"]), ("scalar", mv["scalar"])])
                fw.dma("sync", ARESEG, hb_are[0:1, seg:seg + 512])
                fw.dma("sync", CW2R, AP(tensor=hb_cwr.tensor, offset=seg, ap=[[0, 128], [1, 512]]))
                fw.dma("sync", CW2I, AP(tensor=hb_cwi.tensor, offset=seg, ap=[[0, 128], [1, 512]]))
                ndma = fw.dn["sync"]

                def args1(t, ndma=ndma, mv=mv):
                    t.wait_ge(fw.Dq["sync"], ndma)
                    t.wait_ge(fw.F["vector"], mv["vector"])
                    t.matmul(P0[:], IOTAS[0:1, 0:128], AIMSEG, start=True, stop=True)
                    t.matmul(P2[:], IOTAS[0:1, 0:128], ARESEG, start=True, stop=True)
                    t.matmul(P1[:], IOTAS[0:1, 128:256], AIMSEG, start=True, stop=True)
                    t.matmul(P3[:], IOTAS[0:1, 128:256], ARESEG, start=True, stop=True)
                    t.mark()
                TE(args1)
                mk_a = fw.marks()["tensor"]
                mk_f1 = chain(P0[:], P2[:], mk_a)

                def f1w(v, mk_f1=mk_f1):
                    v.wait_ge(fw.F["scalar"], mk_f1["scalar"])
                    v.tensor_tensor(out=TBL[:, 0:1024:2], in0=EXPT[:], in1=COST[:], op=ALU.mult)
                    v.tensor_tensor(out=TBL[:, 1:1024:2], in0=EXPT[:], in1=SINT[:], op=ALU.mult)
                VE(f1w)
                mk_bp = chain(P1[:], P3[:], mk_a)
                bp_family(mk_bp)
                mtab = fw.marks()

                def tr1(t, mtab=mtab):
                    t.wait_ge(fw.F["vector"], mtab["vector"])
                    for k in range(8):
                        t.transpose(PB0[0:64, k * 128:(k + 1) * 128], TBL[:, k * 64:(k + 1) * 64], IDENT[:])
                    t.mark()
                    for k in range(8):
                        t.transpose(PB1[0:64, k * 128:(k + 1) * 128], TBL[:, 512 + k * 64:512 + (k + 1) * 64], IDENT[:])
                    t.mark()
                TE(tr1)
                mtr1 = fw.marks()["tensor"]

                def evga(v, mtr1=mtr1):
                    v.wait_ge(fw.F["tensor"], mtr1)
                    v.tensor_copy(out=TGA[:, 0:1024], in_=PB0[0:64, :])
                    v.tensor_copy(out=TGA[:, 1024:2048], in_=PB1[0:64, :])
                VE(evga)
                mevga = fw.marks()

                def tr2(t, mevga=mevga):
                    t.wait_ge(fw.F["vector"], mevga["vector"])
                    for k in range(8):
                        t.transpose(PB0[0:64, k * 128:(k + 1) * 128], TBB[:, k * 64:(k + 1) * 64], IDENT[:])
                    t.mark()
                    for k in range(8):
                        t.transpose(PB1[0:64, k * 128:(k + 1) * 128], TBB[:, 512 + k * 64:512 + (k + 1) * 64], IDENT[:])
                    t.mark()
                TE(tr2)
                mtr2 = fw.marks()["tensor"]

                def evgb(v, mtr2=mtr2):
                    v.wait_ge(fw.F["tensor"], mtr2)
                    v.tensor_copy(out=TGB[:, 0:1024], in_=PB0[0:64, :])
                    v.tensor_copy(out=TGB[:, 1024:2048], in_=PB1[0:64, :])
                VE(evgb)
                mevgb = fw.marks()

                def kmm(t, mevgb=mevgb):
                    t.wait_ge(fw.F["vector"], mevgb["vector"])
                    for k in range(16):
                        pb = PW[k // 4]
                        t.matmul(pb[:, (k % 4) * 128:(k % 4 + 1) * 128],
                                 TGA[:, k * 128:(k + 1) * 128], TGB[:, k * 128:(k + 1) * 128],
                                 start=True, stop=True)
                    t.mark()
                TE(kmm)
                mkmm = fw.marks()["tensor"]

                def kev(v, mkmm=mkmm):
                    v.wait_ge(fw.F["tensor"], mkmm)
                    for q, pb in enumerate(PW):
                        mask4 = AP(tensor=MASKT[:].tensor, offset=MASKT[:].offset,
                                   ap=[[128, 128], [0, 4], [1, 128]])
                        dst = (K8A, K8B)[q // 2]
                        v.tensor_tensor(out=dst[:, (q % 2) * 512:(q % 2 + 1) * 512].rearrange("p (a b) -> p a b", a=4),
                                        in0=pb[:].rearrange("p (a b) -> p a b", a=4), in1=mask4, op=ALU.mult)
                VE(kev)
                mkev = fw.marks()

                def gimm(t, mkev=mkev, h0=h0):
                    t.wait_ge(fw.F["vector"], mkev["vector"])
                    for k in range(16):
                        t.matmul(P4[0:64, k * 32:(k + 1) * 32], TBL[:, k * 64:(k + 1) * 64],
                                 u_cols(h0 + k), start=True, stop=True)
                    t.mark()
                    for k in range(16):
                        kt = (K8A, K8B)[k // 8]
                        t.matmul(P5[:, k * 32:(k + 1) * 32], kt[:, (k % 8) * 128:(k % 8 + 1) * 128],
                                 u_cols(h0 + k), start=True, stop=True)
                    t.mark()
                TE(gimm)
                mgi = fw.marks()["tensor"]

                def bdev(v, mgi=mgi, h0=h0, prevst=raws_st["n"]):
                    v.wait_ge(fw.F["tensor"], mgi)
                    if prevst:
                        v.wait_ge(fw.Dq["gpsimd"], prevst)
                    h0l = h0 % 128
                    dstB = AP(tensor=BSLAB.tensor, offset=BSLAB.offset + h0l * 2,
                              ap=[[16384, 64], [2, 16], [1, 2], [256, 16]])
                    v.tensor_copy(out=dstB, in_=P4[0:64, 0:512].rearrange("p (a b c) -> p a b c", a=16, b=2))
                    v.tensor_tensor(out=SINT[:], in0=u_grp(h0), in1=dbc_grp(h0), op=ALU.mult)
                    v.tensor_tensor(out=RAWS[:], in0=SINT[:], in1=P5[:], op=ALU.add)
                VE(bdev)
                mbd = fw.marks()
                fw.dma("gpsimd", hb_slab[ho, :, g * 512:(g + 1) * 512], RAWS[:],
                       wait=[("vector", mbd["vector"])])
                raws_st["n"] = fw.dn["gpsimd"]

            mk_sw1 = fw.marks()
            ngp1 = fw.dn["gpsimd"]

            def rinit(v):
                v.memset(SSTa, 0.0)
            VE(rinit)
            lt_re = PRM[:, 1024 + ho * 32:1024 + (ho + 1) * 32]
            lt_im = PRM[:, 1280 + ho * 32:1280 + (ho + 1) * 32]

            def lt_ap(base):
                return AP(tensor=base.tensor, offset=base.offset,
                          ap=[[base.ap[0][0], 128], [0, 2], [1, 32]])

            def sview(tns, pm):
                return AP(tensor=tns.tensor, offset=tns.offset + pm, ap=[[tns.ap[0][0], 128], [64, 2], [2, 32]])

            for c in range(NCH):
                cur, nxt = (SSTa, SSTb) if c % 2 == 0 else (SSTb, SSTa)
                mvv = fw.marks()

                def rtr(t, c=c, cur=cur, mvv=mvv):
                    t.wait_ge(fw.F["vector"], mvv["vector"])
                    for b in range(B2):
                        t.transpose(P4[0:64, b * 128:(b + 1) * 128], cur[:, b * 64:(b + 1) * 64], IDENTF[:])
                    for b in range(B2):
                        src = AP(tensor=BSLAB.tensor, offset=BSLAB.offset + c * 256 + b,
                                 ap=[[16384, 64], [2, 128]])
                        t.transpose(PB0[:, b * 64:(b + 1) * 64], src, IDENT[0:64, 0:64])
                    t.mark()
                TE(rtr)
                mrt = fw.marks()["tensor"]

                def rup(v, c=c, cur=cur, nxt=nxt, mrt=mrt, lt_re=lt_re, lt_im=lt_im):
                    v.wait_ge(fw.F["tensor"], mrt)
                    dstS = AP(tensor=SSLAB.tensor, offset=SSLAB.offset + c * 256,
                              ap=[[16384, 64], [1, 2], [2, 128]])
                    srcS = AP(tensor=P4[:].tensor, offset=P4[:].offset,
                              ap=[[512, 64], [128, 2], [1, 128]])
                    v.tensor_copy(out=dstS, in_=srcS)
                    pb_re = AP(tensor=PB0[:].tensor, offset=PB0[:].offset, ap=[[1024, 128], [64, 2], [2, 32]])
                    pb_im = AP(tensor=PB0[:].tensor, offset=PB0[:].offset + 1, ap=[[1024, 128], [64, 2], [2, 32]])
                    v.tensor_tensor(out=TMPQ[:, 0:64].rearrange("p (a b) -> p a b", a=2), in0=sview(cur, 0), in1=lt_ap(lt_re), op=ALU.mult)
                    v.tensor_tensor(out=TMPQ[:, 64:128].rearrange("p (a b) -> p a b", a=2), in0=sview(cur, 1), in1=lt_ap(lt_im), op=ALU.mult)
                    v.tensor_tensor(out=TMPQ[:, 128:192], in0=TMPQ[:, 0:64], in1=TMPQ[:, 64:128], op=ALU.subtract)
                    v.tensor_tensor(out=sview(nxt, 0), in0=TMPQ[:, 128:192].rearrange("p (a b) -> p a b", a=2), in1=pb_re, op=ALU.add)
                    v.tensor_tensor(out=TMPQ[:, 0:64].rearrange("p (a b) -> p a b", a=2), in0=sview(cur, 1), in1=lt_ap(lt_re), op=ALU.mult)
                    v.tensor_tensor(out=TMPQ[:, 64:128].rearrange("p (a b) -> p a b", a=2), in0=sview(cur, 0), in1=lt_ap(lt_im), op=ALU.mult)
                    v.tensor_tensor(out=TMPQ[:, 128:192], in0=TMPQ[:, 0:64], in1=TMPQ[:, 64:128], op=ALU.add)
                    v.tensor_tensor(out=sview(nxt, 1), in0=TMPQ[:, 128:192].rearrange("p (a b) -> p a b", a=2), in1=pb_im, op=ALU.add)
                VE(rup)

            for g in range(8):
                h0 = ho * 128 + g * 16
                seg = h0 * N2
                mv = fw.marks()
                fw.dma("sync", AIMSEG, hb_aim[0:1, seg:seg + 512], wait=[("vector", mv["vector"]), ("tensor", mv["tensor"]), ("scalar", mv["scalar"])])
                fw.dma("sync", ARESEG, hb_are[0:1, seg:seg + 512])
                fw.dma("sync", CW2R, AP(tensor=hb_cwr.tensor, offset=seg, ap=[[0, 128], [1, 512]]))
                fw.dma("sync", CW2I, AP(tensor=hb_cwi.tensor, offset=seg, ap=[[0, 128], [1, 512]]))
                ndma = fw.dn["sync"]
                fw.dma("gpsimd", RAWS[:], hb_slab[ho, :, g * 512:(g + 1) * 512],
                       wait=[("vector", mv["vector"]), ("q:gpsimd", raws_st["n"])])
                ngp = fw.dn["gpsimd"]

                def args2(t, ndma=ndma, mv=mv):
                    t.wait_ge(fw.Dq["sync"], ndma)
                    t.wait_ge(fw.F["vector"], mv["vector"])
                    t.matmul(P0[:], IOTAS[0:1, 256:384], AIMSEG, start=True, stop=True)
                    t.matmul(P1[:], IOTAS[0:1, 256:384], ARESEG, start=True, stop=True)
                    t.mark()
                TE(args2)
                mk_a2 = fw.marks()["tensor"]
                mk_p = chain(P0[:], P1[:], mk_a2)
                bp_family(mk_p)
                mtab2 = fw.marks()

                def tr3(t, mtab2=mtab2):
                    t.wait_ge(fw.F["vector"], mtab2["vector"])
                    for k in range(8):
                        t.transpose(PB0[0:64, k * 128:(k + 1) * 128], TBB[:, k * 64:(k + 1) * 64], IDENT[:])
                    t.mark()
                    for k in range(8):
                        t.transpose(PB1[0:64, k * 128:(k + 1) * 128], TBB[:, 512 + k * 64:512 + (k + 1) * 64], IDENT[:])
                    t.mark()
                TE(tr3)
                mtr3 = fw.marks()["tensor"]

                def evp(v, mtr3=mtr3):
                    v.wait_ge(fw.F["tensor"], mtr3)
                    v.tensor_copy(out=TGB[:, 0:1024], in_=PB0[0:64, :])
                    v.tensor_copy(out=TGB[:, 1024:2048], in_=PB1[0:64, :])
                VE(evp)
                mevp = fw.marks()

                def scmm(t, mevp=mevp, h0=h0, ngp=ngp):
                    t.wait_ge(fw.F["vector"], mevp["vector"])
                    t.wait_ge(fw.Dq["gpsimd"], ngp)
                    h0l = h0 % 128
                    for k in range(16):
                        sap = AP(tensor=SSLAB.tensor, offset=SSLAB.offset + (h0l + k) * 2,
                                 ap=[[16384, 64], [1, 2], [256, 16]])
                        t.matmul(P5[:, k * 32:(k + 1) * 32], TGB[:, k * 128:(k + 1) * 128],
                                 sap, start=True, stop=True)
                    t.mark()
                TE(scmm)
                msc = fw.marks()["tensor"]

                def gadd(v, msc=msc):
                    v.wait_ge(fw.F["tensor"], msc)
                    v.tensor_tensor(out=TMP2B[:, 0:512], in0=RAWS[:], in1=P5[:], op=ALU.add)
                VE(gadd)
                mga = fw.marks()

                def gel(s, mga=mga, h0=h0):
                    s.wait_ge(fw.F["vector"], mga["vector"])
                    h0l = h0 % 128
                    dstG = AP(tensor=GOSL.tensor, offset=GOSL.offset + h0l,
                              ap=[[16384, 128], [1, 16], [2048, 2], [128, 16]])
                    s.activation(out=dstG, in_=TMP2B[:, 0:512].rearrange("p (a b c) -> p a b c", a=16, b=2),
                                 func=AF.Gelu)
                SC(gel)

            mgel = fw.marks()
            fev_marks = {}
            for half in range(4):
                b_, ch_ = divmod(half, 2)
                pbuf = PB0 if half % 2 == 0 else PB1

                def ftr(t, b_=b_, ch_=ch_, pbuf=pbuf, half=half, mgel=mgel, fev_marks=fev_marks):
                    if half == 0:
                        t.wait_ge(fw.F["scalar"], mgel["scalar"])
                    if half >= 2:
                        t.wait_ge(fw.F["vector"], fev_marks[half - 2])
                    for k in range(8):
                        cc = ch_ * 8 + k
                        t.transpose(pbuf[:, k * 128:(k + 1) * 128],
                                    GOSL[:, b_ * 2048 + cc * 128:b_ * 2048 + (cc + 1) * 128], IDENT[:])
                    t.mark()
                TE(ftr)
                mtf = fw.marks()["tensor"]

                def fev(v, b_=b_, ch_=ch_, pbuf=pbuf, ho=ho, mtf=mtf):
                    v.wait_ge(fw.F["tensor"], mtf)
                    base = (b_ * 8 + ho) * 2048 + ch_ * 1024
                    v.tensor_copy(out=R2[:, base:base + 1024], in_=pbuf[:])
                VE(fev)
                fev_marks[half] = fw.marks()["vector"]
        fw.barrier()
        if PHASES < 4:
            return _finish(nc, fw, dbg_outs, dbg, R1, R2, R3, STM, PRM, BCB, AUX, hb_aim, hb_are, hb_cwr, hb_cwi, DBC)

        _ctx = dict(locals())
        _ctx["PHASES"] = PHASES
        _part2_emit(_ctx)
        fw.barrier()

        u_dbg = dbg("u_dbg", [128, 32768], BF16)
        if u_dbg is not None:
            fw.dma("gpsimd", u_dbg[:], R1[:])
        m_dbg = dbg("m_dbg", [128, 96], F32)
        if m_dbg is not None:
            fw.dma("gpsimd", m_dbg[:], STM[:])
        prep_dbg = dbg("prep_dbg", [128, 2048], F32)
        if prep_dbg is not None:
            fw.dma("gpsimd", prep_dbg[:], PRM[:])
        convf_dbg = dbg("convf_dbg", [128, 32768], BF16)
        if convf_dbg is not None:
            fw.dma("gpsimd", convf_dbg[:], R2[:])
        x1_dbg = dbg("x1_dbg", [128, 32768], BF16)
        if x1_dbg is not None:
            fw.dma("gpsimd", x1_dbg[:], R2[:])
        x2_dbg = dbg("x2_dbg", [128, 32768], BF16)
        if x2_dbg is not None:
            fw.dma("gpsimd", x2_dbg[:], R2[:])
        xz_dbg = dbg("xz_dbg", [128, 32768], BF16)
        if xz_dbg is not None:
            fw.dma("gpsimd", xz_dbg[:], R1[:])
        ys_dbg = dbg("ys_dbg", [128, 16384], BF16)
        if ys_dbg is not None:
            fw.dma("gpsimd", ys_dbg[:], R3[:])
        x3_dbg = dbg("x3_dbg", [128, 32768], BF16)
        if x3_dbg is not None:
            fw.dma("gpsimd", x3_dbg[:], R2[:])
        for nm, src, shp, dt_ in (
                ("scan_dbg", AUX[:, 0:4096].bitcast(F32), [128, 2048], F32),
                ("cbc2_dbg", AUX[:, 14336:14464].bitcast(F32), [128, 64], F32),
                ("hst_dbg", AUX[:, 4096:8192].bitcast(F32), [128, 2048], F32)):
            tp = dbg(nm, shp, dt_)
            if tp is not None:
                fw.dma("gpsimd", tp[:], src)
        fw.barrier()

    return nc, dbg_outs


def _finish(nc, fw, dbg_outs, dbg, R1, R2, R3, STM, PRM, BCB, AUX, hb_aim, hb_are, hb_cwr, hb_cwi, DBC=None):
    taps = {
        "u_dbg": (R1, [128, 32768], BF16), "m_dbg": (STM, [128, 96], F32),
        "prep_dbg": (PRM, [128, 2048], F32), "convf_dbg": (R2, [128, 32768], BF16),
        "k8_dbg": (R3[:, 8192:10240], [128, 2048], BF16),
        "bslab_dbg": (R3[0:64, 0:4096], [64, 4096], BF16),
        "sslab_dbg": (R3[0:64, 4096:8192], [64, 4096], BF16),
        "gosl_dbg": (R3[:, 10240:14336], [128, 4096], BF16),
        "dbc_dbg": (DBC, [128, 1024], BF16),
        "tbl_dbg": (AUX[:, 0:2048], [128, 2048], BF16),
        "tga_dbg": (AUX[0:64, 16384:20480], [64, 4096], BF16),
        "scf_dbg": (AUX[:, 4096:8192].bitcast(F32), [128, 2048], F32),
        "trig_dbg": (AUX[:, 2048:4096], [128, 2048], BF16),
        "scan_dbg": (AUX[:, 0:4096].bitcast(F32), [128, 2048], F32),
        "cbc2_dbg": (AUX[:, 14336:14464].bitcast(F32), [128, 64], F32),
        "hst_dbg": (AUX[:, 4096:8192].bitcast(F32), [128, 2048], F32),
    }
    for name, (tns, shape, dt) in taps.items():
        t = dbg(name, shape, dt)
        if t is not None:
            fw.dma("gpsimd", t[:], tns if isinstance(tns, AP) else tns[:])
    rows_dbg = dbg("rows_dbg", [4, 32768], F32)
    if rows_dbg is not None:
        fw.dma("gpsimd", rows_dbg[0:1, :], hb_aim[:])
        fw.dma("gpsimd", rows_dbg[1:2, :], hb_are[:])
        fw.dma("gpsimd", rows_dbg[2:3, :], hb_cwr[:])
        fw.dma("gpsimd", rows_dbg[3:4, :], hb_cwi[:])
    fw.barrier()
    return nc, dbg_outs


def make_consts():
    ident = np.eye(128, dtype=ml_dtypes.bfloat16)
    iota_negj = (-np.arange(128, dtype=np.float32))[None, :]
    iota_t = np.arange(128, dtype=np.float32)[None, :]
    iota_t128 = (np.arange(128, dtype=np.float32) + 128.0)[None, :]
    j = np.arange(128)[:, None]; t = np.arange(128)[None, :]
    mask_triu = (t >= j).astype(ml_dtypes.bfloat16)
    ones_row = np.ones((1, 128), np.float32)
    return dict(identb=ident, iota_negj=iota_negj, iota_t=iota_t, iota_t128=iota_t128,
                mask_triu=mask_triu, ones_row=ones_row)


def core_inputs(inputs, core):
    d = {}
    d["x_in"] = np.ascontiguousarray(inputs["L0_attn_a_u"][2 * core:2 * core + 2])
    d["log_dt"] = np.ascontiguousarray(inputs["L0_attn_a_log_dt"]).reshape(1, D)
    d["log_w_real"] = np.ascontiguousarray(inputs["L0_attn_a_log_w_real"]).reshape(1, N2)
    d["w_imag"] = np.ascontiguousarray(inputs["L0_attn_a_w_imag"]).reshape(1, N2)
    d["C_real"] = np.ascontiguousarray(inputs["L0_attn_a_C_real"]).reshape(D, N2)
    d["C_imag"] = np.ascontiguousarray(inputs["L0_attn_a_C_imag"]).reshape(D, N2)
    d["D_a"] = np.ascontiguousarray(inputs["L0_attn_a_D"]).reshape(1, D)
    d["W_out_a"] = np.ascontiguousarray(inputs["L0_attn_a_W_out"]).reshape(D, D)
    d["b_out_a"] = np.ascontiguousarray(inputs["L0_attn_a_b_out"]).reshape(1, D)
    d["tern"] = np.ascontiguousarray(inputs["L0_ffn_ternary_weight"]).reshape(D, D)
    d["bias_f"] = np.ascontiguousarray(inputs["L0_ffn_bias"]).reshape(1, D)
    d["act_scale"] = np.asarray(inputs["L0_ffn_act_scale"], np.float32).reshape(1, 1)
    d["w_in_b"] = np.ascontiguousarray(inputs["L0_attn_b_w_in"]).reshape(2 * D, D)
    d["b_in_b"] = np.ascontiguousarray(inputs["L0_attn_b_b_in"]).reshape(1, 2 * D)
    d["A_b"] = np.ascontiguousarray(inputs["L0_attn_b_A"]).reshape(D, S64)
    d["Bm_b"] = np.ascontiguousarray(inputs["L0_attn_b_Bm"]).reshape(D, S64)
    d["Cm_b"] = np.ascontiguousarray(inputs["L0_attn_b_Cm"]).reshape(D, S64)
    d["Dv_b"] = np.ascontiguousarray(inputs["L0_attn_b_Dv"]).reshape(1, D)
    d["w_out_b"] = np.ascontiguousarray(inputs["L0_attn_b_w_out"]).reshape(D, D)
    d["b_out_b"] = np.ascontiguousarray(inputs["L0_attn_b_b_out"]).reshape(1, D)
    d["g_norm"] = np.ascontiguousarray(inputs["L0_attn_b_g_norm"]).reshape(1, D)
    d.update(make_consts())
    return d


def kernel(**inputs):
    nc, _ = build_program()
    ins = [core_inputs(inputs, i) for i in range(8)]
    res = run_bass_kernel_spmd(nc, ins, core_ids=list(range(8)))
    parts = [np.asarray(res.results[i]["out"]).reshape(B2, L, D) for i in range(8)]
    return np.concatenate(parts, axis=0).astype(np.float32)


# version 03
# speedup vs baseline: 1.1062x; 1.1062x over previous
import math
import numpy as np
from contextlib import ExitStack

import ml_dtypes
import concourse.bass as bass
import concourse.mybir as mybir
from concourse.bass import AP
from concourse.bass_utils import run_bass_kernel_spmd



ENGS = ("tensor", "vector", "scalar", "gpsimd", "sync")
COMPUTE_ENGS = ("tensor", "vector", "scalar", "gpsimd")
OPS = {
    "tensor_tensor", "tensor_scalar", "tensor_copy", "tensor_reduce", "reduce_sum",
    "reduce_max", "scalar_tensor_tensor", "tensor_tensor_reduce", "bn_stats",
    "bn_aggr", "reciprocal", "memset", "select", "copy_predicated", "iota",
    "activation", "copy", "add", "mul", "sign", "sqrt", "square",
    "matmul", "transpose", "ldweights", "tensor_tensor_scan", "stream_shuffle",
    "tensor_add", "tensor_mul", "tensor_sub", "tensor_max", "tensor_relu",
    "tensor_scalar_mul", "tensor_scalar_add", "partition_broadcast",
    "partition_all_reduce",
}


class Eng:

    def __init__(self, fw, name, handle):
        self._fw = fw
        self._name = name
        self._h = handle
        self.last = None

    def __getattr__(self, attr):
        real = getattr(self._h, attr)
        if attr not in OPS:
            return real
        fw, name = self._fw, self._name

        def call(*a, **k):
            inst = real(*a, **k)
            self.last = inst
            if name in ("vector", "scalar", "gpsimd"):
                nb = fw.n[name]
                if nb > 0:
                    inst._wait_ge(fw.F[name], nb)
                inst.then_inc(fw.F[name], 1)
                fw.n[name] += 1
            return inst
        return call

    def mark(self):
        assert self._name == "tensor"
        assert self.last is not None
        self.last.then_inc(self._fw.F["tensor"], 1)
        self._fw.n["tensor"] += 1
        self.last = None
        return self._fw.n["tensor"]


class FW:
    def __init__(self, nc, st, blk):
        self.nc = nc
        self.blk = blk
        self.F = {e: st.enter_context(nc.semaphore(f"F_{e}")) for e in COMPUTE_ENGS}
        self.n = {e: 0 for e in COMPUTE_ENGS}
        self.Dq = {q: st.enter_context(nc.semaphore(f"D_{q}")) for q in ("sync", "gpsimd")}
        self.dn = {q: 0 for q in ("sync", "gpsimd")}
        self._eng_cache = {}

    def _sect(self, name, body):
        getattr(self.blk, name)(body)

    def sec(self, name):
        raise NotImplementedError

    def run(self, name, fn):
        out = {}

        def body(h):
            out["p"] = Eng(self, name, h)
            fn(out["p"])
        self._sect(name, body)
        return out.get("p")

    def dma(self, qname, out_ap, in_ap, wait=()):
        def body(h):
            for (e, m) in wait:
                if m > 0:
                    if e.startswith("q:"):
                        h.wait_ge(self.Dq[e[2:]], m)
                    else:
                        h.wait_ge(self.F[e], m)
            h.dma_start(out=out_ap, in_=in_ap).then_inc(self.Dq[qname], 16)
        self._sect(qname, body)
        self.dn[qname] += 16
        return self.dn[qname]

    def wait(self, eng, deps=(), dmas=()):
        def body(h):
            for (e, m) in deps:
                if m > 0:
                    h.wait_ge(self.F[e], m)
            for (q, c) in dmas:
                if c > 0:
                    h.wait_ge(self.Dq[q], c)
        self._sect(eng, body)

    def marks(self):
        return dict(self.n)

    def barrier(self):
        snap_n = dict(self.n)
        snap_d = dict(self.dn)
        for e in COMPUTE_ENGS:
            def body(h, e=e):
                for o in COMPUTE_ENGS:
                    if o != e and snap_n[o] > 0:
                        h.wait_ge(self.F[o], snap_n[o])
                for q in ("sync", "gpsimd"):
                    if snap_d[q] > 0:
                        h.wait_ge(self.Dq[q], snap_d[q])
            self._sect(e, body)
        for q in ("sync", "gpsimd"):
            def body(h, q=q):
                for o in COMPUTE_ENGS:
                    if snap_n[o] > 0:
                        h.wait_ge(self.F[o], snap_n[o])
                oq = "gpsimd" if q == "sync" else "sync"
                if snap_d[oq] > 0:
                    h.wait_ge(self.Dq[oq], snap_d[oq])
            self._sect(q, body)


def _part2_emit(ctx):
    fw = ctx["fw"]; nc = ctx["nc"]
    VE, SC, TE, GP = ctx["VE"], ctx["SC"], ctx["TE"], ctx["GP"]
    R1, R2, R3, AUX = ctx["R1"], ctx["R2"], ctx["R3"], ctx["AUX"]
    IDENT, IDENTF, IOTAS, ONEC = ctx["IDENT"], ctx["IDENTF"], ctx["IOTAS"], ctx["ONEC"]
    BOB, SML, STT = ctx["BOB"], ctx["SML"], ctx["STT"]
    ASC, BSC, CBC, HST, ZST, GST, T1S, HCS = (ctx["ASC"], ctx["BSC"], ctx["CBC"],
                                              ctx["HST"], ctx["ZST"], ctx["GST"],
                                              ctx["T1S"], ctx["HCS"])
    P0, P1, P2, P3, P4, P5 = ctx["P0"], ctx["P1"], ctx["P2"], ctx["P3"], ctx["P4"], ctx["P5"]
    PB0, PB1, PW = ctx["PB0"], ctx["PB1"], ctx["PW"]
    OTS = None
    W_out_a, tern, w_in_b, w_out_b = ctx["W_out_a"], ctx["tern"], ctx["w_in_b"], ctx["w_out_b"]
    A_b, Bm_b, Cm_b = ctx["A_b"], ctx["Bm_b"], ctx["Cm_b"]
    x_in, out = ctx["x_in"], ctx["out"]
    hb_winT = ctx["hb_winT"]
    dbgf = ctx["dbg"]
    PHASES = ctx["PHASES"] if "PHASES" in ctx else 99

    SCOPE = ctx["SCOPE"]

    def sbslice(col0, col1):
        return AUX[:, col0:col1]

    def wprep(Wsrc, dst, is_tern=False):
        WSTG = AUX[:, 8192:10240]
        for oc in range(8):
            slot = oc % 2
            stg = WSTG[:, slot * 1024:(slot + 1) * 1024]
            mprev = fw.marks()
            nld = fw.dma("gpsimd", stg, Wsrc[oc * 128:(oc + 1) * 128, :],
                         wait=[("vector", mprev["vector"]), ("tensor", mprev["tensor"])])
            if is_tern:
                def cvt(v, stg=stg, nld=nld):
                    v.wait_ge(fw.Dq["gpsimd"], nld)
                    v.tensor_scalar(out=stg, in0=stg, scalar1=-1.0, scalar2=None, op0=ALU.add)
                VE(cvt)
            mk = fw.marks()

            def trw(t, stg=stg, nld=nld, mk=mk, is_tern=is_tern):
                if not is_tern:
                    t.wait_ge(fw.Dq["gpsimd"], nld)
                else:
                    t.wait_ge(fw.F["vector"], mk["vector"])
                for k in range(8):
                    t.transpose(PB0[:, k * 128:(k + 1) * 128], stg[:, k * 128:(k + 1) * 128], IDENT[:])
                t.mark()
            TE(trw)
            mt = fw.marks()["tensor"]

            def evw(v, oc=oc, mt=mt, dst=dst):
                v.wait_ge(fw.F["tensor"], mt)
                dap = AP(tensor=dst.tensor, offset=dst.offset + oc * 128,
                         ap=[[dst.ap[0][0], 128], [1024, 8], [1, 128]])
                v.tensor_copy(out=dap, in_=PB0[:].rearrange("p (a b) -> p a b", a=8))
            VE(evw)

    SCOPE("p2g_wout")
    WoT = AUX[:, 0:8192]
    XS = AUX[:, 10240:14336]
    X1S = AUX[:, 14336:18432]
    XTT = AUX[:, 18432:20480]
    wprep(W_out_a, WoT)
    for tau in range(8):
        b, q = divmod(tau, 4)
        for j in range(4):
            c = q * 4 + j
            slot = j % 2
            xstg = XTT[:, slot * 1024:(slot + 1) * 1024]
            mprev = fw.marks()
            nld = fw.dma("gpsimd", xstg, x_in[b, c * 128:(c + 1) * 128, :],
                         wait=[("tensor", mprev["tensor"]), ("vector", mprev["vector"])])

            def trx(t, xstg=xstg, nld=nld):
                t.wait_ge(fw.Dq["gpsimd"], nld)
                for k in range(8):
                    t.transpose(PB1[:, k * 128:(k + 1) * 128], xstg[:, k * 128:(k + 1) * 128], IDENT[:])
                t.mark()
            TE(trx)
            mt = fw.marks()["tensor"]

            def evx(v, j=j, mt=mt):
                v.wait_ge(fw.F["tensor"], mt)
                dap = AP(tensor=XS.tensor, offset=XS.offset + j * 128,
                         ap=[[XS.ap[0][0], 128], [512, 8], [1, 128]])
                v.tensor_copy(out=dap, in_=PB1[:].rearrange("p (a b) -> p a b", a=8))
            VE(evx)
        mxs = fw.marks()
        for w in range(2):
            mev = fw.marks()

            def wmm(t, b=b, q=q, w=w, mev=mev):
                t.wait_ge(fw.F["vector"], mev["vector"])
                for oci in range(4):
                    oc = w * 4 + oci
                    for k in range(8):
                        t.matmul(PW[oci][:], WoT[:, (k * 8 + oc) * 128:(k * 8 + oc + 1) * 128],
                                 R2[:, (b * 8 + k) * 2048 + q * 512:(b * 8 + k) * 2048 + (q + 1) * 512],
                                 start=(k == 0), stop=(k == 7))
                t.mark()
            TE(wmm)
            mw = fw.marks()["tensor"]

            def wev(v, w=w, mw=mw):
                v.wait_ge(fw.F["tensor"], mw)
                for oci in range(4):
                    oc = w * 4 + oci
                    v.scalar_tensor_tensor(out=X1S[:, oc * 512:(oc + 1) * 512], in0=PW[oci][:],
                                           scalar=BOB[:, oc:oc + 1], in1=XS[:, oc * 512:(oc + 1) * 512],
                                           op0=ALU.add, op1=ALU.add)
            VE(wev)

        def x1wr(v, b=b, q=q):
            dap = AP(tensor=R2[:].tensor, offset=R2[:].offset + b * 16384 + q * 512,
                     ap=[[32768, 128], [2048, 8], [1, 512]])
            v.tensor_copy(out=dap, in_=X1S[:].rearrange("p (a b) -> p a b", a=8))
        VE(x1wr)
    fw.barrier()
    if PHASES < 5:
        return

    def sbar_prep():
        def v(v):
            v.tensor_scalar(out=SML[0:1, 49:50], in0=SML[0:1, 48:49], scalar1=1e-5,
                            scalar2=None, op0=ALU.max)
            v.reciprocal(out=SML[0:1, 50:51], in_=SML[0:1, 49:50])
        VE(v)
        mk = fw.marks()

        def t(t):
            t.wait_ge(fw.F["vector"], mk["vector"])
            t.matmul(P4[:, 0:1], IOTAS[0:1, 384:512], SML[0:1, 49:50], start=True, stop=True)
            t.mark()
        TE(t)
        mt = fw.marks()["tensor"]

        def v2(v):
            v.wait_ge(fw.F["tensor"], mt)
            v.tensor_copy(out=SML[:, 51:52], in_=P4[:, 0:1])
        VE(v2)
    ffn_state = {"sbar_done": False}

    def emit_ffn():
        TTW = AUX[:, 0:8192]
        XQ = AUX[:, 10240:14336]
        QF1 = AUX[:, 14336:15360].bitcast(F32)
        QF2 = AUX[:, 15360:16384].bitcast(F32)
        EPS_ = AUX[:, 16384:20480]
        SQS = EPS_[:, 0:1024]
        BCSM = AUX[:, 8192:9216].bitcast(F32)
        BCSR = AUX[:, 9216:10240].bitcast(F32)
        RWB = AUX[0:1, 16384:17408].bitcast(F32)
        wprep(tern, TTW, is_tern=True)
        if not ffn_state["sbar_done"]:
            sbar_prep()
            ffn_state["sbar_done"] = True
        for tau in range(8):
            b, q = divmod(tau, 4)

            def xdo(k):
                return R2[:, (b * 8 + k) * 2048 + q * 512:(b * 8 + k) * 2048 + (q + 1) * 512]
            smm_marks = {}
            for do in range(8):
                slot = SQS[:, (do % 2) * 512:(do % 2) * 512 + 512]

                def sq(v, do=do, slot=slot):
                    if do >= 2:
                        v.wait_ge(fw.F["tensor"], smm_marks[do - 2])
                    v.tensor_tensor(out=slot, in0=xdo(do), in1=xdo(do), op=ALU.mult)
                VE(sq)
                mq = fw.marks()

                def smm(t, do=do, mq=mq, slot=slot):
                    t.wait_ge(fw.F["vector"], mq["vector"])
                    t.matmul(P4[0:1, 0:512], ONEC[:, 0:1], xdo(do), start=(do == 0), stop=(do == 7))
                    t.matmul(P5[0:1, 0:512], ONEC[:, 0:1], slot, start=(do == 0), stop=(do == 7))
                    t.mark()
                TE(smm)
                smm_marks[do] = fw.marks()["tensor"]
            ms = fw.marks()["tensor"]

            def spost(v, ms=ms):
                v.wait_ge(fw.F["tensor"], ms)
                v.tensor_scalar(out=QF1[0:1, 0:512], in0=P4[0:1, 0:512], scalar1=1.0 / D,
                                scalar2=None, op0=ALU.mult)
                v.tensor_tensor(out=QF2[0:1, 0:512], in0=QF1[0:1, 0:512], in1=QF1[0:1, 0:512], op=ALU.mult)
                v.scalar_tensor_tensor(out=QF2[0:1, 0:512], in0=P5[0:1, 0:512], scalar=1.0 / D,
                                       in1=QF2[0:1, 0:512], op0=ALU.mult, op1=ALU.subtract)
                v.tensor_scalar(out=QF2[0:1, 0:512], in0=QF2[0:1, 0:512], scalar1=EPS_LN,
                                scalar2=None, op0=ALU.add)
                v.reciprocal(out=RWB[:], in_=QF2[0:1, 0:512])
            VE(spost)
            mv = fw.marks()

            def ssq(s, mv=mv):
                s.wait_ge(fw.F["vector"], mv["vector"])
                s.activation(out=QF2[0:1, 0:512], in_=RWB[:], func=AF.Sqrt)
            SC(ssq)
            mss = fw.marks()

            def rfold(v, mss=mss):
                v.wait_ge(fw.F["scalar"], mss["scalar"])
                v.tensor_scalar(out=QF2[0:1, 0:512], in0=QF2[0:1, 0:512],
                                scalar1=SML[0:1, 50:51], scalar2=None, op0=ALU.mult)
            VE(rfold)
            mr = fw.marks()

            def bcmm(t, mr=mr):
                t.wait_ge(fw.F["vector"], mr["vector"])
                t.matmul(P4[:], IOTAS[0:1, 384:512], QF1[0:1, 0:512], start=True, stop=True)
                t.matmul(P5[:], IOTAS[0:1, 384:512], QF2[0:1, 0:512], start=True, stop=True)
                t.mark()
            TE(bcmm)
            mbc = fw.marks()["tensor"]

            def bcev(v, mbc=mbc):
                v.wait_ge(fw.F["tensor"], mbc)
                v.tensor_copy(out=BCSM[:], in_=P4[:])
                v.tensor_copy(out=BCSR[:], in_=P5[:])
            VE(bcev)
            for do in range(8):
                def qn(v, do=do):
                    v.tensor_tensor(out=QF1[:, 0:512], in0=xdo(do), in1=BCSM[:], op=ALU.subtract)
                    v.tensor_tensor(out=QF1[:, 0:512], in0=QF1[:, 0:512], in1=BCSR[:], op=ALU.mult)
                    v.tensor_scalar(out=QF1[:, 0:512], in0=QF1[:, 0:512], scalar1=127.0,
                                    scalar2=-127.0, op0=ALU.min, op1=ALU.max)
                    v.tensor_scalar(out=QF2[:, 0:512], in0=QF1[:, 0:512], scalar1=MAGIC,
                                    scalar2=-MAGIC, op0=ALU.add, op1=ALU.add)
                    v.tensor_scalar(out=XQ[:, do * 512:(do + 1) * 512], in0=QF1[:, 0:512],
                                    scalar1=MAGIC64, scalar2=-MAGIC64, op0=ALU.add, op1=ALU.add)
                    v.tensor_tensor(out=QF1[:, 0:512], in0=XQ[:, do * 512:(do + 1) * 512],
                                    in1=QF2[:, 0:512], op=ALU.subtract)
                    v.scalar_tensor_tensor(out=XQ[:, do * 512:(do + 1) * 512], in0=QF1[:, 0:512],
                                           scalar=SML[:, 53:54], in1=QF2[:, 0:512],
                                           op0=ALU.mult, op1=ALU.add)
                VE(qn)
            for w in range(2):
                mev = fw.marks()

                def fmm(t, w=w, mev=mev):
                    t.wait_ge(fw.F["vector"], mev["vector"])
                    for oci in range(4):
                        oc = w * 4 + oci
                        for k in range(8):
                            t.matmul(PW[oci][:], TTW[:, (k * 8 + oc) * 128:(k * 8 + oc + 1) * 128],
                                     XQ[:, k * 512:(k + 1) * 512], start=(k == 0), stop=(k == 7))
                    t.mark()
                TE(fmm)
                mw = fw.marks()["tensor"]

                def fev(v, w=w, mw=mw, b=b, q=q):
                    v.wait_ge(fw.F["tensor"], mw)
                    for oci in range(4):
                        oc = w * 4 + oci
                        v.tensor_scalar(out=QF1[:, 0:512], in0=PW[oci][:], scalar1=SML[:, 51:52],
                                        scalar2=BOB[:, 8 + oc:8 + oc + 1], op0=ALU.mult, op1=ALU.add)
                        v.tensor_tensor(out=EPS_[:, oc * 512:(oc + 1) * 512], in0=QF1[:, 0:512],
                                        in1=xdo(oc), op=ALU.add)
                VE(fev)

            def x2wr(v, b=b, q=q):
                dap = AP(tensor=R2[:].tensor, offset=R2[:].offset + b * 16384 + q * 512,
                         ap=[[32768, 128], [2048, 8], [1, 512]])
                v.tensor_copy(out=dap, in_=EPS_[:].rearrange("p (a b) -> p a b", a=8))
            VE(x2wr)
        fw.barrier()

    SCOPE("p3_ffn1")
    emit_ffn()
    if PHASES < 6:
        return

    SCOPE("p4a_winT")
    WSTG = AUX[:, 8192:10240]
    for et in range(16):
        slot = et % 2
        stg = WSTG[:, slot * 1024:(slot + 1) * 1024]
        mprev = fw.marks()
        nld = fw.dma("gpsimd", stg, w_in_b[et * 128:(et + 1) * 128, :],
                     wait=[("vector", mprev["vector"]), ("tensor", mprev["tensor"])])

        def trwi(t, stg=stg, nld=nld):
            t.wait_ge(fw.Dq["gpsimd"], nld)
            for k in range(8):
                t.transpose(PB0[:, k * 128:(k + 1) * 128], stg[:, k * 128:(k + 1) * 128], IDENT[:])
            t.mark()
        TE(trwi)
        mt = fw.marks()["tensor"]

        def evwi(v, et=et, mt=mt):
            v.wait_ge(fw.F["tensor"], mt)
            dap = AP(tensor=R3[:].tensor, offset=R3[:].offset + et * 128,
                     ap=[[16384, 128], [2048, 8], [1, 128]])
            v.tensor_copy(out=dap, in_=PB0[:].rearrange("p (a b) -> p a b", a=8))
        VE(evwi)
    SCOPE("p4b_ln3_win")
    WIN2 = AUX[:, 0:4096]
    XSUB = AUX[:, 4096:8192]
    BCS4M = AUX[:, 8192:9216].bitcast(F32)
    BCS4R = AUX[:, 9216:10240].bitcast(F32)
    XN = AUX[:, 10240:14336]
    QF4 = AUX[:, 14336:15360].bitcast(F32)
    QF5 = AUX[:, 15360:16384].bitcast(F32)
    SQ4 = AUX[:, 16384:17408]
    RW3 = AUX[0:1, 17408:18432].bitcast(F32)

    for tau in range(4):
        b, qs = divmod(tau, 2)

        def xe(do):
            base = (b * 8 + do) * 2048 + qs * 1024
            return AP(tensor=R2[:].tensor, offset=R2[:].offset + base,
                      ap=[[32768, 128], [2, 512]])
        smm3_marks = {}
        for do in range(8):
            slot = SQ4[:, (do % 2) * 512:(do % 2) * 512 + 512]

            def cpx(v, do=do, slot=slot):
                if do >= 2:
                    v.wait_ge(fw.F["tensor"], smm3_marks[do - 2])
                v.tensor_copy(out=XSUB[:, do * 512:(do + 1) * 512], in_=xe(do))
                v.tensor_tensor(out=slot, in0=XSUB[:, do * 512:(do + 1) * 512],
                                in1=XSUB[:, do * 512:(do + 1) * 512], op=ALU.mult)
            VE(cpx)
            mq = fw.marks()

            def smm3(t, do=do, mq=mq, slot=slot):
                t.wait_ge(fw.F["vector"], mq["vector"])
                t.matmul(P4[0:1, 0:512], ONEC[:, 0:1], XSUB[:, do * 512:(do + 1) * 512],
                         start=(do == 0), stop=(do == 7))
                t.matmul(P5[0:1, 0:512], ONEC[:, 0:1], slot, start=(do == 0), stop=(do == 7))
                t.mark()
            TE(smm3)
            smm3_marks[do] = fw.marks()["tensor"]
        ms3 = fw.marks()["tensor"]

        def spost3(v, ms3=ms3):
            v.wait_ge(fw.F["tensor"], ms3)
            v.tensor_scalar(out=QF4[0:1, 0:512], in0=P4[0:1, 0:512], scalar1=1.0 / D,
                            scalar2=None, op0=ALU.mult)
            v.tensor_tensor(out=QF5[0:1, 0:512], in0=QF4[0:1, 0:512], in1=QF4[0:1, 0:512], op=ALU.mult)
            v.scalar_tensor_tensor(out=QF5[0:1, 0:512], in0=P5[0:1, 0:512], scalar=1.0 / D,
                                   in1=QF5[0:1, 0:512], op0=ALU.mult, op1=ALU.subtract)
            v.tensor_scalar(out=QF5[0:1, 0:512], in0=QF5[0:1, 0:512], scalar1=1.0 + EPS_RMS,
                            scalar2=EPS_RMS * EPS_LN, op0=ALU.mult, op1=ALU.add)
            v.reciprocal(out=RW3[:], in_=QF5[0:1, 0:512])
        VE(spost3)
        mv3 = fw.marks()

        def ssq3(s, mv3=mv3):
            s.wait_ge(fw.F["vector"], mv3["vector"])
            s.activation(out=QF5[0:1, 0:512], in_=RW3[:], func=AF.Sqrt)
        SC(ssq3)
        ms4 = fw.marks()

        def bc3(t, ms4=ms4):
            t.wait_ge(fw.F["scalar"], ms4["scalar"])
            t.matmul(P4[:], IOTAS[0:1, 384:512], QF4[0:1, 0:512], start=True, stop=True)
            t.matmul(P5[:], IOTAS[0:1, 384:512], QF5[0:1, 0:512], start=True, stop=True)
            t.mark()
        TE(bc3)
        mb3 = fw.marks()["tensor"]

        def bce3(v, mb3=mb3):
            v.wait_ge(fw.F["tensor"], mb3)
            v.tensor_copy(out=BCS4M[:], in_=P4[:])
            v.tensor_copy(out=BCS4R[:], in_=P5[:])
        VE(bce3)
        for do in range(8):
            def nrm(v, do=do):
                v.tensor_tensor(out=QF4[:], in0=XSUB[:, do * 512:(do + 1) * 512],
                                in1=BCS4M[:], op=ALU.subtract)
                v.scalar_tensor_tensor(out=XN[:, do * 512:(do + 1) * 512], in0=QF4[:],
                                       scalar=BOB[:, 24 + do:24 + do + 1], in1=BCS4R[:],
                                       op0=ALU.mult, op1=ALU.mult)
            VE(nrm)
        for w in range(4):
            mev = fw.marks()
            for k in range(8):
                wk = R3[:, k * 2048:(k + 1) * 2048]

                def wmm4(t, w=w, k=k, wk=wk, mev=mev):
                    if k == 0:
                        t.wait_ge(fw.F["vector"], mev["vector"])
                    for eci in range(4):
                        ec = w * 4 + eci
                        t.matmul(PW[eci][:], wk[:, ec * 128:(ec + 1) * 128],
                                 XN[:, k * 512:(k + 1) * 512], start=(k == 0), stop=(k == 7))
                    t.mark()
                TE(wmm4)
            mw4 = fw.marks()["tensor"]

            def wev4(v, w=w, mw4=mw4, b=b, qs=qs):
                v.wait_ge(fw.F["tensor"], mw4)
                for eci in range(4):
                    ec = w * 4 + eci
                    v.tensor_scalar(out=R1[:, (b * 16 + ec) * 1024 + qs * 512:(b * 16 + ec) * 1024 + (qs + 1) * 512],
                                    in0=PW[eci][:], scalar1=SML[:, 16 + ec:16 + ec + 1],
                                    scalar2=None, op0=ALU.add)
            VE(wev4)
    fw.barrier()

    SCOPE("p4c_abc")
    for do in range(8):
        for b in range(B2):
            fw.dma("sync", ASC[:, (b * 8 + do) * 64:(b * 8 + do + 1) * 64],
                   A_b[do * 128:(do + 1) * 128, :])
            fw.dma("sync", BSC[:, (b * 8 + do) * 64:(b * 8 + do + 1) * 64],
                   Bm_b[do * 128:(do + 1) * 128, :])
    for do in range(8):
        fw.dma("sync", GST[:, do * 64:(do + 1) * 64], Cm_b[do * 128:(do + 1) * 128, :])
    ncm = fw.dn["sync"]

    def onef(v):
        v.memset(STT[:, 0:1], 1.0)
        v.memset(HST[:], 0.0)
    VE(onef)
    monef = fw.marks()

    def csum_mm(t, monef=monef):
        t.wait_ge(fw.Dq["sync"], ncm)
        t.wait_ge(fw.F["vector"], monef["vector"])
        for do in range(8):
            t.matmul(P4[0:1, 0:64], STT[:, 0:1], GST[:, do * 64:(do + 1) * 64],
                     start=(do == 0), stop=(do == 7))
        t.mark()
    TE(csum_mm)
    mcs = fw.marks()["tensor"]

    def csum_ev(v, mcs=mcs):
        v.wait_ge(fw.F["tensor"], mcs)
        v.tensor_copy(out=QF4[0:1, 0:64], in_=P4[0:1, 0:64])
    VE(csum_ev)
    mce = fw.marks()

    def cbc_mm(t, mce=mce):
        t.wait_ge(fw.F["vector"], mce["vector"])
        t.matmul(P5[:, 0:64], IOTAS[0:1, 384:512], QF4[0:1, 0:64], start=True, stop=True)
        t.mark()
    TE(cbc_mm)
    mcb = fw.marks()["tensor"]

    def cbc_ev(v, mcb=mcb):
        v.wait_ge(fw.F["tensor"], mcb)
        v.tensor_copy(out=CBC[:], in_=P5[:, 0:64])
    VE(cbc_ev)
    fw.barrier()
    if PHASES < 7:
        return

    SCOPE("p4d_scan")
    def xt_bc(t):
        return AP(tensor=R1[:].tensor, offset=R1[:].offset + t,
                  ap=[[32768, 128], [16384, 2], [1024, 8], [0, 64]])

    cbc_ap = AP(tensor=CBC[:].tensor, offset=CBC[:].offset,
                ap=[[CBC[:].ap[0][0], 128], [0, 2], [0, 8], [1, 64]])
    GB = [GST, AUX[:, 14464:16512].bitcast(F32)]
    HB = [HST, AUX[:, 16512:18560].bitcast(F32)]
    zmark = {}
    smark = {-1: fw.marks()["scalar"]}
    gmark = {}
    for t_ in range(1024):
        def g_op(g, t_=t_):
            if t_ >= 2:
                g.wait_ge(fw.F["vector"], zmark[t_ - 2])
            g.tensor_tensor(out=GB[t_ % 2][:], in0=BSC[:], in1=xt_bc(t_), op=ALU.mult)
        GP(g_op)
        gmark[t_] = fw.marks()["gpsimd"]

        def step_v(v, t_=t_):
            v.wait_ge(fw.F["scalar"], smark[t_ - 1])
            v.wait_ge(fw.F["gpsimd"], gmark[t_])
            v.tensor_tensor(out=T1S[:], in0=HB[t_ % 2][:], in1=ASC[:], op=ALU.mult)
            v.tensor_tensor(out=ZST[:], in0=T1S[:], in1=GB[t_ % 2][:], op=ALU.add)
        VE(step_v)
        zmark[t_] = fw.marks()["vector"]

        def step_s(s, t_=t_):
            s.wait_ge(fw.F["vector"], zmark[t_])
            s.activation(out=HB[(t_ + 1) % 2][:], in_=ZST[:], func=AF.Tanh)
        SC(step_s)
        smark[t_] = fw.marks()["scalar"]

        def step_y(v, t_=t_):
            v.wait_ge(fw.F["scalar"], smark[t_])
            v.tensor_tensor(out=HCS[:], in0=HB[(t_ + 1) % 2][:], in1=cbc_ap, op=ALU.mult)
            ysap = AP(tensor=R3[:].tensor, offset=R3[:].offset + t_,
                      ap=[[16384, 128], [8192, 2], [1024, 8]])
            with nc.allow_low_precision(reason="ys store bf16; DVE reduce accumulates fp32"):
                v.tensor_reduce(out=ysap, in_=HCS[:].rearrange("p (a b c) -> p a b c", a=2, b=8),
                                axis=AX.X, op=ALU.add)
        VE(step_y)
    fw.barrier()
    if PHASES < 8:
        return

    SCOPE("p4e_post")
    for b in range(B2):
        for do in range(8):
            xp = R1[:, (b * 16 + do) * 1024:(b * 16 + do + 1) * 1024]
            gcol = (b * 16 + 8 + do)
            gate = R1[:, gcol * 1024:(gcol + 1) * 1024]
            ys = R3[:, b * 8192 + do * 1024:b * 8192 + (do + 1) * 1024]

            mpv = fw.marks()["vector"]

            def silu(s, gate=gate, mpv=mpv):
                s.wait_ge(fw.F["vector"], mpv)
                s.activation(out=GST[:], in_=gate, func=AF.Silu)
            SC(silu)
            msl = fw.marks()["scalar"]

            def opre(v, xp=xp, gate=gate, ys=ys, do=do, msl=msl):
                v.wait_ge(fw.F["scalar"], msl)
                v.tensor_scalar(out=T1S[:], in0=xp, scalar1=BOB[:, 16 + do:16 + do + 1],
                                scalar2=None, op0=ALU.mult)
                v.tensor_tensor(out=HCS[:], in0=ys, in1=T1S[:], op=ALU.add)
                v.tensor_tensor(out=gate, in0=HCS[:], in1=GST[:], op=ALU.mult)
            VE(opre)
    WOB = AUX[:, 0:8192]
    wprep(w_out_b, WOB)
    OBS = AUX[:, 10240:10752]
    for tau in range(4):
        b, qs = divmod(tau, 2)
        for w in range(2):
            mev = fw.marks()

            def womm(t, b=b, qs=qs, w=w, mev=mev):
                t.wait_ge(fw.F["vector"], mev["vector"])
                for oci in range(4):
                    oc = w * 4 + oci
                    for k in range(8):
                        t.matmul(PW[oci][:], WOB[:, (k * 8 + oc) * 128:(k * 8 + oc + 1) * 128],
                                 R1[:, (b * 16 + 8 + k) * 1024 + qs * 512:(b * 16 + 8 + k) * 1024 + (qs + 1) * 512],
                                 start=(k == 0), stop=(k == 7))
                t.mark()
            TE(womm)
            mwo = fw.marks()["tensor"]

            def woev(v, b=b, qs=qs, w=w, mwo=mwo):
                v.wait_ge(fw.F["tensor"], mwo)
                for oci in range(4):
                    oc = w * 4 + oci
                    v.tensor_scalar(out=OBS[:], in0=PW[oci][:], scalar1=BOB[:, 32 + oc:32 + oc + 1],
                                    scalar2=None, op0=ALU.add)
                    for par in range(2):
                        base = (b * 8 + oc) * 2048 + qs * 1024 + par
                        rap = AP(tensor=R2[:].tensor, offset=R2[:].offset + base,
                                 ap=[[32768, 128], [2, 512]])
                        v.tensor_tensor(out=rap, in0=rap, in1=OBS[:], op=ALU.add)
            VE(woev)
    fw.barrier()
    if PHASES < 9:
        return

    SCOPE("p5_ffn2")
    emit_ffn()

    SCOPE("p6_out")
    OTS_ = [AUX[:, 10240:12288].bitcast(F32), AUX[:, 12288:14336].bitcast(F32)]
    osem = [ctx["st"].enter_context(nc.semaphore("OSEM0")),
            ctx["st"].enter_context(nc.semaphore("OSEM1"))]
    ocnt = [0, 0]
    for i in range(B2 * NCH):
        b, c = divmod(i, NCH)
        sl = i % 2
        mprev = fw.marks()

        def otr(t, b=b, c=c, mprev=mprev):
            t.wait_ge(fw.F["vector"], mprev["vector"])
            for do in range(8):
                t.transpose(PB0[:, do * 128:(do + 1) * 128],
                            R2[:, (b * 8 + do) * 2048 + c * 128:(b * 8 + do) * 2048 + (c + 1) * 128],
                            IDENT[:])
            t.mark()
        TE(otr)
        mo = fw.marks()["tensor"]

        def oev(v, mo=mo, sl=sl, prevd=ocnt[sl]):
            v.wait_ge(fw.F["tensor"], mo)
            if prevd > 0:
                v.wait_ge(osem[sl], prevd)
            v.tensor_copy(out=OTS_[sl][:], in_=PB0[:])
        VE(oev)
        mvv = fw.marks()

        def ostore(h, b=b, c=c, sl=sl, mvv=mvv):
            h.wait_ge(fw.F["vector"], mvv["vector"])
            h.dma_start(out=out[b, c * 128:(c + 1) * 128, :],
                        in_=OTS_[sl][:]).then_inc(osem[sl], 16)
        fw._sect("sync", ostore)
        ocnt[sl] += 16
    def ofin(v, c0=ocnt[0], c1=ocnt[1]):
        v.wait_ge(osem[0], c0)
        v.wait_ge(osem[1], c1)
    VE(ofin)
    fw.barrier()




F32 = mybir.dt.float32
BF16 = mybir.dt.bfloat16
I32 = mybir.dt.int32
AF = mybir.ActivationFunctionType
ALU = mybir.AluOpType
AX = mybir.AxisListType

B2, L, D, NB = 2, 2048, 1024, 8
NCH = 16
N2 = 32
S64 = 64
EPS_LN, EPS_RMS = 1e-5, 1e-6
TWO_PI = 2.0 * math.pi
INV_2PI = 1.0 / TWO_PI
MAGIC = 1.5 * 2.0 ** 23
MAGIC64 = 1.5 * 2.0 ** 52
SLABS = NB
PHASES = 99


def build_program(debug=()):
    nc = bass.Bass("TRN2", target_bir_lowering=False)
    dbg_outs = {}

    def dram_in(name, shape, dt=F32):
        return nc.dram_tensor(name, shape, dt, kind="ExternalInput").ap()

    x_in = dram_in("x_in", [B2, L, D])
    out = nc.dram_tensor("out", [B2, L, D], F32, kind="ExternalOutput").ap()
    log_dt = dram_in("log_dt", [1, D])
    log_w_real = dram_in("log_w_real", [1, N2])
    w_imag = dram_in("w_imag", [1, N2])
    C_real = dram_in("C_real", [D, N2])
    C_imag = dram_in("C_imag", [D, N2])
    D_a = dram_in("D_a", [1, D])
    W_out_a = dram_in("W_out_a", [D, D])
    b_out_a = dram_in("b_out_a", [1, D])
    tern = dram_in("tern", [D, D], I32)
    bias_f = dram_in("bias_f", [1, D])
    act_scale = dram_in("act_scale", [1, 1])
    w_in_b = dram_in("w_in_b", [2 * D, D])
    b_in_b = dram_in("b_in_b", [1, 2 * D])
    A_b = dram_in("A_b", [D, S64])
    Bm_b = dram_in("Bm_b", [D, S64])
    Cm_b = dram_in("Cm_b", [D, S64])
    Dv_b = dram_in("Dv_b", [1, D])
    w_out_b = dram_in("w_out_b", [D, D])
    b_out_b = dram_in("b_out_b", [1, D])
    g_norm = dram_in("g_norm", [1, D])
    identb = dram_in("identb", [128, 128], BF16)
    iota_negj = dram_in("iota_negj", [1, 128])
    iota_t = dram_in("iota_t", [1, 128])
    iota_t128 = dram_in("iota_t128", [1, 128])
    mask_triu = dram_in("mask_triu", [128, 128], BF16)
    ones_row = dram_in("ones_row", [1, 128])

    hb_aim = nc.dram_tensor("hb_aim", [1, D * N2], F32, kind="Internal").ap()
    hb_are = nc.dram_tensor("hb_are", [1, D * N2], F32, kind="Internal").ap()
    hb_cwr = nc.dram_tensor("hb_cwr", [1, D * N2], F32, kind="Internal").ap()
    hb_cwi = nc.dram_tensor("hb_cwi", [1, D * N2], F32, kind="Internal").ap()
    hb_slab = nc.dram_tensor("hb_slab", [NB, 128, B2 * NCH * 128], BF16, kind="Internal").ap()
    hb_winT = nc.dram_tensor("hb_winT", [NB, 128, 2 * D], BF16, kind="Internal").ap()

    def dbg(name, shape, dt):
        if name in debug:
            dbg_outs[name] = nc.dram_tensor(name, shape, dt, kind="ExternalOutput").ap()
            return dbg_outs[name]
        return None

    st = ExitStack()
    sbuf = lambda name, shape, dt=BF16: st.enter_context(nc.sbuf_tensor(name, shape, dt))
    psum = lambda name, shape, dt=F32: st.enter_context(nc.psum_tensor(name, shape, dt))

    with st:
        R1 = sbuf("R1", [128, 32768])
        R2 = sbuf("R2", [128, 32768])
        R3 = sbuf("R3", [128, 16384])
        AUX = sbuf("AUX", [128, 20480])
        IDENT = sbuf("IDENTS", [128, 128])
        IDENTF = sbuf("IDENTF", [128, 128], F32)
        MASKT = sbuf("MASKT", [128, 128])
        IOTAS = sbuf("IOTAS", [1, 512], F32)
        ONEC = sbuf("ONEC", [128, 2])
        STM = sbuf("STM", [128, 96], F32)
        STT = sbuf("STT", [128, 16], F32)
        SML = sbuf("SML", [128, 64], F32)
        BOB = sbuf("BOB", [128, 64], F32)
        DBC = sbuf("DBC", [128, 1024])
        ASC = AUX[:, 0:2048].bitcast(F32)
        BSC = AUX[:, 2048:4096].bitcast(F32)
        HST = AUX[:, 4096:6144].bitcast(F32)
        ZST = AUX[:, 6144:8192].bitcast(F32)
        GST = AUX[:, 8192:10240].bitcast(F32)
        T1S = AUX[:, 10240:12288].bitcast(F32)
        HCS = AUX[:, 12288:14336].bitcast(F32)
        CBC = AUX[:, 14336:14464].bitcast(F32)
        PRM = AUX[:, 8192:12288].bitcast(F32)
        BCB = AUX[:, 12288:14336].bitcast(F32)
        DROW = AUX[0:1, 14336:16384].bitcast(F32)
        XT = AUX[:, 0:4096]

        P0 = psum("P0", [128, 512]); P1 = psum("P1", [128, 512]); P2 = psum("P2", [128, 512])
        P3 = psum("P3", [128, 512]); P4 = psum("P4", [128, 512]); P5 = psum("P5", [128, 512])
        PB0 = psum("PB0", [128, 1024], BF16)
        PB1 = psum("PB1", [128, 1024], BF16)
        PW = [P0, P1, P2, P3]

        st.enter_context(nc.allow_non_contiguous_dma(reason="strided staging loads/stores"))
        blk = st.enter_context(nc.Block())
        fw = FW(nc, st, blk)

        _scope = {"cur": None, "id": None}

        def SCOPE(name):
            if _scope["cur"] is not None:
                nc.leave_named_scope(_scope["cur"], _scope["id"], False)
                _scope["cur"] = None
            if name:
                sid, _ = nc.enter_named_scope(name, False)
                _scope["cur"], _scope["id"] = name, sid

        def VE(fn):
            fw.run("vector", fn)
        def SC(fn):
            fw.run("scalar", fn)
        def TE(fn):
            fw.run("tensor", fn)
        def GP(fn):
            fw.run("gpsimd", fn)

        SCOPE("p0_consts")
        fw.dma("sync", IDENT[:], identb[:])
        fw.dma("sync", MASKT[:], mask_triu[:])
        fw.dma("sync", IOTAS[0:1, 0:128], iota_negj[:])
        fw.dma("sync", IOTAS[0:1, 128:256], iota_t[:])
        fw.dma("sync", IOTAS[0:1, 256:384], iota_t128[:])
        fw.dma("sync", IOTAS[0:1, 384:512], ones_row[:])
        for (dst0, src) in ((0, b_out_a), (8, bias_f), (16, Dv_b), (24, g_norm), (32, b_out_b)):
            fw.dma("sync", BOB[:, dst0:dst0 + 8],
                   AP(tensor=src.tensor, offset=0, ap=[[1, 128], [128, 8]]))
        fw.dma("sync", SML[:, 0:8], AP(tensor=log_dt.tensor, offset=0, ap=[[1, 128], [128, 8]]))
        fw.dma("sync", SML[:, 16:32], AP(tensor=b_in_b.tensor, offset=0, ap=[[1, 128], [128, 16]]))
        fw.dma("sync", SML[0:1, 48:49], act_scale[:])
        fw.dma("sync", PRM[:, 1536:1792].rearrange("p (a b) -> p a b", a=8),
               AP(tensor=C_real.tensor, offset=0, ap=[[N2, 128], [128 * N2, 8], [1, N2]]))
        fw.dma("sync", PRM[:, 1792:2048].rearrange("p (a b) -> p a b", a=8),
               AP(tensor=C_imag.tensor, offset=0, ap=[[N2, 128], [128 * N2, 8], [1, N2]]))
        fw.dma("sync", BCB[0:1, 384:416], log_w_real[:])
        fw.dma("sync", BCB[0:1, 416:448], w_imag[:])
        fw.dma("sync", DROW[0:1, 0:1024], D_a[:])
        nconst = fw.dn["sync"]
        nconst2 = nconst

        def c_init(v):
            v.wait_ge(fw.Dq["sync"], nconst)
            v.tensor_copy(out=IDENTF[:], in_=IDENT[:])
            v.memset(ONEC[:], 1.0)
            v.memset(SML[:, 52:53], 1.0)
            v.tensor_scalar(out=SML[:, 53:54], in0=SML[:, 52:53], scalar1=MAGIC64,
                            scalar2=-MAGIC64, op0=ALU.add, op1=ALU.add)
        VE(c_init)
        m_init = fw.marks()

        SCOPE("p1_ln1")
        NTILE = B2 * NCH
        sx = [st.enter_context(nc.semaphore(f"sx{i}")) for i in range(4)]

        def ph1(i):
            b, c = divmod(i, NCH)
            slot = i % 4
            xt = XT[:, slot * 1024:(slot + 1) * 1024]
            mk0 = fw.marks()

            def gp_ld(h):
                if i >= 4:
                    h.wait_ge(fw.F["vector"], ph1.done[i - 4])
                h.dma_start(out=xt, in_=x_in[b, c * 128:(c + 1) * 128, :]).then_inc(sx[slot], 16)
            fw._sect("gpsimd", gp_ld)

            def ve1(v):
                v.wait_ge(sx[slot], 16 * (i // 4 + 1))
                v.bn_stats(out=STT[:, 0:6], in_=xt[:, 0:512])
                v.bn_stats(out=STT[:, 6:12], in_=xt[:, 512:1024])
                v.bn_aggr(out=STT[:, 12:14], in_=STT[:, 0:12])
                v.tensor_copy(out=STM[:, i:i + 1], in_=STT[:, 12:13])
                v.tensor_scalar(out=STT[:, 14:15], in0=STT[:, 13:14], scalar1=EPS_LN,
                                scalar2=None, op0=ALU.add)
                v.reciprocal(out=STT[:, 15:16], in_=STT[:, 14:15])
            VE(ve1)
            mkv = fw.marks()

            def sc1(s):
                s.wait_ge(fw.F["vector"], mkv["vector"])
                s.activation(out=STM[:, 64 + i:64 + i + 1], in_=STT[:, 15:16], func=AF.Sqrt)
            SC(sc1)
            mks = fw.marks()

            def ve2(v):
                v.wait_ge(fw.F["scalar"], mks["scalar"])
                v.tensor_scalar(out=R1[:, i * 1024:(i + 1) * 1024], in0=xt,
                                scalar1=STM[:, i:i + 1], scalar2=STM[:, 64 + i:64 + i + 1],
                                op0=ALU.subtract, op1=ALU.mult)
            VE(ve2)
            ph1.done[i] = fw.marks()["vector"]
        ph1.done = {}
        for i in range(NTILE):
            ph1(i)
        fw.barrier()
        if PHASES < 2:
            return _finish(nc, fw, dbg_outs, dbg, R1, R2, R3, STM, PRM, BCB, AUX, hb_aim, hb_are, hb_cwr, hb_cwi, DBC)

        SCOPE("p2a_prep")
        SCF = AUX[:, 4096:8192].bitcast(F32)

        def dt3(base, a, bsz, astr, bstr):
            return AP(tensor=base.tensor, offset=base.offset,
                      ap=[list(base.ap[0]), [astr, a], [bstr, bsz]])

        def mm1(t):
            t.wait_ge(fw.Dq["sync"], nconst2)
            t.matmul(P0[:, 0:64], IOTAS[0:1, 384:512], BCB[0:1, 384:448], start=True, stop=True)
            t.matmul(P1[:], IOTAS[0:1, 384:512], DROW[0:1, 0:512], start=True, stop=True)
            t.matmul(P2[:], IOTAS[0:1, 384:512], DROW[0:1, 512:1024], start=True, stop=True)
            t.mark()
        TE(mm1)
        mk = fw.marks()

        def sc_dt(s):
            s.activation(out=SML[:, 8:16], in_=SML[:, 0:8], func=AF.Exp)
        SC(sc_dt)
        mk_dt = fw.marks()

        def ve_bc(v):
            v.wait_ge(fw.F["tensor"], mk["tensor"])
            v.tensor_copy(out=BCB[:, 0:32], in_=P0[:, 0:32])
            v.tensor_copy(out=BCB[:, 32:64], in_=P0[:, 32:64])
        VE(ve_bc)
        mk_bc = fw.marks()

        def sc_wre(s):
            s.wait_ge(fw.F["vector"], mk_bc["vector"])
            s.activation(out=BCB[:, 0:32], in_=BCB[:, 0:32], func=AF.Exp)
        SC(sc_wre)
        mk_wre = fw.marks()

        def ve_args(v):
            v.wait_ge(fw.F["scalar"], mk_wre["scalar"])
            dt_ap = dt3(SML[:, 8:16], 8, 32, 1, 0)
            wim_ap = dt3(BCB[:, 32:64], 8, 32, 0, 1)
            wre_ap = dt3(BCB[:, 0:32], 8, 32, 0, 1)
            v.tensor_tensor(out=PRM[:, 0:256].rearrange("p (a b) -> p a b", a=8),
                            in0=dt_ap, in1=wim_ap, op=ALU.mult)
            v.tensor_tensor(out=PRM[:, 256:512].rearrange("p (a b) -> p a b", a=8),
                            in0=dt_ap, in1=wre_ap, op=ALU.mult)
            v.tensor_scalar(out=PRM[:, 256:512], in0=PRM[:, 256:512], scalar1=-1.0,
                            scalar2=None, op0=ALU.mult)
        VE(ve_args)
        mk_args = fw.marks()

        def sc_rr1(s):
            s.wait_ge(fw.F["vector"], mk_args["vector"])
            s.activation(out=SCF[:, 0:256], in_=PRM[:, 0:256], func=AF.Copy,
                         scale=INV_2PI, bias=MAGIC)
            s.activation(out=SCF[:, 0:256], in_=SCF[:, 0:256], func=AF.Copy, bias=-MAGIC)
        SC(sc_rr1)
        mk_rr1 = fw.marks()

        def ve_rr1(v):
            v.wait_ge(fw.F["scalar"], mk_rr1["scalar"])
            v.tensor_scalar(out=SCF[:, 1024:1280], in0=PRM[:, 0:256], scalar1=INV_2PI,
                            scalar2=None, op0=ALU.mult)
            v.tensor_scalar(out=SCF[:, 1280:1536], in0=SCF[:, 1024:1280], scalar1=MAGIC64,
                            scalar2=-MAGIC64, op0=ALU.add, op1=ALU.add)
            v.tensor_tensor(out=SCF[:, 1280:1536], in0=SCF[:, 1280:1536], in1=SCF[:, 0:256],
                            op=ALU.subtract)
            v.scalar_tensor_tensor(out=SCF[:, 0:256], in0=SCF[:, 1280:1536], scalar=SML[:, 53:54],
                                   in1=SCF[:, 0:256], op0=ALU.mult, op1=ALU.add)
            v.tensor_tensor(out=SCF[:, 256:512], in0=SCF[:, 1024:1280], in1=SCF[:, 0:256],
                            op=ALU.subtract)
            v.tensor_scalar(out=SCF[:, 512:768], in0=SCF[:, 256:512], scalar1=0.25,
                            scalar2=None, op0=ALU.add)
            v.tensor_scalar(out=SCF[:, 768:1024], in0=SCF[:, 512:768], scalar1=0.5,
                            scalar2=None, op0=ALU.is_gt)
            v.tensor_tensor(out=SCF[:, 512:768], in0=SCF[:, 512:768], in1=SCF[:, 768:1024],
                            op=ALU.subtract)
        VE(ve_rr1)
        mk_ve1 = fw.marks()
        rr1_dbg = dbg("rr1_dbg", [128, 1536], F32)
        if rr1_dbg is not None:
            fw.dma("gpsimd", rr1_dbg[:], SCF[:, 0:1536], wait=[("vector", mk_ve1["vector"])])
        stt_dbg = dbg("stt_dbg", [128, 16], F32)
        if stt_dbg is not None:
            fw.dma("gpsimd", stt_dbg[:], STT[:], wait=[("vector", mk_ve1["vector"])])


        def sc_rr2(s):
            s.wait_ge(fw.F["vector"], mk_ve1["vector"])
            s.activation(out=SCF[:, 768:1024], in_=SCF[:, 256:512], func=AF.Sin, scale=TWO_PI)
            s.activation(out=SCF[:, 1024:1280], in_=SCF[:, 512:768], func=AF.Sin, scale=TWO_PI)
            s.activation(out=SCF[:, 1280:1536], in_=PRM[:, 256:512], func=AF.Exp)
        SC(sc_rr2)
        mk_rr2 = fw.marks()

        def ve_cw(v):
            v.wait_ge(fw.F["scalar"], mk_rr2["scalar"])
            v.tensor_tensor(out=PRM[:, 512:768], in0=SCF[:, 1280:1536], in1=SCF[:, 1024:1280], op=ALU.mult)
            v.tensor_tensor(out=PRM[:, 768:1024], in0=SCF[:, 1280:1536], in1=SCF[:, 768:1024], op=ALU.mult)
            v.tensor_tensor(out=BCB[:, 64:96], in0=BCB[:, 0:32], in1=BCB[:, 0:32], op=ALU.mult)
            v.tensor_tensor(out=BCB[:, 96:128], in0=BCB[:, 32:64], in1=BCB[:, 32:64], op=ALU.mult)
            v.tensor_tensor(out=BCB[:, 64:96], in0=BCB[:, 64:96], in1=BCB[:, 96:128], op=ALU.add)
            v.reciprocal(out=BCB[:, 192:224], in_=BCB[:, 64:96])
            v.tensor_scalar(out=SCF[:, 1536:1792], in0=PRM[:, 512:768], scalar1=-1.0, scalar2=None, op0=ALU.add)
            v.tensor_scalar(out=BCB[:, 224:256], in0=BCB[:, 0:32], scalar1=-1.0, scalar2=None, op0=ALU.mult)
            v.tensor_scalar(out=BCB[:, 256:288], in0=BCB[:, 32:64], scalar1=-1.0, scalar2=None, op0=ALU.mult)
            v.tensor_tensor(out=BCB[:, 128:160], in0=BCB[:, 224:256], in1=BCB[:, 192:224], op=ALU.mult)
            v.tensor_tensor(out=BCB[:, 160:192], in0=BCB[:, 256:288], in1=BCB[:, 192:224], op=ALU.mult)
            ir = dt3(BCB[:, 128:160], 8, 32, 0, 1)
            ii = dt3(BCB[:, 160:192], 8, 32, 0, 1)
            nre3 = SCF[:, 1536:1792].rearrange("p (a b) -> p a b", a=8)
            nim3 = PRM[:, 768:1024].rearrange("p (a b) -> p a b", a=8)
            v.tensor_tensor(out=SCF[:, 0:256].rearrange("p (a b) -> p a b", a=8), in0=nre3, in1=ir, op=ALU.mult)
            v.tensor_tensor(out=SCF[:, 256:512].rearrange("p (a b) -> p a b", a=8), in0=nim3, in1=ii, op=ALU.mult)
            v.tensor_tensor(out=SCF[:, 0:256], in0=SCF[:, 0:256], in1=SCF[:, 256:512], op=ALU.subtract)
            v.tensor_tensor(out=SCF[:, 256:512].rearrange("p (a b) -> p a b", a=8), in0=nre3, in1=ii, op=ALU.mult)
            v.tensor_tensor(out=SCF[:, 512:768].rearrange("p (a b) -> p a b", a=8), in0=nim3, in1=ir, op=ALU.mult)
            v.tensor_tensor(out=SCF[:, 256:512], in0=SCF[:, 256:512], in1=SCF[:, 512:768], op=ALU.add)
            v.tensor_tensor(out=SCF[:, 512:768], in0=PRM[:, 1536:1792], in1=SCF[:, 0:256], op=ALU.mult)
            v.tensor_tensor(out=SCF[:, 768:1024], in0=PRM[:, 1792:2048], in1=SCF[:, 256:512], op=ALU.mult)
            v.tensor_tensor(out=SCF[:, 512:768], in0=SCF[:, 512:768], in1=SCF[:, 768:1024], op=ALU.subtract)
            v.tensor_scalar(out=SCF[:, 1536:1792], in0=SCF[:, 512:768], scalar1=2.0, scalar2=None, op0=ALU.mult)
            v.tensor_tensor(out=SCF[:, 512:768], in0=PRM[:, 1536:1792], in1=SCF[:, 256:512], op=ALU.mult)
            v.tensor_tensor(out=SCF[:, 768:1024], in0=PRM[:, 1792:2048], in1=SCF[:, 0:256], op=ALU.mult)
            v.tensor_tensor(out=SCF[:, 512:768], in0=SCF[:, 512:768], in1=SCF[:, 768:1024], op=ALU.add)
            v.tensor_scalar(out=SCF[:, 1792:2048], in0=SCF[:, 512:768], scalar1=2.0, scalar2=None, op0=ALU.mult)
            v.tensor_copy(out=PRM[:, 1024:1280], in_=PRM[:, 512:768])
            v.tensor_copy(out=PRM[:, 1280:1536], in_=PRM[:, 768:1024])
            for _ in range(7):
                v.tensor_tensor(out=SCF[:, 0:256], in0=PRM[:, 1024:1280], in1=PRM[:, 1024:1280], op=ALU.mult)
                v.tensor_tensor(out=SCF[:, 256:512], in0=PRM[:, 1280:1536], in1=PRM[:, 1280:1536], op=ALU.mult)
                v.tensor_tensor(out=SCF[:, 512:768], in0=PRM[:, 1024:1280], in1=PRM[:, 1280:1536], op=ALU.mult)
                v.tensor_tensor(out=PRM[:, 1024:1280], in0=SCF[:, 0:256], in1=SCF[:, 256:512], op=ALU.subtract)
                v.tensor_scalar(out=PRM[:, 1280:1536], in0=SCF[:, 512:768], scalar1=2.0, scalar2=None, op0=ALU.mult)
        VE(ve_cw)
        mk_cw = fw.marks()

        def ve_dbc(v):
            v.tensor_copy(out=DBC[:, 0:512], in_=P1[:])
            v.tensor_copy(out=DBC[:, 512:1024], in_=P2[:])
        VE(ve_dbc)

        def rows_dst(t):
            return AP(tensor=t.tensor, offset=0, ap=[[32, 128], [4096, 8], [1, 32]])
        mnow = fw.marks()
        fw.dma("gpsimd", rows_dst(hb_aim), PRM[:, 0:256].rearrange("p (a b) -> p a b", a=8),
               wait=[("vector", mnow["vector"])])
        fw.dma("gpsimd", rows_dst(hb_are), PRM[:, 256:512].rearrange("p (a b) -> p a b", a=8))
        fw.dma("gpsimd", rows_dst(hb_cwr), SCF[:, 1536:1792].rearrange("p (a b) -> p a b", a=8))
        fw.dma("gpsimd", rows_dst(hb_cwi), SCF[:, 1792:2048].rearrange("p (a b) -> p a b", a=8))
        fw.barrier()
        if PHASES < 3:
            return _finish(nc, fw, dbg_outs, dbg, R1, R2, R3, STM, PRM, BCB, AUX, hb_aim, hb_are, hb_cwr, hb_cwi, DBC)

        TBL = AUX[:, 0:1024]
        TBB = AUX[:, 1024:2048]
        SINT = AUX[:, 2048:2560]
        COST = AUX[:, 2560:3072]
        EXPT = AUX[:, 3072:3584]
        RAWS = AUX[:, 3584:4096]
        TMP2B = AUX[:, 15360:16384]
        TGA = AUX[0:64, 16384:18432]
        TGB = AUX[0:64, 18432:20480]
        BSLAB = R3[0:64, 0:4096]
        SSLAB = R3[0:64, 4096:8192]
        K8A = R3[:, 8192:9216]
        K8B = R3[:, 9216:10240]
        GOSL = R3[:, 10240:14336]
        SSTa = R3[:, 14336:14592].bitcast(F32)
        SSTb = R3[:, 14592:14848].bitcast(F32)
        TMPQ = R3[:, 14848:15360].bitcast(F32)
        AIMSEG = DROW[0:1, 0:512]
        ARESEG = SCF[0:1, 1536:2048]
        CW2R = BCB[:, 0:512]
        CW2I = BCB[:, 512:1024]

        def u_cols(h):
            return AP(tensor=R1[:].tensor, offset=h, ap=[[32768, 128], [NCH * 1024, B2], [1024, NCH]])

        def u_grp(h0):
            return AP(tensor=R1[:].tensor, offset=h0, ap=[[32768, 128], [1, 16], [NCH * 1024, B2], [1024, NCH]])

        def dbc_grp(h0):
            return AP(tensor=DBC[:].tensor, offset=DBC[:].offset + h0,
                      ap=[[1024, 128], [1, 16], [0, B2], [0, NCH]])

        def chain(pim, pexp, mk_args_t):
            def s1(s):
                s.wait_ge(fw.F["tensor"], mk_args_t)
                s.activation(out=SCF[:, 0:512], in_=pim, func=AF.Copy, scale=INV_2PI, bias=MAGIC)
                s.activation(out=SCF[:, 0:512], in_=SCF[:, 0:512], func=AF.Copy, bias=-MAGIC)
            SC(s1)
            m1 = fw.marks()

            def v1(v):
                v.wait_ge(fw.F["scalar"], m1["scalar"])
                v.tensor_scalar(out=SCF[:, 512:1024], in0=pim, scalar1=INV_2PI,
                                scalar2=None, op0=ALU.mult)
                v.tensor_scalar(out=SCF[:, 1024:1536], in0=SCF[:, 512:1024], scalar1=MAGIC64,
                                scalar2=-MAGIC64, op0=ALU.add, op1=ALU.add)
                v.tensor_tensor(out=SCF[:, 1024:1536], in0=SCF[:, 1024:1536], in1=SCF[:, 0:512],
                                op=ALU.subtract)
                v.scalar_tensor_tensor(out=SCF[:, 0:512], in0=SCF[:, 1024:1536], scalar=SML[:, 53:54],
                                       in1=SCF[:, 0:512], op0=ALU.mult, op1=ALU.add)
                v.tensor_tensor(out=SCF[:, 512:1024], in0=SCF[:, 512:1024], in1=SCF[:, 0:512],
                                op=ALU.subtract)
                v.tensor_scalar(out=SCF[:, 1024:1536], in0=SCF[:, 512:1024], scalar1=0.25,
                                scalar2=None, op0=ALU.add)
                v.tensor_scalar(out=SCF[:, 0:512], in0=SCF[:, 1024:1536], scalar1=0.5,
                                scalar2=None, op0=ALU.is_gt)
                v.tensor_tensor(out=SCF[:, 1024:1536], in0=SCF[:, 1024:1536], in1=SCF[:, 0:512],
                                op=ALU.subtract)
            VE(v1)
            m2 = fw.marks()

            def s2(s):
                s.wait_ge(fw.F["vector"], m2["vector"])
                s.activation(out=SINT[:], in_=SCF[:, 512:1024], func=AF.Sin, scale=TWO_PI)
                s.activation(out=COST[:], in_=SCF[:, 1024:1536], func=AF.Sin, scale=TWO_PI)
                s.activation(out=EXPT[:], in_=pexp, func=AF.Exp)
            SC(s2)
            return fw.marks()

        def bp_family(mk_sc):
            def v(v):
                v.wait_ge(fw.F["scalar"], mk_sc["scalar"])
                v.tensor_tensor(out=SCF[:, 0:512], in0=EXPT[:], in1=COST[:], op=ALU.mult)
                v.tensor_tensor(out=SCF[:, 512:1024], in0=EXPT[:], in1=SINT[:], op=ALU.mult)
                v.tensor_tensor(out=SCF[:, 1024:1536], in0=CW2R, in1=SCF[:, 0:512], op=ALU.mult)
                v.tensor_tensor(out=TMP2B[:].bitcast(F32), in0=CW2I, in1=SCF[:, 512:1024], op=ALU.mult)
                v.tensor_tensor(out=TBB[:, 0:1024:2], in0=SCF[:, 1024:1536], in1=TMP2B[:].bitcast(F32), op=ALU.subtract)
                v.tensor_tensor(out=SCF[:, 1024:1536], in0=CW2R, in1=SCF[:, 512:1024], op=ALU.mult)
                v.tensor_tensor(out=TMP2B[:].bitcast(F32), in0=CW2I, in1=SCF[:, 0:512], op=ALU.mult)
                v.tensor_tensor(out=SCF[:, 1024:1536], in0=SCF[:, 1024:1536], in1=TMP2B[:].bitcast(F32), op=ALU.add)
                v.tensor_scalar(out=TBB[:, 1:1024:2], in0=SCF[:, 1024:1536], scalar1=-1.0,
                                scalar2=None, op0=ALU.mult)
            VE(v)
            return fw.marks()

        raws_st = {"n": 0}
        SCOPE("p2b_slabs")
        for ho in range(SLABS):
            for g in range(8):
                h0 = ho * 128 + g * 16
                seg = h0 * N2
                mv = fw.marks()
                fw.dma("sync", AIMSEG, hb_aim[0:1, seg:seg + 512], wait=[("vector", mv["vector"]), ("tensor", mv["tensor"]), ("scalar", mv["scalar"])])
                fw.dma("sync", ARESEG, hb_are[0:1, seg:seg + 512])
                fw.dma("sync", CW2R, AP(tensor=hb_cwr.tensor, offset=seg, ap=[[0, 128], [1, 512]]))
                fw.dma("sync", CW2I, AP(tensor=hb_cwi.tensor, offset=seg, ap=[[0, 128], [1, 512]]))
                ndma = fw.dn["sync"]

                def args1(t, ndma=ndma, mv=mv):
                    t.wait_ge(fw.Dq["sync"], ndma)
                    t.wait_ge(fw.F["vector"], mv["vector"])
                    t.matmul(P0[:], IOTAS[0:1, 0:128], AIMSEG, start=True, stop=True)
                    t.matmul(P2[:], IOTAS[0:1, 0:128], ARESEG, start=True, stop=True)
                    t.matmul(P1[:], IOTAS[0:1, 128:256], AIMSEG, start=True, stop=True)
                    t.matmul(P3[:], IOTAS[0:1, 128:256], ARESEG, start=True, stop=True)
                    t.mark()
                TE(args1)
                mk_a = fw.marks()["tensor"]
                mk_f1 = chain(P0[:], P2[:], mk_a)

                def f1w(v, mk_f1=mk_f1):
                    v.wait_ge(fw.F["scalar"], mk_f1["scalar"])
                    v.tensor_tensor(out=TBL[:, 0:1024:2], in0=EXPT[:], in1=COST[:], op=ALU.mult)
                    v.tensor_tensor(out=TBL[:, 1:1024:2], in0=EXPT[:], in1=SINT[:], op=ALU.mult)
                VE(f1w)
                mk_bp = chain(P1[:], P3[:], mk_a)
                bp_family(mk_bp)
                mtab = fw.marks()

                def tr1(t, mtab=mtab):
                    t.wait_ge(fw.F["vector"], mtab["vector"])
                    for k in range(8):
                        t.transpose(PB0[0:64, k * 128:(k + 1) * 128], TBL[:, k * 64:(k + 1) * 64], IDENT[:])
                    t.mark()
                    for k in range(8):
                        t.transpose(PB1[0:64, k * 128:(k + 1) * 128], TBL[:, 512 + k * 64:512 + (k + 1) * 64], IDENT[:])
                    t.mark()
                TE(tr1)
                mtr1 = fw.marks()["tensor"]

                def evga(v, mtr1=mtr1):
                    v.wait_ge(fw.F["tensor"], mtr1)
                    v.tensor_copy(out=TGA[:, 0:1024], in_=PB0[0:64, :])
                    v.tensor_copy(out=TGA[:, 1024:2048], in_=PB1[0:64, :])
                VE(evga)
                mevga = fw.marks()

                def tr2(t, mevga=mevga):
                    t.wait_ge(fw.F["vector"], mevga["vector"])
                    for k in range(8):
                        t.transpose(PB0[0:64, k * 128:(k + 1) * 128], TBB[:, k * 64:(k + 1) * 64], IDENT[:])
                    t.mark()
                    for k in range(8):
                        t.transpose(PB1[0:64, k * 128:(k + 1) * 128], TBB[:, 512 + k * 64:512 + (k + 1) * 64], IDENT[:])
                    t.mark()
                TE(tr2)
                mtr2 = fw.marks()["tensor"]

                def evgb(v, mtr2=mtr2):
                    v.wait_ge(fw.F["tensor"], mtr2)
                    v.tensor_copy(out=TGB[:, 0:1024], in_=PB0[0:64, :])
                    v.tensor_copy(out=TGB[:, 1024:2048], in_=PB1[0:64, :])
                VE(evgb)
                mevgb = fw.marks()

                def kmm(t, mevgb=mevgb):
                    t.wait_ge(fw.F["vector"], mevgb["vector"])
                    for k in range(16):
                        pb = PW[k // 4]
                        t.matmul(pb[:, (k % 4) * 128:(k % 4 + 1) * 128],
                                 TGA[:, k * 128:(k + 1) * 128], TGB[:, k * 128:(k + 1) * 128],
                                 start=True, stop=True)
                    t.mark()
                TE(kmm)
                mkmm = fw.marks()["tensor"]

                def kev(v, mkmm=mkmm):
                    v.wait_ge(fw.F["tensor"], mkmm)
                    for q, pb in enumerate(PW):
                        mask4 = AP(tensor=MASKT[:].tensor, offset=MASKT[:].offset,
                                   ap=[[128, 128], [0, 4], [1, 128]])
                        dst = (K8A, K8B)[q // 2]
                        v.tensor_tensor(out=dst[:, (q % 2) * 512:(q % 2 + 1) * 512].rearrange("p (a b) -> p a b", a=4),
                                        in0=pb[:].rearrange("p (a b) -> p a b", a=4), in1=mask4, op=ALU.mult)
                VE(kev)
                mkev = fw.marks()

                def gimm(t, mkev=mkev, h0=h0):
                    t.wait_ge(fw.F["vector"], mkev["vector"])
                    for k in range(16):
                        t.matmul(P4[0:64, k * 32:(k + 1) * 32], TBL[:, k * 64:(k + 1) * 64],
                                 u_cols(h0 + k), start=True, stop=True)
                    t.mark()
                    for k in range(16):
                        kt = (K8A, K8B)[k // 8]
                        t.matmul(P5[:, k * 32:(k + 1) * 32], kt[:, (k % 8) * 128:(k % 8 + 1) * 128],
                                 u_cols(h0 + k), start=True, stop=True)
                    t.mark()
                TE(gimm)
                mgi = fw.marks()["tensor"]

                def bdev(v, mgi=mgi, h0=h0, prevst=raws_st["n"]):
                    v.wait_ge(fw.F["tensor"], mgi)
                    if prevst:
                        v.wait_ge(fw.Dq["gpsimd"], prevst)
                    h0l = h0 % 128
                    dstB = AP(tensor=BSLAB.tensor, offset=BSLAB.offset + h0l * 2,
                              ap=[[16384, 64], [2, 16], [1, 2], [256, 16]])
                    v.tensor_copy(out=dstB, in_=P4[0:64, 0:512].rearrange("p (a b c) -> p a b c", a=16, b=2))
                    v.tensor_tensor(out=SINT[:], in0=u_grp(h0), in1=dbc_grp(h0), op=ALU.mult)
                    v.tensor_tensor(out=RAWS[:], in0=SINT[:], in1=P5[:], op=ALU.add)
                VE(bdev)
                mbd = fw.marks()
                fw.dma("gpsimd", hb_slab[ho, :, g * 512:(g + 1) * 512], RAWS[:],
                       wait=[("vector", mbd["vector"])])
                raws_st["n"] = fw.dn["gpsimd"]

            mk_sw1 = fw.marks()
            ngp1 = fw.dn["gpsimd"]

            def rinit(v):
                v.memset(SSTa, 0.0)
            VE(rinit)
            lt_re = PRM[:, 1024 + ho * 32:1024 + (ho + 1) * 32]
            lt_im = PRM[:, 1280 + ho * 32:1280 + (ho + 1) * 32]

            def lt_ap(base):
                return AP(tensor=base.tensor, offset=base.offset,
                          ap=[[base.ap[0][0], 128], [0, 2], [1, 32]])

            def sview(tns, pm):
                return AP(tensor=tns.tensor, offset=tns.offset + pm, ap=[[tns.ap[0][0], 128], [64, 2], [2, 32]])

            for c in range(NCH):
                cur, nxt = (SSTa, SSTb) if c % 2 == 0 else (SSTb, SSTa)
                mvv = fw.marks()

                def rtr(t, c=c, cur=cur, mvv=mvv):
                    t.wait_ge(fw.F["vector"], mvv["vector"])
                    for b in range(B2):
                        t.transpose(P4[0:64, b * 128:(b + 1) * 128], cur[:, b * 64:(b + 1) * 64], IDENTF[:])
                    for b in range(B2):
                        src = AP(tensor=BSLAB.tensor, offset=BSLAB.offset + c * 256 + b,
                                 ap=[[16384, 64], [2, 128]])
                        t.transpose(PB0[:, b * 64:(b + 1) * 64], src, IDENT[0:64, 0:64])
                    t.mark()
                TE(rtr)
                mrt = fw.marks()["tensor"]

                def rup(v, c=c, cur=cur, nxt=nxt, mrt=mrt, lt_re=lt_re, lt_im=lt_im):
                    v.wait_ge(fw.F["tensor"], mrt)
                    dstS = AP(tensor=SSLAB.tensor, offset=SSLAB.offset + c * 256,
                              ap=[[16384, 64], [1, 2], [2, 128]])
                    srcS = AP(tensor=P4[:].tensor, offset=P4[:].offset,
                              ap=[[512, 64], [128, 2], [1, 128]])
                    v.tensor_copy(out=dstS, in_=srcS)
                    pb_re = AP(tensor=PB0[:].tensor, offset=PB0[:].offset, ap=[[1024, 128], [64, 2], [2, 32]])
                    pb_im = AP(tensor=PB0[:].tensor, offset=PB0[:].offset + 1, ap=[[1024, 128], [64, 2], [2, 32]])
                    v.tensor_tensor(out=TMPQ[:, 0:64].rearrange("p (a b) -> p a b", a=2), in0=sview(cur, 0), in1=lt_ap(lt_re), op=ALU.mult)
                    v.tensor_tensor(out=TMPQ[:, 64:128].rearrange("p (a b) -> p a b", a=2), in0=sview(cur, 1), in1=lt_ap(lt_im), op=ALU.mult)
                    v.tensor_tensor(out=TMPQ[:, 128:192], in0=TMPQ[:, 0:64], in1=TMPQ[:, 64:128], op=ALU.subtract)
                    v.tensor_tensor(out=sview(nxt, 0), in0=TMPQ[:, 128:192].rearrange("p (a b) -> p a b", a=2), in1=pb_re, op=ALU.add)
                    v.tensor_tensor(out=TMPQ[:, 0:64].rearrange("p (a b) -> p a b", a=2), in0=sview(cur, 1), in1=lt_ap(lt_re), op=ALU.mult)
                    v.tensor_tensor(out=TMPQ[:, 64:128].rearrange("p (a b) -> p a b", a=2), in0=sview(cur, 0), in1=lt_ap(lt_im), op=ALU.mult)
                    v.tensor_tensor(out=TMPQ[:, 128:192], in0=TMPQ[:, 0:64], in1=TMPQ[:, 64:128], op=ALU.add)
                    v.tensor_tensor(out=sview(nxt, 1), in0=TMPQ[:, 128:192].rearrange("p (a b) -> p a b", a=2), in1=pb_im, op=ALU.add)
                VE(rup)

            for g in range(8):
                h0 = ho * 128 + g * 16
                seg = h0 * N2
                mv = fw.marks()
                fw.dma("sync", AIMSEG, hb_aim[0:1, seg:seg + 512], wait=[("vector", mv["vector"]), ("tensor", mv["tensor"]), ("scalar", mv["scalar"])])
                fw.dma("sync", ARESEG, hb_are[0:1, seg:seg + 512])
                fw.dma("sync", CW2R, AP(tensor=hb_cwr.tensor, offset=seg, ap=[[0, 128], [1, 512]]))
                fw.dma("sync", CW2I, AP(tensor=hb_cwi.tensor, offset=seg, ap=[[0, 128], [1, 512]]))
                ndma = fw.dn["sync"]
                fw.dma("gpsimd", RAWS[:], hb_slab[ho, :, g * 512:(g + 1) * 512],
                       wait=[("vector", mv["vector"]), ("q:gpsimd", raws_st["n"])])
                ngp = fw.dn["gpsimd"]

                def args2(t, ndma=ndma, mv=mv):
                    t.wait_ge(fw.Dq["sync"], ndma)
                    t.wait_ge(fw.F["vector"], mv["vector"])
                    t.matmul(P0[:], IOTAS[0:1, 256:384], AIMSEG, start=True, stop=True)
                    t.matmul(P1[:], IOTAS[0:1, 256:384], ARESEG, start=True, stop=True)
                    t.mark()
                TE(args2)
                mk_a2 = fw.marks()["tensor"]
                mk_p = chain(P0[:], P1[:], mk_a2)
                bp_family(mk_p)
                mtab2 = fw.marks()

                def tr3(t, mtab2=mtab2):
                    t.wait_ge(fw.F["vector"], mtab2["vector"])
                    for k in range(8):
                        t.transpose(PB0[0:64, k * 128:(k + 1) * 128], TBB[:, k * 64:(k + 1) * 64], IDENT[:])
                    t.mark()
                    for k in range(8):
                        t.transpose(PB1[0:64, k * 128:(k + 1) * 128], TBB[:, 512 + k * 64:512 + (k + 1) * 64], IDENT[:])
                    t.mark()
                TE(tr3)
                mtr3 = fw.marks()["tensor"]

                def evp(v, mtr3=mtr3):
                    v.wait_ge(fw.F["tensor"], mtr3)
                    v.tensor_copy(out=TGB[:, 0:1024], in_=PB0[0:64, :])
                    v.tensor_copy(out=TGB[:, 1024:2048], in_=PB1[0:64, :])
                VE(evp)
                mevp = fw.marks()

                def scmm(t, mevp=mevp, h0=h0, ngp=ngp):
                    t.wait_ge(fw.F["vector"], mevp["vector"])
                    t.wait_ge(fw.Dq["gpsimd"], ngp)
                    h0l = h0 % 128
                    for k in range(16):
                        sap = AP(tensor=SSLAB.tensor, offset=SSLAB.offset + (h0l + k) * 2,
                                 ap=[[16384, 64], [1, 2], [256, 16]])
                        t.matmul(P5[:, k * 32:(k + 1) * 32], TGB[:, k * 128:(k + 1) * 128],
                                 sap, start=True, stop=True)
                    t.mark()
                TE(scmm)
                msc = fw.marks()["tensor"]

                def gadd(v, msc=msc):
                    v.wait_ge(fw.F["tensor"], msc)
                    v.tensor_tensor(out=TMP2B[:, 0:512], in0=RAWS[:], in1=P5[:], op=ALU.add)
                VE(gadd)
                mga = fw.marks()

                def gel(s, mga=mga, h0=h0):
                    s.wait_ge(fw.F["vector"], mga["vector"])
                    h0l = h0 % 128
                    dstG = AP(tensor=GOSL.tensor, offset=GOSL.offset + h0l,
                              ap=[[16384, 128], [1, 16], [2048, 2], [128, 16]])
                    s.activation(out=dstG, in_=TMP2B[:, 0:512].rearrange("p (a b c) -> p a b c", a=16, b=2),
                                 func=AF.Gelu)
                SC(gel)

            mgel = fw.marks()
            fev_marks = {}
            for half in range(4):
                b_, ch_ = divmod(half, 2)
                pbuf = PB0 if half % 2 == 0 else PB1

                def ftr(t, b_=b_, ch_=ch_, pbuf=pbuf, half=half, mgel=mgel, fev_marks=fev_marks):
                    if half == 0:
                        t.wait_ge(fw.F["scalar"], mgel["scalar"])
                    if half >= 2:
                        t.wait_ge(fw.F["vector"], fev_marks[half - 2])
                    for k in range(8):
                        cc = ch_ * 8 + k
                        t.transpose(pbuf[:, k * 128:(k + 1) * 128],
                                    GOSL[:, b_ * 2048 + cc * 128:b_ * 2048 + (cc + 1) * 128], IDENT[:])
                    t.mark()
                TE(ftr)
                mtf = fw.marks()["tensor"]

                def fev(v, b_=b_, ch_=ch_, pbuf=pbuf, ho=ho, mtf=mtf):
                    v.wait_ge(fw.F["tensor"], mtf)
                    base = (b_ * 8 + ho) * 2048 + ch_ * 1024
                    v.tensor_copy(out=R2[:, base:base + 1024], in_=pbuf[:])
                VE(fev)
                fev_marks[half] = fw.marks()["vector"]
        fw.barrier()
        if PHASES < 4:
            return _finish(nc, fw, dbg_outs, dbg, R1, R2, R3, STM, PRM, BCB, AUX, hb_aim, hb_are, hb_cwr, hb_cwi, DBC)

        _ctx = dict(locals())
        _ctx["PHASES"] = PHASES
        _part2_emit(_ctx)
        SCOPE(None)
        fw.barrier()

        u_dbg = dbg("u_dbg", [128, 32768], BF16)
        if u_dbg is not None:
            fw.dma("gpsimd", u_dbg[:], R1[:])
        m_dbg = dbg("m_dbg", [128, 96], F32)
        if m_dbg is not None:
            fw.dma("gpsimd", m_dbg[:], STM[:])
        prep_dbg = dbg("prep_dbg", [128, 2048], F32)
        if prep_dbg is not None:
            fw.dma("gpsimd", prep_dbg[:], PRM[:])
        convf_dbg = dbg("convf_dbg", [128, 32768], BF16)
        if convf_dbg is not None:
            fw.dma("gpsimd", convf_dbg[:], R2[:])
        x1_dbg = dbg("x1_dbg", [128, 32768], BF16)
        if x1_dbg is not None:
            fw.dma("gpsimd", x1_dbg[:], R2[:])
        x2_dbg = dbg("x2_dbg", [128, 32768], BF16)
        if x2_dbg is not None:
            fw.dma("gpsimd", x2_dbg[:], R2[:])
        xz_dbg = dbg("xz_dbg", [128, 32768], BF16)
        if xz_dbg is not None:
            fw.dma("gpsimd", xz_dbg[:], R1[:])
        ys_dbg = dbg("ys_dbg", [128, 16384], BF16)
        if ys_dbg is not None:
            fw.dma("gpsimd", ys_dbg[:], R3[:])
        x3_dbg = dbg("x3_dbg", [128, 32768], BF16)
        if x3_dbg is not None:
            fw.dma("gpsimd", x3_dbg[:], R2[:])
        for nm, src, shp, dt_ in (
                ("scan_dbg", AUX[:, 0:4096].bitcast(F32), [128, 2048], F32),
                ("cbc2_dbg", AUX[:, 14336:14464].bitcast(F32), [128, 64], F32),
                ("hst_dbg", AUX[:, 4096:8192].bitcast(F32), [128, 2048], F32)):
            tp = dbg(nm, shp, dt_)
            if tp is not None:
                fw.dma("gpsimd", tp[:], src)
        fw.barrier()

    return nc, dbg_outs


def _finish(nc, fw, dbg_outs, dbg, R1, R2, R3, STM, PRM, BCB, AUX, hb_aim, hb_are, hb_cwr, hb_cwi, DBC=None):
    taps = {
        "u_dbg": (R1, [128, 32768], BF16), "m_dbg": (STM, [128, 96], F32),
        "prep_dbg": (PRM, [128, 2048], F32), "convf_dbg": (R2, [128, 32768], BF16),
        "k8_dbg": (R3[:, 8192:10240], [128, 2048], BF16),
        "bslab_dbg": (R3[0:64, 0:4096], [64, 4096], BF16),
        "sslab_dbg": (R3[0:64, 4096:8192], [64, 4096], BF16),
        "gosl_dbg": (R3[:, 10240:14336], [128, 4096], BF16),
        "dbc_dbg": (DBC, [128, 1024], BF16),
        "tbl_dbg": (AUX[:, 0:2048], [128, 2048], BF16),
        "tga_dbg": (AUX[0:64, 16384:20480], [64, 4096], BF16),
        "scf_dbg": (AUX[:, 4096:8192].bitcast(F32), [128, 2048], F32),
        "trig_dbg": (AUX[:, 2048:4096], [128, 2048], BF16),
        "scan_dbg": (AUX[:, 0:4096].bitcast(F32), [128, 2048], F32),
        "cbc2_dbg": (AUX[:, 14336:14464].bitcast(F32), [128, 64], F32),
        "hst_dbg": (AUX[:, 4096:8192].bitcast(F32), [128, 2048], F32),
    }
    for name, (tns, shape, dt) in taps.items():
        t = dbg(name, shape, dt)
        if t is not None:
            fw.dma("gpsimd", t[:], tns if isinstance(tns, AP) else tns[:])
    rows_dbg = dbg("rows_dbg", [4, 32768], F32)
    if rows_dbg is not None:
        fw.dma("gpsimd", rows_dbg[0:1, :], hb_aim[:])
        fw.dma("gpsimd", rows_dbg[1:2, :], hb_are[:])
        fw.dma("gpsimd", rows_dbg[2:3, :], hb_cwr[:])
        fw.dma("gpsimd", rows_dbg[3:4, :], hb_cwi[:])
    fw.barrier()
    return nc, dbg_outs


def make_consts():
    ident = np.eye(128, dtype=ml_dtypes.bfloat16)
    iota_negj = (-np.arange(128, dtype=np.float32))[None, :]
    iota_t = np.arange(128, dtype=np.float32)[None, :]
    iota_t128 = (np.arange(128, dtype=np.float32) + 128.0)[None, :]
    j = np.arange(128)[:, None]; t = np.arange(128)[None, :]
    mask_triu = (t >= j).astype(ml_dtypes.bfloat16)
    ones_row = np.ones((1, 128), np.float32)
    return dict(identb=ident, iota_negj=iota_negj, iota_t=iota_t, iota_t128=iota_t128,
                mask_triu=mask_triu, ones_row=ones_row)


def core_inputs(inputs, core):
    d = {}
    d["x_in"] = np.ascontiguousarray(inputs["L0_attn_a_u"][2 * core:2 * core + 2])
    d["log_dt"] = np.ascontiguousarray(inputs["L0_attn_a_log_dt"]).reshape(1, D)
    d["log_w_real"] = np.ascontiguousarray(inputs["L0_attn_a_log_w_real"]).reshape(1, N2)
    d["w_imag"] = np.ascontiguousarray(inputs["L0_attn_a_w_imag"]).reshape(1, N2)
    d["C_real"] = np.ascontiguousarray(inputs["L0_attn_a_C_real"]).reshape(D, N2)
    d["C_imag"] = np.ascontiguousarray(inputs["L0_attn_a_C_imag"]).reshape(D, N2)
    d["D_a"] = np.ascontiguousarray(inputs["L0_attn_a_D"]).reshape(1, D)
    d["W_out_a"] = np.ascontiguousarray(inputs["L0_attn_a_W_out"]).reshape(D, D)
    d["b_out_a"] = np.ascontiguousarray(inputs["L0_attn_a_b_out"]).reshape(1, D)
    d["tern"] = np.ascontiguousarray(inputs["L0_ffn_ternary_weight"]).reshape(D, D)
    d["bias_f"] = np.ascontiguousarray(inputs["L0_ffn_bias"]).reshape(1, D)
    d["act_scale"] = np.asarray(inputs["L0_ffn_act_scale"], np.float32).reshape(1, 1)
    d["w_in_b"] = np.ascontiguousarray(inputs["L0_attn_b_w_in"]).reshape(2 * D, D)
    d["b_in_b"] = np.ascontiguousarray(inputs["L0_attn_b_b_in"]).reshape(1, 2 * D)
    d["A_b"] = np.ascontiguousarray(inputs["L0_attn_b_A"]).reshape(D, S64)
    d["Bm_b"] = np.ascontiguousarray(inputs["L0_attn_b_Bm"]).reshape(D, S64)
    d["Cm_b"] = np.ascontiguousarray(inputs["L0_attn_b_Cm"]).reshape(D, S64)
    d["Dv_b"] = np.ascontiguousarray(inputs["L0_attn_b_Dv"]).reshape(1, D)
    d["w_out_b"] = np.ascontiguousarray(inputs["L0_attn_b_w_out"]).reshape(D, D)
    d["b_out_b"] = np.ascontiguousarray(inputs["L0_attn_b_b_out"]).reshape(1, D)
    d["g_norm"] = np.ascontiguousarray(inputs["L0_attn_b_g_norm"]).reshape(1, D)
    d.update(make_consts())
    return d


def kernel(**inputs):
    nc, _ = build_program()
    ins = [core_inputs(inputs, i) for i in range(8)]
    res = run_bass_kernel_spmd(nc, ins, core_ids=list(range(8)))
    parts = [np.asarray(res.results[i]["out"]).reshape(B2, L, D) for i in range(8)]
    return np.concatenate(parts, axis=0).astype(np.float32)


# version 04
# speedup vs baseline: 1.1083x; 1.0018x over previous
import math
import numpy as np
from contextlib import ExitStack

import ml_dtypes
import concourse.bass as bass
import concourse.mybir as mybir
from concourse.bass import AP
from concourse.bass_utils import run_bass_kernel_spmd



ENGS = ("tensor", "vector", "scalar", "gpsimd", "sync")
COMPUTE_ENGS = ("tensor", "vector", "scalar", "gpsimd")
OPS = {
    "tensor_tensor", "tensor_scalar", "tensor_copy", "tensor_reduce", "reduce_sum",
    "reduce_max", "scalar_tensor_tensor", "tensor_tensor_reduce", "bn_stats",
    "bn_aggr", "reciprocal", "memset", "select", "copy_predicated", "iota",
    "activation", "copy", "add", "mul", "sign", "sqrt", "square",
    "matmul", "transpose", "ldweights", "tensor_tensor_scan", "stream_shuffle",
    "tensor_add", "tensor_mul", "tensor_sub", "tensor_max", "tensor_relu",
    "tensor_scalar_mul", "tensor_scalar_add", "partition_broadcast",
    "partition_all_reduce",
}


class Eng:

    def __init__(self, fw, name, handle):
        self._fw = fw
        self._name = name
        self._h = handle
        self.last = None

    def __getattr__(self, attr):
        real = getattr(self._h, attr)
        if attr not in OPS:
            return real
        fw, name = self._fw, self._name

        def call(*a, **k):
            inst = real(*a, **k)
            self.last = inst
            if name in ("vector", "scalar", "gpsimd"):
                nb = fw.n[name]
                if nb > 0:
                    inst._wait_ge(fw.F[name], nb)
                inst.then_inc(fw.F[name], 1)
                fw.n[name] += 1
            return inst
        return call

    def mark(self):
        assert self._name == "tensor"
        assert self.last is not None
        self.last.then_inc(self._fw.F["tensor"], 1)
        self._fw.n["tensor"] += 1
        self.last = None
        return self._fw.n["tensor"]


class FW:
    def __init__(self, nc, st, blk):
        self.nc = nc
        self.blk = blk
        self.F = {e: st.enter_context(nc.semaphore(f"F_{e}")) for e in COMPUTE_ENGS}
        self.n = {e: 0 for e in COMPUTE_ENGS}
        self.Dq = {q: st.enter_context(nc.semaphore(f"D_{q}")) for q in ("sync", "gpsimd")}
        self.dn = {q: 0 for q in ("sync", "gpsimd")}
        self._eng_cache = {}

    def _sect(self, name, body):
        getattr(self.blk, name)(body)

    def sec(self, name):
        raise NotImplementedError

    def run(self, name, fn):
        out = {}

        def body(h):
            out["p"] = Eng(self, name, h)
            fn(out["p"])
        self._sect(name, body)
        return out.get("p")

    def dma(self, qname, out_ap, in_ap, wait=()):
        def body(h):
            for (e, m) in wait:
                if m > 0:
                    if e.startswith("q:"):
                        h.wait_ge(self.Dq[e[2:]], m)
                    else:
                        h.wait_ge(self.F[e], m)
            h.dma_start(out=out_ap, in_=in_ap).then_inc(self.Dq[qname], 16)
        self._sect(qname, body)
        self.dn[qname] += 16
        return self.dn[qname]

    def wait(self, eng, deps=(), dmas=()):
        def body(h):
            for (e, m) in deps:
                if m > 0:
                    h.wait_ge(self.F[e], m)
            for (q, c) in dmas:
                if c > 0:
                    h.wait_ge(self.Dq[q], c)
        self._sect(eng, body)

    def marks(self):
        return dict(self.n)

    def barrier(self):
        snap_n = dict(self.n)
        snap_d = dict(self.dn)
        for e in COMPUTE_ENGS:
            def body(h, e=e):
                for o in COMPUTE_ENGS:
                    if o != e and snap_n[o] > 0:
                        h.wait_ge(self.F[o], snap_n[o])
                for q in ("sync", "gpsimd"):
                    if snap_d[q] > 0:
                        h.wait_ge(self.Dq[q], snap_d[q])
            self._sect(e, body)
        for q in ("sync", "gpsimd"):
            def body(h, q=q):
                for o in COMPUTE_ENGS:
                    if snap_n[o] > 0:
                        h.wait_ge(self.F[o], snap_n[o])
                oq = "gpsimd" if q == "sync" else "sync"
                if snap_d[oq] > 0:
                    h.wait_ge(self.Dq[oq], snap_d[oq])
            self._sect(q, body)


def _part2_emit(ctx):
    fw = ctx["fw"]; nc = ctx["nc"]
    VE, SC, TE, GP = ctx["VE"], ctx["SC"], ctx["TE"], ctx["GP"]
    R1, R2, R3, AUX = ctx["R1"], ctx["R2"], ctx["R3"], ctx["AUX"]
    IDENT, IDENTF, IOTAS, ONEC = ctx["IDENT"], ctx["IDENTF"], ctx["IOTAS"], ctx["ONEC"]
    BOB, SML, STT = ctx["BOB"], ctx["SML"], ctx["STT"]
    ASC, BSC, CBC, HST, ZST, GST, T1S, HCS = (ctx["ASC"], ctx["BSC"], ctx["CBC"],
                                              ctx["HST"], ctx["ZST"], ctx["GST"],
                                              ctx["T1S"], ctx["HCS"])
    P0, P1, P2, P3, P4, P5 = ctx["P0"], ctx["P1"], ctx["P2"], ctx["P3"], ctx["P4"], ctx["P5"]
    PB0, PB1, PW = ctx["PB0"], ctx["PB1"], ctx["PW"]
    OTS = None
    W_out_a, tern, w_in_b, w_out_b = ctx["W_out_a"], ctx["tern"], ctx["w_in_b"], ctx["w_out_b"]
    A_b, Bm_b, Cm_b = ctx["A_b"], ctx["Bm_b"], ctx["Cm_b"]
    x_in, out = ctx["x_in"], ctx["out"]
    hb_winT = ctx["hb_winT"]
    dbgf = ctx["dbg"]
    PHASES = ctx["PHASES"] if "PHASES" in ctx else 99

    SCOPE = ctx["SCOPE"]

    def sbslice(col0, col1):
        return AUX[:, col0:col1]

    def wprep(Wsrc, dst, is_tern=False):
        WSTG = AUX[:, 8192:10240]
        for oc in range(8):
            slot = oc % 2
            stg = WSTG[:, slot * 1024:(slot + 1) * 1024]
            mprev = fw.marks()
            nld = fw.dma("gpsimd", stg, Wsrc[oc * 128:(oc + 1) * 128, :],
                         wait=[("vector", mprev["vector"]), ("tensor", mprev["tensor"])])
            if is_tern:
                def cvt(v, stg=stg, nld=nld):
                    v.wait_ge(fw.Dq["gpsimd"], nld)
                    v.tensor_scalar(out=stg, in0=stg, scalar1=-1.0, scalar2=None, op0=ALU.add)
                VE(cvt)
            mk = fw.marks()

            def trw(t, stg=stg, nld=nld, mk=mk, is_tern=is_tern):
                if not is_tern:
                    t.wait_ge(fw.Dq["gpsimd"], nld)
                else:
                    t.wait_ge(fw.F["vector"], mk["vector"])
                for k in range(8):
                    t.transpose(PB0[:, k * 128:(k + 1) * 128], stg[:, k * 128:(k + 1) * 128], IDENT[:])
                t.mark()
            TE(trw)
            mt = fw.marks()["tensor"]

            def evw(v, oc=oc, mt=mt, dst=dst):
                v.wait_ge(fw.F["tensor"], mt)
                dap = AP(tensor=dst.tensor, offset=dst.offset + oc * 128,
                         ap=[[dst.ap[0][0], 128], [1024, 8], [1, 128]])
                v.tensor_copy(out=dap, in_=PB0[:].rearrange("p (a b) -> p a b", a=8))
            VE(evw)

    SCOPE("p2g_wout")
    WoT = AUX[:, 0:8192]
    XS = AUX[:, 10240:14336]
    X1S = AUX[:, 14336:18432]
    XTT = AUX[:, 18432:20480]
    wprep(W_out_a, WoT)
    for tau in range(8):
        b, q = divmod(tau, 4)
        for j in range(4):
            c = q * 4 + j
            slot = j % 2
            xstg = XTT[:, slot * 1024:(slot + 1) * 1024]
            mprev = fw.marks()
            nld = fw.dma("gpsimd", xstg, x_in[b, c * 128:(c + 1) * 128, :],
                         wait=[("tensor", mprev["tensor"]), ("vector", mprev["vector"])])

            def trx(t, xstg=xstg, nld=nld):
                t.wait_ge(fw.Dq["gpsimd"], nld)
                for k in range(8):
                    t.transpose(PB1[:, k * 128:(k + 1) * 128], xstg[:, k * 128:(k + 1) * 128], IDENT[:])
                t.mark()
            TE(trx)
            mt = fw.marks()["tensor"]

            def evx(v, j=j, mt=mt):
                v.wait_ge(fw.F["tensor"], mt)
                dap = AP(tensor=XS.tensor, offset=XS.offset + j * 128,
                         ap=[[XS.ap[0][0], 128], [512, 8], [1, 128]])
                v.tensor_copy(out=dap, in_=PB1[:].rearrange("p (a b) -> p a b", a=8))
            VE(evx)
        mxs = fw.marks()
        for w in range(2):
            mev = fw.marks()

            def wmm(t, b=b, q=q, w=w, mev=mev):
                t.wait_ge(fw.F["vector"], mev["vector"])
                for oci in range(4):
                    oc = w * 4 + oci
                    for k in range(8):
                        t.matmul(PW[oci][:], WoT[:, (k * 8 + oc) * 128:(k * 8 + oc + 1) * 128],
                                 R2[:, (b * 8 + k) * 2048 + q * 512:(b * 8 + k) * 2048 + (q + 1) * 512],
                                 start=(k == 0), stop=(k == 7))
                t.mark()
            TE(wmm)
            mw = fw.marks()["tensor"]

            def wev(v, w=w, mw=mw):
                v.wait_ge(fw.F["tensor"], mw)
                for oci in range(4):
                    oc = w * 4 + oci
                    v.scalar_tensor_tensor(out=X1S[:, oc * 512:(oc + 1) * 512], in0=PW[oci][:],
                                           scalar=BOB[:, oc:oc + 1], in1=XS[:, oc * 512:(oc + 1) * 512],
                                           op0=ALU.add, op1=ALU.add)
            VE(wev)

        def x1wr(v, b=b, q=q):
            dap = AP(tensor=R2[:].tensor, offset=R2[:].offset + b * 16384 + q * 512,
                     ap=[[32768, 128], [2048, 8], [1, 512]])
            v.tensor_copy(out=dap, in_=X1S[:].rearrange("p (a b) -> p a b", a=8))
        VE(x1wr)
    fw.barrier()
    if PHASES < 5:
        return

    def sbar_prep():
        def v(v):
            v.tensor_scalar(out=SML[0:1, 49:50], in0=SML[0:1, 48:49], scalar1=1e-5,
                            scalar2=None, op0=ALU.max)
            v.reciprocal(out=SML[0:1, 50:51], in_=SML[0:1, 49:50])
        VE(v)
        mk = fw.marks()

        def t(t):
            t.wait_ge(fw.F["vector"], mk["vector"])
            t.matmul(P4[:, 0:1], IOTAS[0:1, 384:512], SML[0:1, 49:50], start=True, stop=True)
            t.mark()
        TE(t)
        mt = fw.marks()["tensor"]

        def v2(v):
            v.wait_ge(fw.F["tensor"], mt)
            v.tensor_copy(out=SML[:, 51:52], in_=P4[:, 0:1])
        VE(v2)
    ffn_state = {"sbar_done": False}

    def emit_ffn():
        TTW = AUX[:, 0:8192]
        XQ = AUX[:, 10240:14336]
        QF1 = AUX[:, 14336:15360].bitcast(F32)
        QF2 = AUX[:, 15360:16384].bitcast(F32)
        EPS_ = AUX[:, 16384:20480]
        SQS = EPS_[:, 0:1024]
        BCSM = AUX[:, 8192:9216].bitcast(F32)
        BCSR = AUX[:, 9216:10240].bitcast(F32)
        RWB = AUX[0:1, 16384:17408].bitcast(F32)
        wprep(tern, TTW, is_tern=True)
        if not ffn_state["sbar_done"]:
            sbar_prep()
            ffn_state["sbar_done"] = True
        for tau in range(8):
            b, q = divmod(tau, 4)

            def xdo(k):
                return R2[:, (b * 8 + k) * 2048 + q * 512:(b * 8 + k) * 2048 + (q + 1) * 512]
            smm_marks = {}
            for do in range(8):
                slot = SQS[:, (do % 2) * 512:(do % 2) * 512 + 512]

                def sq(v, do=do, slot=slot):
                    if do >= 2:
                        v.wait_ge(fw.F["tensor"], smm_marks[do - 2])
                    v.tensor_tensor(out=slot, in0=xdo(do), in1=xdo(do), op=ALU.mult)
                VE(sq)
                mq = fw.marks()

                def smm(t, do=do, mq=mq, slot=slot):
                    t.wait_ge(fw.F["vector"], mq["vector"])
                    t.matmul(P4[0:1, 0:512], ONEC[:, 0:1], xdo(do), start=(do == 0), stop=(do == 7))
                    t.matmul(P5[0:1, 0:512], ONEC[:, 0:1], slot, start=(do == 0), stop=(do == 7))
                    t.mark()
                TE(smm)
                smm_marks[do] = fw.marks()["tensor"]
            ms = fw.marks()["tensor"]

            def spost(v, ms=ms):
                v.wait_ge(fw.F["tensor"], ms)
                v.tensor_scalar(out=QF1[0:1, 0:512], in0=P4[0:1, 0:512], scalar1=1.0 / D,
                                scalar2=None, op0=ALU.mult)
                v.tensor_tensor(out=QF2[0:1, 0:512], in0=QF1[0:1, 0:512], in1=QF1[0:1, 0:512], op=ALU.mult)
                v.scalar_tensor_tensor(out=QF2[0:1, 0:512], in0=P5[0:1, 0:512], scalar=1.0 / D,
                                       in1=QF2[0:1, 0:512], op0=ALU.mult, op1=ALU.subtract)
                v.tensor_scalar(out=QF2[0:1, 0:512], in0=QF2[0:1, 0:512], scalar1=EPS_LN,
                                scalar2=None, op0=ALU.add)
                v.reciprocal(out=RWB[:], in_=QF2[0:1, 0:512])
            VE(spost)
            mv = fw.marks()

            def ssq(s, mv=mv):
                s.wait_ge(fw.F["vector"], mv["vector"])
                s.activation(out=QF2[0:1, 0:512], in_=RWB[:], func=AF.Sqrt)
            SC(ssq)
            mss = fw.marks()

            def rfold(v, mss=mss):
                v.wait_ge(fw.F["scalar"], mss["scalar"])
                v.tensor_scalar(out=QF2[0:1, 0:512], in0=QF2[0:1, 0:512],
                                scalar1=SML[0:1, 50:51], scalar2=None, op0=ALU.mult)
            VE(rfold)
            mr = fw.marks()

            def bcmm(t, mr=mr):
                t.wait_ge(fw.F["vector"], mr["vector"])
                t.matmul(P4[:], IOTAS[0:1, 384:512], QF1[0:1, 0:512], start=True, stop=True)
                t.matmul(P5[:], IOTAS[0:1, 384:512], QF2[0:1, 0:512], start=True, stop=True)
                t.mark()
            TE(bcmm)
            mbc = fw.marks()["tensor"]

            def bcev(v, mbc=mbc):
                v.wait_ge(fw.F["tensor"], mbc)
                v.tensor_copy(out=BCSM[:], in_=P4[:])
                v.tensor_copy(out=BCSR[:], in_=P5[:])
            VE(bcev)
            for do in range(8):
                def qn(v, do=do):
                    v.tensor_tensor(out=QF1[:, 0:512], in0=xdo(do), in1=BCSM[:], op=ALU.subtract)
                    v.tensor_tensor(out=QF1[:, 0:512], in0=QF1[:, 0:512], in1=BCSR[:], op=ALU.mult)
                    v.tensor_scalar(out=QF1[:, 0:512], in0=QF1[:, 0:512], scalar1=127.0,
                                    scalar2=-127.0, op0=ALU.min, op1=ALU.max)
                    v.tensor_scalar(out=QF2[:, 0:512], in0=QF1[:, 0:512], scalar1=MAGIC,
                                    scalar2=-MAGIC, op0=ALU.add, op1=ALU.add)
                    v.tensor_scalar(out=XQ[:, do * 512:(do + 1) * 512], in0=QF1[:, 0:512],
                                    scalar1=MAGIC64, scalar2=-MAGIC64, op0=ALU.add, op1=ALU.add)
                    v.tensor_tensor(out=QF1[:, 0:512], in0=XQ[:, do * 512:(do + 1) * 512],
                                    in1=QF2[:, 0:512], op=ALU.subtract)
                    v.scalar_tensor_tensor(out=XQ[:, do * 512:(do + 1) * 512], in0=QF1[:, 0:512],
                                           scalar=SML[:, 53:54], in1=QF2[:, 0:512],
                                           op0=ALU.mult, op1=ALU.add)
                VE(qn)
            for w in range(2):
                mev = fw.marks()

                def fmm(t, w=w, mev=mev):
                    t.wait_ge(fw.F["vector"], mev["vector"])
                    for oci in range(4):
                        oc = w * 4 + oci
                        for k in range(8):
                            t.matmul(PW[oci][:], TTW[:, (k * 8 + oc) * 128:(k * 8 + oc + 1) * 128],
                                     XQ[:, k * 512:(k + 1) * 512], start=(k == 0), stop=(k == 7))
                    t.mark()
                TE(fmm)
                mw = fw.marks()["tensor"]

                def fev(v, w=w, mw=mw, b=b, q=q):
                    v.wait_ge(fw.F["tensor"], mw)
                    for oci in range(4):
                        oc = w * 4 + oci
                        v.tensor_scalar(out=QF1[:, 0:512], in0=PW[oci][:], scalar1=SML[:, 51:52],
                                        scalar2=BOB[:, 8 + oc:8 + oc + 1], op0=ALU.mult, op1=ALU.add)
                        v.tensor_tensor(out=EPS_[:, oc * 512:(oc + 1) * 512], in0=QF1[:, 0:512],
                                        in1=xdo(oc), op=ALU.add)
                VE(fev)

            def x2wr(v, b=b, q=q):
                dap = AP(tensor=R2[:].tensor, offset=R2[:].offset + b * 16384 + q * 512,
                         ap=[[32768, 128], [2048, 8], [1, 512]])
                v.tensor_copy(out=dap, in_=EPS_[:].rearrange("p (a b) -> p a b", a=8))
            VE(x2wr)
        fw.barrier()

    SCOPE("p3_ffn1")
    emit_ffn()
    if PHASES < 6:
        return

    SCOPE("p4a_winT")
    WSTG = AUX[:, 8192:10240]
    for et in range(16):
        slot = et % 2
        stg = WSTG[:, slot * 1024:(slot + 1) * 1024]
        mprev = fw.marks()
        nld = fw.dma("gpsimd", stg, w_in_b[et * 128:(et + 1) * 128, :],
                     wait=[("vector", mprev["vector"]), ("tensor", mprev["tensor"])])

        def trwi(t, stg=stg, nld=nld):
            t.wait_ge(fw.Dq["gpsimd"], nld)
            for k in range(8):
                t.transpose(PB0[:, k * 128:(k + 1) * 128], stg[:, k * 128:(k + 1) * 128], IDENT[:])
            t.mark()
        TE(trwi)
        mt = fw.marks()["tensor"]

        def evwi(v, et=et, mt=mt):
            v.wait_ge(fw.F["tensor"], mt)
            dap = AP(tensor=R3[:].tensor, offset=R3[:].offset + et * 128,
                     ap=[[16384, 128], [2048, 8], [1, 128]])
            v.tensor_copy(out=dap, in_=PB0[:].rearrange("p (a b) -> p a b", a=8))
        VE(evwi)
    SCOPE("p4b_ln3_win")
    WIN2 = AUX[:, 0:4096]
    XSUB = AUX[:, 4096:8192]
    BCS4M = AUX[:, 8192:9216].bitcast(F32)
    BCS4R = AUX[:, 9216:10240].bitcast(F32)
    XN = AUX[:, 10240:14336]
    QF4 = AUX[:, 14336:15360].bitcast(F32)
    QF5 = AUX[:, 15360:16384].bitcast(F32)
    SQ4 = AUX[:, 16384:17408]
    RW3 = AUX[0:1, 17408:18432].bitcast(F32)

    for tau in range(4):
        b, qs = divmod(tau, 2)

        def xe(do):
            base = (b * 8 + do) * 2048 + qs * 1024
            return AP(tensor=R2[:].tensor, offset=R2[:].offset + base,
                      ap=[[32768, 128], [2, 512]])
        smm3_marks = {}
        for do in range(8):
            slot = SQ4[:, (do % 2) * 512:(do % 2) * 512 + 512]

            def cpx(v, do=do, slot=slot):
                if do >= 2:
                    v.wait_ge(fw.F["tensor"], smm3_marks[do - 2])
                v.tensor_copy(out=XSUB[:, do * 512:(do + 1) * 512], in_=xe(do))
                v.tensor_tensor(out=slot, in0=XSUB[:, do * 512:(do + 1) * 512],
                                in1=XSUB[:, do * 512:(do + 1) * 512], op=ALU.mult)
            VE(cpx)
            mq = fw.marks()

            def smm3(t, do=do, mq=mq, slot=slot):
                t.wait_ge(fw.F["vector"], mq["vector"])
                t.matmul(P4[0:1, 0:512], ONEC[:, 0:1], XSUB[:, do * 512:(do + 1) * 512],
                         start=(do == 0), stop=(do == 7))
                t.matmul(P5[0:1, 0:512], ONEC[:, 0:1], slot, start=(do == 0), stop=(do == 7))
                t.mark()
            TE(smm3)
            smm3_marks[do] = fw.marks()["tensor"]
        ms3 = fw.marks()["tensor"]

        def spost3(v, ms3=ms3):
            v.wait_ge(fw.F["tensor"], ms3)
            v.tensor_scalar(out=QF4[0:1, 0:512], in0=P4[0:1, 0:512], scalar1=1.0 / D,
                            scalar2=None, op0=ALU.mult)
            v.tensor_tensor(out=QF5[0:1, 0:512], in0=QF4[0:1, 0:512], in1=QF4[0:1, 0:512], op=ALU.mult)
            v.scalar_tensor_tensor(out=QF5[0:1, 0:512], in0=P5[0:1, 0:512], scalar=1.0 / D,
                                   in1=QF5[0:1, 0:512], op0=ALU.mult, op1=ALU.subtract)
            v.tensor_scalar(out=QF5[0:1, 0:512], in0=QF5[0:1, 0:512], scalar1=1.0 + EPS_RMS,
                            scalar2=EPS_RMS * EPS_LN, op0=ALU.mult, op1=ALU.add)
            v.reciprocal(out=RW3[:], in_=QF5[0:1, 0:512])
        VE(spost3)
        mv3 = fw.marks()

        def ssq3(s, mv3=mv3):
            s.wait_ge(fw.F["vector"], mv3["vector"])
            s.activation(out=QF5[0:1, 0:512], in_=RW3[:], func=AF.Sqrt)
        SC(ssq3)
        ms4 = fw.marks()

        def bc3(t, ms4=ms4):
            t.wait_ge(fw.F["scalar"], ms4["scalar"])
            t.matmul(P4[:], IOTAS[0:1, 384:512], QF4[0:1, 0:512], start=True, stop=True)
            t.matmul(P5[:], IOTAS[0:1, 384:512], QF5[0:1, 0:512], start=True, stop=True)
            t.mark()
        TE(bc3)
        mb3 = fw.marks()["tensor"]

        def bce3(v, mb3=mb3):
            v.wait_ge(fw.F["tensor"], mb3)
            v.tensor_copy(out=BCS4M[:], in_=P4[:])
            v.tensor_copy(out=BCS4R[:], in_=P5[:])
        VE(bce3)
        for do in range(8):
            def nrm(v, do=do):
                v.tensor_tensor(out=QF4[:], in0=XSUB[:, do * 512:(do + 1) * 512],
                                in1=BCS4M[:], op=ALU.subtract)
                v.scalar_tensor_tensor(out=XN[:, do * 512:(do + 1) * 512], in0=QF4[:],
                                       scalar=BOB[:, 24 + do:24 + do + 1], in1=BCS4R[:],
                                       op0=ALU.mult, op1=ALU.mult)
            VE(nrm)
        for w in range(4):
            mev = fw.marks()
            for k in range(8):
                wk = R3[:, k * 2048:(k + 1) * 2048]

                def wmm4(t, w=w, k=k, wk=wk, mev=mev):
                    if k == 0:
                        t.wait_ge(fw.F["vector"], mev["vector"])
                    for eci in range(4):
                        ec = w * 4 + eci
                        t.matmul(PW[eci][:], wk[:, ec * 128:(ec + 1) * 128],
                                 XN[:, k * 512:(k + 1) * 512], start=(k == 0), stop=(k == 7))
                    t.mark()
                TE(wmm4)
            mw4 = fw.marks()["tensor"]

            def wev4(v, w=w, mw4=mw4, b=b, qs=qs):
                v.wait_ge(fw.F["tensor"], mw4)
                for eci in range(4):
                    ec = w * 4 + eci
                    v.tensor_scalar(out=R1[:, (b * 16 + ec) * 1024 + qs * 512:(b * 16 + ec) * 1024 + (qs + 1) * 512],
                                    in0=PW[eci][:], scalar1=SML[:, 16 + ec:16 + ec + 1],
                                    scalar2=None, op0=ALU.add)
            VE(wev4)
    fw.barrier()

    SCOPE("p4c_abc")
    for do in range(8):
        for b in range(B2):
            fw.dma("sync", ASC[:, (b * 8 + do) * 64:(b * 8 + do + 1) * 64],
                   A_b[do * 128:(do + 1) * 128, :])
            fw.dma("sync", BSC[:, (b * 8 + do) * 64:(b * 8 + do + 1) * 64],
                   Bm_b[do * 128:(do + 1) * 128, :])
    for do in range(8):
        fw.dma("sync", GST[:, do * 64:(do + 1) * 64], Cm_b[do * 128:(do + 1) * 128, :])
    ncm = fw.dn["sync"]

    def onef(v):
        v.memset(STT[:, 0:1], 1.0)
        v.memset(HST[:], 0.0)
    VE(onef)
    monef = fw.marks()

    def csum_mm(t, monef=monef):
        t.wait_ge(fw.Dq["sync"], ncm)
        t.wait_ge(fw.F["vector"], monef["vector"])
        for do in range(8):
            t.matmul(P4[0:1, 0:64], STT[:, 0:1], GST[:, do * 64:(do + 1) * 64],
                     start=(do == 0), stop=(do == 7))
        t.mark()
    TE(csum_mm)
    mcs = fw.marks()["tensor"]

    def csum_ev(v, mcs=mcs):
        v.wait_ge(fw.F["tensor"], mcs)
        v.tensor_copy(out=QF4[0:1, 0:64], in_=P4[0:1, 0:64])
    VE(csum_ev)
    mce = fw.marks()

    def cbc_mm(t, mce=mce):
        t.wait_ge(fw.F["vector"], mce["vector"])
        t.matmul(P5[:, 0:64], IOTAS[0:1, 384:512], QF4[0:1, 0:64], start=True, stop=True)
        t.mark()
    TE(cbc_mm)
    mcb = fw.marks()["tensor"]

    def cbc_ev(v, mcb=mcb):
        v.wait_ge(fw.F["tensor"], mcb)
        v.tensor_copy(out=CBC[:], in_=P5[:, 0:64])
    VE(cbc_ev)
    fw.barrier()
    if PHASES < 7:
        return

    SCOPE("p4d_scan")
    def xt_bc(t):
        return AP(tensor=R1[:].tensor, offset=R1[:].offset + t,
                  ap=[[32768, 128], [16384, 2], [1024, 8], [0, 64]])

    cbc_ap = AP(tensor=CBC[:].tensor, offset=CBC[:].offset,
                ap=[[CBC[:].ap[0][0], 128], [0, 2], [0, 8], [1, 64]])
    GB = [GST, AUX[:, 14464:16512].bitcast(F32)]
    HB = [HST, AUX[:, 16512:18560].bitcast(F32)]
    smark = {-1: fw.marks()["scalar"]}
    zmark = {}
    gmark = {}
    GS = 512

    def g_gp(g, t_):
        if t_ >= 2:
            g.wait_ge(fw.F["vector"], zmark[t_ - 2])
        gap = AP(tensor=R1[:].tensor, offset=R1[:].offset + t_,
                 ap=[[32768, 128], [1024, 8], [0, 64]])
        g.tensor_tensor(out=GB[t_ % 2][:, 0:GS], in0=BSC[:, 0:GS], in1=gap, op=ALU.mult)

    def g_ve(v, t_):
        gap = AP(tensor=R1[:].tensor, offset=R1[:].offset + t_ + 16384,
                 ap=[[32768, 128], [1024, 8], [0, 64]])
        v.tensor_tensor(out=GB[t_ % 2][:, GS:1024], in0=BSC[:, GS:1024], in1=gap, op=ALU.mult)

    def y_emit(v, t_):
        v.tensor_tensor(out=HCS[:], in0=HB[(t_ + 1) % 2][:], in1=cbc_ap, op=ALU.mult)
        ysap = AP(tensor=R3[:].tensor, offset=R3[:].offset + t_,
                  ap=[[16384, 128], [8192, 2], [1024, 8]])
        with nc.allow_low_precision(reason="ys store bf16; DVE reduce accumulates fp32"):
            v.tensor_reduce(out=ysap, in_=HCS[:].rearrange("p (a b c) -> p a b c", a=2, b=8),
                            axis=AX.X, op=ALU.add)

    def prolog(g):
        g_gp(g, 0)
        g_gp(g, 1)
    GP(prolog)
    gmark[0] = gmark[1] = fw.marks()["gpsimd"]

    def prolog_v(v):
        g_ve(v, 0)
        g_ve(v, 1)
    VE(prolog_v)
    for t_ in range(1024):
        def step_v(v, t_=t_):
            v.wait_ge(fw.F["scalar"], smark[t_ - 1])
            if t_ >= 1:
                y_emit(v, t_ - 1)
            v.tensor_tensor(out=T1S[:], in0=HB[t_ % 2][:], in1=ASC[:], op=ALU.mult)
            v.wait_ge(fw.F["gpsimd"], gmark[t_])
            v.tensor_tensor(out=ZST[:], in0=T1S[:], in1=GB[t_ % 2][:], op=ALU.add)
        VE(step_v)
        zmark[t_] = fw.marks()["vector"]
        if t_ + 2 < 1024:
            def gp_sec(g, t_=t_):
                g_gp(g, t_ + 2)
            GP(gp_sec)
            gmark[t_ + 2] = fw.marks()["gpsimd"]

        def step_s(s, t_=t_):
            s.wait_ge(fw.F["vector"], zmark[t_])
            s.activation(out=HB[(t_ + 1) % 2][:], in_=ZST[:], func=AF.Tanh)
        SC(step_s)
        smark[t_] = fw.marks()["scalar"]
        if t_ + 2 < 1024:
            def gve_sec(v, t_=t_):
                g_ve(v, t_ + 2)
            VE(gve_sec)

    def ylast(v):
        v.wait_ge(fw.F["scalar"], smark[1023])
        y_emit(v, 1023)
    VE(ylast)
    fw.barrier()
    if PHASES < 8:
        return

    SCOPE("p4e_post")
    for b in range(B2):
        for do in range(8):
            xp = R1[:, (b * 16 + do) * 1024:(b * 16 + do + 1) * 1024]
            gcol = (b * 16 + 8 + do)
            gate = R1[:, gcol * 1024:(gcol + 1) * 1024]
            ys = R3[:, b * 8192 + do * 1024:b * 8192 + (do + 1) * 1024]

            mpv = fw.marks()["vector"]

            def silu(s, gate=gate, mpv=mpv):
                s.wait_ge(fw.F["vector"], mpv)
                s.activation(out=GST[:], in_=gate, func=AF.Silu)
            SC(silu)
            msl = fw.marks()["scalar"]

            def opre(v, xp=xp, gate=gate, ys=ys, do=do, msl=msl):
                v.wait_ge(fw.F["scalar"], msl)
                v.tensor_scalar(out=T1S[:], in0=xp, scalar1=BOB[:, 16 + do:16 + do + 1],
                                scalar2=None, op0=ALU.mult)
                v.tensor_tensor(out=HCS[:], in0=ys, in1=T1S[:], op=ALU.add)
                v.tensor_tensor(out=gate, in0=HCS[:], in1=GST[:], op=ALU.mult)
            VE(opre)
    WOB = AUX[:, 0:8192]
    wprep(w_out_b, WOB)
    OBS = AUX[:, 10240:10752]
    for tau in range(4):
        b, qs = divmod(tau, 2)
        for w in range(2):
            mev = fw.marks()

            def womm(t, b=b, qs=qs, w=w, mev=mev):
                t.wait_ge(fw.F["vector"], mev["vector"])
                for oci in range(4):
                    oc = w * 4 + oci
                    for k in range(8):
                        t.matmul(PW[oci][:], WOB[:, (k * 8 + oc) * 128:(k * 8 + oc + 1) * 128],
                                 R1[:, (b * 16 + 8 + k) * 1024 + qs * 512:(b * 16 + 8 + k) * 1024 + (qs + 1) * 512],
                                 start=(k == 0), stop=(k == 7))
                t.mark()
            TE(womm)
            mwo = fw.marks()["tensor"]

            def woev(v, b=b, qs=qs, w=w, mwo=mwo):
                v.wait_ge(fw.F["tensor"], mwo)
                for oci in range(4):
                    oc = w * 4 + oci
                    v.tensor_scalar(out=OBS[:], in0=PW[oci][:], scalar1=BOB[:, 32 + oc:32 + oc + 1],
                                    scalar2=None, op0=ALU.add)
                    for par in range(2):
                        base = (b * 8 + oc) * 2048 + qs * 1024 + par
                        rap = AP(tensor=R2[:].tensor, offset=R2[:].offset + base,
                                 ap=[[32768, 128], [2, 512]])
                        v.tensor_tensor(out=rap, in0=rap, in1=OBS[:], op=ALU.add)
            VE(woev)
    fw.barrier()
    if PHASES < 9:
        return

    SCOPE("p5_ffn2")
    emit_ffn()

    SCOPE("p6_out")
    OTS_ = [AUX[:, 10240:12288].bitcast(F32), AUX[:, 12288:14336].bitcast(F32)]
    osem = [ctx["st"].enter_context(nc.semaphore("OSEM0")),
            ctx["st"].enter_context(nc.semaphore("OSEM1"))]
    ocnt = [0, 0]
    for i in range(B2 * NCH):
        b, c = divmod(i, NCH)
        sl = i % 2
        mprev = fw.marks()

        def otr(t, b=b, c=c, mprev=mprev):
            t.wait_ge(fw.F["vector"], mprev["vector"])
            for do in range(8):
                t.transpose(PB0[:, do * 128:(do + 1) * 128],
                            R2[:, (b * 8 + do) * 2048 + c * 128:(b * 8 + do) * 2048 + (c + 1) * 128],
                            IDENT[:])
            t.mark()
        TE(otr)
        mo = fw.marks()["tensor"]

        def oev(v, mo=mo, sl=sl, prevd=ocnt[sl]):
            v.wait_ge(fw.F["tensor"], mo)
            if prevd > 0:
                v.wait_ge(osem[sl], prevd)
            v.tensor_copy(out=OTS_[sl][:], in_=PB0[:])
        VE(oev)
        mvv = fw.marks()

        def ostore(h, b=b, c=c, sl=sl, mvv=mvv):
            h.wait_ge(fw.F["vector"], mvv["vector"])
            h.dma_start(out=out[b, c * 128:(c + 1) * 128, :],
                        in_=OTS_[sl][:]).then_inc(osem[sl], 16)
        fw._sect("sync", ostore)
        ocnt[sl] += 16
    def ofin(v, c0=ocnt[0], c1=ocnt[1]):
        v.wait_ge(osem[0], c0)
        v.wait_ge(osem[1], c1)
    VE(ofin)
    fw.barrier()




F32 = mybir.dt.float32
BF16 = mybir.dt.bfloat16
I32 = mybir.dt.int32
AF = mybir.ActivationFunctionType
ALU = mybir.AluOpType
AX = mybir.AxisListType

B2, L, D, NB = 2, 2048, 1024, 8
NCH = 16
N2 = 32
S64 = 64
EPS_LN, EPS_RMS = 1e-5, 1e-6
TWO_PI = 2.0 * math.pi
INV_2PI = 1.0 / TWO_PI
MAGIC = 1.5 * 2.0 ** 23
MAGIC64 = 1.5 * 2.0 ** 52
SLABS = NB
PHASES = 99


def build_program(debug=()):
    nc = bass.Bass("TRN2", target_bir_lowering=False)
    dbg_outs = {}

    def dram_in(name, shape, dt=F32):
        return nc.dram_tensor(name, shape, dt, kind="ExternalInput").ap()

    x_in = dram_in("x_in", [B2, L, D])
    out = nc.dram_tensor("out", [B2, L, D], F32, kind="ExternalOutput").ap()
    log_dt = dram_in("log_dt", [1, D])
    log_w_real = dram_in("log_w_real", [1, N2])
    w_imag = dram_in("w_imag", [1, N2])
    C_real = dram_in("C_real", [D, N2])
    C_imag = dram_in("C_imag", [D, N2])
    D_a = dram_in("D_a", [1, D])
    W_out_a = dram_in("W_out_a", [D, D])
    b_out_a = dram_in("b_out_a", [1, D])
    tern = dram_in("tern", [D, D], I32)
    bias_f = dram_in("bias_f", [1, D])
    act_scale = dram_in("act_scale", [1, 1])
    w_in_b = dram_in("w_in_b", [2 * D, D])
    b_in_b = dram_in("b_in_b", [1, 2 * D])
    A_b = dram_in("A_b", [D, S64])
    Bm_b = dram_in("Bm_b", [D, S64])
    Cm_b = dram_in("Cm_b", [D, S64])
    Dv_b = dram_in("Dv_b", [1, D])
    w_out_b = dram_in("w_out_b", [D, D])
    b_out_b = dram_in("b_out_b", [1, D])
    g_norm = dram_in("g_norm", [1, D])
    identb = dram_in("identb", [128, 128], BF16)
    iota_negj = dram_in("iota_negj", [1, 128])
    iota_t = dram_in("iota_t", [1, 128])
    iota_t128 = dram_in("iota_t128", [1, 128])
    mask_triu = dram_in("mask_triu", [128, 128], BF16)
    ones_row = dram_in("ones_row", [1, 128])

    hb_aim = nc.dram_tensor("hb_aim", [1, D * N2], F32, kind="Internal").ap()
    hb_are = nc.dram_tensor("hb_are", [1, D * N2], F32, kind="Internal").ap()
    hb_cwr = nc.dram_tensor("hb_cwr", [1, D * N2], F32, kind="Internal").ap()
    hb_cwi = nc.dram_tensor("hb_cwi", [1, D * N2], F32, kind="Internal").ap()
    hb_slab = nc.dram_tensor("hb_slab", [NB, 128, B2 * NCH * 128], BF16, kind="Internal").ap()
    hb_winT = nc.dram_tensor("hb_winT", [NB, 128, 2 * D], BF16, kind="Internal").ap()

    def dbg(name, shape, dt):
        if name in debug:
            dbg_outs[name] = nc.dram_tensor(name, shape, dt, kind="ExternalOutput").ap()
            return dbg_outs[name]
        return None

    st = ExitStack()
    sbuf = lambda name, shape, dt=BF16: st.enter_context(nc.sbuf_tensor(name, shape, dt))
    psum = lambda name, shape, dt=F32: st.enter_context(nc.psum_tensor(name, shape, dt))

    with st:
        R1 = sbuf("R1", [128, 32768])
        R2 = sbuf("R2", [128, 32768])
        R3 = sbuf("R3", [128, 16384])
        AUX = sbuf("AUX", [128, 20480])
        IDENT = sbuf("IDENTS", [128, 128])
        IDENTF = sbuf("IDENTF", [128, 128], F32)
        MASKT = sbuf("MASKT", [128, 128])
        IOTAS = sbuf("IOTAS", [1, 512], F32)
        ONEC = sbuf("ONEC", [128, 2])
        STM = sbuf("STM", [128, 96], F32)
        STT = sbuf("STT", [128, 16], F32)
        SML = sbuf("SML", [128, 64], F32)
        BOB = sbuf("BOB", [128, 64], F32)
        DBC = sbuf("DBC", [128, 1024])
        ASC = AUX[:, 0:2048].bitcast(F32)
        BSC = AUX[:, 2048:4096].bitcast(F32)
        HST = AUX[:, 4096:6144].bitcast(F32)
        ZST = AUX[:, 6144:8192].bitcast(F32)
        GST = AUX[:, 8192:10240].bitcast(F32)
        T1S = AUX[:, 10240:12288].bitcast(F32)
        HCS = AUX[:, 12288:14336].bitcast(F32)
        CBC = AUX[:, 14336:14464].bitcast(F32)
        PRM = AUX[:, 8192:12288].bitcast(F32)
        BCB = AUX[:, 12288:14336].bitcast(F32)
        DROW = AUX[0:1, 14336:16384].bitcast(F32)
        XT = AUX[:, 0:4096]

        P0 = psum("P0", [128, 512]); P1 = psum("P1", [128, 512]); P2 = psum("P2", [128, 512])
        P3 = psum("P3", [128, 512]); P4 = psum("P4", [128, 512]); P5 = psum("P5", [128, 512])
        PB0 = psum("PB0", [128, 1024], BF16)
        PB1 = psum("PB1", [128, 1024], BF16)
        PW = [P0, P1, P2, P3]

        st.enter_context(nc.allow_non_contiguous_dma(reason="strided staging loads/stores"))
        blk = st.enter_context(nc.Block())
        fw = FW(nc, st, blk)

        _scope = {"cur": None, "id": None}

        def SCOPE(name):
            if _scope["cur"] is not None:
                nc.leave_named_scope(_scope["cur"], _scope["id"], False)
                _scope["cur"] = None
            if name:
                sid, _ = nc.enter_named_scope(name, False)
                _scope["cur"], _scope["id"] = name, sid

        def VE(fn):
            fw.run("vector", fn)
        def SC(fn):
            fw.run("scalar", fn)
        def TE(fn):
            fw.run("tensor", fn)
        def GP(fn):
            fw.run("gpsimd", fn)

        SCOPE("p0_consts")
        fw.dma("sync", IDENT[:], identb[:])
        fw.dma("sync", MASKT[:], mask_triu[:])
        fw.dma("sync", IOTAS[0:1, 0:128], iota_negj[:])
        fw.dma("sync", IOTAS[0:1, 128:256], iota_t[:])
        fw.dma("sync", IOTAS[0:1, 256:384], iota_t128[:])
        fw.dma("sync", IOTAS[0:1, 384:512], ones_row[:])
        for (dst0, src) in ((0, b_out_a), (8, bias_f), (16, Dv_b), (24, g_norm), (32, b_out_b)):
            fw.dma("sync", BOB[:, dst0:dst0 + 8],
                   AP(tensor=src.tensor, offset=0, ap=[[1, 128], [128, 8]]))
        fw.dma("sync", SML[:, 0:8], AP(tensor=log_dt.tensor, offset=0, ap=[[1, 128], [128, 8]]))
        fw.dma("sync", SML[:, 16:32], AP(tensor=b_in_b.tensor, offset=0, ap=[[1, 128], [128, 16]]))
        fw.dma("sync", SML[0:1, 48:49], act_scale[:])
        fw.dma("sync", PRM[:, 1536:1792].rearrange("p (a b) -> p a b", a=8),
               AP(tensor=C_real.tensor, offset=0, ap=[[N2, 128], [128 * N2, 8], [1, N2]]))
        fw.dma("sync", PRM[:, 1792:2048].rearrange("p (a b) -> p a b", a=8),
               AP(tensor=C_imag.tensor, offset=0, ap=[[N2, 128], [128 * N2, 8], [1, N2]]))
        fw.dma("sync", BCB[0:1, 384:416], log_w_real[:])
        fw.dma("sync", BCB[0:1, 416:448], w_imag[:])
        fw.dma("sync", DROW[0:1, 0:1024], D_a[:])
        nconst = fw.dn["sync"]
        nconst2 = nconst

        def c_init(v):
            v.wait_ge(fw.Dq["sync"], nconst)
            v.tensor_copy(out=IDENTF[:], in_=IDENT[:])
            v.memset(ONEC[:], 1.0)
            v.memset(SML[:, 52:53], 1.0)
            v.tensor_scalar(out=SML[:, 53:54], in0=SML[:, 52:53], scalar1=MAGIC64,
                            scalar2=-MAGIC64, op0=ALU.add, op1=ALU.add)
        VE(c_init)
        m_init = fw.marks()

        SCOPE("p1_ln1")
        NTILE = B2 * NCH
        sx = [st.enter_context(nc.semaphore(f"sx{i}")) for i in range(4)]

        def ph1(i):
            b, c = divmod(i, NCH)
            slot = i % 4
            xt = XT[:, slot * 1024:(slot + 1) * 1024]
            mk0 = fw.marks()

            def gp_ld(h):
                if i >= 4:
                    h.wait_ge(fw.F["vector"], ph1.done[i - 4])
                h.dma_start(out=xt, in_=x_in[b, c * 128:(c + 1) * 128, :]).then_inc(sx[slot], 16)
            fw._sect("gpsimd", gp_ld)

            def ve1(v):
                v.wait_ge(sx[slot], 16 * (i // 4 + 1))
                v.bn_stats(out=STT[:, 0:6], in_=xt[:, 0:512])
                v.bn_stats(out=STT[:, 6:12], in_=xt[:, 512:1024])
                v.bn_aggr(out=STT[:, 12:14], in_=STT[:, 0:12])
                v.tensor_copy(out=STM[:, i:i + 1], in_=STT[:, 12:13])
                v.tensor_scalar(out=STT[:, 14:15], in0=STT[:, 13:14], scalar1=EPS_LN,
                                scalar2=None, op0=ALU.add)
                v.reciprocal(out=STT[:, 15:16], in_=STT[:, 14:15])
            VE(ve1)
            mkv = fw.marks()

            def sc1(s):
                s.wait_ge(fw.F["vector"], mkv["vector"])
                s.activation(out=STM[:, 64 + i:64 + i + 1], in_=STT[:, 15:16], func=AF.Sqrt)
            SC(sc1)
            mks = fw.marks()

            def ve2(v):
                v.wait_ge(fw.F["scalar"], mks["scalar"])
                v.tensor_scalar(out=R1[:, i * 1024:(i + 1) * 1024], in0=xt,
                                scalar1=STM[:, i:i + 1], scalar2=STM[:, 64 + i:64 + i + 1],
                                op0=ALU.subtract, op1=ALU.mult)
            VE(ve2)
            ph1.done[i] = fw.marks()["vector"]
        ph1.done = {}
        for i in range(NTILE):
            ph1(i)
        fw.barrier()
        if PHASES < 2:
            return _finish(nc, fw, dbg_outs, dbg, R1, R2, R3, STM, PRM, BCB, AUX, hb_aim, hb_are, hb_cwr, hb_cwi, DBC)

        SCOPE("p2a_prep")
        SCF = AUX[:, 4096:8192].bitcast(F32)

        def dt3(base, a, bsz, astr, bstr):
            return AP(tensor=base.tensor, offset=base.offset,
                      ap=[list(base.ap[0]), [astr, a], [bstr, bsz]])

        def mm1(t):
            t.wait_ge(fw.Dq["sync"], nconst2)
            t.matmul(P0[:, 0:64], IOTAS[0:1, 384:512], BCB[0:1, 384:448], start=True, stop=True)
            t.matmul(P1[:], IOTAS[0:1, 384:512], DROW[0:1, 0:512], start=True, stop=True)
            t.matmul(P2[:], IOTAS[0:1, 384:512], DROW[0:1, 512:1024], start=True, stop=True)
            t.mark()
        TE(mm1)
        mk = fw.marks()

        def sc_dt(s):
            s.activation(out=SML[:, 8:16], in_=SML[:, 0:8], func=AF.Exp)
        SC(sc_dt)
        mk_dt = fw.marks()

        def ve_bc(v):
            v.wait_ge(fw.F["tensor"], mk["tensor"])
            v.tensor_copy(out=BCB[:, 0:32], in_=P0[:, 0:32])
            v.tensor_copy(out=BCB[:, 32:64], in_=P0[:, 32:64])
        VE(ve_bc)
        mk_bc = fw.marks()

        def sc_wre(s):
            s.wait_ge(fw.F["vector"], mk_bc["vector"])
            s.activation(out=BCB[:, 0:32], in_=BCB[:, 0:32], func=AF.Exp)
        SC(sc_wre)
        mk_wre = fw.marks()

        def ve_args(v):
            v.wait_ge(fw.F["scalar"], mk_wre["scalar"])
            dt_ap = dt3(SML[:, 8:16], 8, 32, 1, 0)
            wim_ap = dt3(BCB[:, 32:64], 8, 32, 0, 1)
            wre_ap = dt3(BCB[:, 0:32], 8, 32, 0, 1)
            v.tensor_tensor(out=PRM[:, 0:256].rearrange("p (a b) -> p a b", a=8),
                            in0=dt_ap, in1=wim_ap, op=ALU.mult)
            v.tensor_tensor(out=PRM[:, 256:512].rearrange("p (a b) -> p a b", a=8),
                            in0=dt_ap, in1=wre_ap, op=ALU.mult)
            v.tensor_scalar(out=PRM[:, 256:512], in0=PRM[:, 256:512], scalar1=-1.0,
                            scalar2=None, op0=ALU.mult)
        VE(ve_args)
        mk_args = fw.marks()

        def sc_rr1(s):
            s.wait_ge(fw.F["vector"], mk_args["vector"])
            s.activation(out=SCF[:, 0:256], in_=PRM[:, 0:256], func=AF.Copy,
                         scale=INV_2PI, bias=MAGIC)
            s.activation(out=SCF[:, 0:256], in_=SCF[:, 0:256], func=AF.Copy, bias=-MAGIC)
        SC(sc_rr1)
        mk_rr1 = fw.marks()

        def ve_rr1(v):
            v.wait_ge(fw.F["scalar"], mk_rr1["scalar"])
            v.tensor_scalar(out=SCF[:, 1024:1280], in0=PRM[:, 0:256], scalar1=INV_2PI,
                            scalar2=None, op0=ALU.mult)
            v.tensor_scalar(out=SCF[:, 1280:1536], in0=SCF[:, 1024:1280], scalar1=MAGIC64,
                            scalar2=-MAGIC64, op0=ALU.add, op1=ALU.add)
            v.tensor_tensor(out=SCF[:, 1280:1536], in0=SCF[:, 1280:1536], in1=SCF[:, 0:256],
                            op=ALU.subtract)
            v.scalar_tensor_tensor(out=SCF[:, 0:256], in0=SCF[:, 1280:1536], scalar=SML[:, 53:54],
                                   in1=SCF[:, 0:256], op0=ALU.mult, op1=ALU.add)
            v.tensor_tensor(out=SCF[:, 256:512], in0=SCF[:, 1024:1280], in1=SCF[:, 0:256],
                            op=ALU.subtract)
            v.tensor_scalar(out=SCF[:, 512:768], in0=SCF[:, 256:512], scalar1=0.25,
                            scalar2=None, op0=ALU.add)
            v.tensor_scalar(out=SCF[:, 768:1024], in0=SCF[:, 512:768], scalar1=0.5,
                            scalar2=None, op0=ALU.is_gt)
            v.tensor_tensor(out=SCF[:, 512:768], in0=SCF[:, 512:768], in1=SCF[:, 768:1024],
                            op=ALU.subtract)
        VE(ve_rr1)
        mk_ve1 = fw.marks()
        rr1_dbg = dbg("rr1_dbg", [128, 1536], F32)
        if rr1_dbg is not None:
            fw.dma("gpsimd", rr1_dbg[:], SCF[:, 0:1536], wait=[("vector", mk_ve1["vector"])])
        stt_dbg = dbg("stt_dbg", [128, 16], F32)
        if stt_dbg is not None:
            fw.dma("gpsimd", stt_dbg[:], STT[:], wait=[("vector", mk_ve1["vector"])])


        def sc_rr2(s):
            s.wait_ge(fw.F["vector"], mk_ve1["vector"])
            s.activation(out=SCF[:, 768:1024], in_=SCF[:, 256:512], func=AF.Sin, scale=TWO_PI)
            s.activation(out=SCF[:, 1024:1280], in_=SCF[:, 512:768], func=AF.Sin, scale=TWO_PI)
            s.activation(out=SCF[:, 1280:1536], in_=PRM[:, 256:512], func=AF.Exp)
        SC(sc_rr2)
        mk_rr2 = fw.marks()

        def ve_cw(v):
            v.wait_ge(fw.F["scalar"], mk_rr2["scalar"])
            v.tensor_tensor(out=PRM[:, 512:768], in0=SCF[:, 1280:1536], in1=SCF[:, 1024:1280], op=ALU.mult)
            v.tensor_tensor(out=PRM[:, 768:1024], in0=SCF[:, 1280:1536], in1=SCF[:, 768:1024], op=ALU.mult)
            v.tensor_tensor(out=BCB[:, 64:96], in0=BCB[:, 0:32], in1=BCB[:, 0:32], op=ALU.mult)
            v.tensor_tensor(out=BCB[:, 96:128], in0=BCB[:, 32:64], in1=BCB[:, 32:64], op=ALU.mult)
            v.tensor_tensor(out=BCB[:, 64:96], in0=BCB[:, 64:96], in1=BCB[:, 96:128], op=ALU.add)
            v.reciprocal(out=BCB[:, 192:224], in_=BCB[:, 64:96])
            v.tensor_scalar(out=SCF[:, 1536:1792], in0=PRM[:, 512:768], scalar1=-1.0, scalar2=None, op0=ALU.add)
            v.tensor_scalar(out=BCB[:, 224:256], in0=BCB[:, 0:32], scalar1=-1.0, scalar2=None, op0=ALU.mult)
            v.tensor_scalar(out=BCB[:, 256:288], in0=BCB[:, 32:64], scalar1=-1.0, scalar2=None, op0=ALU.mult)
            v.tensor_tensor(out=BCB[:, 128:160], in0=BCB[:, 224:256], in1=BCB[:, 192:224], op=ALU.mult)
            v.tensor_tensor(out=BCB[:, 160:192], in0=BCB[:, 256:288], in1=BCB[:, 192:224], op=ALU.mult)
            ir = dt3(BCB[:, 128:160], 8, 32, 0, 1)
            ii = dt3(BCB[:, 160:192], 8, 32, 0, 1)
            nre3 = SCF[:, 1536:1792].rearrange("p (a b) -> p a b", a=8)
            nim3 = PRM[:, 768:1024].rearrange("p (a b) -> p a b", a=8)
            v.tensor_tensor(out=SCF[:, 0:256].rearrange("p (a b) -> p a b", a=8), in0=nre3, in1=ir, op=ALU.mult)
            v.tensor_tensor(out=SCF[:, 256:512].rearrange("p (a b) -> p a b", a=8), in0=nim3, in1=ii, op=ALU.mult)
            v.tensor_tensor(out=SCF[:, 0:256], in0=SCF[:, 0:256], in1=SCF[:, 256:512], op=ALU.subtract)
            v.tensor_tensor(out=SCF[:, 256:512].rearrange("p (a b) -> p a b", a=8), in0=nre3, in1=ii, op=ALU.mult)
            v.tensor_tensor(out=SCF[:, 512:768].rearrange("p (a b) -> p a b", a=8), in0=nim3, in1=ir, op=ALU.mult)
            v.tensor_tensor(out=SCF[:, 256:512], in0=SCF[:, 256:512], in1=SCF[:, 512:768], op=ALU.add)
            v.tensor_tensor(out=SCF[:, 512:768], in0=PRM[:, 1536:1792], in1=SCF[:, 0:256], op=ALU.mult)
            v.tensor_tensor(out=SCF[:, 768:1024], in0=PRM[:, 1792:2048], in1=SCF[:, 256:512], op=ALU.mult)
            v.tensor_tensor(out=SCF[:, 512:768], in0=SCF[:, 512:768], in1=SCF[:, 768:1024], op=ALU.subtract)
            v.tensor_scalar(out=SCF[:, 1536:1792], in0=SCF[:, 512:768], scalar1=2.0, scalar2=None, op0=ALU.mult)
            v.tensor_tensor(out=SCF[:, 512:768], in0=PRM[:, 1536:1792], in1=SCF[:, 256:512], op=ALU.mult)
            v.tensor_tensor(out=SCF[:, 768:1024], in0=PRM[:, 1792:2048], in1=SCF[:, 0:256], op=ALU.mult)
            v.tensor_tensor(out=SCF[:, 512:768], in0=SCF[:, 512:768], in1=SCF[:, 768:1024], op=ALU.add)
            v.tensor_scalar(out=SCF[:, 1792:2048], in0=SCF[:, 512:768], scalar1=2.0, scalar2=None, op0=ALU.mult)
            v.tensor_copy(out=PRM[:, 1024:1280], in_=PRM[:, 512:768])
            v.tensor_copy(out=PRM[:, 1280:1536], in_=PRM[:, 768:1024])
            for _ in range(7):
                v.tensor_tensor(out=SCF[:, 0:256], in0=PRM[:, 1024:1280], in1=PRM[:, 1024:1280], op=ALU.mult)
                v.tensor_tensor(out=SCF[:, 256:512], in0=PRM[:, 1280:1536], in1=PRM[:, 1280:1536], op=ALU.mult)
                v.tensor_tensor(out=SCF[:, 512:768], in0=PRM[:, 1024:1280], in1=PRM[:, 1280:1536], op=ALU.mult)
                v.tensor_tensor(out=PRM[:, 1024:1280], in0=SCF[:, 0:256], in1=SCF[:, 256:512], op=ALU.subtract)
                v.tensor_scalar(out=PRM[:, 1280:1536], in0=SCF[:, 512:768], scalar1=2.0, scalar2=None, op0=ALU.mult)
        VE(ve_cw)
        mk_cw = fw.marks()

        def ve_dbc(v):
            v.tensor_copy(out=DBC[:, 0:512], in_=P1[:])
            v.tensor_copy(out=DBC[:, 512:1024], in_=P2[:])
        VE(ve_dbc)

        def rows_dst(t):
            return AP(tensor=t.tensor, offset=0, ap=[[32, 128], [4096, 8], [1, 32]])
        mnow = fw.marks()
        fw.dma("gpsimd", rows_dst(hb_aim), PRM[:, 0:256].rearrange("p (a b) -> p a b", a=8),
               wait=[("vector", mnow["vector"])])
        fw.dma("gpsimd", rows_dst(hb_are), PRM[:, 256:512].rearrange("p (a b) -> p a b", a=8))
        fw.dma("gpsimd", rows_dst(hb_cwr), SCF[:, 1536:1792].rearrange("p (a b) -> p a b", a=8))
        fw.dma("gpsimd", rows_dst(hb_cwi), SCF[:, 1792:2048].rearrange("p (a b) -> p a b", a=8))
        fw.barrier()
        if PHASES < 3:
            return _finish(nc, fw, dbg_outs, dbg, R1, R2, R3, STM, PRM, BCB, AUX, hb_aim, hb_are, hb_cwr, hb_cwi, DBC)

        TBL = AUX[:, 0:1024]
        TBB = AUX[:, 1024:2048]
        SINT = AUX[:, 2048:2560]
        COST = AUX[:, 2560:3072]
        EXPT = AUX[:, 3072:3584]
        RAWS = AUX[:, 3584:4096]
        TMP2B = AUX[:, 15360:16384]
        TGA = AUX[0:64, 16384:18432]
        TGB = AUX[0:64, 18432:20480]
        BSLAB = R3[0:64, 0:4096]
        SSLAB = R3[0:64, 4096:8192]
        K8A = R3[:, 8192:9216]
        K8B = R3[:, 9216:10240]
        GOSL = R3[:, 10240:14336]
        SSTa = R3[:, 14336:14592].bitcast(F32)
        SSTb = R3[:, 14592:14848].bitcast(F32)
        TMPQ = R3[:, 14848:15360].bitcast(F32)
        AIMSEG = DROW[0:1, 0:512]
        ARESEG = SCF[0:1, 1536:2048]
        CW2R = BCB[:, 0:512]
        CW2I = BCB[:, 512:1024]

        def u_cols(h):
            return AP(tensor=R1[:].tensor, offset=h, ap=[[32768, 128], [NCH * 1024, B2], [1024, NCH]])

        def u_grp(h0):
            return AP(tensor=R1[:].tensor, offset=h0, ap=[[32768, 128], [1, 16], [NCH * 1024, B2], [1024, NCH]])

        def dbc_grp(h0):
            return AP(tensor=DBC[:].tensor, offset=DBC[:].offset + h0,
                      ap=[[1024, 128], [1, 16], [0, B2], [0, NCH]])

        def chain(pim, pexp, mk_args_t):
            def s1(s):
                s.wait_ge(fw.F["tensor"], mk_args_t)
                s.activation(out=SCF[:, 0:512], in_=pim, func=AF.Copy, scale=INV_2PI, bias=MAGIC)
                s.activation(out=SCF[:, 0:512], in_=SCF[:, 0:512], func=AF.Copy, bias=-MAGIC)
            SC(s1)
            m1 = fw.marks()

            def v1(v):
                v.wait_ge(fw.F["scalar"], m1["scalar"])
                v.tensor_scalar(out=SCF[:, 512:1024], in0=pim, scalar1=INV_2PI,
                                scalar2=None, op0=ALU.mult)
                v.tensor_scalar(out=SCF[:, 1024:1536], in0=SCF[:, 512:1024], scalar1=MAGIC64,
                                scalar2=-MAGIC64, op0=ALU.add, op1=ALU.add)
                v.tensor_tensor(out=SCF[:, 1024:1536], in0=SCF[:, 1024:1536], in1=SCF[:, 0:512],
                                op=ALU.subtract)
                v.scalar_tensor_tensor(out=SCF[:, 0:512], in0=SCF[:, 1024:1536], scalar=SML[:, 53:54],
                                       in1=SCF[:, 0:512], op0=ALU.mult, op1=ALU.add)
                v.tensor_tensor(out=SCF[:, 512:1024], in0=SCF[:, 512:1024], in1=SCF[:, 0:512],
                                op=ALU.subtract)
                v.tensor_scalar(out=SCF[:, 1024:1536], in0=SCF[:, 512:1024], scalar1=0.25,
                                scalar2=None, op0=ALU.add)
                v.tensor_scalar(out=SCF[:, 0:512], in0=SCF[:, 1024:1536], scalar1=0.5,
                                scalar2=None, op0=ALU.is_gt)
                v.tensor_tensor(out=SCF[:, 1024:1536], in0=SCF[:, 1024:1536], in1=SCF[:, 0:512],
                                op=ALU.subtract)
            VE(v1)
            m2 = fw.marks()

            def s2(s):
                s.wait_ge(fw.F["vector"], m2["vector"])
                s.activation(out=SINT[:], in_=SCF[:, 512:1024], func=AF.Sin, scale=TWO_PI)
                s.activation(out=COST[:], in_=SCF[:, 1024:1536], func=AF.Sin, scale=TWO_PI)
                s.activation(out=EXPT[:], in_=pexp, func=AF.Exp)
            SC(s2)
            return fw.marks()

        def bp_family(mk_sc):
            def v(v):
                v.wait_ge(fw.F["scalar"], mk_sc["scalar"])
                v.tensor_tensor(out=SCF[:, 0:512], in0=EXPT[:], in1=COST[:], op=ALU.mult)
                v.tensor_tensor(out=SCF[:, 512:1024], in0=EXPT[:], in1=SINT[:], op=ALU.mult)
                v.tensor_tensor(out=SCF[:, 1024:1536], in0=CW2R, in1=SCF[:, 0:512], op=ALU.mult)
                v.tensor_tensor(out=TMP2B[:].bitcast(F32), in0=CW2I, in1=SCF[:, 512:1024], op=ALU.mult)
                v.tensor_tensor(out=TBB[:, 0:1024:2], in0=SCF[:, 1024:1536], in1=TMP2B[:].bitcast(F32), op=ALU.subtract)
                v.tensor_tensor(out=SCF[:, 1024:1536], in0=CW2R, in1=SCF[:, 512:1024], op=ALU.mult)
                v.tensor_tensor(out=TMP2B[:].bitcast(F32), in0=CW2I, in1=SCF[:, 0:512], op=ALU.mult)
                v.tensor_tensor(out=SCF[:, 1024:1536], in0=SCF[:, 1024:1536], in1=TMP2B[:].bitcast(F32), op=ALU.add)
                v.tensor_scalar(out=TBB[:, 1:1024:2], in0=SCF[:, 1024:1536], scalar1=-1.0,
                                scalar2=None, op0=ALU.mult)
            VE(v)
            return fw.marks()

        raws_st = {"n": 0}
        SCOPE("p2b_slabs")
        for ho in range(SLABS):
            if ho == 0:
                SCOPE("sw1_0")
            elif ho == 1:
                SCOPE("slabs_rest")
            for g in range(8):
                h0 = ho * 128 + g * 16
                seg = h0 * N2
                mv = fw.marks()
                fw.dma("sync", AIMSEG, hb_aim[0:1, seg:seg + 512], wait=[("vector", mv["vector"]), ("tensor", mv["tensor"]), ("scalar", mv["scalar"])])
                fw.dma("sync", ARESEG, hb_are[0:1, seg:seg + 512])
                fw.dma("sync", CW2R, AP(tensor=hb_cwr.tensor, offset=seg, ap=[[0, 128], [1, 512]]))
                fw.dma("sync", CW2I, AP(tensor=hb_cwi.tensor, offset=seg, ap=[[0, 128], [1, 512]]))
                ndma = fw.dn["sync"]

                def args1(t, ndma=ndma, mv=mv):
                    t.wait_ge(fw.Dq["sync"], ndma)
                    t.wait_ge(fw.F["vector"], mv["vector"])
                    t.matmul(P0[:], IOTAS[0:1, 0:128], AIMSEG, start=True, stop=True)
                    t.matmul(P2[:], IOTAS[0:1, 0:128], ARESEG, start=True, stop=True)
                    t.matmul(P1[:], IOTAS[0:1, 128:256], AIMSEG, start=True, stop=True)
                    t.matmul(P3[:], IOTAS[0:1, 128:256], ARESEG, start=True, stop=True)
                    t.mark()
                TE(args1)
                mk_a = fw.marks()["tensor"]
                mk_f1 = chain(P0[:], P2[:], mk_a)

                def f1w(v, mk_f1=mk_f1):
                    v.wait_ge(fw.F["scalar"], mk_f1["scalar"])
                    v.tensor_tensor(out=TBL[:, 0:1024:2], in0=EXPT[:], in1=COST[:], op=ALU.mult)
                    v.tensor_tensor(out=TBL[:, 1:1024:2], in0=EXPT[:], in1=SINT[:], op=ALU.mult)
                VE(f1w)
                mk_bp = chain(P1[:], P3[:], mk_a)
                bp_family(mk_bp)
                mtab = fw.marks()

                def tr1(t, mtab=mtab):
                    t.wait_ge(fw.F["vector"], mtab["vector"])
                    for k in range(8):
                        t.transpose(PB0[0:64, k * 128:(k + 1) * 128], TBL[:, k * 64:(k + 1) * 64], IDENT[:])
                    t.mark()
                    for k in range(8):
                        t.transpose(PB1[0:64, k * 128:(k + 1) * 128], TBL[:, 512 + k * 64:512 + (k + 1) * 64], IDENT[:])
                    t.mark()
                TE(tr1)
                mtr1 = fw.marks()["tensor"]

                def evga(v, mtr1=mtr1):
                    v.wait_ge(fw.F["tensor"], mtr1)
                    v.tensor_copy(out=TGA[:, 0:1024], in_=PB0[0:64, :])
                    v.tensor_copy(out=TGA[:, 1024:2048], in_=PB1[0:64, :])
                VE(evga)
                mevga = fw.marks()

                def tr2(t, mevga=mevga):
                    t.wait_ge(fw.F["vector"], mevga["vector"])
                    for k in range(8):
                        t.transpose(PB0[0:64, k * 128:(k + 1) * 128], TBB[:, k * 64:(k + 1) * 64], IDENT[:])
                    t.mark()
                    for k in range(8):
                        t.transpose(PB1[0:64, k * 128:(k + 1) * 128], TBB[:, 512 + k * 64:512 + (k + 1) * 64], IDENT[:])
                    t.mark()
                TE(tr2)
                mtr2 = fw.marks()["tensor"]

                def evgb(v, mtr2=mtr2):
                    v.wait_ge(fw.F["tensor"], mtr2)
                    v.tensor_copy(out=TGB[:, 0:1024], in_=PB0[0:64, :])
                    v.tensor_copy(out=TGB[:, 1024:2048], in_=PB1[0:64, :])
                VE(evgb)
                mevgb = fw.marks()

                def kmm(t, mevgb=mevgb):
                    t.wait_ge(fw.F["vector"], mevgb["vector"])
                    for k in range(16):
                        pb = PW[k // 4]
                        t.matmul(pb[:, (k % 4) * 128:(k % 4 + 1) * 128],
                                 TGA[:, k * 128:(k + 1) * 128], TGB[:, k * 128:(k + 1) * 128],
                                 start=True, stop=True)
                    t.mark()
                TE(kmm)
                mkmm = fw.marks()["tensor"]

                def kev(v, mkmm=mkmm):
                    v.wait_ge(fw.F["tensor"], mkmm)
                    for q, pb in enumerate(PW):
                        mask4 = AP(tensor=MASKT[:].tensor, offset=MASKT[:].offset,
                                   ap=[[128, 128], [0, 4], [1, 128]])
                        dst = (K8A, K8B)[q // 2]
                        v.tensor_tensor(out=dst[:, (q % 2) * 512:(q % 2 + 1) * 512].rearrange("p (a b) -> p a b", a=4),
                                        in0=pb[:].rearrange("p (a b) -> p a b", a=4), in1=mask4, op=ALU.mult)
                VE(kev)
                mkev = fw.marks()

                def gimm(t, mkev=mkev, h0=h0):
                    t.wait_ge(fw.F["vector"], mkev["vector"])
                    for k in range(16):
                        t.matmul(P4[0:64, k * 32:(k + 1) * 32], TBL[:, k * 64:(k + 1) * 64],
                                 u_cols(h0 + k), start=True, stop=True)
                    t.mark()
                    for k in range(16):
                        kt = (K8A, K8B)[k // 8]
                        t.matmul(P5[:, k * 32:(k + 1) * 32], kt[:, (k % 8) * 128:(k % 8 + 1) * 128],
                                 u_cols(h0 + k), start=True, stop=True)
                    t.mark()
                TE(gimm)
                mgi = fw.marks()["tensor"]

                def bdev(v, mgi=mgi, h0=h0, prevst=raws_st["n"]):
                    v.wait_ge(fw.F["tensor"], mgi)
                    if prevst:
                        v.wait_ge(fw.Dq["gpsimd"], prevst)
                    h0l = h0 % 128
                    dstB = AP(tensor=BSLAB.tensor, offset=BSLAB.offset + h0l * 2,
                              ap=[[16384, 64], [2, 16], [1, 2], [256, 16]])
                    v.tensor_copy(out=dstB, in_=P4[0:64, 0:512].rearrange("p (a b c) -> p a b c", a=16, b=2))
                    v.tensor_tensor(out=SINT[:], in0=u_grp(h0), in1=dbc_grp(h0), op=ALU.mult)
                    v.tensor_tensor(out=RAWS[:], in0=SINT[:], in1=P5[:], op=ALU.add)
                VE(bdev)
                mbd = fw.marks()
                fw.dma("gpsimd", hb_slab[ho, :, g * 512:(g + 1) * 512], RAWS[:],
                       wait=[("vector", mbd["vector"])])
                raws_st["n"] = fw.dn["gpsimd"]

            if ho == 0:
                SCOPE("rec_0")
            mk_sw1 = fw.marks()
            ngp1 = fw.dn["gpsimd"]

            def rinit(v):
                v.memset(SSTa, 0.0)
            VE(rinit)
            lt_re = PRM[:, 1024 + ho * 32:1024 + (ho + 1) * 32]
            lt_im = PRM[:, 1280 + ho * 32:1280 + (ho + 1) * 32]

            def lt_ap(base):
                return AP(tensor=base.tensor, offset=base.offset,
                          ap=[[base.ap[0][0], 128], [0, 2], [1, 32]])

            def sview(tns, pm):
                return AP(tensor=tns.tensor, offset=tns.offset + pm, ap=[[tns.ap[0][0], 128], [64, 2], [2, 32]])

            for c in range(NCH):
                cur, nxt = (SSTa, SSTb) if c % 2 == 0 else (SSTb, SSTa)
                mvv = fw.marks()

                def rtr(t, c=c, cur=cur, mvv=mvv):
                    t.wait_ge(fw.F["vector"], mvv["vector"])
                    for b in range(B2):
                        t.transpose(P4[0:64, b * 128:(b + 1) * 128], cur[:, b * 64:(b + 1) * 64], IDENTF[:])
                    for b in range(B2):
                        src = AP(tensor=BSLAB.tensor, offset=BSLAB.offset + c * 256 + b,
                                 ap=[[16384, 64], [2, 128]])
                        t.transpose(PB0[:, b * 64:(b + 1) * 64], src, IDENT[0:64, 0:64])
                    t.mark()
                TE(rtr)
                mrt = fw.marks()["tensor"]

                def rup(v, c=c, cur=cur, nxt=nxt, mrt=mrt, lt_re=lt_re, lt_im=lt_im):
                    v.wait_ge(fw.F["tensor"], mrt)
                    dstS = AP(tensor=SSLAB.tensor, offset=SSLAB.offset + c * 256,
                              ap=[[16384, 64], [1, 2], [2, 128]])
                    srcS = AP(tensor=P4[:].tensor, offset=P4[:].offset,
                              ap=[[512, 64], [128, 2], [1, 128]])
                    v.tensor_copy(out=dstS, in_=srcS)
                    pb_re = AP(tensor=PB0[:].tensor, offset=PB0[:].offset, ap=[[1024, 128], [64, 2], [2, 32]])
                    pb_im = AP(tensor=PB0[:].tensor, offset=PB0[:].offset + 1, ap=[[1024, 128], [64, 2], [2, 32]])
                    v.tensor_tensor(out=TMPQ[:, 0:64].rearrange("p (a b) -> p a b", a=2), in0=sview(cur, 0), in1=lt_ap(lt_re), op=ALU.mult)
                    v.tensor_tensor(out=TMPQ[:, 64:128].rearrange("p (a b) -> p a b", a=2), in0=sview(cur, 1), in1=lt_ap(lt_im), op=ALU.mult)
                    v.tensor_tensor(out=TMPQ[:, 128:192], in0=TMPQ[:, 0:64], in1=TMPQ[:, 64:128], op=ALU.subtract)
                    v.tensor_tensor(out=sview(nxt, 0), in0=TMPQ[:, 128:192].rearrange("p (a b) -> p a b", a=2), in1=pb_re, op=ALU.add)
                    v.tensor_tensor(out=TMPQ[:, 0:64].rearrange("p (a b) -> p a b", a=2), in0=sview(cur, 1), in1=lt_ap(lt_re), op=ALU.mult)
                    v.tensor_tensor(out=TMPQ[:, 64:128].rearrange("p (a b) -> p a b", a=2), in0=sview(cur, 0), in1=lt_ap(lt_im), op=ALU.mult)
                    v.tensor_tensor(out=TMPQ[:, 128:192], in0=TMPQ[:, 0:64], in1=TMPQ[:, 64:128], op=ALU.add)
                    v.tensor_tensor(out=sview(nxt, 1), in0=TMPQ[:, 128:192].rearrange("p (a b) -> p a b", a=2), in1=pb_im, op=ALU.add)
                VE(rup)

            if ho == 0:
                SCOPE("sw2_0")
            for g in range(8):
                h0 = ho * 128 + g * 16
                seg = h0 * N2
                mv = fw.marks()
                fw.dma("sync", AIMSEG, hb_aim[0:1, seg:seg + 512], wait=[("vector", mv["vector"]), ("tensor", mv["tensor"]), ("scalar", mv["scalar"])])
                fw.dma("sync", ARESEG, hb_are[0:1, seg:seg + 512])
                fw.dma("sync", CW2R, AP(tensor=hb_cwr.tensor, offset=seg, ap=[[0, 128], [1, 512]]))
                fw.dma("sync", CW2I, AP(tensor=hb_cwi.tensor, offset=seg, ap=[[0, 128], [1, 512]]))
                ndma = fw.dn["sync"]
                fw.dma("gpsimd", RAWS[:], hb_slab[ho, :, g * 512:(g + 1) * 512],
                       wait=[("vector", mv["vector"]), ("q:gpsimd", raws_st["n"])])
                ngp = fw.dn["gpsimd"]

                def args2(t, ndma=ndma, mv=mv):
                    t.wait_ge(fw.Dq["sync"], ndma)
                    t.wait_ge(fw.F["vector"], mv["vector"])
                    t.matmul(P0[:], IOTAS[0:1, 256:384], AIMSEG, start=True, stop=True)
                    t.matmul(P1[:], IOTAS[0:1, 256:384], ARESEG, start=True, stop=True)
                    t.mark()
                TE(args2)
                mk_a2 = fw.marks()["tensor"]
                mk_p = chain(P0[:], P1[:], mk_a2)
                bp_family(mk_p)
                mtab2 = fw.marks()

                def tr3(t, mtab2=mtab2):
                    t.wait_ge(fw.F["vector"], mtab2["vector"])
                    for k in range(8):
                        t.transpose(PB0[0:64, k * 128:(k + 1) * 128], TBB[:, k * 64:(k + 1) * 64], IDENT[:])
                    t.mark()
                    for k in range(8):
                        t.transpose(PB1[0:64, k * 128:(k + 1) * 128], TBB[:, 512 + k * 64:512 + (k + 1) * 64], IDENT[:])
                    t.mark()
                TE(tr3)
                mtr3 = fw.marks()["tensor"]

                def evp(v, mtr3=mtr3):
                    v.wait_ge(fw.F["tensor"], mtr3)
                    v.tensor_copy(out=TGB[:, 0:1024], in_=PB0[0:64, :])
                    v.tensor_copy(out=TGB[:, 1024:2048], in_=PB1[0:64, :])
                VE(evp)
                mevp = fw.marks()

                def scmm(t, mevp=mevp, h0=h0, ngp=ngp):
                    t.wait_ge(fw.F["vector"], mevp["vector"])
                    t.wait_ge(fw.Dq["gpsimd"], ngp)
                    h0l = h0 % 128
                    for k in range(16):
                        sap = AP(tensor=SSLAB.tensor, offset=SSLAB.offset + (h0l + k) * 2,
                                 ap=[[16384, 64], [1, 2], [256, 16]])
                        t.matmul(P5[:, k * 32:(k + 1) * 32], TGB[:, k * 128:(k + 1) * 128],
                                 sap, start=True, stop=True)
                    t.mark()
                TE(scmm)
                msc = fw.marks()["tensor"]

                def gadd(v, msc=msc):
                    v.wait_ge(fw.F["tensor"], msc)
                    v.tensor_tensor(out=TMP2B[:, 0:512], in0=RAWS[:], in1=P5[:], op=ALU.add)
                VE(gadd)
                mga = fw.marks()

                def gel(s, mga=mga, h0=h0):
                    s.wait_ge(fw.F["vector"], mga["vector"])
                    h0l = h0 % 128
                    dstG = AP(tensor=GOSL.tensor, offset=GOSL.offset + h0l,
                              ap=[[16384, 128], [1, 16], [2048, 2], [128, 16]])
                    s.activation(out=dstG, in_=TMP2B[:, 0:512].rearrange("p (a b c) -> p a b c", a=16, b=2),
                                 func=AF.Gelu)
                SC(gel)

            if ho == 0:
                SCOPE("fmaj_0")
            mgel = fw.marks()
            fev_marks = {}
            for half in range(4):
                b_, ch_ = divmod(half, 2)
                pbuf = PB0 if half % 2 == 0 else PB1

                def ftr(t, b_=b_, ch_=ch_, pbuf=pbuf, half=half, mgel=mgel, fev_marks=fev_marks):
                    if half == 0:
                        t.wait_ge(fw.F["scalar"], mgel["scalar"])
                    if half >= 2:
                        t.wait_ge(fw.F["vector"], fev_marks[half - 2])
                    for k in range(8):
                        cc = ch_ * 8 + k
                        t.transpose(pbuf[:, k * 128:(k + 1) * 128],
                                    GOSL[:, b_ * 2048 + cc * 128:b_ * 2048 + (cc + 1) * 128], IDENT[:])
                    t.mark()
                TE(ftr)
                mtf = fw.marks()["tensor"]

                def fev(v, b_=b_, ch_=ch_, pbuf=pbuf, ho=ho, mtf=mtf):
                    v.wait_ge(fw.F["tensor"], mtf)
                    base = (b_ * 8 + ho) * 2048 + ch_ * 1024
                    v.tensor_copy(out=R2[:, base:base + 1024], in_=pbuf[:])
                VE(fev)
                fev_marks[half] = fw.marks()["vector"]
        fw.barrier()
        if PHASES < 4:
            return _finish(nc, fw, dbg_outs, dbg, R1, R2, R3, STM, PRM, BCB, AUX, hb_aim, hb_are, hb_cwr, hb_cwi, DBC)

        _ctx = dict(locals())
        _ctx["PHASES"] = PHASES
        _part2_emit(_ctx)
        SCOPE(None)
        fw.barrier()

        u_dbg = dbg("u_dbg", [128, 32768], BF16)
        if u_dbg is not None:
            fw.dma("gpsimd", u_dbg[:], R1[:])
        m_dbg = dbg("m_dbg", [128, 96], F32)
        if m_dbg is not None:
            fw.dma("gpsimd", m_dbg[:], STM[:])
        prep_dbg = dbg("prep_dbg", [128, 2048], F32)
        if prep_dbg is not None:
            fw.dma("gpsimd", prep_dbg[:], PRM[:])
        convf_dbg = dbg("convf_dbg", [128, 32768], BF16)
        if convf_dbg is not None:
            fw.dma("gpsimd", convf_dbg[:], R2[:])
        x1_dbg = dbg("x1_dbg", [128, 32768], BF16)
        if x1_dbg is not None:
            fw.dma("gpsimd", x1_dbg[:], R2[:])
        x2_dbg = dbg("x2_dbg", [128, 32768], BF16)
        if x2_dbg is not None:
            fw.dma("gpsimd", x2_dbg[:], R2[:])
        xz_dbg = dbg("xz_dbg", [128, 32768], BF16)
        if xz_dbg is not None:
            fw.dma("gpsimd", xz_dbg[:], R1[:])
        ys_dbg = dbg("ys_dbg", [128, 16384], BF16)
        if ys_dbg is not None:
            fw.dma("gpsimd", ys_dbg[:], R3[:])
        x3_dbg = dbg("x3_dbg", [128, 32768], BF16)
        if x3_dbg is not None:
            fw.dma("gpsimd", x3_dbg[:], R2[:])
        for nm, src, shp, dt_ in (
                ("scan_dbg", AUX[:, 0:4096].bitcast(F32), [128, 2048], F32),
                ("cbc2_dbg", AUX[:, 14336:14464].bitcast(F32), [128, 64], F32),
                ("hst_dbg", AUX[:, 4096:8192].bitcast(F32), [128, 2048], F32)):
            tp = dbg(nm, shp, dt_)
            if tp is not None:
                fw.dma("gpsimd", tp[:], src)
        fw.barrier()

    return nc, dbg_outs


def _finish(nc, fw, dbg_outs, dbg, R1, R2, R3, STM, PRM, BCB, AUX, hb_aim, hb_are, hb_cwr, hb_cwi, DBC=None):
    taps = {
        "u_dbg": (R1, [128, 32768], BF16), "m_dbg": (STM, [128, 96], F32),
        "prep_dbg": (PRM, [128, 2048], F32), "convf_dbg": (R2, [128, 32768], BF16),
        "k8_dbg": (R3[:, 8192:10240], [128, 2048], BF16),
        "bslab_dbg": (R3[0:64, 0:4096], [64, 4096], BF16),
        "sslab_dbg": (R3[0:64, 4096:8192], [64, 4096], BF16),
        "gosl_dbg": (R3[:, 10240:14336], [128, 4096], BF16),
        "dbc_dbg": (DBC, [128, 1024], BF16),
        "tbl_dbg": (AUX[:, 0:2048], [128, 2048], BF16),
        "tga_dbg": (AUX[0:64, 16384:20480], [64, 4096], BF16),
        "scf_dbg": (AUX[:, 4096:8192].bitcast(F32), [128, 2048], F32),
        "trig_dbg": (AUX[:, 2048:4096], [128, 2048], BF16),
        "scan_dbg": (AUX[:, 0:4096].bitcast(F32), [128, 2048], F32),
        "cbc2_dbg": (AUX[:, 14336:14464].bitcast(F32), [128, 64], F32),
        "hst_dbg": (AUX[:, 4096:8192].bitcast(F32), [128, 2048], F32),
    }
    for name, (tns, shape, dt) in taps.items():
        t = dbg(name, shape, dt)
        if t is not None:
            fw.dma("gpsimd", t[:], tns if isinstance(tns, AP) else tns[:])
    rows_dbg = dbg("rows_dbg", [4, 32768], F32)
    if rows_dbg is not None:
        fw.dma("gpsimd", rows_dbg[0:1, :], hb_aim[:])
        fw.dma("gpsimd", rows_dbg[1:2, :], hb_are[:])
        fw.dma("gpsimd", rows_dbg[2:3, :], hb_cwr[:])
        fw.dma("gpsimd", rows_dbg[3:4, :], hb_cwi[:])
    fw.barrier()
    return nc, dbg_outs


def make_consts():
    ident = np.eye(128, dtype=ml_dtypes.bfloat16)
    iota_negj = (-np.arange(128, dtype=np.float32))[None, :]
    iota_t = np.arange(128, dtype=np.float32)[None, :]
    iota_t128 = (np.arange(128, dtype=np.float32) + 128.0)[None, :]
    j = np.arange(128)[:, None]; t = np.arange(128)[None, :]
    mask_triu = (t >= j).astype(ml_dtypes.bfloat16)
    ones_row = np.ones((1, 128), np.float32)
    return dict(identb=ident, iota_negj=iota_negj, iota_t=iota_t, iota_t128=iota_t128,
                mask_triu=mask_triu, ones_row=ones_row)


def core_inputs(inputs, core):
    d = {}
    d["x_in"] = np.ascontiguousarray(inputs["L0_attn_a_u"][2 * core:2 * core + 2])
    d["log_dt"] = np.ascontiguousarray(inputs["L0_attn_a_log_dt"]).reshape(1, D)
    d["log_w_real"] = np.ascontiguousarray(inputs["L0_attn_a_log_w_real"]).reshape(1, N2)
    d["w_imag"] = np.ascontiguousarray(inputs["L0_attn_a_w_imag"]).reshape(1, N2)
    d["C_real"] = np.ascontiguousarray(inputs["L0_attn_a_C_real"]).reshape(D, N2)
    d["C_imag"] = np.ascontiguousarray(inputs["L0_attn_a_C_imag"]).reshape(D, N2)
    d["D_a"] = np.ascontiguousarray(inputs["L0_attn_a_D"]).reshape(1, D)
    d["W_out_a"] = np.ascontiguousarray(inputs["L0_attn_a_W_out"]).reshape(D, D)
    d["b_out_a"] = np.ascontiguousarray(inputs["L0_attn_a_b_out"]).reshape(1, D)
    d["tern"] = np.ascontiguousarray(inputs["L0_ffn_ternary_weight"]).reshape(D, D)
    d["bias_f"] = np.ascontiguousarray(inputs["L0_ffn_bias"]).reshape(1, D)
    d["act_scale"] = np.asarray(inputs["L0_ffn_act_scale"], np.float32).reshape(1, 1)
    d["w_in_b"] = np.ascontiguousarray(inputs["L0_attn_b_w_in"]).reshape(2 * D, D)
    d["b_in_b"] = np.ascontiguousarray(inputs["L0_attn_b_b_in"]).reshape(1, 2 * D)
    d["A_b"] = np.ascontiguousarray(inputs["L0_attn_b_A"]).reshape(D, S64)
    d["Bm_b"] = np.ascontiguousarray(inputs["L0_attn_b_Bm"]).reshape(D, S64)
    d["Cm_b"] = np.ascontiguousarray(inputs["L0_attn_b_Cm"]).reshape(D, S64)
    d["Dv_b"] = np.ascontiguousarray(inputs["L0_attn_b_Dv"]).reshape(1, D)
    d["w_out_b"] = np.ascontiguousarray(inputs["L0_attn_b_w_out"]).reshape(D, D)
    d["b_out_b"] = np.ascontiguousarray(inputs["L0_attn_b_b_out"]).reshape(1, D)
    d["g_norm"] = np.ascontiguousarray(inputs["L0_attn_b_g_norm"]).reshape(1, D)
    d.update(make_consts())
    return d


def kernel(**inputs):
    nc, _ = build_program()
    ins = [core_inputs(inputs, i) for i in range(8)]
    res = run_bass_kernel_spmd(nc, ins, core_ids=list(range(8)))
    parts = [np.asarray(res.results[i]["out"]).reshape(B2, L, D) for i in range(8)]
    return np.concatenate(parts, axis=0).astype(np.float32)


# version 06
# speedup vs baseline: 1.1177x; 1.0086x over previous
import math
import numpy as np
from contextlib import ExitStack

import ml_dtypes
import concourse.bass as bass
import concourse.mybir as mybir
from concourse.bass import AP
from concourse.bass_utils import run_bass_kernel_spmd



ENGS = ("tensor", "vector", "scalar", "gpsimd", "sync")
COMPUTE_ENGS = ("tensor", "vector", "scalar", "gpsimd")
OPS = {
    "tensor_tensor", "tensor_scalar", "tensor_copy", "tensor_reduce", "reduce_sum",
    "reduce_max", "scalar_tensor_tensor", "tensor_tensor_reduce", "bn_stats",
    "bn_aggr", "reciprocal", "memset", "select", "copy_predicated", "iota",
    "activation", "copy", "add", "mul", "sign", "sqrt", "square",
    "matmul", "transpose", "ldweights", "tensor_tensor_scan", "stream_shuffle",
    "tensor_add", "tensor_mul", "tensor_sub", "tensor_max", "tensor_relu",
    "tensor_scalar_mul", "tensor_scalar_add", "partition_broadcast",
    "partition_all_reduce",
}


class Eng:

    def __init__(self, fw, name, handle):
        self._fw = fw
        self._name = name
        self._h = handle
        self.last = None

    def __getattr__(self, attr):
        real = getattr(self._h, attr)
        if attr not in OPS:
            return real
        fw, name = self._fw, self._name

        def call(*a, **k):
            inst = real(*a, **k)
            self.last = inst
            if name in ("vector", "scalar", "gpsimd"):
                nb = fw.n[name]
                if nb > 0:
                    inst._wait_ge(fw.F[name], nb)
                inst.then_inc(fw.F[name], 1)
                fw.n[name] += 1
            return inst
        return call

    def mark(self):
        assert self._name == "tensor"
        assert self.last is not None
        self.last.then_inc(self._fw.F["tensor"], 1)
        self._fw.n["tensor"] += 1
        self.last = None
        return self._fw.n["tensor"]


class FW:
    def __init__(self, nc, st, blk):
        self.nc = nc
        self.blk = blk
        self.F = {e: st.enter_context(nc.semaphore(f"F_{e}")) for e in COMPUTE_ENGS}
        self.n = {e: 0 for e in COMPUTE_ENGS}
        self.Dq = {q: st.enter_context(nc.semaphore(f"D_{q}")) for q in ("sync", "gpsimd")}
        self.dn = {q: 0 for q in ("sync", "gpsimd")}
        self._eng_cache = {}

    def _sect(self, name, body):
        getattr(self.blk, name)(body)

    def sec(self, name):
        raise NotImplementedError

    def run(self, name, fn):
        out = {}

        def body(h):
            out["p"] = Eng(self, name, h)
            fn(out["p"])
        self._sect(name, body)
        return out.get("p")

    def dma(self, qname, out_ap, in_ap, wait=()):
        def body(h):
            for (e, m) in wait:
                if m > 0:
                    if e.startswith("q:"):
                        h.wait_ge(self.Dq[e[2:]], m)
                    else:
                        h.wait_ge(self.F[e], m)
            h.dma_start(out=out_ap, in_=in_ap).then_inc(self.Dq[qname], 16)
        self._sect(qname, body)
        self.dn[qname] += 16
        return self.dn[qname]

    def wait(self, eng, deps=(), dmas=()):
        def body(h):
            for (e, m) in deps:
                if m > 0:
                    h.wait_ge(self.F[e], m)
            for (q, c) in dmas:
                if c > 0:
                    h.wait_ge(self.Dq[q], c)
        self._sect(eng, body)

    def marks(self):
        return dict(self.n)

    def barrier(self):
        snap_n = dict(self.n)
        snap_d = dict(self.dn)
        for e in COMPUTE_ENGS:
            def body(h, e=e):
                for o in COMPUTE_ENGS:
                    if o != e and snap_n[o] > 0:
                        h.wait_ge(self.F[o], snap_n[o])
                for q in ("sync", "gpsimd"):
                    if snap_d[q] > 0:
                        h.wait_ge(self.Dq[q], snap_d[q])
            self._sect(e, body)
        for q in ("sync", "gpsimd"):
            def body(h, q=q):
                for o in COMPUTE_ENGS:
                    if snap_n[o] > 0:
                        h.wait_ge(self.F[o], snap_n[o])
                oq = "gpsimd" if q == "sync" else "sync"
                if snap_d[oq] > 0:
                    h.wait_ge(self.Dq[oq], snap_d[oq])
            self._sect(q, body)


def _part2_emit(ctx):
    fw = ctx["fw"]; nc = ctx["nc"]
    VE, SC, TE, GP = ctx["VE"], ctx["SC"], ctx["TE"], ctx["GP"]
    R1, R2, R3, AUX = ctx["R1"], ctx["R2"], ctx["R3"], ctx["AUX"]
    IDENT, IDENTF, IOTAS, ONEC = ctx["IDENT"], ctx["IDENTF"], ctx["IOTAS"], ctx["ONEC"]
    BOB, SML, STT = ctx["BOB"], ctx["SML"], ctx["STT"]
    ASC, BSC, CBC, HST, ZST, GST, T1S, HCS = (ctx["ASC"], ctx["BSC"], ctx["CBC"],
                                              ctx["HST"], ctx["ZST"], ctx["GST"],
                                              ctx["T1S"], ctx["HCS"])
    P0, P1, P2, P3, P4, P5 = ctx["P0"], ctx["P1"], ctx["P2"], ctx["P3"], ctx["P4"], ctx["P5"]
    PB0, PB1, PW = ctx["PB0"], ctx["PB1"], ctx["PW"]
    OTS = None
    W_out_a, tern, w_in_b, w_out_b = ctx["W_out_a"], ctx["tern"], ctx["w_in_b"], ctx["w_out_b"]
    A_b, Bm_b, Cm_b = ctx["A_b"], ctx["Bm_b"], ctx["Cm_b"]
    x_in, out = ctx["x_in"], ctx["out"]
    hb_winT = ctx["hb_winT"]
    dbgf = ctx["dbg"]
    PHASES = ctx["PHASES"] if "PHASES" in ctx else 99

    SCOPE = ctx["SCOPE"]

    def sbslice(col0, col1):
        return AUX[:, col0:col1]

    def wprep(Wsrc, dst, is_tern=False):
        WSTG = AUX[:, 8192:10240]
        for oc in range(8):
            slot = oc % 2
            stg = WSTG[:, slot * 1024:(slot + 1) * 1024]
            mprev = fw.marks()
            nld = fw.dma("gpsimd", stg, Wsrc[oc * 128:(oc + 1) * 128, :],
                         wait=[("vector", mprev["vector"]), ("tensor", mprev["tensor"])])
            if is_tern:
                def cvt(v, stg=stg, nld=nld):
                    v.wait_ge(fw.Dq["gpsimd"], nld)
                    v.tensor_scalar(out=stg, in0=stg, scalar1=-1.0, scalar2=None, op0=ALU.add)
                VE(cvt)
            mk = fw.marks()

            def trw(t, stg=stg, nld=nld, mk=mk, is_tern=is_tern):
                if not is_tern:
                    t.wait_ge(fw.Dq["gpsimd"], nld)
                else:
                    t.wait_ge(fw.F["vector"], mk["vector"])
                for k in range(8):
                    t.transpose(PB0[:, k * 128:(k + 1) * 128], stg[:, k * 128:(k + 1) * 128], IDENT[:])
                t.mark()
            TE(trw)
            mt = fw.marks()["tensor"]

            def evw(v, oc=oc, mt=mt, dst=dst):
                v.wait_ge(fw.F["tensor"], mt)
                dap = AP(tensor=dst.tensor, offset=dst.offset + oc * 128,
                         ap=[[dst.ap[0][0], 128], [1024, 8], [1, 128]])
                v.tensor_copy(out=dap, in_=PB0[:].rearrange("p (a b) -> p a b", a=8))
            VE(evw)

    SCOPE("p2g_wout")
    WoT = AUX[:, 0:8192]
    XS = AUX[:, 10240:14336]
    X1S = AUX[:, 14336:18432]
    XTT = AUX[:, 18432:20480]
    wprep(W_out_a, WoT)
    for tau in range(8):
        b, q = divmod(tau, 4)
        for j in range(4):
            c = q * 4 + j
            slot = j % 2
            xstg = XTT[:, slot * 1024:(slot + 1) * 1024]
            mprev = fw.marks()
            nld = fw.dma("gpsimd", xstg, x_in[b, c * 128:(c + 1) * 128, :],
                         wait=[("tensor", mprev["tensor"]), ("vector", mprev["vector"])])

            def trx(t, xstg=xstg, nld=nld):
                t.wait_ge(fw.Dq["gpsimd"], nld)
                for k in range(8):
                    t.transpose(PB1[:, k * 128:(k + 1) * 128], xstg[:, k * 128:(k + 1) * 128], IDENT[:])
                t.mark()
            TE(trx)
            mt = fw.marks()["tensor"]

            def evx(v, j=j, mt=mt):
                v.wait_ge(fw.F["tensor"], mt)
                dap = AP(tensor=XS.tensor, offset=XS.offset + j * 128,
                         ap=[[XS.ap[0][0], 128], [512, 8], [1, 128]])
                v.tensor_copy(out=dap, in_=PB1[:].rearrange("p (a b) -> p a b", a=8))
            VE(evx)
        mxs = fw.marks()
        for w in range(2):
            mev = fw.marks()

            def wmm(t, b=b, q=q, w=w, mev=mev):
                t.wait_ge(fw.F["vector"], mev["vector"])
                for oci in range(4):
                    oc = w * 4 + oci
                    for k in range(8):
                        t.matmul(PW[oci][:], WoT[:, (k * 8 + oc) * 128:(k * 8 + oc + 1) * 128],
                                 R2[:, (b * 8 + k) * 2048 + q * 512:(b * 8 + k) * 2048 + (q + 1) * 512],
                                 start=(k == 0), stop=(k == 7))
                t.mark()
            TE(wmm)
            mw = fw.marks()["tensor"]

            def wev(v, w=w, mw=mw):
                v.wait_ge(fw.F["tensor"], mw)
                for oci in range(4):
                    oc = w * 4 + oci
                    v.scalar_tensor_tensor(out=X1S[:, oc * 512:(oc + 1) * 512], in0=PW[oci][:],
                                           scalar=BOB[:, oc:oc + 1], in1=XS[:, oc * 512:(oc + 1) * 512],
                                           op0=ALU.add, op1=ALU.add)
            VE(wev)

        def x1wr(v, b=b, q=q):
            dap = AP(tensor=R2[:].tensor, offset=R2[:].offset + b * 16384 + q * 512,
                     ap=[[32768, 128], [2048, 8], [1, 512]])
            v.tensor_copy(out=dap, in_=X1S[:].rearrange("p (a b) -> p a b", a=8))
        VE(x1wr)
    fw.barrier()
    if PHASES < 5:
        return

    def sbar_prep():
        def v(v):
            v.tensor_scalar(out=SML[0:1, 49:50], in0=SML[0:1, 48:49], scalar1=1e-5,
                            scalar2=None, op0=ALU.max)
            v.reciprocal(out=SML[0:1, 50:51], in_=SML[0:1, 49:50])
        VE(v)
        mk = fw.marks()

        def t(t):
            t.wait_ge(fw.F["vector"], mk["vector"])
            t.matmul(P4[:, 0:1], IOTAS[0:1, 384:512], SML[0:1, 49:50], start=True, stop=True)
            t.mark()
        TE(t)
        mt = fw.marks()["tensor"]

        def v2(v):
            v.wait_ge(fw.F["tensor"], mt)
            v.tensor_copy(out=SML[:, 51:52], in_=P4[:, 0:1])
        VE(v2)
    ffn_state = {"sbar_done": False}

    def emit_ffn():
        TTW = AUX[:, 0:8192]
        XQ = AUX[:, 10240:14336]
        QF1 = AUX[:, 14336:15360].bitcast(F32)
        QF2 = AUX[:, 15360:16384].bitcast(F32)
        EPS_ = AUX[:, 16384:20480]
        SQS = EPS_[:, 0:1024]
        BCSM = AUX[:, 8192:9216].bitcast(F32)
        BCSR = AUX[:, 9216:10240].bitcast(F32)
        RWB = AUX[0:1, 16384:17408].bitcast(F32)
        wprep(tern, TTW, is_tern=True)
        if not ffn_state["sbar_done"]:
            sbar_prep()
            ffn_state["sbar_done"] = True
        for tau in range(8):
            b, q = divmod(tau, 4)

            def xdo(k):
                return R2[:, (b * 8 + k) * 2048 + q * 512:(b * 8 + k) * 2048 + (q + 1) * 512]
            smm_marks = {}
            for do in range(8):
                slot = SQS[:, (do % 2) * 512:(do % 2) * 512 + 512]

                def sq(v, do=do, slot=slot):
                    if do >= 2:
                        v.wait_ge(fw.F["tensor"], smm_marks[do - 2])
                    v.tensor_tensor(out=slot, in0=xdo(do), in1=xdo(do), op=ALU.mult)
                VE(sq)
                mq = fw.marks()

                def smm(t, do=do, mq=mq, slot=slot):
                    t.wait_ge(fw.F["vector"], mq["vector"])
                    t.matmul(P4[0:1, 0:512], ONEC[:, 0:1], xdo(do), start=(do == 0), stop=(do == 7))
                    t.matmul(P5[0:1, 0:512], ONEC[:, 0:1], slot, start=(do == 0), stop=(do == 7))
                    t.mark()
                TE(smm)
                smm_marks[do] = fw.marks()["tensor"]
            ms = fw.marks()["tensor"]

            def spost(v, ms=ms):
                v.wait_ge(fw.F["tensor"], ms)
                v.tensor_scalar(out=QF1[0:1, 0:512], in0=P4[0:1, 0:512], scalar1=1.0 / D,
                                scalar2=None, op0=ALU.mult)
                v.tensor_tensor(out=QF2[0:1, 0:512], in0=QF1[0:1, 0:512], in1=QF1[0:1, 0:512], op=ALU.mult)
                v.scalar_tensor_tensor(out=QF2[0:1, 0:512], in0=P5[0:1, 0:512], scalar=1.0 / D,
                                       in1=QF2[0:1, 0:512], op0=ALU.mult, op1=ALU.subtract)
                v.tensor_scalar(out=QF2[0:1, 0:512], in0=QF2[0:1, 0:512], scalar1=EPS_LN,
                                scalar2=None, op0=ALU.add)
                v.reciprocal(out=RWB[:], in_=QF2[0:1, 0:512])
            VE(spost)
            mv = fw.marks()

            def ssq(s, mv=mv):
                s.wait_ge(fw.F["vector"], mv["vector"])
                s.activation(out=QF2[0:1, 0:512], in_=RWB[:], func=AF.Sqrt)
            SC(ssq)
            mss = fw.marks()

            def rfold(v, mss=mss):
                v.wait_ge(fw.F["scalar"], mss["scalar"])
                v.tensor_scalar(out=QF2[0:1, 0:512], in0=QF2[0:1, 0:512],
                                scalar1=SML[0:1, 50:51], scalar2=None, op0=ALU.mult)
            VE(rfold)
            mr = fw.marks()

            def bcmm(t, mr=mr):
                t.wait_ge(fw.F["vector"], mr["vector"])
                t.matmul(P4[:], IOTAS[0:1, 384:512], QF1[0:1, 0:512], start=True, stop=True)
                t.matmul(P5[:], IOTAS[0:1, 384:512], QF2[0:1, 0:512], start=True, stop=True)
                t.mark()
            TE(bcmm)
            mbc = fw.marks()["tensor"]

            def bcev(v, mbc=mbc):
                v.wait_ge(fw.F["tensor"], mbc)
                v.tensor_copy(out=BCSM[:], in_=P4[:])
                v.tensor_copy(out=BCSR[:], in_=P5[:])
            VE(bcev)
            def qn(v, b=b, q=q):
                x8 = AP(tensor=R2[:].tensor, offset=R2[:].offset + b * 16384 + q * 512,
                        ap=[[32768, 128], [2048, 8], [1, 512]])
                mh = AP(tensor=BCSM[:].tensor, offset=BCSM[:].offset,
                        ap=[[BCSM[:].ap[0][0], 128], [0, 8], [1, 512]])
                rh = AP(tensor=BCSR[:].tensor, offset=BCSR[:].offset,
                        ap=[[BCSR[:].ap[0][0], 128], [0, 8], [1, 512]])
                SH = R3[:, 0:8192].bitcast(F32).rearrange("p (a b) -> p a b", a=8)
                GH = R3[:, 8192:16384].bitcast(F32).rearrange("p (a b) -> p a b", a=8)
                xq8 = XQ[:].rearrange("p (a b) -> p a b", a=8)
                v.tensor_tensor(out=SH, in0=x8, in1=mh, op=ALU.subtract)
                v.tensor_tensor(out=SH, in0=SH, in1=rh, op=ALU.mult)
                v.tensor_scalar(out=SH, in0=SH, scalar1=127.0,
                                scalar2=-127.0, op0=ALU.min, op1=ALU.max)
                v.tensor_scalar(out=GH, in0=SH, scalar1=MAGIC,
                                scalar2=-MAGIC, op0=ALU.add, op1=ALU.add)
                v.tensor_scalar(out=SH, in0=SH, scalar1=MAGIC64,
                                scalar2=-MAGIC64, op0=ALU.add, op1=ALU.add)
                v.tensor_tensor(out=SH, in0=SH, in1=GH, op=ALU.subtract)
                v.scalar_tensor_tensor(out=xq8, in0=SH, scalar=SML[:, 53:54], in1=GH,
                                       op0=ALU.mult, op1=ALU.add)
            VE(qn)
            for w in range(2):
                mev = fw.marks()

                def fmm(t, w=w, mev=mev):
                    t.wait_ge(fw.F["vector"], mev["vector"])
                    for oci in range(4):
                        oc = w * 4 + oci
                        for k in range(8):
                            t.matmul(PW[oci][:], TTW[:, (k * 8 + oc) * 128:(k * 8 + oc + 1) * 128],
                                     XQ[:, k * 512:(k + 1) * 512], start=(k == 0), stop=(k == 7))
                    t.mark()
                TE(fmm)
                mw = fw.marks()["tensor"]

                def fev(v, w=w, mw=mw, b=b, q=q):
                    v.wait_ge(fw.F["tensor"], mw)
                    for oci in range(4):
                        oc = w * 4 + oci
                        v.tensor_scalar(out=QF1[:, 0:512], in0=PW[oci][:], scalar1=SML[:, 51:52],
                                        scalar2=BOB[:, 8 + oc:8 + oc + 1], op0=ALU.mult, op1=ALU.add)
                        v.tensor_tensor(out=EPS_[:, oc * 512:(oc + 1) * 512], in0=QF1[:, 0:512],
                                        in1=xdo(oc), op=ALU.add)
                VE(fev)

            def x2wr(v, b=b, q=q):
                dap = AP(tensor=R2[:].tensor, offset=R2[:].offset + b * 16384 + q * 512,
                         ap=[[32768, 128], [2048, 8], [1, 512]])
                v.tensor_copy(out=dap, in_=EPS_[:].rearrange("p (a b) -> p a b", a=8))
            VE(x2wr)
        fw.barrier()

    SCOPE("p3_ffn1")
    emit_ffn()
    if PHASES < 6:
        return

    SCOPE("p4a_winT")
    WSTG = AUX[:, 8192:10240]
    for et in range(16):
        slot = et % 2
        stg = WSTG[:, slot * 1024:(slot + 1) * 1024]
        mprev = fw.marks()
        nld = fw.dma("gpsimd", stg, w_in_b[et * 128:(et + 1) * 128, :],
                     wait=[("vector", mprev["vector"]), ("tensor", mprev["tensor"])])

        def trwi(t, stg=stg, nld=nld):
            t.wait_ge(fw.Dq["gpsimd"], nld)
            for k in range(8):
                t.transpose(PB0[:, k * 128:(k + 1) * 128], stg[:, k * 128:(k + 1) * 128], IDENT[:])
            t.mark()
        TE(trwi)
        mt = fw.marks()["tensor"]

        def evwi(v, et=et, mt=mt):
            v.wait_ge(fw.F["tensor"], mt)
            dap = AP(tensor=R3[:].tensor, offset=R3[:].offset + et * 128,
                     ap=[[16384, 128], [2048, 8], [1, 128]])
            v.tensor_copy(out=dap, in_=PB0[:].rearrange("p (a b) -> p a b", a=8))
        VE(evwi)
    SCOPE("p4b_ln3_win")
    WIN2 = AUX[:, 0:4096]
    XSUB = AUX[:, 4096:8192]
    BCS4M = AUX[:, 8192:9216].bitcast(F32)
    BCS4R = AUX[:, 9216:10240].bitcast(F32)
    XN = AUX[:, 10240:14336]
    QF4 = AUX[:, 14336:15360].bitcast(F32)
    QF5 = AUX[:, 15360:16384].bitcast(F32)
    SQ4 = AUX[:, 16384:17408]
    RW3 = AUX[0:1, 17408:18432].bitcast(F32)

    for tau in range(4):
        b, qs = divmod(tau, 2)

        def xe(do):
            base = (b * 8 + do) * 2048 + qs * 1024
            return AP(tensor=R2[:].tensor, offset=R2[:].offset + base,
                      ap=[[32768, 128], [2, 512]])
        smm3_marks = {}
        for do in range(8):
            slot = SQ4[:, (do % 2) * 512:(do % 2) * 512 + 512]

            def cpx(v, do=do, slot=slot):
                if do >= 2:
                    v.wait_ge(fw.F["tensor"], smm3_marks[do - 2])
                v.tensor_copy(out=XSUB[:, do * 512:(do + 1) * 512], in_=xe(do))
                v.tensor_tensor(out=slot, in0=XSUB[:, do * 512:(do + 1) * 512],
                                in1=XSUB[:, do * 512:(do + 1) * 512], op=ALU.mult)
            VE(cpx)
            mq = fw.marks()

            def smm3(t, do=do, mq=mq, slot=slot):
                t.wait_ge(fw.F["vector"], mq["vector"])
                t.matmul(P4[0:1, 0:512], ONEC[:, 0:1], XSUB[:, do * 512:(do + 1) * 512],
                         start=(do == 0), stop=(do == 7))
                t.matmul(P5[0:1, 0:512], ONEC[:, 0:1], slot, start=(do == 0), stop=(do == 7))
                t.mark()
            TE(smm3)
            smm3_marks[do] = fw.marks()["tensor"]
        ms3 = fw.marks()["tensor"]

        def spost3(v, ms3=ms3):
            v.wait_ge(fw.F["tensor"], ms3)
            v.tensor_scalar(out=QF4[0:1, 0:512], in0=P4[0:1, 0:512], scalar1=1.0 / D,
                            scalar2=None, op0=ALU.mult)
            v.tensor_tensor(out=QF5[0:1, 0:512], in0=QF4[0:1, 0:512], in1=QF4[0:1, 0:512], op=ALU.mult)
            v.scalar_tensor_tensor(out=QF5[0:1, 0:512], in0=P5[0:1, 0:512], scalar=1.0 / D,
                                   in1=QF5[0:1, 0:512], op0=ALU.mult, op1=ALU.subtract)
            v.tensor_scalar(out=QF5[0:1, 0:512], in0=QF5[0:1, 0:512], scalar1=1.0 + EPS_RMS,
                            scalar2=EPS_RMS * EPS_LN, op0=ALU.mult, op1=ALU.add)
            v.reciprocal(out=RW3[:], in_=QF5[0:1, 0:512])
        VE(spost3)
        mv3 = fw.marks()

        def ssq3(s, mv3=mv3):
            s.wait_ge(fw.F["vector"], mv3["vector"])
            s.activation(out=QF5[0:1, 0:512], in_=RW3[:], func=AF.Sqrt)
        SC(ssq3)
        ms4 = fw.marks()

        def bc3(t, ms4=ms4):
            t.wait_ge(fw.F["scalar"], ms4["scalar"])
            t.matmul(P4[:], IOTAS[0:1, 384:512], QF4[0:1, 0:512], start=True, stop=True)
            t.matmul(P5[:], IOTAS[0:1, 384:512], QF5[0:1, 0:512], start=True, stop=True)
            t.mark()
        TE(bc3)
        mb3 = fw.marks()["tensor"]

        def bce3(v, mb3=mb3):
            v.wait_ge(fw.F["tensor"], mb3)
            v.tensor_copy(out=BCS4M[:], in_=P4[:])
            v.tensor_copy(out=BCS4R[:], in_=P5[:])
        VE(bce3)
        for do in range(8):
            def nrm(v, do=do):
                v.tensor_tensor(out=QF4[:], in0=XSUB[:, do * 512:(do + 1) * 512],
                                in1=BCS4M[:], op=ALU.subtract)
                v.scalar_tensor_tensor(out=XN[:, do * 512:(do + 1) * 512], in0=QF4[:],
                                       scalar=BOB[:, 24 + do:24 + do + 1], in1=BCS4R[:],
                                       op0=ALU.mult, op1=ALU.mult)
            VE(nrm)
        for w in range(4):
            mev = fw.marks()
            for k in range(8):
                wk = R3[:, k * 2048:(k + 1) * 2048]

                def wmm4(t, w=w, k=k, wk=wk, mev=mev):
                    if k == 0:
                        t.wait_ge(fw.F["vector"], mev["vector"])
                    for eci in range(4):
                        ec = w * 4 + eci
                        t.matmul(PW[eci][:], wk[:, ec * 128:(ec + 1) * 128],
                                 XN[:, k * 512:(k + 1) * 512], start=(k == 0), stop=(k == 7))
                    t.mark()
                TE(wmm4)
            mw4 = fw.marks()["tensor"]

            def wev4(v, w=w, mw4=mw4, b=b, qs=qs):
                v.wait_ge(fw.F["tensor"], mw4)
                for eci in range(4):
                    ec = w * 4 + eci
                    v.tensor_scalar(out=R1[:, (b * 16 + ec) * 1024 + qs * 512:(b * 16 + ec) * 1024 + (qs + 1) * 512],
                                    in0=PW[eci][:], scalar1=SML[:, 16 + ec:16 + ec + 1],
                                    scalar2=None, op0=ALU.add)
            VE(wev4)
    fw.barrier()

    SCOPE("p4c_abc")
    for do in range(8):
        for b in range(B2):
            fw.dma("sync", ASC[:, (b * 8 + do) * 64:(b * 8 + do + 1) * 64],
                   A_b[do * 128:(do + 1) * 128, :])
            fw.dma("sync", BSC[:, (b * 8 + do) * 64:(b * 8 + do + 1) * 64],
                   Bm_b[do * 128:(do + 1) * 128, :])
    for do in range(8):
        fw.dma("sync", GST[:, do * 64:(do + 1) * 64], Cm_b[do * 128:(do + 1) * 128, :])
    ncm = fw.dn["sync"]

    def onef(v):
        v.memset(STT[:, 0:1], 1.0)
        v.memset(HST[:], 0.0)
    VE(onef)
    monef = fw.marks()

    def csum_mm(t, monef=monef):
        t.wait_ge(fw.Dq["sync"], ncm)
        t.wait_ge(fw.F["vector"], monef["vector"])
        for do in range(8):
            t.matmul(P4[0:1, 0:64], STT[:, 0:1], GST[:, do * 64:(do + 1) * 64],
                     start=(do == 0), stop=(do == 7))
        t.mark()
    TE(csum_mm)
    mcs = fw.marks()["tensor"]

    def csum_ev(v, mcs=mcs):
        v.wait_ge(fw.F["tensor"], mcs)
        v.tensor_copy(out=QF4[0:1, 0:64], in_=P4[0:1, 0:64])
    VE(csum_ev)
    mce = fw.marks()

    def cbc_mm(t, mce=mce):
        t.wait_ge(fw.F["vector"], mce["vector"])
        t.matmul(P5[:, 0:64], IOTAS[0:1, 384:512], QF4[0:1, 0:64], start=True, stop=True)
        t.mark()
    TE(cbc_mm)
    mcb = fw.marks()["tensor"]

    def cbc_ev(v, mcb=mcb):
        v.wait_ge(fw.F["tensor"], mcb)
        v.tensor_copy(out=CBC[:], in_=P5[:, 0:64])
    VE(cbc_ev)
    fw.barrier()
    if PHASES < 7:
        return

    SCOPE("p4d_scan")
    def xt_bc(t):
        return AP(tensor=R1[:].tensor, offset=R1[:].offset + t,
                  ap=[[32768, 128], [16384, 2], [1024, 8], [0, 64]])

    cbc_ap = AP(tensor=CBC[:].tensor, offset=CBC[:].offset,
                ap=[[CBC[:].ap[0][0], 128], [0, 2], [0, 8], [1, 64]])
    GB = [GST, AUX[:, 14464:16512].bitcast(F32)]
    HB = [HST, AUX[:, 16512:18560].bitcast(F32)]
    smark = {-1: fw.marks()["scalar"]}
    zmark = {}
    gmark = {}
    GS = 512

    def g_gp(g, t_):
        if t_ >= 2:
            g.wait_ge(fw.F["vector"], zmark[t_ - 2])
        gap = AP(tensor=R1[:].tensor, offset=R1[:].offset + t_,
                 ap=[[32768, 128], [1024, 8], [0, 64]])
        g.tensor_tensor(out=GB[t_ % 2][:, 0:GS], in0=BSC[:, 0:GS], in1=gap, op=ALU.mult)

    def g_ve(v, t_):
        gap = AP(tensor=R1[:].tensor, offset=R1[:].offset + t_ + 16384,
                 ap=[[32768, 128], [1024, 8], [0, 64]])
        v.tensor_tensor(out=GB[t_ % 2][:, GS:1024], in0=BSC[:, GS:1024], in1=gap, op=ALU.mult)

    def y_emit(v, t_):
        v.tensor_tensor(out=HCS[:], in0=HB[(t_ + 1) % 2][:], in1=cbc_ap, op=ALU.mult)
        ysap = AP(tensor=R3[:].tensor, offset=R3[:].offset + t_,
                  ap=[[16384, 128], [8192, 2], [1024, 8]])
        with nc.allow_low_precision(reason="ys store bf16; DVE reduce accumulates fp32"):
            v.tensor_reduce(out=ysap, in_=HCS[:].rearrange("p (a b c) -> p a b c", a=2, b=8),
                            axis=AX.X, op=ALU.add)

    def prolog(g):
        g_gp(g, 0)
        g_gp(g, 1)
    GP(prolog)
    gmark[0] = gmark[1] = fw.marks()["gpsimd"]

    def prolog_v(v):
        g_ve(v, 0)
        g_ve(v, 1)
    VE(prolog_v)
    for t_ in range(1024):
        def step_v(v, t_=t_):
            v.wait_ge(fw.F["scalar"], smark[t_ - 1])
            if t_ >= 1:
                y_emit(v, t_ - 1)
            v.tensor_tensor(out=T1S[:], in0=HB[t_ % 2][:], in1=ASC[:], op=ALU.mult)
            v.wait_ge(fw.F["gpsimd"], gmark[t_])
            v.tensor_tensor(out=ZST[:], in0=T1S[:], in1=GB[t_ % 2][:], op=ALU.add)
        VE(step_v)
        zmark[t_] = fw.marks()["vector"]
        if t_ + 2 < 1024:
            def gp_sec(g, t_=t_):
                g_gp(g, t_ + 2)
            GP(gp_sec)
            gmark[t_ + 2] = fw.marks()["gpsimd"]

        def step_s(s, t_=t_):
            s.wait_ge(fw.F["vector"], zmark[t_])
            s.activation(out=HB[(t_ + 1) % 2][:], in_=ZST[:], func=AF.Tanh)
        SC(step_s)
        smark[t_] = fw.marks()["scalar"]
        if t_ + 2 < 1024:
            def gve_sec(v, t_=t_):
                g_ve(v, t_ + 2)
            VE(gve_sec)

    def ylast(v):
        v.wait_ge(fw.F["scalar"], smark[1023])
        y_emit(v, 1023)
    VE(ylast)
    fw.barrier()
    if PHASES < 8:
        return

    SCOPE("p4e_post")
    for b in range(B2):
        for do in range(8):
            xp = R1[:, (b * 16 + do) * 1024:(b * 16 + do + 1) * 1024]
            gcol = (b * 16 + 8 + do)
            gate = R1[:, gcol * 1024:(gcol + 1) * 1024]
            ys = R3[:, b * 8192 + do * 1024:b * 8192 + (do + 1) * 1024]

            mpv = fw.marks()["vector"]

            def silu(s, gate=gate, mpv=mpv):
                s.wait_ge(fw.F["vector"], mpv)
                s.activation(out=GST[:], in_=gate, func=AF.Silu)
            SC(silu)
            msl = fw.marks()["scalar"]

            def opre(v, xp=xp, gate=gate, ys=ys, do=do, msl=msl):
                v.wait_ge(fw.F["scalar"], msl)
                v.tensor_scalar(out=T1S[:], in0=xp, scalar1=BOB[:, 16 + do:16 + do + 1],
                                scalar2=None, op0=ALU.mult)
                v.tensor_tensor(out=HCS[:], in0=ys, in1=T1S[:], op=ALU.add)
                v.tensor_tensor(out=gate, in0=HCS[:], in1=GST[:], op=ALU.mult)
            VE(opre)
    WOB = AUX[:, 0:8192]
    wprep(w_out_b, WOB)
    OBS = AUX[:, 10240:10752]
    for tau in range(4):
        b, qs = divmod(tau, 2)
        for w in range(2):
            mev = fw.marks()

            def womm(t, b=b, qs=qs, w=w, mev=mev):
                t.wait_ge(fw.F["vector"], mev["vector"])
                for oci in range(4):
                    oc = w * 4 + oci
                    for k in range(8):
                        t.matmul(PW[oci][:], WOB[:, (k * 8 + oc) * 128:(k * 8 + oc + 1) * 128],
                                 R1[:, (b * 16 + 8 + k) * 1024 + qs * 512:(b * 16 + 8 + k) * 1024 + (qs + 1) * 512],
                                 start=(k == 0), stop=(k == 7))
                t.mark()
            TE(womm)
            mwo = fw.marks()["tensor"]

            def woev(v, b=b, qs=qs, w=w, mwo=mwo):
                v.wait_ge(fw.F["tensor"], mwo)
                for oci in range(4):
                    oc = w * 4 + oci
                    v.tensor_scalar(out=OBS[:], in0=PW[oci][:], scalar1=BOB[:, 32 + oc:32 + oc + 1],
                                    scalar2=None, op0=ALU.add)
                    for par in range(2):
                        base = (b * 8 + oc) * 2048 + qs * 1024 + par
                        rap = AP(tensor=R2[:].tensor, offset=R2[:].offset + base,
                                 ap=[[32768, 128], [2, 512]])
                        v.tensor_tensor(out=rap, in0=rap, in1=OBS[:], op=ALU.add)
            VE(woev)
    fw.barrier()
    if PHASES < 9:
        return

    SCOPE("p5_ffn2")
    emit_ffn()

    SCOPE("p6_out")
    OTS_ = [AUX[:, 10240:12288].bitcast(F32), AUX[:, 12288:14336].bitcast(F32)]
    osem = [ctx["st"].enter_context(nc.semaphore("OSEM0")),
            ctx["st"].enter_context(nc.semaphore("OSEM1"))]
    ocnt = [0, 0]
    for i in range(B2 * NCH):
        b, c = divmod(i, NCH)
        sl = i % 2
        mprev = fw.marks()

        def otr(t, b=b, c=c, mprev=mprev):
            t.wait_ge(fw.F["vector"], mprev["vector"])
            for do in range(8):
                t.transpose(PB0[:, do * 128:(do + 1) * 128],
                            R2[:, (b * 8 + do) * 2048 + c * 128:(b * 8 + do) * 2048 + (c + 1) * 128],
                            IDENT[:])
            t.mark()
        TE(otr)
        mo = fw.marks()["tensor"]

        def oev(v, mo=mo, sl=sl, prevd=ocnt[sl]):
            v.wait_ge(fw.F["tensor"], mo)
            if prevd > 0:
                v.wait_ge(osem[sl], prevd)
            v.tensor_copy(out=OTS_[sl][:], in_=PB0[:])
        VE(oev)
        mvv = fw.marks()

        def ostore(h, b=b, c=c, sl=sl, mvv=mvv):
            h.wait_ge(fw.F["vector"], mvv["vector"])
            h.dma_start(out=out[b, c * 128:(c + 1) * 128, :],
                        in_=OTS_[sl][:]).then_inc(osem[sl], 16)
        fw._sect("sync", ostore)
        ocnt[sl] += 16
    def ofin(v, c0=ocnt[0], c1=ocnt[1]):
        v.wait_ge(osem[0], c0)
        v.wait_ge(osem[1], c1)
    VE(ofin)
    fw.barrier()




F32 = mybir.dt.float32
BF16 = mybir.dt.bfloat16
I32 = mybir.dt.int32
AF = mybir.ActivationFunctionType
ALU = mybir.AluOpType
AX = mybir.AxisListType

B2, L, D, NB = 2, 2048, 1024, 8
NCH = 16
N2 = 32
S64 = 64
EPS_LN, EPS_RMS = 1e-5, 1e-6
TWO_PI = 2.0 * math.pi
INV_2PI = 1.0 / TWO_PI
MAGIC = 1.5 * 2.0 ** 23
MAGIC64 = 1.5 * 2.0 ** 52
SLABS = NB
PHASES = 99


def build_program(debug=()):
    nc = bass.Bass("TRN2", target_bir_lowering=False)
    dbg_outs = {}

    def dram_in(name, shape, dt=F32):
        return nc.dram_tensor(name, shape, dt, kind="ExternalInput").ap()

    x_in = dram_in("x_in", [B2, L, D])
    out = nc.dram_tensor("out", [B2, L, D], F32, kind="ExternalOutput").ap()
    log_dt = dram_in("log_dt", [1, D])
    log_w_real = dram_in("log_w_real", [1, N2])
    w_imag = dram_in("w_imag", [1, N2])
    C_real = dram_in("C_real", [D, N2])
    C_imag = dram_in("C_imag", [D, N2])
    D_a = dram_in("D_a", [1, D])
    W_out_a = dram_in("W_out_a", [D, D])
    b_out_a = dram_in("b_out_a", [1, D])
    tern = dram_in("tern", [D, D], I32)
    bias_f = dram_in("bias_f", [1, D])
    act_scale = dram_in("act_scale", [1, 1])
    w_in_b = dram_in("w_in_b", [2 * D, D])
    b_in_b = dram_in("b_in_b", [1, 2 * D])
    A_b = dram_in("A_b", [D, S64])
    Bm_b = dram_in("Bm_b", [D, S64])
    Cm_b = dram_in("Cm_b", [D, S64])
    Dv_b = dram_in("Dv_b", [1, D])
    w_out_b = dram_in("w_out_b", [D, D])
    b_out_b = dram_in("b_out_b", [1, D])
    g_norm = dram_in("g_norm", [1, D])
    identb = dram_in("identb", [128, 128], BF16)
    iota_negj = dram_in("iota_negj", [1, 128])
    iota_t = dram_in("iota_t", [1, 128])
    iota_t128 = dram_in("iota_t128", [1, 128])
    mask_triu = dram_in("mask_triu", [128, 128], BF16)
    ones_row = dram_in("ones_row", [1, 128])

    hb_aim = nc.dram_tensor("hb_aim", [1, D * N2], F32, kind="Internal").ap()
    hb_are = nc.dram_tensor("hb_are", [1, D * N2], F32, kind="Internal").ap()
    hb_cwr = nc.dram_tensor("hb_cwr", [1, D * N2], F32, kind="Internal").ap()
    hb_cwi = nc.dram_tensor("hb_cwi", [1, D * N2], F32, kind="Internal").ap()
    hb_slab = nc.dram_tensor("hb_slab", [NB, 128, B2 * NCH * 128], BF16, kind="Internal").ap()
    hb_winT = nc.dram_tensor("hb_winT", [NB, 128, 2 * D], BF16, kind="Internal").ap()

    def dbg(name, shape, dt):
        if name in debug:
            dbg_outs[name] = nc.dram_tensor(name, shape, dt, kind="ExternalOutput").ap()
            return dbg_outs[name]
        return None

    st = ExitStack()
    sbuf = lambda name, shape, dt=BF16: st.enter_context(nc.sbuf_tensor(name, shape, dt))
    psum = lambda name, shape, dt=F32: st.enter_context(nc.psum_tensor(name, shape, dt))

    with st:
        R1 = sbuf("R1", [128, 32768])
        R2 = sbuf("R2", [128, 32768])
        R3 = sbuf("R3", [128, 16384])
        AUX = sbuf("AUX", [128, 20480])
        IDENT = sbuf("IDENTS", [128, 128])
        IDENTF = sbuf("IDENTF", [128, 128], F32)
        MASKT = sbuf("MASKT", [128, 128])
        IOTAS = sbuf("IOTAS", [1, 512], F32)
        ONEC = sbuf("ONEC", [128, 2])
        STM = sbuf("STM", [128, 96], F32)
        STT = sbuf("STT", [128, 16], F32)
        SML = sbuf("SML", [128, 64], F32)
        BOB = sbuf("BOB", [128, 64], F32)
        DBC = sbuf("DBC", [128, 1024])
        ASC = AUX[:, 0:2048].bitcast(F32)
        BSC = AUX[:, 2048:4096].bitcast(F32)
        HST = AUX[:, 4096:6144].bitcast(F32)
        ZST = AUX[:, 6144:8192].bitcast(F32)
        GST = AUX[:, 8192:10240].bitcast(F32)
        T1S = AUX[:, 10240:12288].bitcast(F32)
        HCS = AUX[:, 12288:14336].bitcast(F32)
        CBC = AUX[:, 14336:14464].bitcast(F32)
        PRM = AUX[:, 8192:12288].bitcast(F32)
        BCB = AUX[:, 12288:14336].bitcast(F32)
        DROW = AUX[0:1, 14336:16384].bitcast(F32)
        XT = AUX[:, 0:4096]

        P0 = psum("P0", [128, 512]); P1 = psum("P1", [128, 512]); P2 = psum("P2", [128, 512])
        P3 = psum("P3", [128, 512]); P4 = psum("P4", [128, 512]); P5 = psum("P5", [128, 512])
        PB0 = psum("PB0", [128, 1024], BF16)
        PB1 = psum("PB1", [128, 1024], BF16)
        PW = [P0, P1, P2, P3]

        st.enter_context(nc.allow_non_contiguous_dma(reason="strided staging loads/stores"))
        blk = st.enter_context(nc.Block())
        fw = FW(nc, st, blk)

        _scope = {"cur": None, "id": None}

        def SCOPE(name):
            if _scope["cur"] is not None:
                nc.leave_named_scope(_scope["cur"], _scope["id"], False)
                _scope["cur"] = None
            if name:
                sid, _ = nc.enter_named_scope(name, False)
                _scope["cur"], _scope["id"] = name, sid

        def VE(fn):
            fw.run("vector", fn)
        def SC(fn):
            fw.run("scalar", fn)
        def TE(fn):
            fw.run("tensor", fn)
        def GP(fn):
            fw.run("gpsimd", fn)

        SCOPE("p0_consts")
        fw.dma("sync", IDENT[:], identb[:])
        fw.dma("sync", MASKT[:], mask_triu[:])
        fw.dma("sync", IOTAS[0:1, 0:128], iota_negj[:])
        fw.dma("sync", IOTAS[0:1, 128:256], iota_t[:])
        fw.dma("sync", IOTAS[0:1, 256:384], iota_t128[:])
        fw.dma("sync", IOTAS[0:1, 384:512], ones_row[:])
        for (dst0, src) in ((0, b_out_a), (8, bias_f), (16, Dv_b), (24, g_norm), (32, b_out_b)):
            fw.dma("sync", BOB[:, dst0:dst0 + 8],
                   AP(tensor=src.tensor, offset=0, ap=[[1, 128], [128, 8]]))
        fw.dma("sync", SML[:, 0:8], AP(tensor=log_dt.tensor, offset=0, ap=[[1, 128], [128, 8]]))
        fw.dma("sync", SML[:, 16:32], AP(tensor=b_in_b.tensor, offset=0, ap=[[1, 128], [128, 16]]))
        fw.dma("sync", SML[0:1, 48:49], act_scale[:])
        fw.dma("sync", PRM[:, 1536:1792].rearrange("p (a b) -> p a b", a=8),
               AP(tensor=C_real.tensor, offset=0, ap=[[N2, 128], [128 * N2, 8], [1, N2]]))
        fw.dma("sync", PRM[:, 1792:2048].rearrange("p (a b) -> p a b", a=8),
               AP(tensor=C_imag.tensor, offset=0, ap=[[N2, 128], [128 * N2, 8], [1, N2]]))
        fw.dma("sync", BCB[0:1, 384:416], log_w_real[:])
        fw.dma("sync", BCB[0:1, 416:448], w_imag[:])
        fw.dma("sync", DROW[0:1, 0:1024], D_a[:])
        nconst = fw.dn["sync"]
        nconst2 = nconst

        def c_init(v):
            v.wait_ge(fw.Dq["sync"], nconst)
            v.tensor_copy(out=IDENTF[:], in_=IDENT[:])
            v.memset(ONEC[:], 1.0)
            v.memset(SML[:, 52:53], 1.0)
            v.tensor_scalar(out=SML[:, 53:54], in0=SML[:, 52:53], scalar1=MAGIC64,
                            scalar2=-MAGIC64, op0=ALU.add, op1=ALU.add)
        VE(c_init)
        m_init = fw.marks()

        SCOPE("p1_ln1")
        NTILE = B2 * NCH
        sx = [st.enter_context(nc.semaphore(f"sx{i}")) for i in range(4)]

        def ph1(i):
            b, c = divmod(i, NCH)
            slot = i % 4
            xt = XT[:, slot * 1024:(slot + 1) * 1024]
            mk0 = fw.marks()

            def gp_ld(h):
                if i >= 4:
                    h.wait_ge(fw.F["vector"], ph1.done[i - 4])
                h.dma_start(out=xt, in_=x_in[b, c * 128:(c + 1) * 128, :]).then_inc(sx[slot], 16)
            fw._sect("gpsimd", gp_ld)

            def ve1(v):
                v.wait_ge(sx[slot], 16 * (i // 4 + 1))
                v.bn_stats(out=STT[:, 0:6], in_=xt[:, 0:512])
                v.bn_stats(out=STT[:, 6:12], in_=xt[:, 512:1024])
                v.bn_aggr(out=STT[:, 12:14], in_=STT[:, 0:12])
                v.tensor_copy(out=STM[:, i:i + 1], in_=STT[:, 12:13])
                v.tensor_scalar(out=STT[:, 14:15], in0=STT[:, 13:14], scalar1=EPS_LN,
                                scalar2=None, op0=ALU.add)
                v.reciprocal(out=STT[:, 15:16], in_=STT[:, 14:15])
            VE(ve1)
            mkv = fw.marks()

            def sc1(s):
                s.wait_ge(fw.F["vector"], mkv["vector"])
                s.activation(out=STM[:, 64 + i:64 + i + 1], in_=STT[:, 15:16], func=AF.Sqrt)
            SC(sc1)
            mks = fw.marks()

            def ve2(v):
                v.wait_ge(fw.F["scalar"], mks["scalar"])
                v.tensor_scalar(out=R1[:, i * 1024:(i + 1) * 1024], in0=xt,
                                scalar1=STM[:, i:i + 1], scalar2=STM[:, 64 + i:64 + i + 1],
                                op0=ALU.subtract, op1=ALU.mult)
            VE(ve2)
            ph1.done[i] = fw.marks()["vector"]
        ph1.done = {}
        for i in range(NTILE):
            ph1(i)
        fw.barrier()
        if PHASES < 2:
            return _finish(nc, fw, dbg_outs, dbg, R1, R2, R3, STM, PRM, BCB, AUX, hb_aim, hb_are, hb_cwr, hb_cwi, DBC)

        SCOPE("p2a_prep")
        SCF = AUX[:, 4096:8192].bitcast(F32)

        def dt3(base, a, bsz, astr, bstr):
            return AP(tensor=base.tensor, offset=base.offset,
                      ap=[list(base.ap[0]), [astr, a], [bstr, bsz]])

        def mm1(t):
            t.wait_ge(fw.Dq["sync"], nconst2)
            t.matmul(P0[:, 0:64], IOTAS[0:1, 384:512], BCB[0:1, 384:448], start=True, stop=True)
            t.matmul(P1[:], IOTAS[0:1, 384:512], DROW[0:1, 0:512], start=True, stop=True)
            t.matmul(P2[:], IOTAS[0:1, 384:512], DROW[0:1, 512:1024], start=True, stop=True)
            t.mark()
        TE(mm1)
        mk = fw.marks()

        def sc_dt(s):
            s.activation(out=SML[:, 8:16], in_=SML[:, 0:8], func=AF.Exp)
        SC(sc_dt)
        mk_dt = fw.marks()

        def ve_bc(v):
            v.wait_ge(fw.F["tensor"], mk["tensor"])
            v.tensor_copy(out=BCB[:, 0:32], in_=P0[:, 0:32])
            v.tensor_copy(out=BCB[:, 32:64], in_=P0[:, 32:64])
        VE(ve_bc)
        mk_bc = fw.marks()

        def sc_wre(s):
            s.wait_ge(fw.F["vector"], mk_bc["vector"])
            s.activation(out=BCB[:, 0:32], in_=BCB[:, 0:32], func=AF.Exp)
        SC(sc_wre)
        mk_wre = fw.marks()

        def ve_args(v):
            v.wait_ge(fw.F["scalar"], mk_wre["scalar"])
            dt_ap = dt3(SML[:, 8:16], 8, 32, 1, 0)
            wim_ap = dt3(BCB[:, 32:64], 8, 32, 0, 1)
            wre_ap = dt3(BCB[:, 0:32], 8, 32, 0, 1)
            v.tensor_tensor(out=PRM[:, 0:256].rearrange("p (a b) -> p a b", a=8),
                            in0=dt_ap, in1=wim_ap, op=ALU.mult)
            v.tensor_tensor(out=PRM[:, 256:512].rearrange("p (a b) -> p a b", a=8),
                            in0=dt_ap, in1=wre_ap, op=ALU.mult)
            v.tensor_scalar(out=PRM[:, 256:512], in0=PRM[:, 256:512], scalar1=-1.0,
                            scalar2=None, op0=ALU.mult)
        VE(ve_args)
        mk_args = fw.marks()

        def sc_rr1(s):
            s.wait_ge(fw.F["vector"], mk_args["vector"])
            s.activation(out=SCF[:, 0:256], in_=PRM[:, 0:256], func=AF.Copy,
                         scale=INV_2PI, bias=MAGIC)
            s.activation(out=SCF[:, 0:256], in_=SCF[:, 0:256], func=AF.Copy, bias=-MAGIC)
        SC(sc_rr1)
        mk_rr1 = fw.marks()

        def ve_rr1(v):
            v.wait_ge(fw.F["scalar"], mk_rr1["scalar"])
            v.tensor_scalar(out=SCF[:, 1024:1280], in0=PRM[:, 0:256], scalar1=INV_2PI,
                            scalar2=None, op0=ALU.mult)
            v.tensor_scalar(out=SCF[:, 1280:1536], in0=SCF[:, 1024:1280], scalar1=MAGIC64,
                            scalar2=-MAGIC64, op0=ALU.add, op1=ALU.add)
            v.tensor_tensor(out=SCF[:, 1280:1536], in0=SCF[:, 1280:1536], in1=SCF[:, 0:256],
                            op=ALU.subtract)
            v.scalar_tensor_tensor(out=SCF[:, 0:256], in0=SCF[:, 1280:1536], scalar=SML[:, 53:54],
                                   in1=SCF[:, 0:256], op0=ALU.mult, op1=ALU.add)
            v.tensor_tensor(out=SCF[:, 256:512], in0=SCF[:, 1024:1280], in1=SCF[:, 0:256],
                            op=ALU.subtract)
            v.tensor_scalar(out=SCF[:, 512:768], in0=SCF[:, 256:512], scalar1=0.25,
                            scalar2=None, op0=ALU.add)
            v.tensor_scalar(out=SCF[:, 768:1024], in0=SCF[:, 512:768], scalar1=0.5,
                            scalar2=None, op0=ALU.is_gt)
            v.tensor_tensor(out=SCF[:, 512:768], in0=SCF[:, 512:768], in1=SCF[:, 768:1024],
                            op=ALU.subtract)
        VE(ve_rr1)
        mk_ve1 = fw.marks()
        rr1_dbg = dbg("rr1_dbg", [128, 1536], F32)
        if rr1_dbg is not None:
            fw.dma("gpsimd", rr1_dbg[:], SCF[:, 0:1536], wait=[("vector", mk_ve1["vector"])])
        stt_dbg = dbg("stt_dbg", [128, 16], F32)
        if stt_dbg is not None:
            fw.dma("gpsimd", stt_dbg[:], STT[:], wait=[("vector", mk_ve1["vector"])])


        def sc_rr2(s):
            s.wait_ge(fw.F["vector"], mk_ve1["vector"])
            s.activation(out=SCF[:, 768:1024], in_=SCF[:, 256:512], func=AF.Sin, scale=TWO_PI)
            s.activation(out=SCF[:, 1024:1280], in_=SCF[:, 512:768], func=AF.Sin, scale=TWO_PI)
            s.activation(out=SCF[:, 1280:1536], in_=PRM[:, 256:512], func=AF.Exp)
        SC(sc_rr2)
        mk_rr2 = fw.marks()

        def ve_cw(v):
            v.wait_ge(fw.F["scalar"], mk_rr2["scalar"])
            v.tensor_tensor(out=PRM[:, 512:768], in0=SCF[:, 1280:1536], in1=SCF[:, 1024:1280], op=ALU.mult)
            v.tensor_tensor(out=PRM[:, 768:1024], in0=SCF[:, 1280:1536], in1=SCF[:, 768:1024], op=ALU.mult)
            v.tensor_tensor(out=BCB[:, 64:96], in0=BCB[:, 0:32], in1=BCB[:, 0:32], op=ALU.mult)
            v.tensor_tensor(out=BCB[:, 96:128], in0=BCB[:, 32:64], in1=BCB[:, 32:64], op=ALU.mult)
            v.tensor_tensor(out=BCB[:, 64:96], in0=BCB[:, 64:96], in1=BCB[:, 96:128], op=ALU.add)
            v.reciprocal(out=BCB[:, 192:224], in_=BCB[:, 64:96])
            v.tensor_scalar(out=SCF[:, 1536:1792], in0=PRM[:, 512:768], scalar1=-1.0, scalar2=None, op0=ALU.add)
            v.tensor_scalar(out=BCB[:, 224:256], in0=BCB[:, 0:32], scalar1=-1.0, scalar2=None, op0=ALU.mult)
            v.tensor_scalar(out=BCB[:, 256:288], in0=BCB[:, 32:64], scalar1=-1.0, scalar2=None, op0=ALU.mult)
            v.tensor_tensor(out=BCB[:, 128:160], in0=BCB[:, 224:256], in1=BCB[:, 192:224], op=ALU.mult)
            v.tensor_tensor(out=BCB[:, 160:192], in0=BCB[:, 256:288], in1=BCB[:, 192:224], op=ALU.mult)
            ir = dt3(BCB[:, 128:160], 8, 32, 0, 1)
            ii = dt3(BCB[:, 160:192], 8, 32, 0, 1)
            nre3 = SCF[:, 1536:1792].rearrange("p (a b) -> p a b", a=8)
            nim3 = PRM[:, 768:1024].rearrange("p (a b) -> p a b", a=8)
            v.tensor_tensor(out=SCF[:, 0:256].rearrange("p (a b) -> p a b", a=8), in0=nre3, in1=ir, op=ALU.mult)
            v.tensor_tensor(out=SCF[:, 256:512].rearrange("p (a b) -> p a b", a=8), in0=nim3, in1=ii, op=ALU.mult)
            v.tensor_tensor(out=SCF[:, 0:256], in0=SCF[:, 0:256], in1=SCF[:, 256:512], op=ALU.subtract)
            v.tensor_tensor(out=SCF[:, 256:512].rearrange("p (a b) -> p a b", a=8), in0=nre3, in1=ii, op=ALU.mult)
            v.tensor_tensor(out=SCF[:, 512:768].rearrange("p (a b) -> p a b", a=8), in0=nim3, in1=ir, op=ALU.mult)
            v.tensor_tensor(out=SCF[:, 256:512], in0=SCF[:, 256:512], in1=SCF[:, 512:768], op=ALU.add)
            v.tensor_tensor(out=SCF[:, 512:768], in0=PRM[:, 1536:1792], in1=SCF[:, 0:256], op=ALU.mult)
            v.tensor_tensor(out=SCF[:, 768:1024], in0=PRM[:, 1792:2048], in1=SCF[:, 256:512], op=ALU.mult)
            v.tensor_tensor(out=SCF[:, 512:768], in0=SCF[:, 512:768], in1=SCF[:, 768:1024], op=ALU.subtract)
            v.tensor_scalar(out=SCF[:, 1536:1792], in0=SCF[:, 512:768], scalar1=2.0, scalar2=None, op0=ALU.mult)
            v.tensor_tensor(out=SCF[:, 512:768], in0=PRM[:, 1536:1792], in1=SCF[:, 256:512], op=ALU.mult)
            v.tensor_tensor(out=SCF[:, 768:1024], in0=PRM[:, 1792:2048], in1=SCF[:, 0:256], op=ALU.mult)
            v.tensor_tensor(out=SCF[:, 512:768], in0=SCF[:, 512:768], in1=SCF[:, 768:1024], op=ALU.add)
            v.tensor_scalar(out=SCF[:, 1792:2048], in0=SCF[:, 512:768], scalar1=2.0, scalar2=None, op0=ALU.mult)
            v.tensor_copy(out=PRM[:, 1024:1280], in_=PRM[:, 512:768])
            v.tensor_copy(out=PRM[:, 1280:1536], in_=PRM[:, 768:1024])
            for _ in range(7):
                v.tensor_tensor(out=SCF[:, 0:256], in0=PRM[:, 1024:1280], in1=PRM[:, 1024:1280], op=ALU.mult)
                v.tensor_tensor(out=SCF[:, 256:512], in0=PRM[:, 1280:1536], in1=PRM[:, 1280:1536], op=ALU.mult)
                v.tensor_tensor(out=SCF[:, 512:768], in0=PRM[:, 1024:1280], in1=PRM[:, 1280:1536], op=ALU.mult)
                v.tensor_tensor(out=PRM[:, 1024:1280], in0=SCF[:, 0:256], in1=SCF[:, 256:512], op=ALU.subtract)
                v.tensor_scalar(out=PRM[:, 1280:1536], in0=SCF[:, 512:768], scalar1=2.0, scalar2=None, op0=ALU.mult)
        VE(ve_cw)
        mk_cw = fw.marks()

        def ve_dbc(v):
            v.tensor_copy(out=DBC[:, 0:512], in_=P1[:])
            v.tensor_copy(out=DBC[:, 512:1024], in_=P2[:])
        VE(ve_dbc)

        def rows_dst(t):
            return AP(tensor=t.tensor, offset=0, ap=[[32, 128], [4096, 8], [1, 32]])
        mnow = fw.marks()
        fw.dma("gpsimd", rows_dst(hb_aim), PRM[:, 0:256].rearrange("p (a b) -> p a b", a=8),
               wait=[("vector", mnow["vector"])])
        fw.dma("gpsimd", rows_dst(hb_are), PRM[:, 256:512].rearrange("p (a b) -> p a b", a=8))
        fw.dma("gpsimd", rows_dst(hb_cwr), SCF[:, 1536:1792].rearrange("p (a b) -> p a b", a=8))
        fw.dma("gpsimd", rows_dst(hb_cwi), SCF[:, 1792:2048].rearrange("p (a b) -> p a b", a=8))
        fw.barrier()
        if PHASES < 3:
            return _finish(nc, fw, dbg_outs, dbg, R1, R2, R3, STM, PRM, BCB, AUX, hb_aim, hb_are, hb_cwr, hb_cwi, DBC)

        TBL = AUX[:, 0:1024]
        TBB = AUX[:, 1024:2048]
        SINT = AUX[:, 2048:2560]
        COST = AUX[:, 2560:3072]
        EXPT = AUX[:, 3072:3584]
        RAWS = AUX[:, 3584:4096]
        TMP2B = AUX[:, 15360:16384]
        TGA = AUX[0:64, 16384:18432]
        TGB = AUX[0:64, 18432:20480]
        BSLAB = R3[0:64, 0:4096]
        SSLAB = R3[0:64, 4096:8192]
        K8A = R3[:, 8192:9216]
        K8B = R3[:, 9216:10240]
        GOSL = R3[:, 10240:14336]
        SSTa = R3[:, 14336:14592].bitcast(F32)
        SSTb = R3[:, 14592:14848].bitcast(F32)
        TMPQ = R3[:, 14848:15360].bitcast(F32)
        AIMSEG = DROW[0:1, 0:512]
        ARESEG = SCF[0:1, 1536:2048]
        CW2R = BCB[:, 0:512]
        CW2I = BCB[:, 512:1024]

        def u_cols(h):
            return AP(tensor=R1[:].tensor, offset=h, ap=[[32768, 128], [NCH * 1024, B2], [1024, NCH]])

        def u_grp(h0):
            return AP(tensor=R1[:].tensor, offset=h0, ap=[[32768, 128], [1, 16], [NCH * 1024, B2], [1024, NCH]])

        def dbc_grp(h0):
            return AP(tensor=DBC[:].tensor, offset=DBC[:].offset + h0,
                      ap=[[1024, 128], [1, 16], [0, B2], [0, NCH]])

        def chain(pim, pexp, mk_args_t):
            def s1(s):
                s.wait_ge(fw.F["tensor"], mk_args_t)
                s.activation(out=SCF[:, 0:512], in_=pim, func=AF.Copy, scale=INV_2PI, bias=MAGIC)
                s.activation(out=SCF[:, 0:512], in_=SCF[:, 0:512], func=AF.Copy, bias=-MAGIC)
            SC(s1)
            m1 = fw.marks()

            def v1(v):
                v.wait_ge(fw.F["scalar"], m1["scalar"])
                v.tensor_scalar(out=SCF[:, 512:1024], in0=pim, scalar1=INV_2PI,
                                scalar2=None, op0=ALU.mult)
                v.tensor_scalar(out=SCF[:, 1024:1536], in0=SCF[:, 512:1024], scalar1=MAGIC64,
                                scalar2=-MAGIC64, op0=ALU.add, op1=ALU.add)
                v.tensor_tensor(out=SCF[:, 1024:1536], in0=SCF[:, 1024:1536], in1=SCF[:, 0:512],
                                op=ALU.subtract)
                v.scalar_tensor_tensor(out=SCF[:, 0:512], in0=SCF[:, 1024:1536], scalar=SML[:, 53:54],
                                       in1=SCF[:, 0:512], op0=ALU.mult, op1=ALU.add)
                v.tensor_tensor(out=SCF[:, 512:1024], in0=SCF[:, 512:1024], in1=SCF[:, 0:512],
                                op=ALU.subtract)
                v.tensor_scalar(out=SCF[:, 1024:1536], in0=SCF[:, 512:1024], scalar1=0.25,
                                scalar2=None, op0=ALU.add)
                v.tensor_scalar(out=SCF[:, 0:512], in0=SCF[:, 1024:1536], scalar1=0.5,
                                scalar2=None, op0=ALU.is_gt)
                v.tensor_tensor(out=SCF[:, 1024:1536], in0=SCF[:, 1024:1536], in1=SCF[:, 0:512],
                                op=ALU.subtract)
            VE(v1)
            m2 = fw.marks()

            def s2(s):
                s.wait_ge(fw.F["vector"], m2["vector"])
                s.activation(out=SINT[:], in_=SCF[:, 512:1024], func=AF.Sin, scale=TWO_PI)
                s.activation(out=COST[:], in_=SCF[:, 1024:1536], func=AF.Sin, scale=TWO_PI)
                s.activation(out=EXPT[:], in_=pexp, func=AF.Exp)
            SC(s2)
            return fw.marks()

        def bp_family(mk_sc):
            def v(v):
                v.wait_ge(fw.F["scalar"], mk_sc["scalar"])
                v.tensor_tensor(out=SCF[:, 0:512], in0=EXPT[:], in1=COST[:], op=ALU.mult)
                v.tensor_tensor(out=SCF[:, 512:1024], in0=EXPT[:], in1=SINT[:], op=ALU.mult)
                v.tensor_tensor(out=SCF[:, 1024:1536], in0=CW2R, in1=SCF[:, 0:512], op=ALU.mult)
                v.tensor_tensor(out=TMP2B[:].bitcast(F32), in0=CW2I, in1=SCF[:, 512:1024], op=ALU.mult)
                v.tensor_tensor(out=TBB[:, 0:1024:2], in0=SCF[:, 1024:1536], in1=TMP2B[:].bitcast(F32), op=ALU.subtract)
                v.tensor_tensor(out=SCF[:, 1024:1536], in0=CW2R, in1=SCF[:, 512:1024], op=ALU.mult)
                v.tensor_tensor(out=TMP2B[:].bitcast(F32), in0=CW2I, in1=SCF[:, 0:512], op=ALU.mult)
                v.tensor_tensor(out=SCF[:, 1024:1536], in0=SCF[:, 1024:1536], in1=TMP2B[:].bitcast(F32), op=ALU.add)
                v.tensor_scalar(out=TBB[:, 1:1024:2], in0=SCF[:, 1024:1536], scalar1=-1.0,
                                scalar2=None, op0=ALU.mult)
            VE(v)
            return fw.marks()

        raws_st = {"n": 0}
        rowp = {"v": 0, "t": 0, "kev_v": 0, "bp_s": 0, "ga_v": 0}
        SCOPE("p2b_slabs")
        for ho in range(SLABS):
            if ho == 0:
                SCOPE("sw1_0")
            elif ho == 1:
                SCOPE("slabs_rest")
            for g in range(8):
                h0 = ho * 128 + g * 16
                seg = h0 * N2
                mv = fw.marks()
                fw.dma("sync", AIMSEG, hb_aim[0:1, seg:seg + 512], wait=[("vector", mv["vector"]), ("tensor", mv["tensor"]), ("scalar", mv["scalar"])])
                fw.dma("sync", ARESEG, hb_are[0:1, seg:seg + 512])
                fw.dma("sync", CW2R, AP(tensor=hb_cwr.tensor, offset=seg, ap=[[0, 128], [1, 512]]))
                fw.dma("sync", CW2I, AP(tensor=hb_cwi.tensor, offset=seg, ap=[[0, 128], [1, 512]]))
                ndma = fw.dn["sync"]

                def args1(t, ndma=ndma, mv=mv):
                    t.wait_ge(fw.Dq["sync"], ndma)
                    t.wait_ge(fw.F["vector"], mv["vector"])
                    t.matmul(P0[:], IOTAS[0:1, 0:128], AIMSEG, start=True, stop=True)
                    t.matmul(P2[:], IOTAS[0:1, 0:128], ARESEG, start=True, stop=True)
                    t.matmul(P1[:], IOTAS[0:1, 128:256], AIMSEG, start=True, stop=True)
                    t.matmul(P3[:], IOTAS[0:1, 128:256], ARESEG, start=True, stop=True)
                    t.mark()
                TE(args1)
                mk_a = fw.marks()["tensor"]
                mk_f1 = chain(P0[:], P2[:], mk_a)

                def f1w(v, mk_f1=mk_f1):
                    v.wait_ge(fw.F["scalar"], mk_f1["scalar"])
                    v.tensor_tensor(out=TBL[:, 0:1024:2], in0=EXPT[:], in1=COST[:], op=ALU.mult)
                    v.tensor_tensor(out=TBL[:, 1:1024:2], in0=EXPT[:], in1=SINT[:], op=ALU.mult)
                VE(f1w)
                mk_bp = chain(P1[:], P3[:], mk_a)
                bp_family(mk_bp)
                mtab = fw.marks()
                rowp["v"], rowp["t"] = mtab["vector"], mk_a
                rowp["bp_s"] = mk_bp["scalar"]

                def tr1(t, mtab=mtab):
                    t.wait_ge(fw.F["vector"], mtab["vector"])
                    for k in range(8):
                        t.transpose(PB0[0:64, k * 128:(k + 1) * 128], TBL[:, k * 64:(k + 1) * 64], IDENT[:])
                    t.mark()
                    for k in range(8):
                        t.transpose(PB1[0:64, k * 128:(k + 1) * 128], TBL[:, 512 + k * 64:512 + (k + 1) * 64], IDENT[:])
                    t.mark()
                TE(tr1)
                mtr1 = fw.marks()["tensor"]

                def evga(v, mtr1=mtr1):
                    v.wait_ge(fw.F["tensor"], mtr1)
                    v.tensor_copy(out=TGA[:, 0:1024], in_=PB0[0:64, :])
                    v.tensor_copy(out=TGA[:, 1024:2048], in_=PB1[0:64, :])
                VE(evga)
                mevga = fw.marks()

                def tr2(t, mevga=mevga):
                    t.wait_ge(fw.F["vector"], mevga["vector"])
                    for k in range(8):
                        t.transpose(PB0[0:64, k * 128:(k + 1) * 128], TBB[:, k * 64:(k + 1) * 64], IDENT[:])
                    t.mark()
                    for k in range(8):
                        t.transpose(PB1[0:64, k * 128:(k + 1) * 128], TBB[:, 512 + k * 64:512 + (k + 1) * 64], IDENT[:])
                    t.mark()
                TE(tr2)
                mtr2 = fw.marks()["tensor"]

                def evgb(v, mtr2=mtr2):
                    v.wait_ge(fw.F["tensor"], mtr2)
                    v.tensor_copy(out=TGB[:, 0:1024], in_=PB0[0:64, :])
                    v.tensor_copy(out=TGB[:, 1024:2048], in_=PB1[0:64, :])
                VE(evgb)
                mevgb = fw.marks()

                def kmm(t, mevgb=mevgb):
                    t.wait_ge(fw.F["vector"], mevgb["vector"])
                    for k in range(16):
                        pb = PW[k // 4]
                        t.matmul(pb[:, (k % 4) * 128:(k % 4 + 1) * 128],
                                 TGA[:, k * 128:(k + 1) * 128], TGB[:, k * 128:(k + 1) * 128],
                                 start=True, stop=True)
                    t.mark()
                TE(kmm)
                mkmm = fw.marks()["tensor"]

                def kev(v, mkmm=mkmm):
                    v.wait_ge(fw.F["tensor"], mkmm)
                    for q, pb in enumerate(PW):
                        mask4 = AP(tensor=MASKT[:].tensor, offset=MASKT[:].offset,
                                   ap=[[128, 128], [0, 4], [1, 128]])
                        dst = (K8A, K8B)[q // 2]
                        v.tensor_tensor(out=dst[:, (q % 2) * 512:(q % 2 + 1) * 512].rearrange("p (a b) -> p a b", a=4),
                                        in0=pb[:].rearrange("p (a b) -> p a b", a=4), in1=mask4, op=ALU.mult)
                VE(kev)
                mkev = fw.marks()
                rowp["kev_v"] = mkev["vector"]

                def gimm(t, mkev=mkev, h0=h0):
                    t.wait_ge(fw.F["vector"], mkev["vector"])
                    for k in range(16):
                        t.matmul(P4[0:64, k * 32:(k + 1) * 32], TBL[:, k * 64:(k + 1) * 64],
                                 u_cols(h0 + k), start=True, stop=True)
                    t.mark()
                    for k in range(16):
                        kt = (K8A, K8B)[k // 8]
                        t.matmul(P5[:, k * 32:(k + 1) * 32], kt[:, (k % 8) * 128:(k % 8 + 1) * 128],
                                 u_cols(h0 + k), start=True, stop=True)
                    t.mark()
                TE(gimm)
                mgi = fw.marks()["tensor"]

                def bdev(v, mgi=mgi, h0=h0, prevst=raws_st["n"]):
                    v.wait_ge(fw.F["tensor"], mgi)
                    if prevst:
                        v.wait_ge(fw.Dq["gpsimd"], prevst)
                    h0l = h0 % 128
                    dstB = AP(tensor=BSLAB.tensor, offset=BSLAB.offset + h0l * 2,
                              ap=[[16384, 64], [2, 16], [1, 2], [256, 16]])
                    v.tensor_copy(out=dstB, in_=P4[0:64, 0:512].rearrange("p (a b c) -> p a b c", a=16, b=2))
                    v.tensor_tensor(out=SINT[:], in0=u_grp(h0), in1=dbc_grp(h0), op=ALU.mult)
                    v.tensor_tensor(out=RAWS[:], in0=SINT[:], in1=P5[:], op=ALU.add)
                VE(bdev)
                mbd = fw.marks()
                fw.dma("gpsimd", hb_slab[ho, :, g * 512:(g + 1) * 512], RAWS[:],
                       wait=[("vector", mbd["vector"])])
                raws_st["n"] = fw.dn["gpsimd"]

            if ho == 0:
                SCOPE("rec_0")
            mk_sw1 = fw.marks()
            ngp1 = fw.dn["gpsimd"]

            def rinit(v):
                v.memset(SSTa, 0.0)
            VE(rinit)
            lt_re = PRM[:, 1024 + ho * 32:1024 + (ho + 1) * 32]
            lt_im = PRM[:, 1280 + ho * 32:1280 + (ho + 1) * 32]

            def lt_ap(base):
                return AP(tensor=base.tensor, offset=base.offset,
                          ap=[[base.ap[0][0], 128], [0, 2], [1, 32]])

            def sview(tns, pm):
                return AP(tensor=tns.tensor, offset=tns.offset + pm, ap=[[tns.ap[0][0], 128], [64, 2], [2, 32]])

            for c in range(NCH):
                cur, nxt = (SSTa, SSTb) if c % 2 == 0 else (SSTb, SSTa)
                mvv = fw.marks()

                def rtr(t, c=c, cur=cur, mvv=mvv):
                    t.wait_ge(fw.F["vector"], mvv["vector"])
                    for b in range(B2):
                        t.transpose(P4[0:64, b * 128:(b + 1) * 128], cur[:, b * 64:(b + 1) * 64], IDENTF[:])
                    for b in range(B2):
                        src = AP(tensor=BSLAB.tensor, offset=BSLAB.offset + c * 256 + b,
                                 ap=[[16384, 64], [2, 128]])
                        t.transpose(PB0[:, b * 64:(b + 1) * 64], src, IDENT[0:64, 0:64])
                    t.mark()
                TE(rtr)
                mrt = fw.marks()["tensor"]

                def rup(v, c=c, cur=cur, nxt=nxt, mrt=mrt, lt_re=lt_re, lt_im=lt_im):
                    v.wait_ge(fw.F["tensor"], mrt)
                    dstS = AP(tensor=SSLAB.tensor, offset=SSLAB.offset + c * 256,
                              ap=[[16384, 64], [1, 2], [2, 128]])
                    srcS = AP(tensor=P4[:].tensor, offset=P4[:].offset,
                              ap=[[512, 64], [128, 2], [1, 128]])
                    v.tensor_copy(out=dstS, in_=srcS)
                    pb_re = AP(tensor=PB0[:].tensor, offset=PB0[:].offset, ap=[[1024, 128], [64, 2], [2, 32]])
                    pb_im = AP(tensor=PB0[:].tensor, offset=PB0[:].offset + 1, ap=[[1024, 128], [64, 2], [2, 32]])
                    v.tensor_tensor(out=TMPQ[:, 0:64].rearrange("p (a b) -> p a b", a=2), in0=sview(cur, 0), in1=lt_ap(lt_re), op=ALU.mult)
                    v.tensor_tensor(out=TMPQ[:, 64:128].rearrange("p (a b) -> p a b", a=2), in0=sview(cur, 1), in1=lt_ap(lt_im), op=ALU.mult)
                    v.tensor_tensor(out=TMPQ[:, 128:192], in0=TMPQ[:, 0:64], in1=TMPQ[:, 64:128], op=ALU.subtract)
                    v.tensor_tensor(out=sview(nxt, 0), in0=TMPQ[:, 128:192].rearrange("p (a b) -> p a b", a=2), in1=pb_re, op=ALU.add)
                    v.tensor_tensor(out=TMPQ[:, 0:64].rearrange("p (a b) -> p a b", a=2), in0=sview(cur, 1), in1=lt_ap(lt_re), op=ALU.mult)
                    v.tensor_tensor(out=TMPQ[:, 64:128].rearrange("p (a b) -> p a b", a=2), in0=sview(cur, 0), in1=lt_ap(lt_im), op=ALU.mult)
                    v.tensor_tensor(out=TMPQ[:, 128:192], in0=TMPQ[:, 0:64], in1=TMPQ[:, 64:128], op=ALU.add)
                    v.tensor_tensor(out=sview(nxt, 1), in0=TMPQ[:, 128:192].rearrange("p (a b) -> p a b", a=2), in1=pb_im, op=ALU.add)
                VE(rup)

            if ho == 0:
                SCOPE("sw2_0")
            for g in range(8):
                h0 = ho * 128 + g * 16
                seg = h0 * N2
                mv = fw.marks()
                fw.dma("sync", AIMSEG, hb_aim[0:1, seg:seg + 512], wait=[("vector", mv["vector"]), ("tensor", mv["tensor"]), ("scalar", mv["scalar"])])
                fw.dma("sync", ARESEG, hb_are[0:1, seg:seg + 512])
                fw.dma("sync", CW2R, AP(tensor=hb_cwr.tensor, offset=seg, ap=[[0, 128], [1, 512]]))
                fw.dma("sync", CW2I, AP(tensor=hb_cwi.tensor, offset=seg, ap=[[0, 128], [1, 512]]))
                ndma = fw.dn["sync"]
                fw.dma("gpsimd", RAWS[:], hb_slab[ho, :, g * 512:(g + 1) * 512],
                       wait=[("vector", mv["vector"]), ("q:gpsimd", raws_st["n"])])
                ngp = fw.dn["gpsimd"]

                def args2(t, ndma=ndma, mv=mv):
                    t.wait_ge(fw.Dq["sync"], ndma)
                    t.wait_ge(fw.F["vector"], mv["vector"])
                    t.matmul(P0[:], IOTAS[0:1, 256:384], AIMSEG, start=True, stop=True)
                    t.matmul(P1[:], IOTAS[0:1, 256:384], ARESEG, start=True, stop=True)
                    t.mark()
                TE(args2)
                mk_a2 = fw.marks()["tensor"]
                mk_p = chain(P0[:], P1[:], mk_a2)
                bp_family(mk_p)
                mtab2 = fw.marks()
                rowp["v"], rowp["t"] = mtab2["vector"], mk_a2
                rowp["bp_s"] = mk_p["scalar"]

                def tr3(t, mtab2=mtab2):
                    t.wait_ge(fw.F["vector"], mtab2["vector"])
                    for k in range(8):
                        t.transpose(PB0[0:64, k * 128:(k + 1) * 128], TBB[:, k * 64:(k + 1) * 64], IDENT[:])
                    t.mark()
                    for k in range(8):
                        t.transpose(PB1[0:64, k * 128:(k + 1) * 128], TBB[:, 512 + k * 64:512 + (k + 1) * 64], IDENT[:])
                    t.mark()
                TE(tr3)
                mtr3 = fw.marks()["tensor"]

                def evp(v, mtr3=mtr3):
                    v.wait_ge(fw.F["tensor"], mtr3)
                    v.tensor_copy(out=TGB[:, 0:1024], in_=PB0[0:64, :])
                    v.tensor_copy(out=TGB[:, 1024:2048], in_=PB1[0:64, :])
                VE(evp)
                mevp = fw.marks()

                def scmm(t, mevp=mevp, h0=h0, ngp=ngp):
                    t.wait_ge(fw.F["vector"], mevp["vector"])
                    t.wait_ge(fw.Dq["gpsimd"], ngp)
                    h0l = h0 % 128
                    for k in range(16):
                        sap = AP(tensor=SSLAB.tensor, offset=SSLAB.offset + (h0l + k) * 2,
                                 ap=[[16384, 64], [1, 2], [256, 16]])
                        t.matmul(P5[:, k * 32:(k + 1) * 32], TGB[:, k * 128:(k + 1) * 128],
                                 sap, start=True, stop=True)
                    t.mark()
                TE(scmm)
                msc = fw.marks()["tensor"]

                def gadd(v, msc=msc):
                    v.wait_ge(fw.F["tensor"], msc)
                    v.tensor_tensor(out=TMP2B[:, 0:512], in0=RAWS[:], in1=P5[:], op=ALU.add)
                VE(gadd)
                mga = fw.marks()
                rowp["ga_v"] = mga["vector"]
                rowp["kev_v"] = mga["vector"]

                def gel(s, mga=mga, h0=h0):
                    s.wait_ge(fw.F["vector"], mga["vector"])
                    h0l = h0 % 128
                    dstG = AP(tensor=GOSL.tensor, offset=GOSL.offset + h0l,
                              ap=[[16384, 128], [1, 16], [2048, 2], [128, 16]])
                    s.activation(out=dstG, in_=TMP2B[:, 0:512].rearrange("p (a b c) -> p a b c", a=16, b=2),
                                 func=AF.Gelu)
                SC(gel)

            if ho == 0:
                SCOPE("fmaj_0")
            mgel = fw.marks()
            fev_marks = {}
            for half in range(4):
                b_, ch_ = divmod(half, 2)
                pbuf = PB0 if half % 2 == 0 else PB1

                def ftr(t, b_=b_, ch_=ch_, pbuf=pbuf, half=half, mgel=mgel, fev_marks=fev_marks):
                    if half == 0:
                        t.wait_ge(fw.F["scalar"], mgel["scalar"])
                    if half >= 2:
                        t.wait_ge(fw.F["vector"], fev_marks[half - 2])
                    for k in range(8):
                        cc = ch_ * 8 + k
                        t.transpose(pbuf[:, k * 128:(k + 1) * 128],
                                    GOSL[:, b_ * 2048 + cc * 128:b_ * 2048 + (cc + 1) * 128], IDENT[:])
                    t.mark()
                TE(ftr)
                mtf = fw.marks()["tensor"]

                def fev(v, b_=b_, ch_=ch_, pbuf=pbuf, ho=ho, mtf=mtf):
                    v.wait_ge(fw.F["tensor"], mtf)
                    base = (b_ * 8 + ho) * 2048 + ch_ * 1024
                    v.tensor_copy(out=R2[:, base:base + 1024], in_=pbuf[:])
                VE(fev)
                fev_marks[half] = fw.marks()["vector"]
        fw.barrier()
        if PHASES < 4:
            return _finish(nc, fw, dbg_outs, dbg, R1, R2, R3, STM, PRM, BCB, AUX, hb_aim, hb_are, hb_cwr, hb_cwi, DBC)

        _ctx = dict(locals())
        _ctx["PHASES"] = PHASES
        _part2_emit(_ctx)
        SCOPE(None)
        fw.barrier()

        u_dbg = dbg("u_dbg", [128, 32768], BF16)
        if u_dbg is not None:
            fw.dma("gpsimd", u_dbg[:], R1[:])
        m_dbg = dbg("m_dbg", [128, 96], F32)
        if m_dbg is not None:
            fw.dma("gpsimd", m_dbg[:], STM[:])
        prep_dbg = dbg("prep_dbg", [128, 2048], F32)
        if prep_dbg is not None:
            fw.dma("gpsimd", prep_dbg[:], PRM[:])
        convf_dbg = dbg("convf_dbg", [128, 32768], BF16)
        if convf_dbg is not None:
            fw.dma("gpsimd", convf_dbg[:], R2[:])
        x1_dbg = dbg("x1_dbg", [128, 32768], BF16)
        if x1_dbg is not None:
            fw.dma("gpsimd", x1_dbg[:], R2[:])
        x2_dbg = dbg("x2_dbg", [128, 32768], BF16)
        if x2_dbg is not None:
            fw.dma("gpsimd", x2_dbg[:], R2[:])
        xz_dbg = dbg("xz_dbg", [128, 32768], BF16)
        if xz_dbg is not None:
            fw.dma("gpsimd", xz_dbg[:], R1[:])
        ys_dbg = dbg("ys_dbg", [128, 16384], BF16)
        if ys_dbg is not None:
            fw.dma("gpsimd", ys_dbg[:], R3[:])
        x3_dbg = dbg("x3_dbg", [128, 32768], BF16)
        if x3_dbg is not None:
            fw.dma("gpsimd", x3_dbg[:], R2[:])
        for nm, src, shp, dt_ in (
                ("scan_dbg", AUX[:, 0:4096].bitcast(F32), [128, 2048], F32),
                ("cbc2_dbg", AUX[:, 14336:14464].bitcast(F32), [128, 64], F32),
                ("hst_dbg", AUX[:, 4096:8192].bitcast(F32), [128, 2048], F32)):
            tp = dbg(nm, shp, dt_)
            if tp is not None:
                fw.dma("gpsimd", tp[:], src)
        fw.barrier()

    return nc, dbg_outs


def _finish(nc, fw, dbg_outs, dbg, R1, R2, R3, STM, PRM, BCB, AUX, hb_aim, hb_are, hb_cwr, hb_cwi, DBC=None):
    taps = {
        "u_dbg": (R1, [128, 32768], BF16), "m_dbg": (STM, [128, 96], F32),
        "prep_dbg": (PRM, [128, 2048], F32), "convf_dbg": (R2, [128, 32768], BF16),
        "k8_dbg": (R3[:, 8192:10240], [128, 2048], BF16),
        "bslab_dbg": (R3[0:64, 0:4096], [64, 4096], BF16),
        "sslab_dbg": (R3[0:64, 4096:8192], [64, 4096], BF16),
        "gosl_dbg": (R3[:, 10240:14336], [128, 4096], BF16),
        "dbc_dbg": (DBC, [128, 1024], BF16),
        "tbl_dbg": (AUX[:, 0:2048], [128, 2048], BF16),
        "tga_dbg": (AUX[0:64, 16384:20480], [64, 4096], BF16),
        "scf_dbg": (AUX[:, 4096:8192].bitcast(F32), [128, 2048], F32),
        "trig_dbg": (AUX[:, 2048:4096], [128, 2048], BF16),
        "scan_dbg": (AUX[:, 0:4096].bitcast(F32), [128, 2048], F32),
        "cbc2_dbg": (AUX[:, 14336:14464].bitcast(F32), [128, 64], F32),
        "hst_dbg": (AUX[:, 4096:8192].bitcast(F32), [128, 2048], F32),
    }
    for name, (tns, shape, dt) in taps.items():
        t = dbg(name, shape, dt)
        if t is not None:
            fw.dma("gpsimd", t[:], tns if isinstance(tns, AP) else tns[:])
    rows_dbg = dbg("rows_dbg", [4, 32768], F32)
    if rows_dbg is not None:
        fw.dma("gpsimd", rows_dbg[0:1, :], hb_aim[:])
        fw.dma("gpsimd", rows_dbg[1:2, :], hb_are[:])
        fw.dma("gpsimd", rows_dbg[2:3, :], hb_cwr[:])
        fw.dma("gpsimd", rows_dbg[3:4, :], hb_cwi[:])
    fw.barrier()
    return nc, dbg_outs


def make_consts():
    ident = np.eye(128, dtype=ml_dtypes.bfloat16)
    iota_negj = (-np.arange(128, dtype=np.float32))[None, :]
    iota_t = np.arange(128, dtype=np.float32)[None, :]
    iota_t128 = (np.arange(128, dtype=np.float32) + 128.0)[None, :]
    j = np.arange(128)[:, None]; t = np.arange(128)[None, :]
    mask_triu = (t >= j).astype(ml_dtypes.bfloat16)
    ones_row = np.ones((1, 128), np.float32)
    return dict(identb=ident, iota_negj=iota_negj, iota_t=iota_t, iota_t128=iota_t128,
                mask_triu=mask_triu, ones_row=ones_row)


def core_inputs(inputs, core):
    d = {}
    d["x_in"] = np.ascontiguousarray(inputs["L0_attn_a_u"][2 * core:2 * core + 2])
    d["log_dt"] = np.ascontiguousarray(inputs["L0_attn_a_log_dt"]).reshape(1, D)
    d["log_w_real"] = np.ascontiguousarray(inputs["L0_attn_a_log_w_real"]).reshape(1, N2)
    d["w_imag"] = np.ascontiguousarray(inputs["L0_attn_a_w_imag"]).reshape(1, N2)
    d["C_real"] = np.ascontiguousarray(inputs["L0_attn_a_C_real"]).reshape(D, N2)
    d["C_imag"] = np.ascontiguousarray(inputs["L0_attn_a_C_imag"]).reshape(D, N2)
    d["D_a"] = np.ascontiguousarray(inputs["L0_attn_a_D"]).reshape(1, D)
    d["W_out_a"] = np.ascontiguousarray(inputs["L0_attn_a_W_out"]).reshape(D, D)
    d["b_out_a"] = np.ascontiguousarray(inputs["L0_attn_a_b_out"]).reshape(1, D)
    d["tern"] = np.ascontiguousarray(inputs["L0_ffn_ternary_weight"]).reshape(D, D)
    d["bias_f"] = np.ascontiguousarray(inputs["L0_ffn_bias"]).reshape(1, D)
    d["act_scale"] = np.asarray(inputs["L0_ffn_act_scale"], np.float32).reshape(1, 1)
    d["w_in_b"] = np.ascontiguousarray(inputs["L0_attn_b_w_in"]).reshape(2 * D, D)
    d["b_in_b"] = np.ascontiguousarray(inputs["L0_attn_b_b_in"]).reshape(1, 2 * D)
    d["A_b"] = np.ascontiguousarray(inputs["L0_attn_b_A"]).reshape(D, S64)
    d["Bm_b"] = np.ascontiguousarray(inputs["L0_attn_b_Bm"]).reshape(D, S64)
    d["Cm_b"] = np.ascontiguousarray(inputs["L0_attn_b_Cm"]).reshape(D, S64)
    d["Dv_b"] = np.ascontiguousarray(inputs["L0_attn_b_Dv"]).reshape(1, D)
    d["w_out_b"] = np.ascontiguousarray(inputs["L0_attn_b_w_out"]).reshape(D, D)
    d["b_out_b"] = np.ascontiguousarray(inputs["L0_attn_b_b_out"]).reshape(1, D)
    d["g_norm"] = np.ascontiguousarray(inputs["L0_attn_b_g_norm"]).reshape(1, D)
    d.update(make_consts())
    return d


def kernel(**inputs):
    nc, _ = build_program()
    ins = [core_inputs(inputs, i) for i in range(8)]
    res = run_bass_kernel_spmd(nc, ins, core_ids=list(range(8)))
    parts = [np.asarray(res.results[i]["out"]).reshape(B2, L, D) for i in range(8)]
    return np.concatenate(parts, axis=0).astype(np.float32)


# version 07 (final)
# speedup vs baseline: 1.2075x; 1.0803x over previous
import math
import numpy as np
from contextlib import ExitStack

import ml_dtypes
import concourse.bass as bass
import concourse.mybir as mybir
from concourse.bass import AP
from concourse.bass_utils import run_bass_kernel_spmd



ENGS = ("tensor", "vector", "scalar", "gpsimd", "sync")
COMPUTE_ENGS = ("tensor", "vector", "scalar", "gpsimd")
OPS = {
    "tensor_tensor", "tensor_scalar", "tensor_copy", "tensor_reduce", "reduce_sum",
    "reduce_max", "scalar_tensor_tensor", "tensor_tensor_reduce", "bn_stats",
    "bn_aggr", "reciprocal", "memset", "select", "copy_predicated", "iota",
    "activation", "copy", "add", "mul", "sign", "sqrt", "square",
    "matmul", "transpose", "ldweights", "tensor_tensor_scan", "stream_shuffle",
    "tensor_add", "tensor_mul", "tensor_sub", "tensor_max", "tensor_relu",
    "tensor_scalar_mul", "tensor_scalar_add", "partition_broadcast",
    "partition_all_reduce",
}


class Eng:

    def __init__(self, fw, name, handle):
        self._fw = fw
        self._name = name
        self._h = handle
        self.last = None

    def __getattr__(self, attr):
        real = getattr(self._h, attr)
        if attr not in OPS:
            return real
        fw, name = self._fw, self._name

        def call(*a, **k):
            inst = real(*a, **k)
            self.last = inst
            if name in ("vector", "scalar", "gpsimd"):
                nb = fw.n[name]
                if nb > 0:
                    inst._wait_ge(fw.F[name], nb)
                inst.then_inc(fw.F[name], 1)
                fw.n[name] += 1
            return inst
        return call

    def mark(self):
        assert self._name == "tensor"
        assert self.last is not None
        self.last.then_inc(self._fw.F["tensor"], 1)
        self._fw.n["tensor"] += 1
        self.last = None
        return self._fw.n["tensor"]


class FW:
    def __init__(self, nc, st, blk):
        self.nc = nc
        self.blk = blk
        self.F = {e: st.enter_context(nc.semaphore(f"F_{e}")) for e in COMPUTE_ENGS}
        self.n = {e: 0 for e in COMPUTE_ENGS}
        self.Dq = {q: st.enter_context(nc.semaphore(f"D_{q}")) for q in ("sync", "gpsimd")}
        self.dn = {q: 0 for q in ("sync", "gpsimd")}
        self._eng_cache = {}

    def _sect(self, name, body):
        getattr(self.blk, name)(body)

    def sec(self, name):
        raise NotImplementedError

    def run(self, name, fn):
        out = {}

        def body(h):
            out["p"] = Eng(self, name, h)
            fn(out["p"])
        self._sect(name, body)
        return out.get("p")

    def dma(self, qname, out_ap, in_ap, wait=()):
        def body(h):
            for (e, m) in wait:
                if m > 0:
                    if e.startswith("q:"):
                        h.wait_ge(self.Dq[e[2:]], m)
                    else:
                        h.wait_ge(self.F[e], m)
            h.dma_start(out=out_ap, in_=in_ap).then_inc(self.Dq[qname], 16)
        self._sect(qname, body)
        self.dn[qname] += 16
        return self.dn[qname]

    def wait(self, eng, deps=(), dmas=()):
        def body(h):
            for (e, m) in deps:
                if m > 0:
                    h.wait_ge(self.F[e], m)
            for (q, c) in dmas:
                if c > 0:
                    h.wait_ge(self.Dq[q], c)
        self._sect(eng, body)

    def marks(self):
        return dict(self.n)

    def barrier(self):
        snap_n = dict(self.n)
        snap_d = dict(self.dn)
        for e in COMPUTE_ENGS:
            def body(h, e=e):
                for o in COMPUTE_ENGS:
                    if o != e and snap_n[o] > 0:
                        h.wait_ge(self.F[o], snap_n[o])
                for q in ("sync", "gpsimd"):
                    if snap_d[q] > 0:
                        h.wait_ge(self.Dq[q], snap_d[q])
            self._sect(e, body)
        for q in ("sync", "gpsimd"):
            def body(h, q=q):
                for o in COMPUTE_ENGS:
                    if snap_n[o] > 0:
                        h.wait_ge(self.F[o], snap_n[o])
                oq = "gpsimd" if q == "sync" else "sync"
                if snap_d[oq] > 0:
                    h.wait_ge(self.Dq[oq], snap_d[oq])
            self._sect(q, body)


def _part2_emit(ctx):
    fw = ctx["fw"]; nc = ctx["nc"]
    VE, SC, TE, GP = ctx["VE"], ctx["SC"], ctx["TE"], ctx["GP"]
    R1, R2, R3, AUX = ctx["R1"], ctx["R2"], ctx["R3"], ctx["AUX"]
    IDENT, IDENTF, IOTAS, ONEC = ctx["IDENT"], ctx["IDENTF"], ctx["IOTAS"], ctx["ONEC"]
    BOB, SML, STT = ctx["BOB"], ctx["SML"], ctx["STT"]
    ASC, BSC, CBC, HST, ZST, GST, T1S, HCS = (ctx["ASC"], ctx["BSC"], ctx["CBC"],
                                              ctx["HST"], ctx["ZST"], ctx["GST"],
                                              ctx["T1S"], ctx["HCS"])
    P0, P1, P2, P3, P4, P5 = ctx["P0"], ctx["P1"], ctx["P2"], ctx["P3"], ctx["P4"], ctx["P5"]
    PB0, PB1, PW = ctx["PB0"], ctx["PB1"], ctx["PW"]
    OTS = None
    W_out_a, tern, w_in_b, w_out_b = ctx["W_out_a"], ctx["tern"], ctx["w_in_b"], ctx["w_out_b"]
    A_b, Bm_b, Cm_b = ctx["A_b"], ctx["Bm_b"], ctx["Cm_b"]
    x_in, out = ctx["x_in"], ctx["out"]
    hb_winT = ctx["hb_winT"]
    dbgf = ctx["dbg"]
    PHASES = ctx["PHASES"] if "PHASES" in ctx else 99

    SCOPE = ctx["SCOPE"]

    def sbslice(col0, col1):
        return AUX[:, col0:col1]

    def wprep(Wsrc, dst, is_tern=False):
        WSTG = AUX[:, 8192:10240]
        for oc in range(8):
            slot = oc % 2
            stg = WSTG[:, slot * 1024:(slot + 1) * 1024]
            mprev = fw.marks()
            nld = fw.dma("gpsimd", stg, Wsrc[oc * 128:(oc + 1) * 128, :],
                         wait=[("vector", mprev["vector"]), ("tensor", mprev["tensor"])])
            if is_tern:
                def cvt(v, stg=stg, nld=nld):
                    v.wait_ge(fw.Dq["gpsimd"], nld)
                    v.tensor_scalar(out=stg, in0=stg, scalar1=-1.0, scalar2=None, op0=ALU.add)
                VE(cvt)
            mk = fw.marks()

            def trw(t, stg=stg, nld=nld, mk=mk, is_tern=is_tern):
                if not is_tern:
                    t.wait_ge(fw.Dq["gpsimd"], nld)
                else:
                    t.wait_ge(fw.F["vector"], mk["vector"])
                for k in range(8):
                    t.transpose(PB0[:, k * 128:(k + 1) * 128], stg[:, k * 128:(k + 1) * 128], IDENT[:])
                t.mark()
            TE(trw)
            mt = fw.marks()["tensor"]

            def evw(v, oc=oc, mt=mt, dst=dst):
                v.wait_ge(fw.F["tensor"], mt)
                dap = AP(tensor=dst.tensor, offset=dst.offset + oc * 128,
                         ap=[[dst.ap[0][0], 128], [1024, 8], [1, 128]])
                v.tensor_copy(out=dap, in_=PB0[:].rearrange("p (a b) -> p a b", a=8))
            VE(evw)

    SCOPE("p2g_wout")
    WoT = AUX[:, 0:8192]
    XS = AUX[:, 10240:14336]
    X1S = AUX[:, 14336:18432]
    XTT = AUX[:, 18432:20480]
    wprep(W_out_a, WoT)
    for tau in range(8):
        b, q = divmod(tau, 4)
        for j in range(4):
            c = q * 4 + j
            slot = j % 2
            xstg = XTT[:, slot * 1024:(slot + 1) * 1024]
            mprev = fw.marks()
            nld = fw.dma("gpsimd", xstg, x_in[b, c * 128:(c + 1) * 128, :],
                         wait=[("tensor", mprev["tensor"]), ("vector", mprev["vector"])])

            def trx(t, xstg=xstg, nld=nld):
                t.wait_ge(fw.Dq["gpsimd"], nld)
                for k in range(8):
                    t.transpose(PB1[:, k * 128:(k + 1) * 128], xstg[:, k * 128:(k + 1) * 128], IDENT[:])
                t.mark()
            TE(trx)
            mt = fw.marks()["tensor"]

            def evx(v, j=j, mt=mt):
                v.wait_ge(fw.F["tensor"], mt)
                dap = AP(tensor=XS.tensor, offset=XS.offset + j * 128,
                         ap=[[XS.ap[0][0], 128], [512, 8], [1, 128]])
                v.tensor_copy(out=dap, in_=PB1[:].rearrange("p (a b) -> p a b", a=8))
            VE(evx)
        mxs = fw.marks()
        for w in range(2):
            mev = fw.marks()

            def wmm(t, b=b, q=q, w=w, mev=mev):
                t.wait_ge(fw.F["vector"], mev["vector"])
                for oci in range(4):
                    oc = w * 4 + oci
                    for k in range(8):
                        t.matmul(PW[oci][:], WoT[:, (k * 8 + oc) * 128:(k * 8 + oc + 1) * 128],
                                 R2[:, (b * 8 + k) * 2048 + q * 512:(b * 8 + k) * 2048 + (q + 1) * 512],
                                 start=(k == 0), stop=(k == 7))
                t.mark()
            TE(wmm)
            mw = fw.marks()["tensor"]

            def wev(v, w=w, mw=mw):
                v.wait_ge(fw.F["tensor"], mw)
                for oci in range(4):
                    oc = w * 4 + oci
                    v.scalar_tensor_tensor(out=X1S[:, oc * 512:(oc + 1) * 512], in0=PW[oci][:],
                                           scalar=BOB[:, oc:oc + 1], in1=XS[:, oc * 512:(oc + 1) * 512],
                                           op0=ALU.add, op1=ALU.add)
            VE(wev)

        def x1wr(v, b=b, q=q):
            dap = AP(tensor=R2[:].tensor, offset=R2[:].offset + b * 16384 + q * 512,
                     ap=[[32768, 128], [2048, 8], [1, 512]])
            v.tensor_copy(out=dap, in_=X1S[:].rearrange("p (a b) -> p a b", a=8))
        VE(x1wr)
    fw.barrier()
    if PHASES < 5:
        return

    def sbar_prep():
        def v(v):
            v.tensor_scalar(out=SML[0:1, 49:50], in0=SML[0:1, 48:49], scalar1=1e-5,
                            scalar2=None, op0=ALU.max)
            v.reciprocal(out=SML[0:1, 50:51], in_=SML[0:1, 49:50])
        VE(v)
        mk = fw.marks()

        def t(t):
            t.wait_ge(fw.F["vector"], mk["vector"])
            t.matmul(P4[:, 0:1], IOTAS[0:1, 384:512], SML[0:1, 49:50], start=True, stop=True)
            t.mark()
        TE(t)
        mt = fw.marks()["tensor"]

        def v2(v):
            v.wait_ge(fw.F["tensor"], mt)
            v.tensor_copy(out=SML[:, 51:52], in_=P4[:, 0:1])
        VE(v2)
    ffn_state = {"sbar_done": False}

    def emit_ffn():
        TTW = AUX[:, 0:8192]
        XQ = AUX[:, 10240:14336]
        QF1 = AUX[:, 14336:15360].bitcast(F32)
        QF2 = AUX[:, 15360:16384].bitcast(F32)
        EPS_ = AUX[:, 16384:20480]
        SQS = EPS_[:, 0:1024]
        BCSM = AUX[:, 8192:9216].bitcast(F32)
        BCSR = AUX[:, 9216:10240].bitcast(F32)
        RWB = AUX[0:1, 16384:17408].bitcast(F32)
        wprep(tern, TTW, is_tern=True)
        if not ffn_state["sbar_done"]:
            sbar_prep()
            ffn_state["sbar_done"] = True
        for tau in range(8):
            b, q = divmod(tau, 4)

            def xdo(k):
                return R2[:, (b * 8 + k) * 2048 + q * 512:(b * 8 + k) * 2048 + (q + 1) * 512]
            smm_marks = {}
            for do in range(8):
                slot = SQS[:, (do % 2) * 512:(do % 2) * 512 + 512]

                def sq(v, do=do, slot=slot):
                    if do >= 2:
                        v.wait_ge(fw.F["tensor"], smm_marks[do - 2])
                    v.tensor_tensor(out=slot, in0=xdo(do), in1=xdo(do), op=ALU.mult)
                VE(sq)
                mq = fw.marks()

                def smm(t, do=do, mq=mq, slot=slot):
                    t.wait_ge(fw.F["vector"], mq["vector"])
                    t.matmul(P4[0:1, 0:512], ONEC[:, 0:1], xdo(do), start=(do == 0), stop=(do == 7))
                    t.matmul(P5[0:1, 0:512], ONEC[:, 0:1], slot, start=(do == 0), stop=(do == 7))
                    t.mark()
                TE(smm)
                smm_marks[do] = fw.marks()["tensor"]
            ms = fw.marks()["tensor"]

            def spost(v, ms=ms):
                v.wait_ge(fw.F["tensor"], ms)
                v.tensor_scalar(out=QF1[0:1, 0:512], in0=P4[0:1, 0:512], scalar1=1.0 / D,
                                scalar2=None, op0=ALU.mult)
                v.tensor_tensor(out=QF2[0:1, 0:512], in0=QF1[0:1, 0:512], in1=QF1[0:1, 0:512], op=ALU.mult)
                v.scalar_tensor_tensor(out=QF2[0:1, 0:512], in0=P5[0:1, 0:512], scalar=1.0 / D,
                                       in1=QF2[0:1, 0:512], op0=ALU.mult, op1=ALU.subtract)
                v.tensor_scalar(out=QF2[0:1, 0:512], in0=QF2[0:1, 0:512], scalar1=EPS_LN,
                                scalar2=None, op0=ALU.add)
                v.reciprocal(out=RWB[:], in_=QF2[0:1, 0:512])
            VE(spost)
            mv = fw.marks()

            def ssq(s, mv=mv):
                s.wait_ge(fw.F["vector"], mv["vector"])
                s.activation(out=QF2[0:1, 0:512], in_=RWB[:], func=AF.Sqrt)
            SC(ssq)
            mss = fw.marks()

            def rfold(v, mss=mss):
                v.wait_ge(fw.F["scalar"], mss["scalar"])
                v.tensor_scalar(out=QF2[0:1, 0:512], in0=QF2[0:1, 0:512],
                                scalar1=SML[0:1, 50:51], scalar2=None, op0=ALU.mult)
            VE(rfold)
            mr = fw.marks()

            def bcmm(t, mr=mr):
                t.wait_ge(fw.F["vector"], mr["vector"])
                t.matmul(P4[:], IOTAS[0:1, 384:512], QF1[0:1, 0:512], start=True, stop=True)
                t.matmul(P5[:], IOTAS[0:1, 384:512], QF2[0:1, 0:512], start=True, stop=True)
                t.mark()
            TE(bcmm)
            mbc = fw.marks()["tensor"]

            def bcev(v, mbc=mbc):
                v.wait_ge(fw.F["tensor"], mbc)
                v.tensor_copy(out=BCSM[:], in_=P4[:])
                v.tensor_copy(out=BCSR[:], in_=P5[:])
            VE(bcev)
            def qn(v, b=b, q=q):
                x8 = AP(tensor=R2[:].tensor, offset=R2[:].offset + b * 16384 + q * 512,
                        ap=[[32768, 128], [2048, 8], [1, 512]])
                mh = AP(tensor=BCSM[:].tensor, offset=BCSM[:].offset,
                        ap=[[BCSM[:].ap[0][0], 128], [0, 8], [1, 512]])
                rh = AP(tensor=BCSR[:].tensor, offset=BCSR[:].offset,
                        ap=[[BCSR[:].ap[0][0], 128], [0, 8], [1, 512]])
                SH = R3[:, 0:8192].bitcast(F32).rearrange("p (a b) -> p a b", a=8)
                GH = R3[:, 8192:16384].bitcast(F32).rearrange("p (a b) -> p a b", a=8)
                xq8 = XQ[:].rearrange("p (a b) -> p a b", a=8)
                v.tensor_tensor(out=SH, in0=x8, in1=mh, op=ALU.subtract)
                v.tensor_tensor(out=SH, in0=SH, in1=rh, op=ALU.mult)
                v.tensor_scalar(out=SH, in0=SH, scalar1=127.0,
                                scalar2=-127.0, op0=ALU.min, op1=ALU.max)
                v.tensor_scalar(out=GH, in0=SH, scalar1=MAGIC,
                                scalar2=-MAGIC, op0=ALU.add, op1=ALU.add)
                v.tensor_scalar(out=SH, in0=SH, scalar1=MAGIC64,
                                scalar2=-MAGIC64, op0=ALU.add, op1=ALU.add)
                v.tensor_tensor(out=SH, in0=SH, in1=GH, op=ALU.subtract)
                v.scalar_tensor_tensor(out=xq8, in0=SH, scalar=SML[:, 53:54], in1=GH,
                                       op0=ALU.mult, op1=ALU.add)
            VE(qn)
            for w in range(2):
                mev = fw.marks()

                def fmm(t, w=w, mev=mev):
                    t.wait_ge(fw.F["vector"], mev["vector"])
                    for oci in range(4):
                        oc = w * 4 + oci
                        for k in range(8):
                            t.matmul(PW[oci][:], TTW[:, (k * 8 + oc) * 128:(k * 8 + oc + 1) * 128],
                                     XQ[:, k * 512:(k + 1) * 512], start=(k == 0), stop=(k == 7))
                    t.mark()
                TE(fmm)
                mw = fw.marks()["tensor"]

                def fev(v, w=w, mw=mw, b=b, q=q):
                    v.wait_ge(fw.F["tensor"], mw)
                    for oci in range(4):
                        oc = w * 4 + oci
                        v.tensor_scalar(out=QF1[:, 0:512], in0=PW[oci][:], scalar1=SML[:, 51:52],
                                        scalar2=BOB[:, 8 + oc:8 + oc + 1], op0=ALU.mult, op1=ALU.add)
                        v.tensor_tensor(out=EPS_[:, oc * 512:(oc + 1) * 512], in0=QF1[:, 0:512],
                                        in1=xdo(oc), op=ALU.add)
                VE(fev)

            def x2wr(v, b=b, q=q):
                dap = AP(tensor=R2[:].tensor, offset=R2[:].offset + b * 16384 + q * 512,
                         ap=[[32768, 128], [2048, 8], [1, 512]])
                v.tensor_copy(out=dap, in_=EPS_[:].rearrange("p (a b) -> p a b", a=8))
            VE(x2wr)
        fw.barrier()

    SCOPE("p3_ffn1")
    emit_ffn()
    if PHASES < 6:
        return

    SCOPE("p4a_winT")
    WSTG = AUX[:, 8192:10240]
    for et in range(16):
        slot = et % 2
        stg = WSTG[:, slot * 1024:(slot + 1) * 1024]
        mprev = fw.marks()
        nld = fw.dma("gpsimd", stg, w_in_b[et * 128:(et + 1) * 128, :],
                     wait=[("vector", mprev["vector"]), ("tensor", mprev["tensor"])])

        def trwi(t, stg=stg, nld=nld):
            t.wait_ge(fw.Dq["gpsimd"], nld)
            for k in range(8):
                t.transpose(PB0[:, k * 128:(k + 1) * 128], stg[:, k * 128:(k + 1) * 128], IDENT[:])
            t.mark()
        TE(trwi)
        mt = fw.marks()["tensor"]

        def evwi(v, et=et, mt=mt):
            v.wait_ge(fw.F["tensor"], mt)
            dap = AP(tensor=R3[:].tensor, offset=R3[:].offset + et * 128,
                     ap=[[16384, 128], [2048, 8], [1, 128]])
            v.tensor_copy(out=dap, in_=PB0[:].rearrange("p (a b) -> p a b", a=8))
        VE(evwi)
    SCOPE("p4b_ln3_win")
    WIN2 = AUX[:, 0:4096]
    XSUB = AUX[:, 4096:8192]
    BCS4M = AUX[:, 8192:9216].bitcast(F32)
    BCS4R = AUX[:, 9216:10240].bitcast(F32)
    XN = AUX[:, 10240:14336]
    QF4 = AUX[:, 14336:15360].bitcast(F32)
    QF5 = AUX[:, 15360:16384].bitcast(F32)
    SQ4 = AUX[:, 16384:17408]
    RW3 = AUX[0:1, 17408:18432].bitcast(F32)

    for tau in range(4):
        b, qs = divmod(tau, 2)

        def xe(do):
            base = (b * 8 + do) * 2048 + qs * 1024
            return AP(tensor=R2[:].tensor, offset=R2[:].offset + base,
                      ap=[[32768, 128], [2, 512]])
        smm3_marks = {}
        for do in range(8):
            slot = SQ4[:, (do % 2) * 512:(do % 2) * 512 + 512]

            def cpx(v, do=do, slot=slot):
                if do >= 2:
                    v.wait_ge(fw.F["tensor"], smm3_marks[do - 2])
                v.tensor_copy(out=XSUB[:, do * 512:(do + 1) * 512], in_=xe(do))
                v.tensor_tensor(out=slot, in0=XSUB[:, do * 512:(do + 1) * 512],
                                in1=XSUB[:, do * 512:(do + 1) * 512], op=ALU.mult)
            VE(cpx)
            mq = fw.marks()

            def smm3(t, do=do, mq=mq, slot=slot):
                t.wait_ge(fw.F["vector"], mq["vector"])
                t.matmul(P4[0:1, 0:512], ONEC[:, 0:1], XSUB[:, do * 512:(do + 1) * 512],
                         start=(do == 0), stop=(do == 7))
                t.matmul(P5[0:1, 0:512], ONEC[:, 0:1], slot, start=(do == 0), stop=(do == 7))
                t.mark()
            TE(smm3)
            smm3_marks[do] = fw.marks()["tensor"]
        ms3 = fw.marks()["tensor"]

        def spost3(v, ms3=ms3):
            v.wait_ge(fw.F["tensor"], ms3)
            v.tensor_scalar(out=QF4[0:1, 0:512], in0=P4[0:1, 0:512], scalar1=1.0 / D,
                            scalar2=None, op0=ALU.mult)
            v.tensor_tensor(out=QF5[0:1, 0:512], in0=QF4[0:1, 0:512], in1=QF4[0:1, 0:512], op=ALU.mult)
            v.scalar_tensor_tensor(out=QF5[0:1, 0:512], in0=P5[0:1, 0:512], scalar=1.0 / D,
                                   in1=QF5[0:1, 0:512], op0=ALU.mult, op1=ALU.subtract)
            v.tensor_scalar(out=QF5[0:1, 0:512], in0=QF5[0:1, 0:512], scalar1=1.0 + EPS_RMS,
                            scalar2=EPS_RMS * EPS_LN, op0=ALU.mult, op1=ALU.add)
            v.reciprocal(out=RW3[:], in_=QF5[0:1, 0:512])
        VE(spost3)
        mv3 = fw.marks()

        def ssq3(s, mv3=mv3):
            s.wait_ge(fw.F["vector"], mv3["vector"])
            s.activation(out=QF5[0:1, 0:512], in_=RW3[:], func=AF.Sqrt)
        SC(ssq3)
        ms4 = fw.marks()

        def bc3(t, ms4=ms4):
            t.wait_ge(fw.F["scalar"], ms4["scalar"])
            t.matmul(P4[:], IOTAS[0:1, 384:512], QF4[0:1, 0:512], start=True, stop=True)
            t.matmul(P5[:], IOTAS[0:1, 384:512], QF5[0:1, 0:512], start=True, stop=True)
            t.mark()
        TE(bc3)
        mb3 = fw.marks()["tensor"]

        def bce3(v, mb3=mb3):
            v.wait_ge(fw.F["tensor"], mb3)
            v.tensor_copy(out=BCS4M[:], in_=P4[:])
            v.tensor_copy(out=BCS4R[:], in_=P5[:])
        VE(bce3)
        for do in range(8):
            def nrm(v, do=do):
                v.tensor_tensor(out=QF4[:], in0=XSUB[:, do * 512:(do + 1) * 512],
                                in1=BCS4M[:], op=ALU.subtract)
                v.scalar_tensor_tensor(out=XN[:, do * 512:(do + 1) * 512], in0=QF4[:],
                                       scalar=BOB[:, 24 + do:24 + do + 1], in1=BCS4R[:],
                                       op0=ALU.mult, op1=ALU.mult)
            VE(nrm)
        for w in range(4):
            mev = fw.marks()
            for k in range(8):
                wk = R3[:, k * 2048:(k + 1) * 2048]

                def wmm4(t, w=w, k=k, wk=wk, mev=mev):
                    if k == 0:
                        t.wait_ge(fw.F["vector"], mev["vector"])
                    for eci in range(4):
                        ec = w * 4 + eci
                        t.matmul(PW[eci][:], wk[:, ec * 128:(ec + 1) * 128],
                                 XN[:, k * 512:(k + 1) * 512], start=(k == 0), stop=(k == 7))
                    t.mark()
                TE(wmm4)
            mw4 = fw.marks()["tensor"]

            def wev4(v, w=w, mw4=mw4, b=b, qs=qs):
                v.wait_ge(fw.F["tensor"], mw4)
                for eci in range(4):
                    ec = w * 4 + eci
                    v.tensor_scalar(out=R1[:, (b * 16 + ec) * 1024 + qs * 512:(b * 16 + ec) * 1024 + (qs + 1) * 512],
                                    in0=PW[eci][:], scalar1=SML[:, 16 + ec:16 + ec + 1],
                                    scalar2=None, op0=ALU.add)
            VE(wev4)
    fw.barrier()

    SCOPE("p4c_abc")
    for do in range(8):
        for b in range(B2):
            fw.dma("sync", ASC[:, (b * 8 + do) * 64:(b * 8 + do + 1) * 64],
                   A_b[do * 128:(do + 1) * 128, :])
            fw.dma("sync", BSC[:, (b * 8 + do) * 64:(b * 8 + do + 1) * 64],
                   Bm_b[do * 128:(do + 1) * 128, :])
    for do in range(8):
        fw.dma("sync", GST[:, do * 64:(do + 1) * 64], Cm_b[do * 128:(do + 1) * 128, :])
    ncm = fw.dn["sync"]

    def onef(v):
        v.memset(STT[:, 0:1], 1.0)
        v.memset(HST[:], 0.0)
    VE(onef)
    monef = fw.marks()

    def csum_mm(t, monef=monef):
        t.wait_ge(fw.Dq["sync"], ncm)
        t.wait_ge(fw.F["vector"], monef["vector"])
        for do in range(8):
            t.matmul(P4[0:1, 0:64], STT[:, 0:1], GST[:, do * 64:(do + 1) * 64],
                     start=(do == 0), stop=(do == 7))
        t.mark()
    TE(csum_mm)
    mcs = fw.marks()["tensor"]

    def csum_ev(v, mcs=mcs):
        v.wait_ge(fw.F["tensor"], mcs)
        v.tensor_copy(out=QF4[0:1, 0:64], in_=P4[0:1, 0:64])
    VE(csum_ev)
    mce = fw.marks()

    def cbc_mm(t, mce=mce):
        t.wait_ge(fw.F["vector"], mce["vector"])
        t.matmul(P5[:, 0:64], IOTAS[0:1, 384:512], QF4[0:1, 0:64], start=True, stop=True)
        t.mark()
    TE(cbc_mm)
    mcb = fw.marks()["tensor"]

    def cbc_ev(v, mcb=mcb):
        v.wait_ge(fw.F["tensor"], mcb)
        v.tensor_copy(out=CBC[:], in_=P5[:, 0:64])
    VE(cbc_ev)
    fw.barrier()
    if PHASES < 7:
        return

    SCOPE("p4d_scan")
    def xt_bc(t):
        return AP(tensor=R1[:].tensor, offset=R1[:].offset + t,
                  ap=[[32768, 128], [16384, 2], [1024, 8], [0, 64]])

    cbc_ap = AP(tensor=CBC[:].tensor, offset=CBC[:].offset,
                ap=[[CBC[:].ap[0][0], 128], [0, 2], [0, 8], [1, 64]])
    GB = [GST, AUX[:, 14464:16512].bitcast(F32)]
    HB = [HST, AUX[:, 16512:18560].bitcast(F32)]
    smark = {-1: fw.marks()["scalar"]}
    zmark = {}
    gmark = {}
    GS = 512

    def g_gp(g, t_):
        if t_ >= 2:
            g.wait_ge(fw.F["vector"], zmark[t_ - 2])
        gap = AP(tensor=R1[:].tensor, offset=R1[:].offset + t_,
                 ap=[[32768, 128], [1024, 8], [0, 64]])
        g.tensor_tensor(out=GB[t_ % 2][:, 0:GS], in0=BSC[:, 0:GS], in1=gap, op=ALU.mult)

    def g_ve(v, t_):
        gap = AP(tensor=R1[:].tensor, offset=R1[:].offset + t_ + 16384,
                 ap=[[32768, 128], [1024, 8], [0, 64]])
        v.tensor_tensor(out=GB[t_ % 2][:, GS:1024], in0=BSC[:, GS:1024], in1=gap, op=ALU.mult)

    def y_emit(v, t_):
        v.tensor_tensor(out=HCS[:], in0=HB[(t_ + 1) % 2][:], in1=cbc_ap, op=ALU.mult)
        ysap = AP(tensor=R3[:].tensor, offset=R3[:].offset + t_,
                  ap=[[16384, 128], [8192, 2], [1024, 8]])
        with nc.allow_low_precision(reason="ys store bf16; DVE reduce accumulates fp32"):
            v.tensor_reduce(out=ysap, in_=HCS[:].rearrange("p (a b c) -> p a b c", a=2, b=8),
                            axis=AX.X, op=ALU.add)

    def prolog(g):
        g_gp(g, 0)
        g_gp(g, 1)
    GP(prolog)
    gmark[0] = gmark[1] = fw.marks()["gpsimd"]

    def prolog_v(v):
        g_ve(v, 0)
        g_ve(v, 1)
    VE(prolog_v)
    for t_ in range(1024):
        def step_v(v, t_=t_):
            v.wait_ge(fw.F["scalar"], smark[t_ - 1])
            if t_ >= 1:
                y_emit(v, t_ - 1)
            v.tensor_tensor(out=T1S[:], in0=HB[t_ % 2][:], in1=ASC[:], op=ALU.mult)
            v.wait_ge(fw.F["gpsimd"], gmark[t_])
            v.tensor_tensor(out=ZST[:], in0=T1S[:], in1=GB[t_ % 2][:], op=ALU.add)
        VE(step_v)
        zmark[t_] = fw.marks()["vector"]
        if t_ + 2 < 1024:
            def gp_sec(g, t_=t_):
                g_gp(g, t_ + 2)
            GP(gp_sec)
            gmark[t_ + 2] = fw.marks()["gpsimd"]

        def step_s(s, t_=t_):
            s.wait_ge(fw.F["vector"], zmark[t_])
            s.activation(out=HB[(t_ + 1) % 2][:], in_=ZST[:], func=AF.Tanh)
        SC(step_s)
        smark[t_] = fw.marks()["scalar"]
        if t_ + 2 < 1024:
            def gve_sec(v, t_=t_):
                g_ve(v, t_ + 2)
            VE(gve_sec)

    def ylast(v):
        v.wait_ge(fw.F["scalar"], smark[1023])
        y_emit(v, 1023)
    VE(ylast)
    fw.barrier()
    if PHASES < 8:
        return

    SCOPE("p4e_post")
    for b in range(B2):
        for do in range(8):
            xp = R1[:, (b * 16 + do) * 1024:(b * 16 + do + 1) * 1024]
            gcol = (b * 16 + 8 + do)
            gate = R1[:, gcol * 1024:(gcol + 1) * 1024]
            ys = R3[:, b * 8192 + do * 1024:b * 8192 + (do + 1) * 1024]

            mpv = fw.marks()["vector"]

            def silu(s, gate=gate, mpv=mpv):
                s.wait_ge(fw.F["vector"], mpv)
                s.activation(out=GST[:], in_=gate, func=AF.Silu)
            SC(silu)
            msl = fw.marks()["scalar"]

            def opre(v, xp=xp, gate=gate, ys=ys, do=do, msl=msl):
                v.wait_ge(fw.F["scalar"], msl)
                v.tensor_scalar(out=T1S[:], in0=xp, scalar1=BOB[:, 16 + do:16 + do + 1],
                                scalar2=None, op0=ALU.mult)
                v.tensor_tensor(out=HCS[:], in0=ys, in1=T1S[:], op=ALU.add)
                v.tensor_tensor(out=gate, in0=HCS[:], in1=GST[:], op=ALU.mult)
            VE(opre)
    WOB = AUX[:, 0:8192]
    wprep(w_out_b, WOB)
    OBS = AUX[:, 10240:10752]
    for tau in range(4):
        b, qs = divmod(tau, 2)
        for w in range(2):
            mev = fw.marks()

            def womm(t, b=b, qs=qs, w=w, mev=mev):
                t.wait_ge(fw.F["vector"], mev["vector"])
                for oci in range(4):
                    oc = w * 4 + oci
                    for k in range(8):
                        t.matmul(PW[oci][:], WOB[:, (k * 8 + oc) * 128:(k * 8 + oc + 1) * 128],
                                 R1[:, (b * 16 + 8 + k) * 1024 + qs * 512:(b * 16 + 8 + k) * 1024 + (qs + 1) * 512],
                                 start=(k == 0), stop=(k == 7))
                t.mark()
            TE(womm)
            mwo = fw.marks()["tensor"]

            def woev(v, b=b, qs=qs, w=w, mwo=mwo):
                v.wait_ge(fw.F["tensor"], mwo)
                for oci in range(4):
                    oc = w * 4 + oci
                    v.tensor_scalar(out=OBS[:], in0=PW[oci][:], scalar1=BOB[:, 32 + oc:32 + oc + 1],
                                    scalar2=None, op0=ALU.add)
                    for par in range(2):
                        base = (b * 8 + oc) * 2048 + qs * 1024 + par
                        rap = AP(tensor=R2[:].tensor, offset=R2[:].offset + base,
                                 ap=[[32768, 128], [2, 512]])
                        v.tensor_tensor(out=rap, in0=rap, in1=OBS[:], op=ALU.add)
            VE(woev)
    fw.barrier()
    if PHASES < 9:
        return

    SCOPE("p5_ffn2")
    emit_ffn()

    SCOPE("p6_out")
    OTS_ = [AUX[:, 10240:12288].bitcast(F32), AUX[:, 12288:14336].bitcast(F32)]
    osem = [ctx["st"].enter_context(nc.semaphore("OSEM0")),
            ctx["st"].enter_context(nc.semaphore("OSEM1"))]
    ocnt = [0, 0]
    for i in range(B2 * NCH):
        b, c = divmod(i, NCH)
        sl = i % 2
        mprev = fw.marks()

        def otr(t, b=b, c=c, mprev=mprev):
            t.wait_ge(fw.F["vector"], mprev["vector"])
            for do in range(8):
                t.transpose(PB0[:, do * 128:(do + 1) * 128],
                            R2[:, (b * 8 + do) * 2048 + c * 128:(b * 8 + do) * 2048 + (c + 1) * 128],
                            IDENT[:])
            t.mark()
        TE(otr)
        mo = fw.marks()["tensor"]

        def oev(v, mo=mo, sl=sl, prevd=ocnt[sl]):
            v.wait_ge(fw.F["tensor"], mo)
            if prevd > 0:
                v.wait_ge(osem[sl], prevd)
            v.tensor_copy(out=OTS_[sl][:], in_=PB0[:])
        VE(oev)
        mvv = fw.marks()

        def ostore(h, b=b, c=c, sl=sl, mvv=mvv):
            h.wait_ge(fw.F["vector"], mvv["vector"])
            h.dma_start(out=out[b, c * 128:(c + 1) * 128, :],
                        in_=OTS_[sl][:]).then_inc(osem[sl], 16)
        fw._sect("sync", ostore)
        ocnt[sl] += 16
    def ofin(v, c0=ocnt[0], c1=ocnt[1]):
        v.wait_ge(osem[0], c0)
        v.wait_ge(osem[1], c1)
    VE(ofin)
    fw.barrier()




F32 = mybir.dt.float32
BF16 = mybir.dt.bfloat16
I32 = mybir.dt.int32
AF = mybir.ActivationFunctionType
ALU = mybir.AluOpType
AX = mybir.AxisListType

B2, L, D, NB = 2, 2048, 1024, 8
NCH = 16
N2 = 32
S64 = 64
EPS_LN, EPS_RMS = 1e-5, 1e-6
TWO_PI = 2.0 * math.pi
INV_2PI = 1.0 / TWO_PI
MAGIC = 1.5 * 2.0 ** 23
MAGIC64 = 1.5 * 2.0 ** 52
SLABS = NB
PHASES = 99


def build_program(debug=()):
    nc = bass.Bass("TRN2", target_bir_lowering=False)
    dbg_outs = {}

    def dram_in(name, shape, dt=F32):
        return nc.dram_tensor(name, shape, dt, kind="ExternalInput").ap()

    x_in = dram_in("x_in", [B2, L, D])
    out = nc.dram_tensor("out", [B2, L, D], F32, kind="ExternalOutput").ap()
    log_dt = dram_in("log_dt", [1, D])
    log_w_real = dram_in("log_w_real", [1, N2])
    w_imag = dram_in("w_imag", [1, N2])
    C_real = dram_in("C_real", [D, N2])
    C_imag = dram_in("C_imag", [D, N2])
    D_a = dram_in("D_a", [1, D])
    W_out_a = dram_in("W_out_a", [D, D])
    b_out_a = dram_in("b_out_a", [1, D])
    tern = dram_in("tern", [D, D], I32)
    bias_f = dram_in("bias_f", [1, D])
    act_scale = dram_in("act_scale", [1, 1])
    w_in_b = dram_in("w_in_b", [2 * D, D])
    b_in_b = dram_in("b_in_b", [1, 2 * D])
    A_b = dram_in("A_b", [D, S64])
    Bm_b = dram_in("Bm_b", [D, S64])
    Cm_b = dram_in("Cm_b", [D, S64])
    Dv_b = dram_in("Dv_b", [1, D])
    w_out_b = dram_in("w_out_b", [D, D])
    b_out_b = dram_in("b_out_b", [1, D])
    g_norm = dram_in("g_norm", [1, D])
    identb = dram_in("identb", [128, 128], BF16)
    iota_negj = dram_in("iota_negj", [1, 128])
    iota_t = dram_in("iota_t", [1, 128])
    iota_t128 = dram_in("iota_t128", [1, 128])
    mask_triu = dram_in("mask_triu", [128, 128], BF16)
    ones_row = dram_in("ones_row", [1, 128])

    hb_aim = nc.dram_tensor("hb_aim", [1, D * N2], F32, kind="Internal").ap()
    hb_are = nc.dram_tensor("hb_are", [1, D * N2], F32, kind="Internal").ap()
    hb_cwr = nc.dram_tensor("hb_cwr", [1, D * N2], F32, kind="Internal").ap()
    hb_cwi = nc.dram_tensor("hb_cwi", [1, D * N2], F32, kind="Internal").ap()
    hb_slab = nc.dram_tensor("hb_slab", [NB, 128, B2 * NCH * 128], BF16, kind="Internal").ap()
    hb_winT = nc.dram_tensor("hb_winT", [NB, 128, 2 * D], BF16, kind="Internal").ap()

    def dbg(name, shape, dt):
        if name in debug:
            dbg_outs[name] = nc.dram_tensor(name, shape, dt, kind="ExternalOutput").ap()
            return dbg_outs[name]
        return None

    st = ExitStack()
    sbuf = lambda name, shape, dt=BF16: st.enter_context(nc.sbuf_tensor(name, shape, dt))
    psum = lambda name, shape, dt=F32: st.enter_context(nc.psum_tensor(name, shape, dt))

    with st:
        R1 = sbuf("R1", [128, 32768])
        R2 = sbuf("R2", [128, 32768])
        R3 = sbuf("R3", [128, 16384])
        AUX = sbuf("AUX", [128, 20480])
        IDENT = sbuf("IDENTS", [128, 128])
        IDENTF = sbuf("IDENTF", [128, 128], F32)
        MASKT = sbuf("MASKT", [128, 128])
        IOTAS = sbuf("IOTAS", [1, 512], F32)
        ONEC = sbuf("ONEC", [128, 2])
        STM = sbuf("STM", [128, 96], F32)
        STT = sbuf("STT", [128, 16], F32)
        SML = sbuf("SML", [128, 64], F32)
        BOB = sbuf("BOB", [128, 64], F32)
        DBC = sbuf("DBC", [128, 1024])
        ASC = AUX[:, 0:2048].bitcast(F32)
        BSC = AUX[:, 2048:4096].bitcast(F32)
        HST = AUX[:, 4096:6144].bitcast(F32)
        ZST = AUX[:, 6144:8192].bitcast(F32)
        GST = AUX[:, 8192:10240].bitcast(F32)
        T1S = AUX[:, 10240:12288].bitcast(F32)
        HCS = AUX[:, 12288:14336].bitcast(F32)
        CBC = AUX[:, 14336:14464].bitcast(F32)
        PRM = AUX[:, 8192:12288].bitcast(F32)
        BCB = AUX[:, 12288:14336].bitcast(F32)
        DROW = AUX[0:1, 14336:16384].bitcast(F32)
        XT = AUX[:, 0:4096]

        P0 = psum("P0", [128, 512]); P1 = psum("P1", [128, 512]); P2 = psum("P2", [128, 512])
        P3 = psum("P3", [128, 512]); P4 = psum("P4", [128, 512]); P5 = psum("P5", [128, 512])
        PB0 = psum("PB0", [128, 1024], BF16)
        PB1 = psum("PB1", [128, 1024], BF16)
        PW = [P0, P1, P2, P3]

        st.enter_context(nc.allow_non_contiguous_dma(reason="strided staging loads/stores"))
        blk = st.enter_context(nc.Block())
        fw = FW(nc, st, blk)

        _scope = {"cur": None, "id": None}

        def SCOPE(name):
            if _scope["cur"] is not None:
                nc.leave_named_scope(_scope["cur"], _scope["id"], False)
                _scope["cur"] = None
            if name:
                sid, _ = nc.enter_named_scope(name, False)
                _scope["cur"], _scope["id"] = name, sid

        def VE(fn):
            fw.run("vector", fn)
        def SC(fn):
            fw.run("scalar", fn)
        def TE(fn):
            fw.run("tensor", fn)
        def GP(fn):
            fw.run("gpsimd", fn)

        SCOPE("p0_consts")
        fw.dma("sync", IDENT[:], identb[:])
        fw.dma("sync", MASKT[:], mask_triu[:])
        fw.dma("sync", IOTAS[0:1, 0:128], iota_negj[:])
        fw.dma("sync", IOTAS[0:1, 128:256], iota_t[:])
        fw.dma("sync", IOTAS[0:1, 256:384], iota_t128[:])
        fw.dma("sync", IOTAS[0:1, 384:512], ones_row[:])
        for (dst0, src) in ((0, b_out_a), (8, bias_f), (16, Dv_b), (24, g_norm), (32, b_out_b)):
            fw.dma("sync", BOB[:, dst0:dst0 + 8],
                   AP(tensor=src.tensor, offset=0, ap=[[1, 128], [128, 8]]))
        fw.dma("sync", SML[:, 0:8], AP(tensor=log_dt.tensor, offset=0, ap=[[1, 128], [128, 8]]))
        fw.dma("sync", SML[:, 16:32], AP(tensor=b_in_b.tensor, offset=0, ap=[[1, 128], [128, 16]]))
        fw.dma("sync", SML[0:1, 48:49], act_scale[:])
        fw.dma("sync", PRM[:, 1536:1792].rearrange("p (a b) -> p a b", a=8),
               AP(tensor=C_real.tensor, offset=0, ap=[[N2, 128], [128 * N2, 8], [1, N2]]))
        fw.dma("sync", PRM[:, 1792:2048].rearrange("p (a b) -> p a b", a=8),
               AP(tensor=C_imag.tensor, offset=0, ap=[[N2, 128], [128 * N2, 8], [1, N2]]))
        fw.dma("sync", BCB[0:1, 384:416], log_w_real[:])
        fw.dma("sync", BCB[0:1, 416:448], w_imag[:])
        fw.dma("sync", DROW[0:1, 0:1024], D_a[:])
        nconst = fw.dn["sync"]
        nconst2 = nconst

        def c_init(v):
            v.wait_ge(fw.Dq["sync"], nconst)
            v.tensor_copy(out=IDENTF[:], in_=IDENT[:])
            v.memset(ONEC[:], 1.0)
            v.memset(SML[:, 52:53], 1.0)
            v.tensor_scalar(out=SML[:, 53:54], in0=SML[:, 52:53], scalar1=MAGIC64,
                            scalar2=-MAGIC64, op0=ALU.add, op1=ALU.add)
        VE(c_init)
        m_init = fw.marks()

        SCOPE("p1_ln1")
        NTILE = B2 * NCH
        sx = [st.enter_context(nc.semaphore(f"sx{i}")) for i in range(4)]

        def ph1(i):
            b, c = divmod(i, NCH)
            slot = i % 4
            xt = XT[:, slot * 1024:(slot + 1) * 1024]
            mk0 = fw.marks()

            def gp_ld(h):
                if i >= 4:
                    h.wait_ge(fw.F["vector"], ph1.done[i - 4])
                h.dma_start(out=xt, in_=x_in[b, c * 128:(c + 1) * 128, :]).then_inc(sx[slot], 16)
            fw._sect("gpsimd", gp_ld)

            def ve1(v):
                v.wait_ge(sx[slot], 16 * (i // 4 + 1))
                v.bn_stats(out=STT[:, 0:6], in_=xt[:, 0:512])
                v.bn_stats(out=STT[:, 6:12], in_=xt[:, 512:1024])
                v.bn_aggr(out=STT[:, 12:14], in_=STT[:, 0:12])
                v.tensor_copy(out=STM[:, i:i + 1], in_=STT[:, 12:13])
                v.tensor_scalar(out=STT[:, 14:15], in0=STT[:, 13:14], scalar1=EPS_LN,
                                scalar2=None, op0=ALU.add)
                v.reciprocal(out=STT[:, 15:16], in_=STT[:, 14:15])
            VE(ve1)
            mkv = fw.marks()

            def sc1(s):
                s.wait_ge(fw.F["vector"], mkv["vector"])
                s.activation(out=STM[:, 64 + i:64 + i + 1], in_=STT[:, 15:16], func=AF.Sqrt)
            SC(sc1)
            mks = fw.marks()

            def ve2(v):
                v.wait_ge(fw.F["scalar"], mks["scalar"])
                v.tensor_scalar(out=R1[:, i * 1024:(i + 1) * 1024], in0=xt,
                                scalar1=STM[:, i:i + 1], scalar2=STM[:, 64 + i:64 + i + 1],
                                op0=ALU.subtract, op1=ALU.mult)
            VE(ve2)
            ph1.done[i] = fw.marks()["vector"]
        ph1.done = {}
        for i in range(NTILE):
            ph1(i)
        fw.barrier()
        if PHASES < 2:
            return _finish(nc, fw, dbg_outs, dbg, R1, R2, R3, STM, PRM, BCB, AUX, hb_aim, hb_are, hb_cwr, hb_cwi, DBC)

        SCOPE("p2a_prep")
        SCF = AUX[:, 4096:8192].bitcast(F32)

        def dt3(base, a, bsz, astr, bstr):
            return AP(tensor=base.tensor, offset=base.offset,
                      ap=[list(base.ap[0]), [astr, a], [bstr, bsz]])

        def mm1(t):
            t.wait_ge(fw.Dq["sync"], nconst2)
            t.matmul(P0[:, 0:64], IOTAS[0:1, 384:512], BCB[0:1, 384:448], start=True, stop=True)
            t.matmul(P1[:], IOTAS[0:1, 384:512], DROW[0:1, 0:512], start=True, stop=True)
            t.matmul(P2[:], IOTAS[0:1, 384:512], DROW[0:1, 512:1024], start=True, stop=True)
            t.mark()
        TE(mm1)
        mk = fw.marks()

        def sc_dt(s):
            s.activation(out=SML[:, 8:16], in_=SML[:, 0:8], func=AF.Exp)
        SC(sc_dt)
        mk_dt = fw.marks()

        def ve_bc(v):
            v.wait_ge(fw.F["tensor"], mk["tensor"])
            v.tensor_copy(out=BCB[:, 0:32], in_=P0[:, 0:32])
            v.tensor_copy(out=BCB[:, 32:64], in_=P0[:, 32:64])
        VE(ve_bc)
        mk_bc = fw.marks()

        def sc_wre(s):
            s.wait_ge(fw.F["vector"], mk_bc["vector"])
            s.activation(out=BCB[:, 0:32], in_=BCB[:, 0:32], func=AF.Exp)
        SC(sc_wre)
        mk_wre = fw.marks()

        def ve_args(v):
            v.wait_ge(fw.F["scalar"], mk_wre["scalar"])
            dt_ap = dt3(SML[:, 8:16], 8, 32, 1, 0)
            wim_ap = dt3(BCB[:, 32:64], 8, 32, 0, 1)
            wre_ap = dt3(BCB[:, 0:32], 8, 32, 0, 1)
            v.tensor_tensor(out=PRM[:, 0:256].rearrange("p (a b) -> p a b", a=8),
                            in0=dt_ap, in1=wim_ap, op=ALU.mult)
            v.tensor_tensor(out=PRM[:, 256:512].rearrange("p (a b) -> p a b", a=8),
                            in0=dt_ap, in1=wre_ap, op=ALU.mult)
            v.tensor_scalar(out=PRM[:, 256:512], in0=PRM[:, 256:512], scalar1=-1.0,
                            scalar2=None, op0=ALU.mult)
        VE(ve_args)
        mk_args = fw.marks()

        def sc_rr1(s):
            s.wait_ge(fw.F["vector"], mk_args["vector"])
            s.activation(out=SCF[:, 0:256], in_=PRM[:, 0:256], func=AF.Copy,
                         scale=INV_2PI, bias=MAGIC)
            s.activation(out=SCF[:, 0:256], in_=SCF[:, 0:256], func=AF.Copy, bias=-MAGIC)
        SC(sc_rr1)
        mk_rr1 = fw.marks()

        def ve_rr1(v):
            v.wait_ge(fw.F["scalar"], mk_rr1["scalar"])
            v.tensor_scalar(out=SCF[:, 1024:1280], in0=PRM[:, 0:256], scalar1=INV_2PI,
                            scalar2=None, op0=ALU.mult)
            v.tensor_scalar(out=SCF[:, 1280:1536], in0=SCF[:, 1024:1280], scalar1=MAGIC64,
                            scalar2=-MAGIC64, op0=ALU.add, op1=ALU.add)
            v.tensor_tensor(out=SCF[:, 1280:1536], in0=SCF[:, 1280:1536], in1=SCF[:, 0:256],
                            op=ALU.subtract)
            v.scalar_tensor_tensor(out=SCF[:, 0:256], in0=SCF[:, 1280:1536], scalar=SML[:, 53:54],
                                   in1=SCF[:, 0:256], op0=ALU.mult, op1=ALU.add)
            v.tensor_tensor(out=SCF[:, 256:512], in0=SCF[:, 1024:1280], in1=SCF[:, 0:256],
                            op=ALU.subtract)
            v.tensor_scalar(out=SCF[:, 512:768], in0=SCF[:, 256:512], scalar1=0.25,
                            scalar2=None, op0=ALU.add)
            v.tensor_scalar(out=SCF[:, 768:1024], in0=SCF[:, 512:768], scalar1=0.5,
                            scalar2=None, op0=ALU.is_gt)
            v.tensor_tensor(out=SCF[:, 512:768], in0=SCF[:, 512:768], in1=SCF[:, 768:1024],
                            op=ALU.subtract)
        VE(ve_rr1)
        mk_ve1 = fw.marks()
        rr1_dbg = dbg("rr1_dbg", [128, 1536], F32)
        if rr1_dbg is not None:
            fw.dma("gpsimd", rr1_dbg[:], SCF[:, 0:1536], wait=[("vector", mk_ve1["vector"])])
        stt_dbg = dbg("stt_dbg", [128, 16], F32)
        if stt_dbg is not None:
            fw.dma("gpsimd", stt_dbg[:], STT[:], wait=[("vector", mk_ve1["vector"])])


        def sc_rr2(s):
            s.wait_ge(fw.F["vector"], mk_ve1["vector"])
            s.activation(out=SCF[:, 768:1024], in_=SCF[:, 256:512], func=AF.Sin, scale=TWO_PI)
            s.activation(out=SCF[:, 1024:1280], in_=SCF[:, 512:768], func=AF.Sin, scale=TWO_PI)
            s.activation(out=SCF[:, 1280:1536], in_=PRM[:, 256:512], func=AF.Exp)
        SC(sc_rr2)
        mk_rr2 = fw.marks()

        def ve_cw(v):
            v.wait_ge(fw.F["scalar"], mk_rr2["scalar"])
            v.tensor_tensor(out=PRM[:, 512:768], in0=SCF[:, 1280:1536], in1=SCF[:, 1024:1280], op=ALU.mult)
            v.tensor_tensor(out=PRM[:, 768:1024], in0=SCF[:, 1280:1536], in1=SCF[:, 768:1024], op=ALU.mult)
            v.tensor_tensor(out=BCB[:, 64:96], in0=BCB[:, 0:32], in1=BCB[:, 0:32], op=ALU.mult)
            v.tensor_tensor(out=BCB[:, 96:128], in0=BCB[:, 32:64], in1=BCB[:, 32:64], op=ALU.mult)
            v.tensor_tensor(out=BCB[:, 64:96], in0=BCB[:, 64:96], in1=BCB[:, 96:128], op=ALU.add)
            v.reciprocal(out=BCB[:, 192:224], in_=BCB[:, 64:96])
            v.tensor_scalar(out=SCF[:, 1536:1792], in0=PRM[:, 512:768], scalar1=-1.0, scalar2=None, op0=ALU.add)
            v.tensor_scalar(out=BCB[:, 224:256], in0=BCB[:, 0:32], scalar1=-1.0, scalar2=None, op0=ALU.mult)
            v.tensor_scalar(out=BCB[:, 256:288], in0=BCB[:, 32:64], scalar1=-1.0, scalar2=None, op0=ALU.mult)
            v.tensor_tensor(out=BCB[:, 128:160], in0=BCB[:, 224:256], in1=BCB[:, 192:224], op=ALU.mult)
            v.tensor_tensor(out=BCB[:, 160:192], in0=BCB[:, 256:288], in1=BCB[:, 192:224], op=ALU.mult)
            ir = dt3(BCB[:, 128:160], 8, 32, 0, 1)
            ii = dt3(BCB[:, 160:192], 8, 32, 0, 1)
            nre3 = SCF[:, 1536:1792].rearrange("p (a b) -> p a b", a=8)
            nim3 = PRM[:, 768:1024].rearrange("p (a b) -> p a b", a=8)
            v.tensor_tensor(out=SCF[:, 0:256].rearrange("p (a b) -> p a b", a=8), in0=nre3, in1=ir, op=ALU.mult)
            v.tensor_tensor(out=SCF[:, 256:512].rearrange("p (a b) -> p a b", a=8), in0=nim3, in1=ii, op=ALU.mult)
            v.tensor_tensor(out=SCF[:, 0:256], in0=SCF[:, 0:256], in1=SCF[:, 256:512], op=ALU.subtract)
            v.tensor_tensor(out=SCF[:, 256:512].rearrange("p (a b) -> p a b", a=8), in0=nre3, in1=ii, op=ALU.mult)
            v.tensor_tensor(out=SCF[:, 512:768].rearrange("p (a b) -> p a b", a=8), in0=nim3, in1=ir, op=ALU.mult)
            v.tensor_tensor(out=SCF[:, 256:512], in0=SCF[:, 256:512], in1=SCF[:, 512:768], op=ALU.add)
            v.tensor_tensor(out=SCF[:, 512:768], in0=PRM[:, 1536:1792], in1=SCF[:, 0:256], op=ALU.mult)
            v.tensor_tensor(out=SCF[:, 768:1024], in0=PRM[:, 1792:2048], in1=SCF[:, 256:512], op=ALU.mult)
            v.tensor_tensor(out=SCF[:, 512:768], in0=SCF[:, 512:768], in1=SCF[:, 768:1024], op=ALU.subtract)
            v.tensor_scalar(out=SCF[:, 1536:1792], in0=SCF[:, 512:768], scalar1=2.0, scalar2=None, op0=ALU.mult)
            v.tensor_tensor(out=SCF[:, 512:768], in0=PRM[:, 1536:1792], in1=SCF[:, 256:512], op=ALU.mult)
            v.tensor_tensor(out=SCF[:, 768:1024], in0=PRM[:, 1792:2048], in1=SCF[:, 0:256], op=ALU.mult)
            v.tensor_tensor(out=SCF[:, 512:768], in0=SCF[:, 512:768], in1=SCF[:, 768:1024], op=ALU.add)
            v.tensor_scalar(out=SCF[:, 1792:2048], in0=SCF[:, 512:768], scalar1=2.0, scalar2=None, op0=ALU.mult)
            v.tensor_copy(out=PRM[:, 1024:1280], in_=PRM[:, 512:768])
            v.tensor_copy(out=PRM[:, 1280:1536], in_=PRM[:, 768:1024])
            for _ in range(7):
                v.tensor_tensor(out=SCF[:, 0:256], in0=PRM[:, 1024:1280], in1=PRM[:, 1024:1280], op=ALU.mult)
                v.tensor_tensor(out=SCF[:, 256:512], in0=PRM[:, 1280:1536], in1=PRM[:, 1280:1536], op=ALU.mult)
                v.tensor_tensor(out=SCF[:, 512:768], in0=PRM[:, 1024:1280], in1=PRM[:, 1280:1536], op=ALU.mult)
                v.tensor_tensor(out=PRM[:, 1024:1280], in0=SCF[:, 0:256], in1=SCF[:, 256:512], op=ALU.subtract)
                v.tensor_scalar(out=PRM[:, 1280:1536], in0=SCF[:, 512:768], scalar1=2.0, scalar2=None, op0=ALU.mult)
        VE(ve_cw)
        mk_cw = fw.marks()

        def ve_dbc(v):
            v.tensor_copy(out=DBC[:, 0:512], in_=P1[:])
            v.tensor_copy(out=DBC[:, 512:1024], in_=P2[:])
        VE(ve_dbc)

        def rows_dst(t):
            return AP(tensor=t.tensor, offset=0, ap=[[32, 128], [4096, 8], [1, 32]])
        mnow = fw.marks()
        fw.dma("gpsimd", rows_dst(hb_aim), PRM[:, 0:256].rearrange("p (a b) -> p a b", a=8),
               wait=[("vector", mnow["vector"])])
        fw.dma("gpsimd", rows_dst(hb_are), PRM[:, 256:512].rearrange("p (a b) -> p a b", a=8))
        fw.dma("gpsimd", rows_dst(hb_cwr), SCF[:, 1536:1792].rearrange("p (a b) -> p a b", a=8))
        fw.dma("gpsimd", rows_dst(hb_cwi), SCF[:, 1792:2048].rearrange("p (a b) -> p a b", a=8))
        fw.barrier()
        if PHASES < 3:
            return _finish(nc, fw, dbg_outs, dbg, R1, R2, R3, STM, PRM, BCB, AUX, hb_aim, hb_are, hb_cwr, hb_cwi, DBC)

        TBL = AUX[:, 0:1024]
        TBB = AUX[:, 1024:2048]
        SINT = AUX[:, 2048:2560]
        COST = AUX[:, 2560:3072]
        EXPT = AUX[:, 3072:3584]
        RAWS = AUX[:, 3584:4096]
        TMP2B = AUX[:, 15360:16384]
        TGA = AUX[0:64, 16384:18432]
        TGB = AUX[0:64, 18432:20480]
        BSLAB = R3[0:64, 0:4096]
        SSLAB = R3[0:64, 4096:8192]
        K8A = R3[:, 8192:9216]
        K8B = R3[:, 9216:10240]
        GOSL = R3[:, 10240:14336]
        SSTa = R3[:, 14336:14592].bitcast(F32)
        SSTb = R3[:, 14592:14848].bitcast(F32)
        TMPQ = R3[:, 14848:15360].bitcast(F32)
        AIMSEG = DROW[0:1, 0:512]
        ARESEG = SCF[0:1, 1536:2048]
        CW2R = BCB[:, 0:512]
        CW2I = BCB[:, 512:1024]

        def u_cols(h):
            return AP(tensor=R1[:].tensor, offset=h, ap=[[32768, 128], [NCH * 1024, B2], [1024, NCH]])

        def u_grp(h0):
            return AP(tensor=R1[:].tensor, offset=h0, ap=[[32768, 128], [1, 16], [NCH * 1024, B2], [1024, NCH]])

        def dbc_grp(h0):
            return AP(tensor=DBC[:].tensor, offset=DBC[:].offset + h0,
                      ap=[[1024, 128], [1, 16], [0, B2], [0, NCH]])

        def chain(pim, pexp, mk_args_t):
            def s1(s):
                s.wait_ge(fw.F["tensor"], mk_args_t)
                s.activation(out=SCF[:, 0:512], in_=pim, func=AF.Copy, scale=INV_2PI, bias=MAGIC)
                s.activation(out=SCF[:, 0:512], in_=SCF[:, 0:512], func=AF.Copy, bias=-MAGIC)
            SC(s1)
            m1 = fw.marks()

            def v1(v):
                v.wait_ge(fw.F["scalar"], m1["scalar"])
                v.tensor_scalar(out=SCF[:, 512:1024], in0=pim, scalar1=INV_2PI,
                                scalar2=None, op0=ALU.mult)
                v.tensor_scalar(out=SCF[:, 1024:1536], in0=SCF[:, 512:1024], scalar1=MAGIC64,
                                scalar2=-MAGIC64, op0=ALU.add, op1=ALU.add)
                v.tensor_tensor(out=SCF[:, 1024:1536], in0=SCF[:, 1024:1536], in1=SCF[:, 0:512],
                                op=ALU.subtract)
                v.scalar_tensor_tensor(out=SCF[:, 0:512], in0=SCF[:, 1024:1536], scalar=SML[:, 53:54],
                                       in1=SCF[:, 0:512], op0=ALU.mult, op1=ALU.add)
                v.tensor_tensor(out=SCF[:, 512:1024], in0=SCF[:, 512:1024], in1=SCF[:, 0:512],
                                op=ALU.subtract)
                v.tensor_scalar(out=SCF[:, 1024:1536], in0=SCF[:, 512:1024], scalar1=0.25,
                                scalar2=None, op0=ALU.add)
                v.tensor_scalar(out=SCF[:, 0:512], in0=SCF[:, 1024:1536], scalar1=0.5,
                                scalar2=None, op0=ALU.is_gt)
                v.tensor_tensor(out=SCF[:, 1024:1536], in0=SCF[:, 1024:1536], in1=SCF[:, 0:512],
                                op=ALU.subtract)
            VE(v1)
            m2 = fw.marks()

            def s2(s):
                s.wait_ge(fw.F["vector"], m2["vector"])
                s.activation(out=SINT[:], in_=SCF[:, 512:1024], func=AF.Sin, scale=TWO_PI)
                s.activation(out=COST[:], in_=SCF[:, 1024:1536], func=AF.Sin, scale=TWO_PI)
                s.activation(out=EXPT[:], in_=pexp, func=AF.Exp)
            SC(s2)
            return fw.marks()

        def bp_family(mk_sc):
            def v(v):
                v.wait_ge(fw.F["scalar"], mk_sc["scalar"])
                v.tensor_tensor(out=SCF[:, 0:512], in0=EXPT[:], in1=COST[:], op=ALU.mult)
                v.tensor_tensor(out=SCF[:, 512:1024], in0=EXPT[:], in1=SINT[:], op=ALU.mult)
                v.tensor_tensor(out=SCF[:, 1024:1536], in0=CW2R, in1=SCF[:, 0:512], op=ALU.mult)
                v.tensor_tensor(out=TMP2B[:].bitcast(F32), in0=CW2I, in1=SCF[:, 512:1024], op=ALU.mult)
                v.tensor_tensor(out=TBB[:, 0:1024:2], in0=SCF[:, 1024:1536], in1=TMP2B[:].bitcast(F32), op=ALU.subtract)
                v.tensor_tensor(out=SCF[:, 1024:1536], in0=CW2R, in1=SCF[:, 512:1024], op=ALU.mult)
                v.tensor_tensor(out=TMP2B[:].bitcast(F32), in0=CW2I, in1=SCF[:, 0:512], op=ALU.mult)
                v.tensor_tensor(out=SCF[:, 1024:1536], in0=SCF[:, 1024:1536], in1=TMP2B[:].bitcast(F32), op=ALU.add)
                v.tensor_scalar(out=TBB[:, 1:1024:2], in0=SCF[:, 1024:1536], scalar1=-1.0,
                                scalar2=None, op0=ALU.mult)
            VE(v)
            return fw.marks()

        raws_st = {"n": 0}
        rowp = {"v": 0, "t": 0, "kev_v": 0, "bp_s": 0, "ga_v": 0}
        SCOPE("p2b_slabs")
        for ho in range(SLABS):
            if ho == 0:
                SCOPE("sw1_0")
            elif ho == 1:
                SCOPE("slabs_rest")
            for g in range(8):
                h0 = ho * 128 + g * 16
                seg = h0 * N2
                mv = fw.marks()
                fw.dma("sync", AIMSEG, hb_aim[0:1, seg:seg + 512],
                       wait=[("vector", rowp["v"] if rowp["v"] else mv["vector"]),
                             ("tensor", rowp["t"] if rowp["t"] else mv["tensor"])])
                fw.dma("sync", ARESEG, hb_are[0:1, seg:seg + 512])
                fw.dma("sync", CW2R, AP(tensor=hb_cwr.tensor, offset=seg, ap=[[0, 128], [1, 512]]))
                fw.dma("sync", CW2I, AP(tensor=hb_cwi.tensor, offset=seg, ap=[[0, 128], [1, 512]]))
                ndma = fw.dn["sync"]

                def args1(t, ndma=ndma, mv=mv):
                    t.wait_ge(fw.Dq["sync"], ndma)
                    t.wait_ge(fw.F["vector"], rowp["kev_v"] if rowp["kev_v"] else mv["vector"])
                    if rowp["bp_s"]:
                        t.wait_ge(fw.F["scalar"], rowp["bp_s"])
                    t.matmul(P0[:], IOTAS[0:1, 0:128], AIMSEG, start=True, stop=True)
                    t.matmul(P2[:], IOTAS[0:1, 0:128], ARESEG, start=True, stop=True)
                    t.matmul(P1[:], IOTAS[0:1, 128:256], AIMSEG, start=True, stop=True)
                    t.matmul(P3[:], IOTAS[0:1, 128:256], ARESEG, start=True, stop=True)
                    t.mark()
                TE(args1)
                mk_a = fw.marks()["tensor"]
                mk_f1 = chain(P0[:], P2[:], mk_a)

                def f1w(v, mk_f1=mk_f1):
                    v.wait_ge(fw.F["scalar"], mk_f1["scalar"])
                    v.tensor_tensor(out=TBL[:, 0:1024:2], in0=EXPT[:], in1=COST[:], op=ALU.mult)
                    v.tensor_tensor(out=TBL[:, 1:1024:2], in0=EXPT[:], in1=SINT[:], op=ALU.mult)
                VE(f1w)
                mk_bp = chain(P1[:], P3[:], mk_a)
                bp_family(mk_bp)
                mtab = fw.marks()
                rowp["v"], rowp["t"] = mtab["vector"], mk_a
                rowp["bp_s"] = mk_bp["scalar"]

                def tr1(t, mtab=mtab):
                    t.wait_ge(fw.F["vector"], mtab["vector"])
                    for k in range(8):
                        t.transpose(PB0[0:64, k * 128:(k + 1) * 128], TBL[:, k * 64:(k + 1) * 64], IDENT[:])
                    t.mark()
                    for k in range(8):
                        t.transpose(PB1[0:64, k * 128:(k + 1) * 128], TBL[:, 512 + k * 64:512 + (k + 1) * 64], IDENT[:])
                    t.mark()
                TE(tr1)
                mtr1 = fw.marks()["tensor"]

                def evga(v, mtr1=mtr1):
                    v.wait_ge(fw.F["tensor"], mtr1)
                    v.tensor_copy(out=TGA[:, 0:1024], in_=PB0[0:64, :])
                    v.tensor_copy(out=TGA[:, 1024:2048], in_=PB1[0:64, :])
                VE(evga)
                mevga = fw.marks()

                def tr2(t, mevga=mevga):
                    t.wait_ge(fw.F["vector"], mevga["vector"])
                    for k in range(8):
                        t.transpose(PB0[0:64, k * 128:(k + 1) * 128], TBB[:, k * 64:(k + 1) * 64], IDENT[:])
                    t.mark()
                    for k in range(8):
                        t.transpose(PB1[0:64, k * 128:(k + 1) * 128], TBB[:, 512 + k * 64:512 + (k + 1) * 64], IDENT[:])
                    t.mark()
                TE(tr2)
                mtr2 = fw.marks()["tensor"]

                def evgb(v, mtr2=mtr2):
                    v.wait_ge(fw.F["tensor"], mtr2)
                    v.tensor_copy(out=TGB[:, 0:1024], in_=PB0[0:64, :])
                    v.tensor_copy(out=TGB[:, 1024:2048], in_=PB1[0:64, :])
                VE(evgb)
                mevgb = fw.marks()

                def kmm(t, mevgb=mevgb):
                    t.wait_ge(fw.F["vector"], mevgb["vector"])
                    for k in range(16):
                        pb = PW[k // 4]
                        t.matmul(pb[:, (k % 4) * 128:(k % 4 + 1) * 128],
                                 TGA[:, k * 128:(k + 1) * 128], TGB[:, k * 128:(k + 1) * 128],
                                 start=True, stop=True)
                    t.mark()
                TE(kmm)
                mkmm = fw.marks()["tensor"]

                def kev(v, mkmm=mkmm):
                    v.wait_ge(fw.F["tensor"], mkmm)
                    for q, pb in enumerate(PW):
                        mask4 = AP(tensor=MASKT[:].tensor, offset=MASKT[:].offset,
                                   ap=[[128, 128], [0, 4], [1, 128]])
                        dst = (K8A, K8B)[q // 2]
                        v.tensor_tensor(out=dst[:, (q % 2) * 512:(q % 2 + 1) * 512].rearrange("p (a b) -> p a b", a=4),
                                        in0=pb[:].rearrange("p (a b) -> p a b", a=4), in1=mask4, op=ALU.mult)
                VE(kev)
                mkev = fw.marks()
                rowp["kev_v"] = mkev["vector"]

                def gimm(t, mkev=mkev, h0=h0):
                    t.wait_ge(fw.F["vector"], mkev["vector"])
                    for k in range(16):
                        t.matmul(P4[0:64, k * 32:(k + 1) * 32], TBL[:, k * 64:(k + 1) * 64],
                                 u_cols(h0 + k), start=True, stop=True)
                    t.mark()
                    for k in range(16):
                        kt = (K8A, K8B)[k // 8]
                        t.matmul(P5[:, k * 32:(k + 1) * 32], kt[:, (k % 8) * 128:(k % 8 + 1) * 128],
                                 u_cols(h0 + k), start=True, stop=True)
                    t.mark()
                TE(gimm)
                mgi = fw.marks()["tensor"]

                def bdev(v, mgi=mgi, h0=h0, prevst=raws_st["n"]):
                    v.wait_ge(fw.F["tensor"], mgi)
                    if prevst:
                        v.wait_ge(fw.Dq["gpsimd"], prevst)
                    h0l = h0 % 128
                    dstB = AP(tensor=BSLAB.tensor, offset=BSLAB.offset + h0l * 2,
                              ap=[[16384, 64], [2, 16], [1, 2], [256, 16]])
                    v.tensor_copy(out=dstB, in_=P4[0:64, 0:512].rearrange("p (a b c) -> p a b c", a=16, b=2))
                    v.tensor_tensor(out=SINT[:], in0=u_grp(h0), in1=dbc_grp(h0), op=ALU.mult)
                    v.tensor_tensor(out=RAWS[:], in0=SINT[:], in1=P5[:], op=ALU.add)
                VE(bdev)
                mbd = fw.marks()
                fw.dma("gpsimd", hb_slab[ho, :, g * 512:(g + 1) * 512], RAWS[:],
                       wait=[("vector", mbd["vector"])])
                raws_st["n"] = fw.dn["gpsimd"]

            if ho == 0:
                SCOPE("rec_0")
            mk_sw1 = fw.marks()
            ngp1 = fw.dn["gpsimd"]

            def rinit(v):
                v.memset(SSTa, 0.0)
            VE(rinit)
            lt_re = PRM[:, 1024 + ho * 32:1024 + (ho + 1) * 32]
            lt_im = PRM[:, 1280 + ho * 32:1280 + (ho + 1) * 32]

            def lt_ap(base):
                return AP(tensor=base.tensor, offset=base.offset,
                          ap=[[base.ap[0][0], 128], [0, 2], [1, 32]])

            def sview(tns, pm):
                return AP(tensor=tns.tensor, offset=tns.offset + pm, ap=[[tns.ap[0][0], 128], [64, 2], [2, 32]])

            for c in range(NCH):
                cur, nxt = (SSTa, SSTb) if c % 2 == 0 else (SSTb, SSTa)
                mvv = fw.marks()

                def rtr(t, c=c, cur=cur, mvv=mvv):
                    t.wait_ge(fw.F["vector"], mvv["vector"])
                    for b in range(B2):
                        t.transpose(P4[0:64, b * 128:(b + 1) * 128], cur[:, b * 64:(b + 1) * 64], IDENTF[:])
                    for b in range(B2):
                        src = AP(tensor=BSLAB.tensor, offset=BSLAB.offset + c * 256 + b,
                                 ap=[[16384, 64], [2, 128]])
                        t.transpose(PB0[:, b * 64:(b + 1) * 64], src, IDENT[0:64, 0:64])
                    t.mark()
                TE(rtr)
                mrt = fw.marks()["tensor"]

                def rup(v, c=c, cur=cur, nxt=nxt, mrt=mrt, lt_re=lt_re, lt_im=lt_im):
                    v.wait_ge(fw.F["tensor"], mrt)
                    dstS = AP(tensor=SSLAB.tensor, offset=SSLAB.offset + c * 256,
                              ap=[[16384, 64], [1, 2], [2, 128]])
                    srcS = AP(tensor=P4[:].tensor, offset=P4[:].offset,
                              ap=[[512, 64], [128, 2], [1, 128]])
                    v.tensor_copy(out=dstS, in_=srcS)
                    pb_re = AP(tensor=PB0[:].tensor, offset=PB0[:].offset, ap=[[1024, 128], [64, 2], [2, 32]])
                    pb_im = AP(tensor=PB0[:].tensor, offset=PB0[:].offset + 1, ap=[[1024, 128], [64, 2], [2, 32]])
                    v.tensor_tensor(out=TMPQ[:, 0:64].rearrange("p (a b) -> p a b", a=2), in0=sview(cur, 0), in1=lt_ap(lt_re), op=ALU.mult)
                    v.tensor_tensor(out=TMPQ[:, 64:128].rearrange("p (a b) -> p a b", a=2), in0=sview(cur, 1), in1=lt_ap(lt_im), op=ALU.mult)
                    v.tensor_tensor(out=TMPQ[:, 128:192], in0=TMPQ[:, 0:64], in1=TMPQ[:, 64:128], op=ALU.subtract)
                    v.tensor_tensor(out=sview(nxt, 0), in0=TMPQ[:, 128:192].rearrange("p (a b) -> p a b", a=2), in1=pb_re, op=ALU.add)
                    v.tensor_tensor(out=TMPQ[:, 0:64].rearrange("p (a b) -> p a b", a=2), in0=sview(cur, 1), in1=lt_ap(lt_re), op=ALU.mult)
                    v.tensor_tensor(out=TMPQ[:, 64:128].rearrange("p (a b) -> p a b", a=2), in0=sview(cur, 0), in1=lt_ap(lt_im), op=ALU.mult)
                    v.tensor_tensor(out=TMPQ[:, 128:192], in0=TMPQ[:, 0:64], in1=TMPQ[:, 64:128], op=ALU.add)
                    v.tensor_tensor(out=sview(nxt, 1), in0=TMPQ[:, 128:192].rearrange("p (a b) -> p a b", a=2), in1=pb_im, op=ALU.add)
                VE(rup)

            if ho == 0:
                SCOPE("sw2_0")
            for g in range(8):
                h0 = ho * 128 + g * 16
                seg = h0 * N2
                mv = fw.marks()
                fw.dma("sync", AIMSEG, hb_aim[0:1, seg:seg + 512],
                       wait=[("vector", rowp["v"] if rowp["v"] else mv["vector"]),
                             ("tensor", rowp["t"] if rowp["t"] else mv["tensor"])])
                fw.dma("sync", ARESEG, hb_are[0:1, seg:seg + 512])
                fw.dma("sync", CW2R, AP(tensor=hb_cwr.tensor, offset=seg, ap=[[0, 128], [1, 512]]))
                fw.dma("sync", CW2I, AP(tensor=hb_cwi.tensor, offset=seg, ap=[[0, 128], [1, 512]]))
                ndma = fw.dn["sync"]
                fw.dma("gpsimd", RAWS[:], hb_slab[ho, :, g * 512:(g + 1) * 512],
                       wait=[("vector", rowp["ga_v"] if rowp["ga_v"] else mv["vector"]),
                             ("q:gpsimd", raws_st["n"])])
                ngp = fw.dn["gpsimd"]

                def args2(t, ndma=ndma, mv=mv):
                    t.wait_ge(fw.Dq["sync"], ndma)
                    if rowp["bp_s"]:
                        t.wait_ge(fw.F["scalar"], rowp["bp_s"])
                    t.wait_ge(fw.F["vector"], rowp["kev_v"] if rowp["kev_v"] else mv["vector"])
                    t.matmul(P0[:], IOTAS[0:1, 256:384], AIMSEG, start=True, stop=True)
                    t.matmul(P1[:], IOTAS[0:1, 256:384], ARESEG, start=True, stop=True)
                    t.mark()
                TE(args2)
                mk_a2 = fw.marks()["tensor"]
                mk_p = chain(P0[:], P1[:], mk_a2)
                bp_family(mk_p)
                mtab2 = fw.marks()
                rowp["v"], rowp["t"] = mtab2["vector"], mk_a2
                rowp["bp_s"] = mk_p["scalar"]

                def tr3(t, mtab2=mtab2):
                    t.wait_ge(fw.F["vector"], mtab2["vector"])
                    for k in range(8):
                        t.transpose(PB0[0:64, k * 128:(k + 1) * 128], TBB[:, k * 64:(k + 1) * 64], IDENT[:])
                    t.mark()
                    for k in range(8):
                        t.transpose(PB1[0:64, k * 128:(k + 1) * 128], TBB[:, 512 + k * 64:512 + (k + 1) * 64], IDENT[:])
                    t.mark()
                TE(tr3)
                mtr3 = fw.marks()["tensor"]

                def evp(v, mtr3=mtr3):
                    v.wait_ge(fw.F["tensor"], mtr3)
                    v.tensor_copy(out=TGB[:, 0:1024], in_=PB0[0:64, :])
                    v.tensor_copy(out=TGB[:, 1024:2048], in_=PB1[0:64, :])
                VE(evp)
                mevp = fw.marks()

                def scmm(t, mevp=mevp, h0=h0, ngp=ngp):
                    t.wait_ge(fw.F["vector"], mevp["vector"])
                    t.wait_ge(fw.Dq["gpsimd"], ngp)
                    h0l = h0 % 128
                    for k in range(16):
                        sap = AP(tensor=SSLAB.tensor, offset=SSLAB.offset + (h0l + k) * 2,
                                 ap=[[16384, 64], [1, 2], [256, 16]])
                        t.matmul(P5[:, k * 32:(k + 1) * 32], TGB[:, k * 128:(k + 1) * 128],
                                 sap, start=True, stop=True)
                    t.mark()
                TE(scmm)
                msc = fw.marks()["tensor"]

                def gadd(v, msc=msc):
                    v.wait_ge(fw.F["tensor"], msc)
                    v.tensor_tensor(out=TMP2B[:, 0:512], in0=RAWS[:], in1=P5[:], op=ALU.add)
                VE(gadd)
                mga = fw.marks()
                rowp["ga_v"] = mga["vector"]
                rowp["kev_v"] = mga["vector"]

                def gel(s, mga=mga, h0=h0):
                    s.wait_ge(fw.F["vector"], mga["vector"])
                    h0l = h0 % 128
                    dstG = AP(tensor=GOSL.tensor, offset=GOSL.offset + h0l,
                              ap=[[16384, 128], [1, 16], [2048, 2], [128, 16]])
                    s.activation(out=dstG, in_=TMP2B[:, 0:512].rearrange("p (a b c) -> p a b c", a=16, b=2),
                                 func=AF.Gelu)
                SC(gel)

            if ho == 0:
                SCOPE("fmaj_0")
            mgel = fw.marks()
            fev_marks = {}
            for half in range(4):
                b_, ch_ = divmod(half, 2)
                pbuf = PB0 if half % 2 == 0 else PB1

                def ftr(t, b_=b_, ch_=ch_, pbuf=pbuf, half=half, mgel=mgel, fev_marks=fev_marks):
                    if half == 0:
                        t.wait_ge(fw.F["scalar"], mgel["scalar"])
                    if half >= 2:
                        t.wait_ge(fw.F["vector"], fev_marks[half - 2])
                    for k in range(8):
                        cc = ch_ * 8 + k
                        t.transpose(pbuf[:, k * 128:(k + 1) * 128],
                                    GOSL[:, b_ * 2048 + cc * 128:b_ * 2048 + (cc + 1) * 128], IDENT[:])
                    t.mark()
                TE(ftr)
                mtf = fw.marks()["tensor"]

                def fev(v, b_=b_, ch_=ch_, pbuf=pbuf, ho=ho, mtf=mtf):
                    v.wait_ge(fw.F["tensor"], mtf)
                    base = (b_ * 8 + ho) * 2048 + ch_ * 1024
                    v.tensor_copy(out=R2[:, base:base + 1024], in_=pbuf[:])
                VE(fev)
                fev_marks[half] = fw.marks()["vector"]
        fw.barrier()
        if PHASES < 4:
            return _finish(nc, fw, dbg_outs, dbg, R1, R2, R3, STM, PRM, BCB, AUX, hb_aim, hb_are, hb_cwr, hb_cwi, DBC)

        _ctx = dict(locals())
        _ctx["PHASES"] = PHASES
        _part2_emit(_ctx)
        SCOPE(None)
        fw.barrier()

        u_dbg = dbg("u_dbg", [128, 32768], BF16)
        if u_dbg is not None:
            fw.dma("gpsimd", u_dbg[:], R1[:])
        m_dbg = dbg("m_dbg", [128, 96], F32)
        if m_dbg is not None:
            fw.dma("gpsimd", m_dbg[:], STM[:])
        prep_dbg = dbg("prep_dbg", [128, 2048], F32)
        if prep_dbg is not None:
            fw.dma("gpsimd", prep_dbg[:], PRM[:])
        convf_dbg = dbg("convf_dbg", [128, 32768], BF16)
        if convf_dbg is not None:
            fw.dma("gpsimd", convf_dbg[:], R2[:])
        x1_dbg = dbg("x1_dbg", [128, 32768], BF16)
        if x1_dbg is not None:
            fw.dma("gpsimd", x1_dbg[:], R2[:])
        x2_dbg = dbg("x2_dbg", [128, 32768], BF16)
        if x2_dbg is not None:
            fw.dma("gpsimd", x2_dbg[:], R2[:])
        xz_dbg = dbg("xz_dbg", [128, 32768], BF16)
        if xz_dbg is not None:
            fw.dma("gpsimd", xz_dbg[:], R1[:])
        ys_dbg = dbg("ys_dbg", [128, 16384], BF16)
        if ys_dbg is not None:
            fw.dma("gpsimd", ys_dbg[:], R3[:])
        x3_dbg = dbg("x3_dbg", [128, 32768], BF16)
        if x3_dbg is not None:
            fw.dma("gpsimd", x3_dbg[:], R2[:])
        for nm, src, shp, dt_ in (
                ("scan_dbg", AUX[:, 0:4096].bitcast(F32), [128, 2048], F32),
                ("cbc2_dbg", AUX[:, 14336:14464].bitcast(F32), [128, 64], F32),
                ("hst_dbg", AUX[:, 4096:8192].bitcast(F32), [128, 2048], F32)):
            tp = dbg(nm, shp, dt_)
            if tp is not None:
                fw.dma("gpsimd", tp[:], src)
        fw.barrier()

    return nc, dbg_outs


def _finish(nc, fw, dbg_outs, dbg, R1, R2, R3, STM, PRM, BCB, AUX, hb_aim, hb_are, hb_cwr, hb_cwi, DBC=None):
    taps = {
        "u_dbg": (R1, [128, 32768], BF16), "m_dbg": (STM, [128, 96], F32),
        "prep_dbg": (PRM, [128, 2048], F32), "convf_dbg": (R2, [128, 32768], BF16),
        "k8_dbg": (R3[:, 8192:10240], [128, 2048], BF16),
        "bslab_dbg": (R3[0:64, 0:4096], [64, 4096], BF16),
        "sslab_dbg": (R3[0:64, 4096:8192], [64, 4096], BF16),
        "gosl_dbg": (R3[:, 10240:14336], [128, 4096], BF16),
        "dbc_dbg": (DBC, [128, 1024], BF16),
        "tbl_dbg": (AUX[:, 0:2048], [128, 2048], BF16),
        "tga_dbg": (AUX[0:64, 16384:20480], [64, 4096], BF16),
        "scf_dbg": (AUX[:, 4096:8192].bitcast(F32), [128, 2048], F32),
        "trig_dbg": (AUX[:, 2048:4096], [128, 2048], BF16),
        "scan_dbg": (AUX[:, 0:4096].bitcast(F32), [128, 2048], F32),
        "cbc2_dbg": (AUX[:, 14336:14464].bitcast(F32), [128, 64], F32),
        "hst_dbg": (AUX[:, 4096:8192].bitcast(F32), [128, 2048], F32),
    }
    for name, (tns, shape, dt) in taps.items():
        t = dbg(name, shape, dt)
        if t is not None:
            fw.dma("gpsimd", t[:], tns if isinstance(tns, AP) else tns[:])
    rows_dbg = dbg("rows_dbg", [4, 32768], F32)
    if rows_dbg is not None:
        fw.dma("gpsimd", rows_dbg[0:1, :], hb_aim[:])
        fw.dma("gpsimd", rows_dbg[1:2, :], hb_are[:])
        fw.dma("gpsimd", rows_dbg[2:3, :], hb_cwr[:])
        fw.dma("gpsimd", rows_dbg[3:4, :], hb_cwi[:])
    fw.barrier()
    return nc, dbg_outs


def make_consts():
    ident = np.eye(128, dtype=ml_dtypes.bfloat16)
    iota_negj = (-np.arange(128, dtype=np.float32))[None, :]
    iota_t = np.arange(128, dtype=np.float32)[None, :]
    iota_t128 = (np.arange(128, dtype=np.float32) + 128.0)[None, :]
    j = np.arange(128)[:, None]; t = np.arange(128)[None, :]
    mask_triu = (t >= j).astype(ml_dtypes.bfloat16)
    ones_row = np.ones((1, 128), np.float32)
    return dict(identb=ident, iota_negj=iota_negj, iota_t=iota_t, iota_t128=iota_t128,
                mask_triu=mask_triu, ones_row=ones_row)


def core_inputs(inputs, core):
    d = {}
    d["x_in"] = np.ascontiguousarray(inputs["L0_attn_a_u"][2 * core:2 * core + 2])
    d["log_dt"] = np.ascontiguousarray(inputs["L0_attn_a_log_dt"]).reshape(1, D)
    d["log_w_real"] = np.ascontiguousarray(inputs["L0_attn_a_log_w_real"]).reshape(1, N2)
    d["w_imag"] = np.ascontiguousarray(inputs["L0_attn_a_w_imag"]).reshape(1, N2)
    d["C_real"] = np.ascontiguousarray(inputs["L0_attn_a_C_real"]).reshape(D, N2)
    d["C_imag"] = np.ascontiguousarray(inputs["L0_attn_a_C_imag"]).reshape(D, N2)
    d["D_a"] = np.ascontiguousarray(inputs["L0_attn_a_D"]).reshape(1, D)
    d["W_out_a"] = np.ascontiguousarray(inputs["L0_attn_a_W_out"]).reshape(D, D)
    d["b_out_a"] = np.ascontiguousarray(inputs["L0_attn_a_b_out"]).reshape(1, D)
    d["tern"] = np.ascontiguousarray(inputs["L0_ffn_ternary_weight"]).reshape(D, D)
    d["bias_f"] = np.ascontiguousarray(inputs["L0_ffn_bias"]).reshape(1, D)
    d["act_scale"] = np.asarray(inputs["L0_ffn_act_scale"], np.float32).reshape(1, 1)
    d["w_in_b"] = np.ascontiguousarray(inputs["L0_attn_b_w_in"]).reshape(2 * D, D)
    d["b_in_b"] = np.ascontiguousarray(inputs["L0_attn_b_b_in"]).reshape(1, 2 * D)
    d["A_b"] = np.ascontiguousarray(inputs["L0_attn_b_A"]).reshape(D, S64)
    d["Bm_b"] = np.ascontiguousarray(inputs["L0_attn_b_Bm"]).reshape(D, S64)
    d["Cm_b"] = np.ascontiguousarray(inputs["L0_attn_b_Cm"]).reshape(D, S64)
    d["Dv_b"] = np.ascontiguousarray(inputs["L0_attn_b_Dv"]).reshape(1, D)
    d["w_out_b"] = np.ascontiguousarray(inputs["L0_attn_b_w_out"]).reshape(D, D)
    d["b_out_b"] = np.ascontiguousarray(inputs["L0_attn_b_b_out"]).reshape(1, D)
    d["g_norm"] = np.ascontiguousarray(inputs["L0_attn_b_g_norm"]).reshape(1, D)
    d.update(make_consts())
    return d


def kernel(**inputs):
    nc, _ = build_program()
    ins = [core_inputs(inputs, i) for i in range(8)]
    res = run_bass_kernel_spmd(nc, ins, core_ids=list(range(8)))
    parts = [np.asarray(res.results[i]["out"]).reshape(B2, L, D) for i in range(8)]
    return np.concatenate(parts, axis=0).astype(np.float32)
